# version 24
# baseline (speedup 1.0000x reference)
import contextlib
import numpy as np
import concourse.bass as bass
import concourse.mybir as mybir
from concourse.bass_utils import run_bass_kernel_spmd

F32 = mybir.dt.float32
F32R = mybir.dt.float32r
BF16 = mybir.dt.bfloat16
AF = mybir.ActivationFunctionType
ALU = mybir.AluOpType
AX = mybir.AxisListType

NCORES = 8
TOK = 2048
NT = 4
TT = 512
NJ = 8
DM = 1024
NCH = 32
CPT = 8
CH = 64
H = 16
DH = 64
HID = 4096
GH = 16
LNEPS = 1e-5
NEG = -1e9
PITCH = 4096


def A(t, off, dims):
    return bass.AP(t, off, dims)


class Sem:
    def __init__(self, stack, nc, name):
        self.h = stack.enter_context(nc.semaphore(name))
        self.v = 0

    def at(self):
        return (self.h, self.v)


SER_SEMS = []


class Em:

    ENG = ("sync", "gpsimd", "tensor", "vector", "scalar")

    def __init__(self, nc):
        import os
        self.nc = nc
        self.q = {e: [] for e in self.ENG}
        self.ser = bool(int(os.environ.get("EM_SERIALIZE", "0")))

    def op(self, engine, fn, waits=(), inc=None, k=1):
        waits = [w for w in waits if w is not None]
        if self.ser:
            waits = waits + [s.at() for s in SER_SEMS]
        self.q[engine].append((fn, waits, inc, k))
        if inc is not None:
            inc.v += k
        return inc.v if inc is not None else None

    def flush(self):
        with self.nc.Block() as blk:
            def runner(eng, ops):
                for fn, waits, inc, k in ops:
                    for (sh, v) in waits:
                        if v > 0:
                            eng.wait_ge(sh, v)
                    instr = fn(eng)
                    if inc is not None:
                        instr.then_inc(inc.h, k)
            for name in self.ENG:
                ops = self.q[name]
                if ops:
                    getattr(blk, name)(lambda eng, ops=ops: runner(eng, ops))
        self.q = {e: [] for e in self.ENG}



L1_SEMS = ("mm", "dv", "sc", "c", "x", "w0", "w1", "v0", "v1", "v2", "v3",
           "xns", "ss", "qs", "l0", "l1", "t0", "t1", "fin")
L2_SEMS = L1_SEMS + ("sst",)
L3_SEMS = L2_SEMS + ("kb0", "kb1", "vb0", "vb1", "qe", "ke", "me",
                     "s1st", "s1ld", "x1ld", "ost", "cc1", "cc2")


class NS:
    def __init__(self, **kw):
        self.__dict__.update(kw)


def _l1_dram(nc, kind_in="ExternalInput", kind_out="ExternalOutput", t_dt=F32):
    D = NS()
    D.x = nc.dram_tensor("x", [128, NJ, TOK], F32, kind=kind_in)
    D.wq = nc.dram_tensor("wq", [DM, DM], F32, kind=kind_in)
    D.wk = nc.dram_tensor("wk", [DM, DM], F32, kind=kind_in)
    D.wv = nc.dram_tensor("wv", [DM, DM], F32, kind=kind_in)
    D.bq = nc.dram_tensor("bq", [128, NJ], F32, kind=kind_in)
    D.bk = nc.dram_tensor("bk", [128, NJ], F32, kind=kind_in)
    D.bv = nc.dram_tensor("bv", [1, DM], F32, kind=kind_in)
    D.wg1 = nc.dram_tensor("wg1", [DM, GH], F32, kind=kind_in)
    D.bg1 = nc.dram_tensor("bg1", [GH, 1], F32, kind=kind_in)
    D.wg2 = nc.dram_tensor("wg2", [GH, DM], F32, kind=kind_in)
    D.bg2 = nc.dram_tensor("bg2", [128, NJ], F32, kind=kind_in)
    D.w1t = nc.dram_tensor("w1t", [DM, HID], F32, kind=kind_in)
    D.w2t = nc.dram_tensor("w2t", [HID, DM], F32, kind=kind_in)
    D.id64 = nc.dram_tensor("id64", [128, 128], F32, kind=kind_in)
    D.cmask = nc.dram_tensor("cmask", [128, 512], F32, kind=kind_in)
    D.smask = nc.dram_tensor("smask", [128, 512], F32, kind=kind_in)
    D.xn = nc.dram_tensor("xn", [128, NJ, TOK], F32, kind=kind_out)
    D.s = nc.dram_tensor("s", [128, NJ, TOK], F32, kind=kind_out)
    D.qh = nc.dram_tensor("qh", [128, NJ, TOK], F32, kind=kind_out)
    D.lent = nc.dram_tensor("lent", [NCH, 128, 512], F32, kind=kind_out)
    D.dcum = nc.dram_tensor("dcum", [NCH, 128, 8], F32, kind=kind_out)
    D.send = nc.dram_tensor("send", [128, 512], F32, kind=kind_out)
    D.t1 = nc.dram_tensor("t1", [DM, HID], t_dt, kind=kind_out)
    D.t2 = nc.dram_tensor("t2", [HID, DM], t_dt, kind=kind_out)
    D.m1 = nc.dram_tensor("m1", [128, 2], F32, kind=kind_out)
    D.m2 = nc.dram_tensor("m2", [128, 2], F32, kind=kind_out)
    return D


def _l1_emit(nc, st, sems, D):
    E = st.enter_context
    g = {}
    for nm in ("bA", "bB", "bC", "bD", "bE", "bF"):
        g[nm] = E(nc.sbuf_tensor(nm, [128, PITCH], F32))
    for nm in ("pE1", "pE2", "pQT", "pKT", "pKH", "pQH"):
        g[nm] = E(nc.sbuf_tensor(nm, [128, 1024], F32))
    g["wbuf"] = [E(nc.sbuf_tensor(f"wbuf{i}", [128, NJ * 128], F32)) for i in range(2)]
    g["scr"] = [E(nc.sbuf_tensor(f"scr{i}", [128, 512], F32)) for i in range(8)]
    g["wg1_sb"] = E(nc.sbuf_tensor("wg1s", [128, NJ * GH], F32))
    g["wg2_sb"] = E(nc.sbuf_tensor("wg2s", [GH, DM], F32))
    g["bg1_sb"] = E(nc.sbuf_tensor("bg1s", [GH, 1], F32))
    g["bq_sb"] = E(nc.sbuf_tensor("bqs", [128, NJ], F32))
    g["bk_sb"] = E(nc.sbuf_tensor("bks", [128, NJ], F32))
    g["bg2_sb"] = E(nc.sbuf_tensor("bg2s", [128, NJ], F32))
    g["bvbc"] = E(nc.sbuf_tensor("bvbc", [128, DM], F32))
    g["bv_row"] = E(nc.sbuf_tensor("bvrow", [1, DM], F32))
    g["id64"] = E(nc.sbuf_tensor("id64s", [128, 128], F32))
    g["cmask"] = E(nc.sbuf_tensor("cmasks", [128, 512], F32))
    g["smask"] = E(nc.sbuf_tensor("smasks", [128, 512], F32))
    g["ones_c"] = E(nc.sbuf_tensor("onesc", [128, 1], F32))
    g["ones_r"] = E(nc.sbuf_tensor("onesr", [1, 128], F32))
    g["g1_sb"] = E(nc.sbuf_tensor("g1s", [GH, TT], F32))
    g["P_sb"] = E(nc.sbuf_tensor("Psb", [128, 1024], F32))
    g["PT_sb"] = E(nc.sbuf_tensor("PTsb", [128, 1024], F32))
    g["KT_sb"] = E(nc.sbuf_tensor("KTsb", [128, 1024], F32))
    g["At_sb"] = E(nc.sbuf_tensor("Atsb", [128, 1024], F32))
    g["S_all"] = E(nc.sbuf_tensor("Sall", [128, 512], F32))
    g["lstage"] = [E(nc.sbuf_tensor(f"lst{i}", [128, 512], F32)) for i in range(2)]
    g["msb"] = E(nc.sbuf_tensor("msb", [128, 64], F32))
    g["tcb"] = E(nc.sbuf_tensor("tcb", [128, 64], F32))
    g["tcm"] = E(nc.sbuf_tensor("tcms", [128, 64], F32))
    g["etb"] = E(nc.sbuf_tensor("etb", [128, 64], F32))
    g["tcum"] = E(nc.sbuf_tensor("tcum", [128, 8], F32))
    g["dstage"] = E(nc.sbuf_tensor("dstage", [128, NCH * 8], F32))
    g["redc"] = E(nc.sbuf_tensor("redc", [128, 64], F32))
    g["mcols"] = E(nc.sbuf_tensor("mcols", [128, 2], F32))
    g["mrow"] = E(nc.sbuf_tensor("mrow", [1, 2], F32))
    g["sgb"] = E(nc.sbuf_tensor("sgb", [128, 512], F32))
    g["sgb2"] = E(nc.sbuf_tensor("sgb2", [128, 512], F32))
    g["cns"] = E(nc.sbuf_tensor("cns", [128, 2], F32))
    g["tqb"] = [E(nc.sbuf_tensor(f"tqb{i}", [128, 512], F32)) for i in range(2)]
    g["tscr"] = [E(nc.sbuf_tensor(f"tscr{i}", [128, 512], F32)) for i in range(2)]
    g["tred"] = E(nc.sbuf_tensor("tred", [128, 128], F32))
    g["pb"] = [E(nc.psum_tensor(f"pb{i}", [128, 512], F32)) for i in range(8)]
    g.update(sems)
    G = type("G", (), g)

    import os
    lim = int(os.environ.get("L1_PHASES", "99"))
    ph = 0
    with nc.named_scope("l1c"):
        _l1_consts(nc, G, D.id64, D.cmask, D.smask, D.bq, D.bk, D.bg2, D.bg1,
                   D.wg2, D.bv, D.wg1)
    ph += 1
    gens = [_l1_tern_p1(nc, G, D.w1t, DM, HID, 0),
            _l1_tern_p1(nc, G, D.w2t, HID, DM, 64)]

    def drive(k):
        em = Em(nc)
        G.tern_em = em
        while k > 0 and gens:
            try:
                next(gens[0])
                k -= 1
            except StopIteration:
                gens.pop(0)
        em.flush()

    for t in range(NT):
        if ph < lim:
            with nc.named_scope(f"l1p{t}"):
                _l1_tile_proj(nc, G, t, D.x, D.xn, D.wq, D.wk, D.wv)
            ph += 1
        drive(16)
        if ph < lim:
            with nc.named_scope(f"l1k{t}"):
                _l1_tile_chunks(nc, G, t, D.s, D.qh, D.lent)
            ph += 1
        drive(16)
    if ph < lim:
        with nc.named_scope("l1f"):
            _l1_finalize(nc, G, D.send, D.dcum)
        ph += 1
    drive(1 << 30)
    if ph < lim:
        with nc.named_scope("l1t1"):
            _l1_ternary(nc, G, D.w1t, D.t1, D.m1, DM, HID, 0)
        ph += 1
    if ph < lim:
        with nc.named_scope("l1t2"):
            _l1_ternary(nc, G, D.w2t, D.t2, D.m2, HID, DM, 64)
    return G


def build_launch1():
    nc = bass.Bass("TRN2", target_bir_lowering=False)
    D = _l1_dram(nc)
    with contextlib.ExitStack() as st:
        sems = {nm: Sem(st, nc, nm) for nm in L1_SEMS}
        SER_SEMS.clear()
        SER_SEMS.extend([sems["mm"], sems["dv"], sems["sc"]])
        with contextlib.ExitStack() as st_b:
            G = _l1_emit(nc, st_b, sems, D)
        em = Em(nc)
        em.op("gpsimd", lambda eng: eng.nop(),
              waits=[G.fin.at(), G.xns.at(), G.ss.at(), G.qs.at(),
                     G.l0.at(), G.l1.at(), G.t0.at(), G.t1.at()])
        em.flush()
    return nc


def _l1_consts(nc, G, id64_d, cmask_d, smask_d, bq_d, bk_d, bg2_d, bg1_d,
               wg2_d, bv_d, wg1_d):
    em = Em(nc)
    ld, dv, mm = G.c, G.dv, G.mm

    for dst, src in (
        (G.id64[:], id64_d.ap()), (G.cmask[:], cmask_d.ap()), (G.smask[:], smask_d.ap()),
        (G.bq_sb[:], bq_d.ap()), (G.bk_sb[:], bk_d.ap()), (G.bg2_sb[:], bg2_d.ap()),
        (G.bg1_sb[:], bg1_d.ap()), (G.wg2_sb[:], wg2_d.ap()), (G.bv_row[:], bv_d.ap()),
        (G.wg1_sb[:], A(wg1_d, 0, [[GH, 128], [128 * GH, NJ], [1, GH]])),
    ):
        em.op("sync", (lambda eng, a=dst, b=src: eng.dma_start(a, b)), inc=ld, k=16)

    def f(eng):
        eng.memset(G.ones_c[:], 1.0)
        eng.memset(G.ones_r[:], 1.0)
        eng.memset(G.tcum[:], 0.0)
        eng.memset(G.cns[:, 0:1], float(LNEPS))
        eng.memset(G.cns[:, 1:2], 0.1)
        return eng.memset(G.S_all[:], 0.0)
    em.op("vector", f, inc=dv)

    def f(eng):
        eng.matmul(G.pb[0][:, :], G.ones_r[:], G.bv_row[0:1, 0:512], start=True, stop=True)
        return eng.matmul(G.pb[1][:, :], G.ones_r[:], G.bv_row[0:1, 512:1024],
                          start=True, stop=True)
    em.op("tensor", f, waits=[ld.at(), dv.at()], inc=mm)

    def f(eng):
        eng.tensor_copy(G.bvbc[:, 0:512], G.pb[0][:, :])
        return eng.tensor_copy(G.bvbc[:, 512:1024], G.pb[1][:, :])
    em.op("vector", f, waits=[mm.at()], inc=dv)

    em.flush()


def _l1_tile_proj(nc, G, t, x_d, xn_d, wq_d, wk_d, wv_d):
    em = Em(nc)
    mm, dv, sc = G.mm, G.dv, G.sc
    wsem = [G.w0, G.w1]
    vsem = [G.v0, G.v1]
    bA, bB, bC, bD, bE, bF = G.bA, G.bB, G.bC, G.bD, G.bE, G.bF
    pb = G.pb
    t0 = t * TT

    def xv(buf, j, lo=0, n=TT):
        return A(buf, j * TT + lo, [[PITCH, 128], [1, n]])

    for js in range(0, NJ, 2):
        em.op("sync", (lambda eng, js=js: eng.dma_start(
            A(bA, js * TT, [[PITCH, 128], [TT, 2], [1, TT]]),
            A(x_d, t0 + js * TOK, [[NJ * TOK, 128], [TOK, 2], [1, TT]]))),
            waits=[G.c.at()] if js == 0 else [], inc=G.x, k=16)
    w_x = G.x.at()

    mm_sq = {}
    for j in range(NJ):
        waits = [w_x] + ([(mm.h, mm_sq[j - 2])] if j >= 2 else [])
        em.op("scalar", (lambda eng, j=j: eng.activation(
            G.scr[j % 2][:], xv(bA, j), AF.Square)), waits=waits, inc=sc)
        w_sqj = sc.at()
        em.op("tensor", (lambda eng, j=j: eng.matmul(
            pb[1][0:1, :], G.ones_c[:], G.scr[j % 2][:],
            start=(j == 0), stop=(j == NJ - 1))), waits=[w_sqj], inc=mm)
        mm_sq[j] = mm.v
    w_sq_done = mm.at()

    def f(eng):
        for j in range(NJ):
            ii = eng.matmul(pb[0][0:1, :], G.ones_c[:], xv(bA, j),
                            start=(j == 0), stop=(j == NJ - 1))
        return ii
    em.op("tensor", f, waits=[w_x], inc=mm)
    w_stats = mm.at()

    def r1(off, n=TT):
        return A(G.pE1, off, [[1024, 1], [1, n]])

    def r2(off, n=TT):
        return A(G.pE2, off, [[1024, 1], [1, n]])

    def f(eng):
        eng.tensor_scalar(r1(0), pb[0][0:1, :], 1.0 / DM, None, ALU.mult)
        eng.tensor_scalar(r1(512), pb[1][0:1, :], 1.0 / DM, None, ALU.mult)
        eng.tensor_tensor(r2(0), r1(0), r1(0), ALU.mult)
        return eng.tensor_tensor(r2(512), r1(512), r2(0), ALU.subtract)
    em.op("vector", f, waits=[w_stats], inc=dv)

    def f(eng):
        eng.activation(r2(512), r2(512), AF.Ln, bias=G.cns[0:1, 0:1])
        return eng.activation(r2(512), r2(512), AF.Exp, scale=-0.5)
    em.op("scalar", f, waits=[dv.at()], inc=sc)

    em.op("vector", lambda eng: eng.scalar_tensor_tensor(
        r2(0), r1(0), -1.0, r2(512), ALU.mult, ALU.mult), waits=[sc.at()], inc=dv)

    def f(eng):
        eng.matmul(pb[2][:, :], G.ones_r[:], r2(512), start=True, stop=True)
        return eng.matmul(pb[3][:, :], G.ones_r[:], r2(0), start=True, stop=True)
    em.op("tensor", f, waits=[dv.at()], inc=mm)
    w_bc = mm.at()

    for j in range(NJ):
        def f(eng, j=j):
            eng.tensor_tensor(xv(bB, j), xv(bA, j), pb[2][:, :], ALU.mult)
            return eng.tensor_tensor(xv(bB, j), xv(bB, j), pb[3][:, :], ALU.add)
        em.op("vector", f, waits=([w_bc, G.ss.at()] if j == 0 else []), inc=dv)
    w_xn = dv.at()

    for js in range(0, NJ, 2):
        em.op("gpsimd", (lambda eng, js=js: eng.dma_start(
            A(xn_d, t0 + js * TOK, [[NJ * TOK, 128], [TOK, 2], [1, TT]]),
            A(bB, js * TT, [[PITCH, 128], [TT, 2], [1, TT]]))),
            waits=[w_xn] if js == 0 else [], inc=G.xns, k=16)

    wplan = [(wq_d, G.bq_sb, bC, oc) for oc in range(8)] + \
            [(wk_d, G.bk_sb, bD, oc) for oc in range(8)]
    grp_done = {}
    ev_done = {}
    for n, (w_d_, b_sb, dst, oc) in enumerate(wplan):
        bi = n % 2
        dwaits = [(mm.h, grp_done[n - 2])] if n >= 2 else []
        em.op("sync", (lambda eng, w_d_=w_d_, oc=oc, bi=bi: eng.dma_start(
            G.wbuf[bi][:], A(w_d_, 128 * oc, [[DM, 128], [128 * DM, NJ], [1, 128]]))),
            waits=dwaits, inc=wsem[bi], k=16)
        w_wld = wsem[bi].at()
        bank = pb[4 + bi]
        mwaits = [w_wld, w_xn] + ([(dv.h, ev_done[n - 2])] if n >= 2 else [])

        def f(eng, bi=bi, bank=bank):
            for kj in range(NJ):
                ii = eng.matmul(bank[:, :], A(G.wbuf[bi], kj * 128, [[NJ * 128, 128], [1, 128]]),
                                xv(bB, kj), start=(kj == 0), stop=(kj == NJ - 1))
            return ii
        em.op("tensor", f, waits=mwaits, inc=mm)
        grp_done[n] = mm.v
        em.op("vector", (lambda eng, bank=bank, b_sb=b_sb, dst=dst, oc=oc:
                         eng.tensor_scalar(xv(dst, oc), bank[:, :], b_sb[:, oc:oc + 1],
                                           None, ALU.add)),
              waits=[(mm.h, grp_done[n])], inc=dv)
        ev_done[n] = dv.v

    def f(eng):
        for j in range(NJ):
            ii = eng.matmul(pb[6][0:GH, :], A(G.wg1_sb, j * GH, [[NJ * GH, 128], [1, GH]]),
                            xv(bB, j), start=(j == 0), stop=(j == NJ - 1))
        return ii
    em.op("tensor", f, waits=[w_xn], inc=mm)
    em.op("scalar", lambda eng: eng.activation(
        G.g1_sb[:, :], pb[6][0:GH, :], AF.Silu, bias=G.bg1_sb[:, :]),
        waits=[mm.at()], inc=sc)
    w_g1 = sc.at()

    gl_marks = {}
    for j in range(NJ):
        waits = [w_g1] + ([(sc.h, gl_marks[j - 2])] if j >= 2 else [])
        em.op("tensor", (lambda eng, j=j: eng.matmul(
            pb[6 + (j % 2)][:, :], G.wg2_sb[0:GH, 128 * j:128 * (j + 1)], G.g1_sb[:, :],
            start=True, stop=True)), waits=waits, inc=mm)
        em.op("scalar", (lambda eng, j=j: eng.activation(
            xv(bE, j), pb[6 + (j % 2)][:, :], AF.Sigmoid, bias=G.bg2_sb[:, j:j + 1])),
            waits=[mm.at()], inc=sc)
        gl_marks[j] = sc.v
    for j in range(NJ):
        em.op("scalar", (lambda eng, j=j: eng.activation(
            xv(bE, j), xv(bE, j), AF.Ln, scale=0.9, bias=G.cns[:, 1:2])), inc=sc)
    w_L = sc.at()

    vmm = {}
    vev = {}
    for half in range(2):
        for kj in range(NJ):
            n = half * NJ + kj
            bi = n % 2
            dwaits = [w_sq_done] + ([(mm.h, vmm[n - 2])] if n >= 2 else [])
            em.op("sync", (lambda eng, kj=kj, half=half, bi=bi: eng.dma_start(
                G.scr[bi][:], A(wv_d, kj * 128 * DM + half * 512, [[DM, 128], [1, 512]]))),
                waits=dwaits, inc=vsem[bi], k=16)
            w_vld = vsem[bi].at()
            mwaits = [w_vld, w_xn]
            if kj == 0 and half == 1:
                mwaits.append((dv.h, vev[0]))

            def f(eng, kj=kj, bi=bi):
                for ts in range(4):
                    ii = eng.matmul(pb[ts][:, :],
                                    A(bB, kj * TT + ts * 128, [[PITCH, 128], [1, 128]]),
                                    G.scr[bi][:], start=(kj == 0), stop=(kj == NJ - 1))
                return ii
            em.op("tensor", f, waits=mwaits, inc=mm)
            vmm[n] = mm.v
        for ts in range(4):
            em.op("vector", (lambda eng, ts=ts, half=half: eng.tensor_tensor(
                A(bA, ts * 1024 + half * 512, [[PITCH, 128], [1, 512]]),
                pb[ts][:, :], G.bvbc[:, half * 512:(half + 1) * 512], ALU.add)),
                waits=[(mm.h, vmm[half * NJ + NJ - 1])], inc=dv)
        vev[half] = dv.v

    for j in range(NJ):
        for c in range(CPT):
            em.op("vector", (lambda eng, j=j, c=c: eng.tensor_tensor_scan(
                A(bF, j * TT + c * CH, [[PITCH, 128], [1, CH]]),
                A(bE, j * TT + c * CH, [[PITCH, 128], [1, CH]]),
                A(bE, j * TT + c * CH, [[PITCH, 128], [1, CH]]),
                0.0, ALU.add, ALU.bypass)),
                waits=[w_L] if (j == 0 and c == 0) else [], inc=dv)

    def f(eng):
        eng.tensor_copy(G.msb[:, :], A(bF, 31, [[PITCH, 128], [TT, NJ], [CH, CPT]]))
        return eng.tensor_copy(G.tcb[:, :], A(bF, 63, [[PITCH, 128], [TT, NJ], [CH, CPT]]))
    em.op("vector", f, inc=dv)
    em.op("scalar", lambda eng: eng.activation(G.etb[:, :], G.tcb[:, :], AF.Exp),
          waits=[dv.at()], inc=sc)
    em.op("vector", lambda eng: eng.tensor_tensor(
        G.tcb[:, :], G.tcb[:, :], G.msb[:, :], ALU.subtract), waits=[sc.at()], inc=dv)
    em.op("scalar", lambda eng: eng.activation(G.tcm[:, :], G.tcb[:, :], AF.Exp),
          waits=[dv.at()], inc=sc)

    def f(eng):
        for j in range(NJ):
            ii = eng.tensor_tensor(xv(bE, j), xv(bF, j), xv(bE, j), ALU.subtract)
        return ii
    em.op("vector", f, inc=dv)

    em.op("vector", lambda eng: eng.memset(bB[:, :], 0.0),
          waits=[mm.at(), G.xns.at()], inc=dv)

    em.flush()


def _l1_tile_chunks(nc, G, t, s_d, qh_d, lent_d):
    import os
    sub = int(os.environ.get("L1_CHUNK_SUB", "99"))
    em = Em(nc)
    mm, dv, sc = G.mm, G.dv, G.sc
    lsem = [G.l0, G.l1]
    bA, bB, bC, bD, bE, bF = G.bA, G.bB, G.bC, G.bD, G.bE, G.bF
    pb = G.pb
    t0 = t * TT

    last_soft = {0: 0, 1: 0}
    last_at = 0
    last_sadd = 0
    last_ds = 0
    last_tcopy = 0
    prev_pair_mm = 0
    prev_ptr_mm = 0

    npair = int(os.environ.get("L1_PAIRS", "4"))
    for p in range(npair):
        c0, c1 = 2 * p, 2 * p + 1
        def f(eng, p=p):
            for j in range(NJ):
                eng.tensor_tensor(
                    A(G.pE1, j * 128, [[1024, 128], [1, 128]]),
                    A(bE, j * TT + p * 128, [[PITCH, 128], [1, 128]]),
                    A(G.msb, j * CPT + 2 * p, [[64, 128], [1, 2], [0, 64]]),
                    ALU.subtract)
                ii = eng.tensor_tensor(
                    A(G.pE2, j * 128, [[1024, 128], [1, 128]]),
                    A(bF, j * TT + p * 128, [[PITCH, 128], [1, 128]]),
                    A(G.msb, j * CPT + 2 * p, [[64, 128], [1, 2], [0, 64]]),
                    ALU.subtract)
            return ii
        em.op("vector", f, waits=[sc.at(), (mm.h, prev_pair_mm)], inc=dv)

        def f(eng):
            eng.activation(G.pE1[:, :], G.pE1[:, :], AF.Exp)
            return eng.activation(G.pE2[:, :], G.pE2[:, :], AF.Exp, scale=-1.0)
        em.op("scalar", f, waits=[dv.at()], inc=sc)
        w_e12 = sc.at()

        def f(eng, p=p):
            for j in range(NJ):
                eng.tensor_tensor(A(G.pQT, j * 128, [[1024, 128], [1, 128]]),
                                  A(bC, j * TT + p * 128, [[PITCH, 128], [1, 128]]),
                                  A(G.pE1, j * 128, [[1024, 128], [1, 128]]), ALU.mult)
                eng.tensor_tensor(A(G.pKT, j * 128, [[1024, 128], [1, 128]]),
                                  A(bD, j * TT + p * 128, [[PITCH, 128], [1, 128]]),
                                  A(G.pE2, j * 128, [[1024, 128], [1, 128]]), ALU.mult)
                ii = eng.tensor_tensor(
                    A(G.pKH, j * 128, [[1024, 128], [1, 128]]),
                    A(G.pKT, j * 128, [[1024, 128], [1, 128]]),
                    A(G.tcm, j * CPT + 2 * p, [[64, 128], [1, 2], [0, 64]]), ALU.mult)
            return ii
        em.op("vector", f, waits=[w_e12], inc=dv)
        w_qkt = dv.at()

        em.op("scalar", (lambda eng, p=p: eng.activation(
            G.pE1[:, :], A(bE, p * 128, [[PITCH, 128], [TT, NJ], [1, 128]]), AF.Exp)),
            waits=[w_qkt], inc=sc)

        def f(eng, p=p):
            for j in range(NJ):
                ii = eng.tensor_tensor(A(G.pQH, j * 128, [[1024, 128], [1, 128]]),
                                       A(bC, j * TT + p * 128, [[PITCH, 128], [1, 128]]),
                                       A(G.pE1, j * 128, [[1024, 128], [1, 128]]), ALU.mult)
            return ii
        em.op("vector", f, waits=[sc.at(), G.qs.at()], inc=dv)

        em.op("gpsimd", (lambda eng, p=p: eng.dma_start(
            A(qh_d, t0 + p * 128, [[NJ * TOK, 128], [TOK, NJ], [1, 128]]),
            A(G.pQH, 0, [[1024, 128], [128, NJ], [1, 128]]))),
            waits=[dv.at()], inc=G.qs, k=16)

        if sub < 2:
            continue
        for ci, c in enumerate((c0, c1)):
            bank = pb[ci]

            for par in (0, 1):
                def f(eng, c=c, bank=bank, par=par):
                    for h in range(par, H, 2):
                        ii = eng.matmul(
                            A(bank, (h % 2) * 64 * 512 + (h // 2) * 64, [[512, 64], [1, 64]]),
                            A(bC, (h % 2) * 64 * PITCH + (h // 2) * TT + c * 64, [[PITCH, 64], [1, 64]]),
                            A(bD, (h % 2) * 64 * PITCH + (h // 2) * TT + c * 64, [[PITCH, 64], [1, 64]]),
                            start=True, stop=True,
                            tile_position=(64 * (h % 2), 64 * (h % 2)))
                    return ii
                em.op("tensor", f, waits=[(dv.h, last_soft[ci])] if par == 0 else [], inc=mm)
            w_sc_mm = mm.at()

            em.op("vector", (lambda eng, bank=bank: eng.tensor_reduce(
                G.redc[:, 0:8], A(bank, 0, [[512, 128], [64, 8], [1, 64]]),
                AX.X, ALU.max)), waits=[w_sc_mm], inc=dv)
            em.op("vector", (lambda eng, bank=bank: eng.tensor_tensor(
                G.sgb[:, :], bank[:, :], G.cmask[:, :], ALU.add)), inc=dv)
            em.op("vector", (lambda eng: eng.tensor_tensor(
                G.sgb[:, :], G.sgb[:, :], A(G.redc, 0, [[64, 128], [1, 8], [0, 64]]),
                ALU.subtract)), inc=dv)
            em.op("scalar", lambda eng: eng.activation(G.sgb[:, :], G.sgb[:, :], AF.Exp,
                                                       scale=0.125),
                  waits=[dv.at()], inc=sc)

            em.op("vector", lambda eng: eng.tensor_reduce(
                G.redc[:, 8:16], A(G.sgb, 0, [[512, 128], [64, 8], [1, 64]]),
                AX.X, ALU.add), waits=[sc.at()], inc=dv)
            def f(eng):
                eng.activation(G.redc[:, 16:24], G.redc[:, 8:16], AF.Ln)
                return eng.activation(G.redc[:, 16:24], G.redc[:, 16:24], AF.Exp, scale=-1.0)
            em.op("scalar", f, waits=[dv.at()], inc=sc)
            em.op("vector", (lambda eng, ci=ci: eng.tensor_tensor(
                A(G.P_sb, ci * 64, [[1024, 128], [128, 8], [1, 64]]), G.sgb[:, :],
                A(G.redc, 16, [[64, 128], [1, 8], [0, 64]]), ALU.mult)),
                waits=[sc.at(), (mm.h, prev_ptr_mm)], inc=dv)
            last_soft[ci] = dv.v
        w_soft = dv.at()

        if sub < 3:
            continue
        for par in (0, 1):
            def f(eng, par=par):
                for h in range(par, H, 2):
                    ii = eng.transpose(
                        A(pb[2 + h % 2], (h // 2) * 64, [[512, 128], [1, 64]]),
                        A(G.P_sb, (h % 2) * 64 * 1024 + (h // 2) * 128, [[1024, 64], [1, 128]]),
                        G.id64[(h % 2) * 64:(h % 2) * 64 + 64, 0:64])
                return ii
            em.op("tensor", f, waits=[w_soft, (dv.h, last_tcopy)] if par == 0 else [], inc=mm)
        prev_ptr_mm = mm.v

        def f(eng):
            eng.tensor_copy(A(G.PT_sb, 0, [[1024, 128], [128, 8], [1, 64]]),
                            A(pb[2], 0, [[512, 128], [64, 8], [1, 64]]))
            return eng.tensor_copy(A(G.PT_sb, 64, [[1024, 128], [128, 8], [1, 64]]),
                                   A(pb[3], 0, [[512, 128], [64, 8], [1, 64]]))
        em.op("vector", f, waits=[mm.at()], inc=dv)
        w_ptc = dv.at()

        for par in (0, 1):
            def f(eng, par=par):
                for h in range(par, H, 2):
                    ii = eng.transpose(
                        A(pb[2 + h % 2], (h // 2) * 64, [[512, 128], [1, 64]]),
                        A(G.pKH, (h % 2) * 64 * 1024 + (h // 2) * 128, [[1024, 64], [1, 128]]),
                        G.id64[(h % 2) * 64:(h % 2) * 64 + 64, 0:64])
                return ii
            em.op("tensor", f, waits=[w_ptc] if par == 0 else [], inc=mm)

        def f(eng):
            eng.tensor_copy(A(G.KT_sb, 0, [[1024, 128], [128, 8], [1, 64]]),
                            A(pb[2], 0, [[512, 128], [64, 8], [1, 64]]))
            return eng.tensor_copy(A(G.KT_sb, 64, [[1024, 128], [128, 8], [1, 64]]),
                                   A(pb[3], 0, [[512, 128], [64, 8], [1, 64]]))
        em.op("vector", f, waits=[mm.at()], inc=dv)
        last_tcopy = dv.v

        if sub < 4:
            continue
        for ci, c in enumerate((c0, c1)):
            cb = 64 * (c % 2)
            n_glob = t * CPT + c

            for par in (0, 1):
                def f(eng, c=c, cb=cb, par=par):
                    for h in range(par, H, 2):
                        ii = eng.matmul(
                            A(pb[4 + h % 2], cb * 512 + (h // 2) * 64, [[512, 64], [1, 64]]),
                            A(G.pKT, (h % 2) * 64 * 1024 + (h // 2) * 128 + (c % 2) * 64,
                              [[1024, 64], [1, 64]]),
                            A(G.pQT, (h % 2) * 64 * 1024 + (h // 2) * 128 + (c % 2) * 64,
                              [[1024, 64], [1, 64]]),
                            start=True, stop=True,
                            tile_position=(64 * (h % 2), cb))
                    return ii
                em.op("tensor", f,
                      waits=[(dv.h, last_at), (dv.h, last_tcopy)] if par == 0 else [], inc=mm)
            w_atmm = mm.at()

            def f(eng, cb=cb):
                eng.tensor_tensor(A(G.At_sb, cb * 1024, [[1024, 64], [128, 8], [1, 64]]),
                                  A(pb[4], cb * 512, [[512, 64], [64, 8], [1, 64]]),
                                  A(G.smask, cb * 512, [[512, 64], [64, 8], [1, 64]]), ALU.mult)
                return eng.tensor_tensor(A(G.At_sb, cb * 1024 + 64, [[1024, 64], [128, 8], [1, 64]]),
                                         A(pb[5], cb * 512, [[512, 64], [64, 8], [1, 64]]),
                                         A(G.smask, cb * 512, [[512, 64], [64, 8], [1, 64]]), ALU.mult)
            em.op("vector", f, waits=[w_atmm], inc=dv)
            last_at = dv.v
            w_at = dv.at()

            if sub < 5:
                continue

            def f(eng, c=c, cb=cb):
                for h in range(H):
                    vsl = A(bA, cb * PITCH + (c // 2) * 1024 + h * 64, [[PITCH, 64], [1, 64]])
                    eng.matmul(
                        A(pb[6], (h % 2) * 64 * 512 + (h // 2) * 64, [[512, 64], [1, 64]]),
                        vsl,
                        A(G.PT_sb, cb * 1024 + h * 64, [[1024, 64], [1, 64]]),
                        start=True, stop=False, tile_position=(cb, 64 * (h % 2)))
                    ii = eng.matmul(
                        A(pb[6], (h % 2) * 64 * 512 + (h // 2) * 64, [[512, 64], [1, 64]]),
                        vsl,
                        A(G.At_sb, cb * 1024 + h * 64, [[1024, 64], [1, 64]]),
                        start=False, stop=True, tile_position=(cb, 64 * (h % 2)))
                return ii
            em.op("tensor", f, waits=[w_at, (dv.h, last_sadd)], inc=mm)
            w_pv = mm.at()

            if sub < 6:
                continue

            def f(eng, c=c, cb=cb):
                for h in range(H):
                    vsl = A(bA, cb * PITCH + (c // 2) * 1024 + h * 64, [[PITCH, 64], [1, 64]])
                    ii = eng.matmul(
                        A(pb[7], (h % 2) * 64 * 512 + (h // 2) * 64, [[512, 64], [1, 64]]),
                        A(G.KT_sb, cb * 1024 + h * 64, [[1024, 64], [1, 64]]),
                        vsl,
                        start=True, stop=True, tile_position=(cb, 64 * (h % 2)))
                return ii
            em.op("tensor", f, waits=[(dv.h, last_ds)], inc=mm)
            w_ds = mm.at()

            def f(eng, c=c):
                return eng.tensor_tensor(
                    A(bB, c * 64, [[PITCH, 128], [TT, NJ], [1, 64]]),
                    A(bB, c * 64, [[PITCH, 128], [TT, NJ], [1, 64]]),
                    A(pb[6], 0, [[512, 128], [64, NJ], [1, 64]]), ALU.add)
            em.op("vector", f, waits=[w_pv], inc=dv)
            last_sadd = dv.v

            def f(eng, ci=ci):
                return eng.tensor_copy(G.lstage[ci][:, :], G.S_all[:, :])
            em.op("vector", f, waits=[lsem[ci].at()], inc=dv)
            w_snap = dv.at()
            em.op("gpsimd", (lambda eng, ci=ci, n_glob=n_glob: eng.dma_start(
                A(lent_d, n_glob * 128 * 512, [[512, 128], [1, 512]]),
                G.lstage[ci][:, :])), waits=[w_snap], inc=lsem[ci], k=16)

            em.op("scalar", (lambda eng, n_glob=n_glob: eng.activation(
                A(G.dstage, n_glob * 8, [[NCH * 8, 128], [1, 8]]), G.tcum[:, :], AF.Exp)),
                waits=[dv.at()], inc=sc)
            em.op("vector", (lambda eng, c=c: eng.tensor_tensor(
                G.tcum[:, :], G.tcum[:, :],
                A(bF, c * 64 + 63, [[PITCH, 128], [TT, NJ]]), ALU.add)),
                waits=[sc.at()], inc=dv)

            def f(eng, c=c):
                eng.tensor_tensor(G.S_all[:, :], G.S_all[:, :],
                                  A(G.etb, c, [[64, 128], [8, 8], [0, 64]]), ALU.mult)
                return eng.tensor_tensor(G.S_all[:, :], G.S_all[:, :],
                                         A(pb[7], 0, [[512, 128], [64, 8], [1, 64]]), ALU.add)
            em.op("vector", f, waits=[w_snap, w_ds], inc=dv)
            last_ds = dv.v
        prev_pair_mm = mm.v

    for js in range(0, NJ, 2):
        em.op("gpsimd", (lambda eng, js=js: eng.dma_start(
            A(s_d, t0 + js * TOK, [[NJ * TOK, 128], [TOK, 2], [1, TT]]),
            A(bB, js * TT, [[PITCH, 128], [TT, 2], [1, TT]]))),
            waits=[dv.at()] if js == 0 else [], inc=G.ss, k=16)

    em.flush()


def _l1_finalize(nc, G, send_d, dcum_d):
    em = Em(nc)

    em.op("gpsimd", lambda eng: eng.dma_start(send_d.ap(), G.S_all[:, :]),
          waits=[G.dv.at(), G.sc.at()], inc=G.fin, k=16)
    em.op("gpsimd", lambda eng: eng.dma_start(
        A(dcum_d, 0, [[8, 128], [1024, NCH], [1, 8]]), G.dstage[:, :]), inc=G.fin, k=16)
    em.flush()


def _l1_tern_p1(nc, G, w_d, nrows, ncols, roff):
    dv = G.dv
    vsem = [G.v2, G.v3]
    ncc = ncols // 512
    ntile = (nrows // 128) * ncc
    red_marks = {}
    w_prev = (dv.h, dv.v)
    for n in range(ntile):
        pc, cc = n // ncc, n % ncc
        em = G.tern_em
        dwaits = [(dv.h, red_marks[n - 2])] if n >= 2 else [w_prev]
        em.op("sync", (lambda eng, pc=pc, cc=cc, bi=n % 2: eng.dma_start(
            G.tscr[bi][:], A(w_d, pc * 128 * ncols + cc * 512, [[ncols, 128], [1, 512]]))),
            waits=dwaits, inc=vsem[n % 2], k=16)
        em.op("vector", (lambda eng, n=n, bi=n % 2, roff=roff: eng.tensor_reduce(
            G.tred[:, roff + n:roff + n + 1], G.tscr[bi][:], AX.X, ALU.add,
            apply_absolute_value=True)),
            waits=[vsem[n % 2].at()], inc=dv)
        red_marks[n] = dv.v
        yield


def _l1_ternary(nc, G, w_d, t_out, m_out, nrows, ncols, roff):
    mm, dv, sc = G.mm, G.dv, G.sc
    vsem = [G.v0, G.v1, G.v2, G.v3]
    tsem = [G.t0, G.t1]
    npc = nrows // 128
    ncc = ncols // 512
    ntile = npc * ncc
    assert ntile <= 64

    em = Em(nc)
    em.op("vector", (lambda eng: eng.tensor_reduce(
        G.mcols[:, 0:1], G.tred[:, roff:roff + ntile], AX.X, ALU.add)),
        waits=[G.fin.at()], inc=dv)
    em.op("tensor", lambda eng: eng.matmul(
        G.pb[0][0:1, 0:1], G.mcols[:, 0:1], G.ones_c[:, :], start=True, stop=True),
        waits=[dv.at()], inc=mm)

    em.op("vector", lambda eng: eng.tensor_scalar(
        G.mrow[0:1, 0:1], G.pb[0][0:1, 0:1], 1.0 / (nrows * ncols), None, ALU.mult),
        waits=[mm.at()], inc=dv)

    def f(eng):
        eng.tensor_scalar(G.mrow[0:1, 0:1], G.mrow[0:1, 0:1], 1e-5, None, ALU.max)
        return eng.tensor_scalar(G.mrow[0:1, 1:2], G.mrow[0:1, 0:1], 0.5, None, ALU.mult)
    em.op("vector", f, waits=[dv.at()], inc=dv)
    em.op("tensor", lambda eng: eng.matmul(
        G.pb[1][:, 0:2], G.ones_r[:], G.mrow[0:1, 0:2], start=True, stop=True),
        waits=[dv.at()], inc=mm)
    em.op("vector", lambda eng: eng.tensor_copy(G.mcols[:, :], G.pb[1][:, 0:2]),
          waits=[mm.at()], inc=dv)
    w_m = dv.at()
    em.op("gpsimd", lambda eng: eng.dma_start(m_out.ap(), G.mcols[:, :]),
          waits=[w_m], inc=G.fin, k=16)
    em.flush()

    em = Em(nc)
    sgbs = [G.sgb, G.sgb2]
    thr_marks = {}
    w_prev = (dv.h, dv.v)
    for n in range(ntile):
        pc, cc = n // ncc, n % ncc
        bi = n % 4
        si = n % 2
        dwaits = [(dv.h, thr_marks[n - 4])] if n >= 4 else [w_prev]
        em.op("sync", (lambda eng, pc=pc, cc=cc, bi=bi: eng.dma_start(
            G.scr[bi][:], A(w_d, pc * 128 * ncols + cc * 512, [[ncols, 128], [1, 512]]))),
            waits=dwaits, inc=vsem[bi], k=16)
        w_wl = vsem[bi].at()
        swaits = [w_wl] + ([(dv.h, thr_marks[n - 2])] if n >= 2 else [])
        em.op("scalar", (lambda eng, bi=bi, si=si: eng.sign(
            sgbs[si][:, :], G.scr[bi][:])), waits=swaits, inc=sc)
        w_sg = sc.at()

        def f(eng, bi=bi, n=n, si=si):
            eng.tensor_tensor(G.scr[bi][:], G.scr[bi][:], sgbs[si][:, :], ALU.mult)
            eng.tensor_scalar(G.scr[bi][:], G.scr[bi][:], G.mcols[:, 1:2], None, ALU.is_ge)
            return eng.tensor_tensor(G.tqb[n % 2][:, :], sgbs[si][:, :], G.scr[bi][:],
                                     ALU.mult)
        em.op("vector", f, waits=[w_sg, tsem[n % 2].at()], inc=dv)
        thr_marks[n] = dv.v
        em.op("gpsimd", (lambda eng, n=n, pc=pc, cc=cc: eng.dma_start(
            A(t_out, pc * 128 * ncols + cc * 512, [[ncols, 128], [1, 512]]),
            G.tqb[n % 2][:, :])), waits=[(dv.h, thr_marks[n])], inc=tsem[n % 2], k=16)
    em.flush()




def _l2_dram(nc, kind_in="ExternalInput", kind_out="ExternalOutput", D=None):
    if D is None:
        D = NS()
        D.x = nc.dram_tensor("x", [128, NJ, TOK], F32, kind=kind_in)
        D.xn = nc.dram_tensor("xn", [128, NJ, TOK], F32, kind=kind_in)
        D.s = nc.dram_tensor("s", [128, NJ, TOK], F32, kind=kind_in)
        D.qh = nc.dram_tensor("qh", [128, NJ, TOK], F32, kind=kind_in)
        D.lent = nc.dram_tensor("lent", [NCH, 128, 512], F32, kind=kind_in)
        D.dcum = nc.dram_tensor("dcum", [NCH, 128, 8], F32, kind=kind_in)
        D.t1 = nc.dram_tensor("t1", [DM, HID], F32, kind=kind_in)
        D.t2 = nc.dram_tensor("t2", [HID, DM], F32, kind=kind_in)
        D.m1 = nc.dram_tensor("m1", [128, 2], F32, kind=kind_in)
        D.m2 = nc.dram_tensor("m2", [128, 2], F32, kind=kind_in)
        D.sp = nc.dram_tensor("spart", [128, 512], F32, kind=kind_in)
    D.l1g = nc.dram_tensor("l1g", [128, NJ], F32, kind="ExternalInput")
    D.l1b = nc.dram_tensor("l1b", [128, NJ], F32, kind="ExternalInput")
    D.b1 = nc.dram_tensor("b1", [128, 32], F32, kind="ExternalInput")
    D.l2g = nc.dram_tensor("l2g", [128, 32], F32, kind="ExternalInput")
    D.l2b = nc.dram_tensor("l2b", [128, 32], F32, kind="ExternalInput")
    D.b2 = nc.dram_tensor("b2", [128, NJ], F32, kind="ExternalInput")
    D.wk1 = nc.dram_tensor("wk1", [DM, DM], F32, kind="ExternalInput")
    D.wv1 = nc.dram_tensor("wv1", [DM, DM], F32, kind="ExternalInput")
    D.x1 = nc.dram_tensor("x1", [128, NJ, TOK], F32, kind=kind_out)
    D.xn1 = nc.dram_tensor("xn1", [128, NJ, TOK], F32, kind=kind_out)
    D.k1s = [nc.dram_tensor(f"k1_{t}", [128, NJ, TT], F32, kind=kind_out)
             for t in range(NT)]
    D.v1s = [nc.dram_tensor(f"v1_{t}", [128, 4, DM], F32, kind=kind_out)
             for t in range(NT)]
    return D


def _l2_emit(nc, st, sems, D):
    E = st.enter_context
    g = {}
    for nm in ("bS", "bQH", "bXN", "bX", "bG"):
        g[nm] = E(nc.sbuf_tensor(nm, [128, PITCH], F32))
    g["bH"] = E(nc.sbuf_tensor("bH", [128, 32 * 512], F32))
    g["lh_hi"] = E(nc.sbuf_tensor("lh_hi", [128, PITCH], BF16))
    g["lh_lo"] = E(nc.sbuf_tensor("lh_lo", [128, PITCH], BF16))
    g["hh"] = [E(nc.sbuf_tensor(f"hh{i}", [128, 512], BF16)) for i in range(2)]
    g["hl"] = [E(nc.sbuf_tensor(f"hl{i}", [128, 512], BF16)) for i in range(2)]
    g["t1b"] = [E(nc.sbuf_tensor(f"t1b{i}", [128, NJ * 128], BF16)) for i in range(2)]
    g["t2b"] = [E(nc.sbuf_tensor(f"t2b{i}", [128, NJ * 128], BF16)) for i in range(2)]
    g["wbuf"] = [E(nc.sbuf_tensor(f"wbuf{i}", [128, NJ * 128], F32)) for i in range(2)]
    g["scr"] = [E(nc.sbuf_tensor(f"scr{i}", [128, 512], F32)) for i in range(4)]
    g["lbuf"] = [E(nc.sbuf_tensor(f"lbuf{i}", [128, 512], F32)) for i in range(2)]
    g["sentb"] = E(nc.sbuf_tensor("sentb", [128, 512], F32))
    g["spart"] = E(nc.sbuf_tensor("spartb", [128, 512], F32))
    g["dcum"] = E(nc.sbuf_tensor("dcumb", [128, NCH * 8], F32))
    g["l1g"] = E(nc.sbuf_tensor("l1gb", [128, NJ], F32))
    g["l1b"] = E(nc.sbuf_tensor("l1bb", [128, NJ], F32))
    g["b1"] = E(nc.sbuf_tensor("b1b", [128, 32], F32))
    g["l2g"] = E(nc.sbuf_tensor("l2gb", [128, 32], F32))
    g["l2b"] = E(nc.sbuf_tensor("l2bb", [128, 32], F32))
    g["b2"] = E(nc.sbuf_tensor("b2b", [128, NJ], F32))
    g["m1"] = E(nc.sbuf_tensor("m1b", [128, 2], F32))
    g["m2"] = E(nc.sbuf_tensor("m2b", [128, 2], F32))
    g["ones_c"] = E(nc.sbuf_tensor("onesc", [128, 1], F32))
    g["ones_r"] = E(nc.sbuf_tensor("onesr", [1, 128], F32))
    g["cns"] = E(nc.sbuf_tensor("cns", [128, 2], F32))
    g["flagb"] = E(nc.sbuf_tensor("flagb", [128, 1], F32))
    g["pb"] = [E(nc.psum_tensor(f"pb{i}", [128, 512], F32)) for i in range(8)]
    g.update(sems)
    G = type("G", (), g)

    em = Em(nc)
    loads = [
        (G.dcum[:], A(D.dcum, 0, [[8, 128], [1024, NCH], [1, 8]])),
        (G.l1g[:], D.l1g.ap()), (G.l1b[:], D.l1b.ap()), (G.b1[:], D.b1.ap()),
        (G.l2g[:], D.l2g.ap()), (G.l2b[:], D.l2b.ap()), (G.b2[:], D.b2.ap()),
        (G.m1[:], D.m1.ap()), (G.m2[:], D.m2.ap()),
    ]
    spw = []
    if getattr(D, "sp_gate", None) is not None:
        spw = [D.sp_gate]
    em.op("sync", (lambda eng: eng.dma_start(G.spart[:], D.sp_ap)), waits=spw,
          inc=G.c, k=16)
    for dst, src_ in loads:
        em.op("sync", (lambda eng, a=dst, b=src_: eng.dma_start(a, b)), inc=G.c, k=16)
    if getattr(D, "flag", None) is not None:
        em.op("sync", (lambda eng: eng.dma_start(G.flagb[:], D.flag.ap())), inc=G.c, k=16)

    def f(eng):
        eng.memset(G.ones_c[:], 1.0)
        eng.memset(G.ones_r[:], 1.0)
        return eng.memset(G.cns[:, 0:1], float(LNEPS))
    em.op("vector", f, inc=G.dv)
    if getattr(D, "flag", None) is not None:
        em.op("vector", lambda eng: eng.tensor_scalar(
            G.spart[:, :], G.spart[:, :], G.flagb[:, 0:1], None, ALU.mult),
            waits=[G.c.at()], inc=G.dv)
    em.flush()

    cc2 = getattr(G, "cc2", None)
    import os
    cc_defer = bool(int(os.environ.get("KCC_DEFER", "0")))
    pend = []
    for t in range(NT):
        with nc.named_scope(f"l2t{t}"):
            _l2_tile(nc, G, t, D.x, D.xn, D.s, D.qh, D.lent, D.t1, D.t2,
                     D.wk1, D.wv1, D.x1, D.xn1, D.k1s[t], D.v1s[t], D)
        if getattr(D, "kgs", None) is not None:
            w_st = (G.sst.h, G.sst.v)
            pairs = [(D.k1s[t], D.kgs[t], cc2), (D.v1s[t], D.vgs[t], cc2)]
            if t == NT - 1:
                pairs.append((D.xn1col, D.bcg, G.cc1))
            if cc_defer:
                pend.append((w_st, pairs))
                continue
            em = Em(nc)
            for a, b, cs in pairs:
                em.op("gpsimd", (lambda eng, a=a, b=b: eng.collective_compute(
                    "AllGather", ALU.bypass, replica_groups=RG_PAIRS,
                    ins=[a.ap().opt()], outs=[b.ap().opt()])),
                    waits=[w_st], inc=cs, k=1)
            em.flush()
    if pend:
        em = Em(nc)
        w_fin = (G.sst.h, G.sst.v)
        for _, pairs in pend:
            for a, b, cs in pairs:
                em.op("gpsimd", (lambda eng, a=a, b=b: eng.collective_compute(
                    "AllGather", ALU.bypass, replica_groups=RG_PAIRS,
                    ins=[a.ap().opt()], outs=[b.ap().opt()])),
                    waits=[w_fin], inc=cs, k=1)
        em.flush()
    return G


def build_launch2():
    nc = bass.Bass("TRN2", target_bir_lowering=False)
    D = _l2_dram(nc)
    D.sp_ap = D.sp.ap()
    D.flag = None
    with contextlib.ExitStack() as st:
        sems = {nm: Sem(st, nc, nm) for nm in L2_SEMS}
        SER_SEMS.clear()
        SER_SEMS.extend([sems["mm"], sems["dv"], sems["sc"]])
        with contextlib.ExitStack() as st_b:
            G = _l2_emit(nc, st_b, sems, D)
        em = Em(nc)
        em.op("gpsimd", lambda eng: eng.nop(),
              waits=[G.sst.at(), G.l0.at(), G.l1.at()])
        em.flush()
    return nc


def _l2_ln_stats(nc, em, G, src_ap, j, nch, gate, st):
    mm, sc = G.mm, G.sc
    pb = G.pb
    waits = list(gate)
    if j >= 2:
        waits.append((mm.h, st[j - 2]))
    em.op("scalar", (lambda eng: eng.activation(
        G.scr[j % 2][:], src_ap, AF.Square)), waits=waits, inc=sc)

    def f(eng, j=j, nch=nch, src_ap=src_ap):
        eng.matmul(pb[0][0:1, :], G.ones_c[:], src_ap,
                   start=(j == 0), stop=(j == nch - 1))
        return eng.matmul(pb[1][0:1, :], G.ones_c[:], G.scr[j % 2][:],
                          start=(j == 0), stop=(j == nch - 1))
    em.op("tensor", f, waits=[sc.at()], inc=mm)
    st[j] = mm.v


def _l2_ln(nc, em, G, srcv, nch, gate_waits):
    st = {}
    for j in range(nch):
        _l2_ln_stats(nc, em, G, srcv(j), j, nch,
                     (list(gate_waits) if j == 0 else []), st)
    return _l2_ln_fin(nc, em, G, nch)


def _l2_ln_fin(nc, em, G, nch):
    mm, dv, sc = G.mm, G.dv, G.sc
    pb = G.pb
    inv = 1.0 / (nch * 128)

    def row(off, n=512):
        buf = (G.scr[2], G.scr[3], G.sentb)[off // 512]
        return A(buf, 0, [[512, 1], [1, n]])

    def f(eng, inv=inv):
        eng.tensor_scalar(row(0), pb[0][0:1, :], inv, None, ALU.mult)
        eng.tensor_scalar(row(512), pb[1][0:1, :], inv, None, ALU.mult)
        eng.tensor_tensor(row(1024), row(0), row(0), ALU.mult)
        return eng.tensor_tensor(row(512), row(512), row(1024), ALU.subtract)
    em.op("vector", f, waits=[mm.at()], inc=dv)

    def f(eng):
        eng.activation(row(512), row(512), AF.Ln, bias=G.cns[0:1, 0:1])
        return eng.activation(row(512), row(512), AF.Exp, scale=-0.5)
    em.op("scalar", f, waits=[dv.at()], inc=sc)

    em.op("vector", lambda eng: eng.scalar_tensor_tensor(
        row(1024), row(0), -1.0, row(512), ALU.mult, ALU.mult),
        waits=[sc.at()], inc=dv)

    def f(eng):
        eng.matmul(pb[2][:, :], G.ones_r[:], row(512), start=True, stop=True)
        return eng.matmul(pb[3][:, :], G.ones_r[:], row(1024), start=True, stop=True)
    em.op("tensor", f, waits=[dv.at()], inc=mm)
    return mm.at()


def _l2_tile(nc, G, t, x_d, xn_d, s_d, qh_d, lent_d, t1_d, t2_d,
             wk1_d, wv1_d, x1_d, xn1_d, k1_d, v1_d, D=None):
    mm, dv, sc = G.mm, G.dv, G.sc
    pb = G.pb
    t0 = t * TT
    lsem = [G.l0, G.l1]
    tsem = [G.t0, G.t1]
    wsem = [G.w0, G.w1]
    vsem = [G.v0, G.v1]

    def xv(buf, j, lo=0, n=TT):
        return A(buf, j * TT + lo, [[PITCH, 128], [1, n]])

    def hv(oc, lo=0, n=512):
        return A(G.bH, oc * 512 + lo, [[32 * 512, 128], [1, n]])

    em = Em(nc)
    for js in range(0, NJ, 2):
        em.op("sync", (lambda eng, js=js: eng.dma_start(
            A(G.bS, js * TT, [[PITCH, 128], [TT, 2], [1, TT]]),
            A(s_d, t0 + js * TOK, [[NJ * TOK, 128], [TOK, 2], [1, TT]]))),
            waits=[G.c.at(), G.sst.at()] if js == 0 else [], inc=G.c, k=16)
    for js in range(0, NJ, 2):
        em.op("sync", (lambda eng, js=js: eng.dma_start(
            A(G.bQH, js * TT, [[PITCH, 128], [TT, 2], [1, TT]]),
            A(qh_d, t0 + js * TOK, [[NJ * TOK, 128], [TOK, 2], [1, TT]]))),
            waits=[G.sst.at()] if js == 0 else [], inc=G.c, k=16)
    w_sq = G.c.at()

    em.flush()

    em = Em(nc)
    for js in range(0, NJ, 2):
        em.op("sync", (lambda eng, js=js: eng.dma_start(
            A(G.bXN, js * TT, [[PITCH, 128], [TT, 2], [1, TT]]),
            A(xn_d, t0 + js * TOK, [[NJ * TOK, 128], [TOK, 2], [1, TT]]))),
            inc=G.c, k=16)
    w_xn = G.c.at()

    w_bc = _l2_ln(nc, em, G, lambda j: xv(G.bXN, j), NJ, [w_xn])

    for j in range(NJ):
        def f(eng, j=j):
            eng.tensor_tensor(xv(G.bX, j), xv(G.bXN, j), pb[2][:, :], ALU.mult)
            eng.tensor_tensor(xv(G.bX, j), xv(G.bX, j), pb[3][:, :], ALU.add)
            return eng.tensor_scalar(xv(G.bX, j), xv(G.bX, j),
                                     G.l1g[:, j:j + 1], G.l1b[:, j:j + 1],
                                     ALU.mult, ALU.add)
        em.op("vector", f, waits=[w_bc] if j == 0 else [], inc=dv)

        def f(eng, j=j):
            eng.tensor_copy(A(G.lh_hi, j * TT, [[PITCH, 128], [1, TT]]), xv(G.bX, j))
            return eng.tensor_tensor(A(G.lh_lo, j * TT, [[PITCH, 128], [1, TT]]),
                                     xv(G.bX, j),
                                     A(G.lh_hi, j * TT, [[PITCH, 128], [1, TT]]),
                                     ALU.subtract)
        em.op("vector", f, inc=dv)
    w_lh = dv.at()

    up_ev = {}
    ln2_st = {}
    for oc in range(32):
        bi = oc % 2
        dwaits = [(mm.h, up_ev.get(oc - 2, (0, 0))[0])] if oc >= 2 else []
        em.op("gpsimd", (lambda eng, oc=oc, bi=bi: eng.dma_start(
            G.t1b[bi][:], A(t1_d, 128 * oc, [[HID, 128], [128 * HID, NJ], [1, 128]]))),
            waits=dwaits, inc=tsem[bi], k=16)
        w_t = tsem[bi].at()
        bank = pb[4 + bi]
        mwaits = [w_t] + ([w_lh] if oc < 2 else [])
        if oc >= 2:
            mwaits.append((sc.h, up_ev[oc - 2][1]))

        def f(eng, bi=bi, bank=bank):
            for kj in range(NJ):
                eng.matmul(bank[:, :], A(G.t1b[bi], kj * 128, [[NJ * 128, 128], [1, 128]]),
                           A(G.lh_hi, kj * TT, [[PITCH, 128], [1, TT]]),
                           start=(kj == 0), stop=False)
                ii = eng.matmul(bank[:, :], A(G.t1b[bi], kj * 128, [[NJ * 128, 128], [1, 128]]),
                                A(G.lh_lo, kj * TT, [[PITCH, 128], [1, TT]]),
                                start=False, stop=(kj == NJ - 1))
            return ii
        em.op("tensor", f, waits=mwaits, inc=mm)
        w_up = mm.v
        if oc >= 1:
            _l2_ln_stats(nc, em, G, hv(oc - 1), oc - 1, 32, [], ln2_st)
        em.op("scalar", (lambda eng, oc=oc, bank=bank: eng.activation(
            hv(oc), bank[:, :], AF.Gelu, bias=G.b1[:, oc:oc + 1], scale=G.m1[:, 0:1])),
            waits=[(mm.h, w_up)], inc=sc)
        up_ev[oc] = (w_up, sc.v)
    _l2_ln_stats(nc, em, G, hv(31), 31, 32, [], ln2_st)
    em.flush()

    em = Em(nc)
    last_sadd = 0
    sent_marks = {}
    for c in range(CPT):
        n_glob = t * CPT + c
        ci = c % 2
        em.op("sync", (lambda eng, ci=ci, n_glob=n_glob: eng.dma_start(
            G.lbuf[ci][:], A(lent_d, n_glob * 128 * 512, [[512, 128], [1, 512]]))),
            waits=[(G.dv.h, sent_marks[c - 2])] if c >= 2 else [], inc=lsem[ci], k=16)
        w_l = lsem[ci].at()

        def f(eng, n_glob=n_glob, ci=ci):
            eng.tensor_tensor(G.sentb[:, :], G.spart[:, :],
                              A(G.dcum, n_glob * 8, [[NCH * 8, 128], [1, 8], [0, 64]]),
                              ALU.mult)
            return eng.tensor_tensor(G.sentb[:, :], G.sentb[:, :], G.lbuf[ci][:, :],
                                     ALU.add)
        em.op("vector", f, waits=[w_l, (mm.h, mm.v)], inc=dv)
        w_sent = dv.at()
        sent_marks[c] = dv.v
        for par in (0, 1):
            def f(eng, c=c, par=par):
                for h in range(par, H, 2):
                    ii = eng.matmul(
                        A(pb[6], (h % 2) * 64 * 512 + (h // 2) * 64, [[512, 64], [1, 64]]),
                        A(G.sentb, (h % 2) * 64 * 512 + (h // 2) * 64, [[512, 64], [1, 64]]),
                        A(G.bQH, (h % 2) * 64 * PITCH + (h // 2) * TT + c * 64,
                          [[PITCH, 64], [1, 64]]),
                        start=True, stop=True,
                        tile_position=(64 * (h % 2), 64 * (h % 2)))
                return ii
            em.op("tensor", f,
                  waits=([w_sent, w_sq, (dv.h, last_sadd)] if par == 0 else []), inc=mm)
        em.op("vector", (lambda eng, c=c: eng.tensor_tensor(
            A(G.bS, c * 64, [[PITCH, 128], [TT, NJ], [1, 64]]),
            A(G.bS, c * 64, [[PITCH, 128], [TT, NJ], [1, 64]]),
            A(pb[6], 0, [[512, 128], [64, NJ], [1, 64]]), ALU.add)),
            waits=[mm.at()], inc=dv)
        last_sadd = dv.v
    G.w_pa = (dv.h, dv.v)
    em.flush()




    em = Em(nc)
    w_bc2 = _l2_ln_fin(nc, em, G, 32)

    for oc in range(32):
        def f(eng, oc=oc):
            eng.tensor_tensor(hv(oc), hv(oc), pb[2][:, :], ALU.mult)
            eng.tensor_tensor(hv(oc), hv(oc), pb[3][:, :], ALU.add)
            return eng.tensor_scalar(hv(oc), hv(oc), G.l2g[:, oc:oc + 1],
                                     G.l2b[:, oc:oc + 1], ALU.mult, ALU.add)
        em.op("vector", f, waits=[w_bc2] if oc == 0 else [], inc=dv)
    w_n2 = dv.at()

    dn_marks = {}
    for oc in range(32):
        bi = oc % 2
        swaits = [w_n2] if oc < 2 else [(mm.h, dn_marks[oc - 2])]

        def f(eng, oc=oc, bi=bi):
            eng.tensor_copy(G.hh[bi][:, :], hv(oc))
            return eng.tensor_tensor(G.hl[bi][:, :], hv(oc), G.hh[bi][:, :],
                                     ALU.subtract)
        em.op("vector", f, waits=swaits, inc=dv)
        w_split = dv.at()
        em.op("gpsimd", (lambda eng, oc=oc, bi=bi: eng.dma_start(
            G.t2b[bi][:], A(t2_d, 128 * oc * DM, [[DM, 128], [1, NJ * 128]]))),
            waits=[(mm.h, dn_marks.get(oc - 2, 0))], inc=tsem[bi], k=16)
        w_t2 = tsem[bi].at()

        def f(eng, oc=oc, bi=bi):
            for oj in range(NJ):
                eng.matmul(pb[oj][:, :], A(G.t2b[bi], oj * 128, [[NJ * 128, 128], [1, 128]]),
                           G.hh[bi][:, :], start=(oc == 0), stop=False)
                ii = eng.matmul(pb[oj][:, :], A(G.t2b[bi], oj * 128, [[NJ * 128, 128], [1, 128]]),
                                G.hl[bi][:, :], start=False, stop=(oc == 31))
            return ii
        em.op("tensor", f,
              waits=[w_split, w_t2] + ([G.w_pa] if oc == 0 else []), inc=mm)
        dn_marks[oc] = mm.v
    w_dn = mm.at()

    for oj in range(NJ):
        em.op("scalar", (lambda eng, oj=oj: eng.activation(
            xv(G.bG, oj), pb[oj][:, :], AF.Silu, bias=G.b2[:, oj:oj + 1],
            scale=G.m2[:, 0:1])), waits=[w_dn] if oj == 0 else [], inc=sc)
    em.flush()

    em = Em(nc)
    for js in range(0, NJ, 2):
        em.op("sync", (lambda eng, js=js: eng.dma_start(
            A(G.bX, js * TT, [[PITCH, 128], [TT, 2], [1, TT]]),
            A(x_d, t0 + js * TOK, [[NJ * TOK, 128], [TOK, 2], [1, TT]]))),
            waits=[G.sst.at()] if js == 0 else [], inc=G.c, k=16)
    w_x = G.c.at()

    ln3_st = {}
    cmb = {}
    for j in range(NJ):
        def f(eng, j=j):
            eng.tensor_tensor(xv(G.bS, j), xv(G.bS, j), xv(G.bG, j), ALU.mult)
            return eng.tensor_tensor(xv(G.bX, j), xv(G.bX, j), xv(G.bS, j), ALU.add)
        em.op("vector", f, waits=[w_x, sc.at(), G.w_pa] if j == 0 else [], inc=dv)
        cmb[j] = dv.v
        if j >= 1:
            _l2_ln_stats(nc, em, G, xv(G.bX, j - 1), j - 1, NJ,
                         [(dv.h, cmb[j - 1])], ln3_st)
    _l2_ln_stats(nc, em, G, xv(G.bX, NJ - 1), NJ - 1, NJ,
                 [(dv.h, cmb[NJ - 1])], ln3_st)
    w_x1 = dv.at()
    for js in range(0, NJ, 2):
        em.op("gpsimd", (lambda eng, js=js: eng.dma_start(
            A(x1_d, t0 + js * TOK, [[NJ * TOK, 128], [TOK, 2], [1, TT]]),
            A(G.bX, js * TT, [[PITCH, 128], [TT, 2], [1, TT]]))),
            waits=[w_x1] if js == 0 else [], inc=G.sst, k=16)

    w_bc3 = _l2_ln_fin(nc, em, G, NJ)
    for j in range(NJ):
        def f(eng, j=j):
            eng.tensor_tensor(xv(G.bG, j), xv(G.bX, j), pb[2][:, :], ALU.mult)
            return eng.tensor_tensor(xv(G.bG, j), xv(G.bG, j), pb[3][:, :], ALU.add)
        em.op("vector", f, waits=[w_bc3] if j == 0 else [], inc=dv)
    w_xn1 = dv.at()
    for js in range(0, NJ, 2):
        em.op("gpsimd", (lambda eng, js=js: eng.dma_start(
            A(xn1_d, t0 + js * TOK, [[NJ * TOK, 128], [TOK, 2], [1, TT]]),
            A(G.bG, js * TT, [[PITCH, 128], [TT, 2], [1, TT]]))),
            waits=[w_xn1] if js == 0 else [], inc=G.sst, k=16)
    if t == NT - 1 and D is not None and getattr(D, "xn1col", None) is not None:
        em.op("vector", lambda eng: eng.tensor_copy(
            G.sentb[:, 0:NJ], A(G.bG, TT - 1, [[PITCH, 128], [TT, NJ]])),
            waits=[w_xn1], inc=dv)
        em.op("vector", lambda eng: eng.memset(G.sentb[:, NJ:16], 0.0), inc=dv)
        em.op("vector", lambda eng: eng.memset(G.sentb[:, 16:32], 0.0), inc=dv)
        em.op("gpsimd", lambda eng: eng.dma_start(
            D.xn1col.ap(), G.sentb[:, 0:NJ]), waits=[dv.at()], inc=G.sst, k=16)

    grp = {}
    ev = {}
    for oc in range(NJ):
        bi = oc % 2
        dwaits = [(mm.h, grp[oc - 2])] if oc >= 2 else []
        em.op("sync", (lambda eng, oc=oc, bi=bi: eng.dma_start(
            G.wbuf[bi][:], A(wk1_d, 128 * oc, [[DM, 128], [128 * DM, NJ], [1, 128]]))),
            waits=dwaits, inc=wsem[bi], k=16)
        w_w = wsem[bi].at()
        bank = pb[4 + bi]
        mwaits = [w_w] + ([w_xn1] if oc < 2 else []) + (
            [(dv.h, ev[oc - 2])] if oc >= 2 else [])

        def f(eng, bi=bi, bank=bank):
            for kj in range(NJ):
                ii = eng.matmul(bank[:, :], A(G.wbuf[bi], kj * 128, [[NJ * 128, 128], [1, 128]]),
                                xv(G.bG, kj), start=(kj == 0), stop=(kj == NJ - 1))
            return ii
        em.op("tensor", f, waits=mwaits, inc=mm)
        grp[oc] = mm.v
        em.op("vector", (lambda eng, oc=oc, bank=bank: eng.tensor_copy(
            xv(G.bS, oc), bank[:, :])), waits=[mm.at()], inc=dv)
        ev[oc] = dv.v
    for js in range(0, NJ, 2):
        em.op("gpsimd", (lambda eng, js=js: eng.dma_start(
            A(k1_d, js * TT, [[NJ * TT, 128], [TT, 2], [1, TT]]),
            A(G.bS, js * TT, [[PITCH, 128], [TT, 2], [1, TT]]))),
            waits=[dv.at()] if js == 0 else [], inc=G.sst, k=16)

    vmm = {}
    vev = {}
    for half in range(2):
        for kj in range(NJ):
            n = half * NJ + kj
            bi = n % 2
            dwaits = [(mm.h, vmm[n - 2])] if n >= 2 else [(dv.h, dv.v)]
            em.op("sync", (lambda eng, kj=kj, half=half, bi=bi: eng.dma_start(
                G.scr[bi][:], A(wv1_d, kj * 128 * DM + half * 512, [[DM, 128], [1, 512]]))),
                waits=dwaits, inc=vsem[bi], k=16)
            w_v = vsem[bi].at()
            mwaits = [w_v] + ([w_xn1] if n < 2 else [])
            if kj == 0 and half == 0:
                mwaits.append((dv.h, dv.v))
            if kj == 0 and half == 1:
                mwaits.append((dv.h, vev[0]))

            def f(eng, kj=kj, bi=bi):
                for ts in range(4):
                    ii = eng.matmul(pb[ts][:, :],
                                    A(G.bG, kj * TT + ts * 128, [[PITCH, 128], [1, 128]]),
                                    G.scr[bi][:], start=(kj == 0), stop=(kj == NJ - 1))
                return ii
            em.op("tensor", f, waits=mwaits, inc=mm)
            vmm[n] = mm.v
        for ts in range(4):
            em.op("vector", (lambda eng, ts=ts, half=half: eng.tensor_copy(
                A(G.bQH, ts * 1024 + half * 512, [[PITCH, 128], [1, 512]]),
                pb[ts][:, :])), waits=[(mm.h, vmm[half * NJ + NJ - 1])], inc=dv)
        vev[half] = dv.v
    for js in range(4):
        em.op("gpsimd", (lambda eng, js=js: eng.dma_start(
            A(v1_d, js * DM, [[4 * DM, 128], [1, DM]]),
            A(G.bQH, js * DM, [[PITCH, 128], [1, DM]]))),
            waits=[dv.at()] if js == 0 else [], inc=G.sst, k=16)

    em.flush()


def l2_inputs(l1out, ins, spart):
    return {
        "x": l1out["__x"], "xn": l1out["xn"], "s": l1out["s"], "qh": l1out["qh"],
        "lent": l1out["lent"], "dcum": l1out["dcum"], "spart": spart,
        "t1": l1out["t1"], "t2": l1out["t2"], "m1": l1out["m1"], "m2": l1out["m2"],
        "l1g": _vec128(ins["L0_ffn_ln1_g"]), "l1b": _vec128(ins["L0_ffn_ln1_b"]),
        "b1": _vec128(ins["L0_ffn_b1"]), "l2g": _vec128(ins["L0_ffn_ln2_g"]),
        "l2b": _vec128(ins["L0_ffn_ln2_b"]), "b2": _vec128(ins["L0_ffn_b2"]),
        "wk1": np.ascontiguousarray(ins["L1_attn_Wk"].transpose(1, 0, 2).reshape(DM, DM)),
        "wv1": np.ascontiguousarray(ins["L1_attn_Wv"].transpose(1, 0, 2).reshape(DM, DM)),
    }




KTOT = 2 * TOK


def _l3_dram(nc, kind_in="ExternalInput", kind_out="ExternalOutput", D=None):
    import os
    if D is None:
        D = NS()
        D.x1 = nc.dram_tensor("x1", [128, NJ, TOK], F32, kind=kind_in)
        D.xn1 = nc.dram_tensor("xn1", [128, NJ, TOK], F32, kind=kind_in)
        D.kf = nc.dram_tensor("kf", [128, NJ, KTOT], F32, kind=kind_in)
        D.vf = nc.dram_tensor("vf", [128, 32, DM], F32, kind=kind_in)
        D.bcol = nc.dram_tensor("bcol", [128, NJ], F32, kind=kind_in)
    D.wq1 = nc.dram_tensor("wq1", [DM, DM], F32, kind="ExternalInput")
    D.wr = nc.dram_tensor("wr", [DM, DM], F32, kind="ExternalInput")
    D.wkr = nc.dram_tensor("wkr", [DM, DM], F32, kind="ExternalInput")
    D.wvr = nc.dram_tensor("wvr", [DM, DM], F32, kind="ExternalInput")
    D.mur = nc.dram_tensor("mur", [128, NJ], F32, kind="ExternalInput")
    D.muk = nc.dram_tensor("muk", [128, NJ], F32, kind="ExternalInput")
    D.id128 = nc.dram_tensor("id128", [128, 128], F32, kind="ExternalInput")
    D.out = nc.dram_tensor("out", [128, NJ, TOK], F32, kind=kind_out)
    D.s1 = nc.dram_tensor("s1scratch", [128, NJ, TOK], F32)
    dbg = bool(int(os.environ.get("L3_DBG", "0")))
    D.dbg = nc.dram_tensor("dbg", [128, TOK], F32, kind="ExternalOutput") if dbg else None
    D.dbg2 = (nc.dram_tensor("dbg2", [128, NJ, TOK], F32, kind="ExternalOutput")
              if dbg else None)
    return D


def _l3_emit(nc, st, sems, D):
    E = st.enter_context
    s1_dram = D.s1
    g = {}
    g["bXN"] = E(nc.sbuf_tensor("bXN", [128, NJ * TOK], F32))
    g["id128"] = E(nc.sbuf_tensor("id128s", [128, 128], F32))
    g["wbuf"] = [E(nc.sbuf_tensor(f"wbuf{i}", [128, NJ * 128], F32)) for i in range(2)]
    g["mur"] = E(nc.sbuf_tensor("murb", [128, NJ], F32))
    g["muk"] = E(nc.sbuf_tensor("mukb", [128, NJ], F32))
    g["bcol"] = E(nc.sbuf_tensor("bcolb", [128, NJ], F32))
    g["flagb"] = E(nc.sbuf_tensor("flagb", [128, 1], F32))
    g["pb"] = [E(nc.psum_tensor(f"pb{i}", [128, 512], F32)) for i in range(8)]
    g.update(sems)

    with contextlib.ExitStack() as st_a:
        Ea = st_a.enter_context
        g["Kb"] = [Ea(nc.sbuf_tensor(f"Kb{i}", [128, KTOT], F32)) for i in range(2)]
        g["Kbr"] = Ea(nc.sbuf_tensor("Kbr", [128, KTOT], F32R))
        g["Qbr"] = Ea(nc.sbuf_tensor("Qbr", [128, TOK], F32R))
        g["Ve"] = Ea(nc.sbuf_tensor("Ve", [128, 32 * 65], F32))
        g["Qb"] = Ea(nc.sbuf_tensor("Qb", [128, TOK], F32))
        g["Qe"] = Ea(nc.sbuf_tensor("Qe", [65, TOK], F32))
        g["Ke"] = Ea(nc.sbuf_tensor("Ke", [65, KTOT], F32))
        g["Pt"] = Ea(nc.sbuf_tensor("Pt", [128, 1024], F32))
        g["msb"] = Ea(nc.sbuf_tensor("msb", [128, 16], F32))
        g["m16"] = Ea(nc.sbuf_tensor("m16", [16, 128], F32))
        g["mc"] = Ea(nc.sbuf_tensor("mc", [128, 128], F32))
        g["lb"] = Ea(nc.sbuf_tensor("lb", [128, 16], F32))
        g["rl"] = Ea(nc.sbuf_tensor("rl", [128, 16], F32))
        g["rlr"] = Ea(nc.sbuf_tensor("rlr", [1, 512], F32))
        g["rlb"] = Ea(nc.sbuf_tensor("rlb", [64, 512], F32))
        g["stage"] = [Ea(nc.sbuf_tensor(f"stg{i}", [64, 512], F32)) for i in range(2)]
        g["ones_r"] = Ea(nc.sbuf_tensor("onesr3", [1, 64], F32))
        g["pvs"] = {"evac": 0, "rlb": 0, "st": {}, "idx": 0}
        g["dbg_d"] = D.dbg
        G = type("G", (), g)

        em = Em(nc)
        for dst, src_ in (
            (G.id128[:], D.id128.ap()), (G.mur[:], D.mur.ap()),
            (G.muk[:], D.muk.ap()),
        ):
            em.op("sync", (lambda eng, a=dst, b=src_: eng.dma_start(a, b)), inc=G.c, k=16)
        if getattr(D, "bcol", None) is not None:
            em.op("sync", lambda eng: eng.dma_start(G.bcol[:], D.bcol.ap()),
                  inc=G.x, k=16)
        else:
            em.op("sync", lambda eng: eng.dma_start(
                G.bcol[:], A(D.bcg, 0, [[NJ, 128], [1, NJ]])),
                waits=[D.bc_gate], inc=G.x, k=16)
            em.op("sync", lambda eng: eng.dma_start(G.flagb[:], D.flag.ap()),
                  inc=G.x, k=16)
            em.op("vector", lambda eng: eng.tensor_scalar(
                G.bcol[:, :], G.bcol[:, :], G.flagb[:, 0:1], None, ALU.mult),
                waits=[G.x.at()], inc=G.dv)
        for js in range(NJ):
            em.op("sync", (lambda eng, js=js: eng.dma_start(
                A(G.bXN, js * TOK, [[NJ * TOK, 128], [1, TOK]]),
                A(D.xn1, js * TOK, [[NJ * TOK, 128], [1, TOK]]))), inc=G.c, k=16)
        em.op("vector", lambda eng: eng.memset(G.Ve[:], 1.0), inc=G.dv)
        em.op("vector", lambda eng: eng.memset(G.Ke[64:65, :], 1.0), inc=G.dv)
        em.op("vector", lambda eng: eng.memset(G.ones_r[:, :], 1.0), inc=G.dv)
        em.flush()

        pair_done = {}
        for jp in range(8):
            with nc.named_scope(f"hp{jp}"):
                _l3_headpair(nc, G, jp, D, s1_dram, pair_done)

    if D.dbg2 is not None:
        em = Em(nc)
        em.op("gpsimd", lambda eng: eng.dma_start(D.dbg2.ap(), s1_dram.ap()),
              waits=[g["s1st"].at()], inc=g["ost"], k=16)
        em.flush()

    with contextlib.ExitStack() as st_b:
        Eb = st_b.enter_context
        for nm in ("bMX", "bR", "bKA", "bG", "bX1"):
            g[nm] = Eb(nc.sbuf_tensor(nm, [128, PITCH], F32))
        g["bS1"] = [Eb(nc.sbuf_tensor(f"bS1{i}", [128, PITCH], F32)) for i in range(2)]
        g["rw_cmb"] = {}
        G = type("G", (), g)

        for t in range(NT):
            with nc.named_scope(f"rw{t}"):
                _l3_rwkv_tile(nc, G, t, D.x1, D.wr, D.wkr, D.wvr, s1_dram, D.out)

        em = Em(nc)
        em.op("gpsimd", lambda eng: eng.nop(), waits=[G.ost.at()])
        em.flush()
    return type("G", (), g)


def build_launch3():
    nc = bass.Bass("TRN2", target_bir_lowering=False)
    D = _l3_dram(nc)
    with contextlib.ExitStack() as st:
        sems = {nm: Sem(st, nc, nm) for nm in L3_SEMS}
        SER_SEMS.clear()
        SER_SEMS.extend([sems["mm"], sems["dv"], sems["sc"]])
        with contextlib.ExitStack() as st_b:
            _l3_emit(nc, st_b, sems, D)
    return nc


def _l3_headpair(nc, G, jp, D, s1_dram, pair_done):
    mm, dv, sc = G.mm, G.dv, G.sc
    pb = G.pb
    kbsem = [G.kb0, G.kb1]
    vbsem = [G.vb0, G.vb1]
    wsem = [G.w0, G.w1]
    bi = jp % 2

    em = Em(nc)
    dguard = [(mm.h, pair_done[jp - 2])] if jp >= 2 else [(G.c.h, G.c.v)]
    gate = [D.kv_gate] if (getattr(D, "kf", None) is None and jp < 2) else []
    if getattr(D, "kf", None) is not None:
        for js in range(4):
            em.op("sync", (lambda eng, jp=jp, bi=bi, js=js: eng.dma_start(
                A(G.Kb[bi], js * (KTOT // 4), [[KTOT, 128], [1, KTOT // 4]]),
                A(D.kf, jp * KTOT + js * (KTOT // 4),
                  [[NJ * KTOT, 128], [1, KTOT // 4]]))),
                waits=dguard if js == 0 else [], inc=kbsem[bi], k=16)
        w_kb = kbsem[bi].at()
    else:
        for sl in range(2):
            for t in range(NT):
                em.op("sync", (lambda eng, jp=jp, bi=bi, sl=sl, t=t: eng.dma_start(
                    A(G.Kb[bi], sl * TOK + t * TT, [[KTOT, 128], [1, TT]]),
                    A(D.kgs[t], sl * 128 * NJ * TT + jp * TT,
                      [[NJ * TT, 128], [1, TT]]))),
                    waits=(dguard + gate) if (sl == 0 and t == 0) else [],
                    inc=kbsem[bi], k=16)
        w_kb = kbsem[bi].at()
    em.op("sync", (lambda eng, jp=jp, bi=bi: eng.dma_start(
        G.wbuf[bi][:], A(D.wq1, 128 * jp, [[DM, 128], [128 * DM, NJ], [1, 128]]))),
        waits=list(dguard), inc=wsem[bi], k=16)
    w_w = wsem[bi].at()
    em.op("vector", (lambda eng, bi=bi: eng.tensor_copy(G.Kbr[:, :], G.Kb[bi][:, :])),
          waits=[w_kb] + ([(mm.h, pair_done[jp - 1])] if jp >= 1 else []), inc=dv)
    w_kr = dv.at()

    prevg = [(mm.h, pair_done[jp - 1])] if jp >= 1 else []
    qev = {}
    for qc in range(4):
        def f(eng, qc=qc, bi=bi):
            for kj in range(NJ):
                ii = eng.matmul(pb[6 + qc % 2][:, :],
                                A(G.wbuf[bi], kj * 128, [[NJ * 128, 128], [1, 128]]),
                                A(G.bXN, kj * TOK + qc * 512, [[NJ * TOK, 128], [1, 512]]),
                                start=(kj == 0), stop=(kj == NJ - 1))
            return ii
        waits = [w_w] + list(prevg) + ([(dv.h, qev[qc - 2])] if qc >= 2 else [])
        em.op("tensor", f, waits=waits, inc=mm)
        em.op("vector", (lambda eng, qc=qc: eng.tensor_copy(
            A(G.Qb, qc * 512, [[TOK, 128], [1, 512]]), pb[6 + qc % 2][:, :])),
            waits=[mm.at()] + list(prevg), inc=dv)
        qev[qc] = dv.v
    w_q = dv.at()
    em.op("vector", lambda eng: eng.tensor_copy(G.Qbr[:, :], G.Qb[:, :]), inc=dv)
    w_qr = dv.at()

    def emit_extract(hh):
        em.op("gpsimd", (lambda eng, hh=hh: eng.dma_start(
            G.Qe[0:64, :], A(G.Qb, hh * 64 * TOK, [[TOK, 64], [1, TOK]]))),
            waits=[w_q, (mm.h, mm.v)], inc=G.qe, k=16)
        em.op("gpsimd", (lambda eng, hh=hh, bi=bi: eng.dma_start(
            G.Ke[0:64, :], A(G.Kb[bi], hh * 64 * KTOT, [[KTOT, 64], [1, KTOT]]))),
            waits=[w_kb, (mm.h, mm.v)], inc=G.ke, k=16)
        if getattr(D, "vf", None) is not None:
            em.op("gpsimd", (lambda eng, jp=jp, hh=hh: eng.dma_start(
                A(G.Ve, 0, [[32 * 65, 128], [65, 32], [1, 64]]),
                A(D.vf, jp * 128 + hh * 64, [[32 * DM, 128], [DM, 32], [1, 64]]))),
                waits=[(mm.h, mm.v)], inc=G.vb0, k=16)
        else:
            for sl in range(2):
                for t in range(NT):
                    em.op("gpsimd", (lambda eng, jp=jp, hh=hh, sl=sl, t=t: eng.dma_start(
                        A(G.Ve, (sl * 16 + 4 * t) * 65, [[32 * 65, 128], [65, 4], [1, 64]]),
                        A(D.vgs[t], sl * 128 * 4 * DM + jp * 128 + hh * 64,
                          [[4 * DM, 128], [DM, 4], [1, 64]]))),
                        waits=([(mm.h, mm.v)] + gate) if (sl == 0 and t == 0) else [],
                        inc=G.vb0, k=16)
        return G.qe.at(), G.ke.at(), G.vb0.at()

    def gen_passA(hh):
        red_marks = {}
        for qt in range(16):
            for kt in range(8):
                n = qt * 8 + kt
                bk = pb[n % 4]
                if n == 0:
                    waits = [w_qr, w_kr, (dv.h, dv.v)]
                elif n >= 4:
                    waits = [(dv.h, red_marks[n - 4])]
                else:
                    waits = []
                em.op("tensor", (lambda eng, qt=qt, kt=kt, bk=bk, hh=hh: eng.matmul(
                    bk[:, :],
                    A(G.Qbr, hh * 64 * TOK + qt * 128, [[TOK, 64], [1, 128]]),
                    A(G.Kbr, hh * 64 * KTOT + kt * 512, [[KTOT, 64], [1, 512]]),
                    start=True, stop=True)),
                    waits=waits, inc=mm)
                em.op("vector", (lambda eng, n=n, bk=bk: eng.tensor_reduce(
                    G.mc[:, n:n + 1], bk[:, :], AX.X, ALU.max)),
                    waits=[mm.at()], inc=dv)
                red_marks[n] = dv.v
                yield

    def finalize_passA(hh, h):
        for qt in range(16):
            em.op("vector", (lambda eng, qt=qt: eng.tensor_reduce(
                G.msb[:, qt:qt + 1], G.mc[:, qt * 8:qt * 8 + 8], AX.X, ALU.max)),
                inc=dv)
        em.op("vector", lambda eng: eng.memset(G.lb[:, :], 0.0), inc=dv)
        em.op("vector", lambda eng: eng.memset(G.rl[:, :], 0.0), inc=dv)
        em.op("vector", lambda eng: eng.tensor_scalar(
            G.msb[:, :], G.msb[:, :], -1.0, None, ALU.mult), inc=dv)
        w_ma = dv.at()

        em.op("tensor", lambda eng: eng.transpose(
            pb[2][0:16, 0:128], G.msb[:, :], G.id128[:, :]),
            waits=[w_ma, (sc.h, sc.v)], inc=mm)
        em.op("vector", lambda eng: eng.tensor_copy(G.m16[:, :], pb[2][0:16, 0:128]),
              waits=[mm.at()], inc=dv)
        em.op("vector", lambda eng: eng.memset(G.mc[:, :], 0.0), inc=dv)
        em.op("vector", lambda eng: eng.memset(G.lb[:, :], 0.0), inc=dv)
        em.op("gpsimd", lambda eng: eng.dma_start(
            A(G.Qe, 64 * TOK, [[TOK, 1], [128, 16], [1, 128]]),
            A(G.m16, 0, [[128, 16], [1, 128]])), waits=[dv.at()], inc=G.me, k=16)
        w_m = G.me.at()
        if getattr(G, "dbg_d", None) is not None:
            em.op("gpsimd", (lambda eng, h=h: eng.dma_start(
                A(G.dbg_d, h * TOK, [[TOK, 1], [1, TOK]]),
                A(G.Qe, 64 * TOK, [[TOK, 1], [1, TOK]]))),
                waits=[w_m], inc=G.me, k=16)
        return w_m

    def emit_passB(hh, w_m, w_qe, w_ke, w_ve, stepper=None):
        pvs = G.pvs
        for qq in range(4):
            exp_marks = {}
            pv_marks = {}

            def emit_pv(kt, qq=qq):
                def f(eng, kt=kt):
                    return eng.matmul(
                        pb[4][0:65, :],
                        A(G.Ve, kt * 65, [[32 * 65, 128], [1, 65]]),
                        A(G.Pt, (kt % 2) * 512, [[1024, 128], [1, 512]]),
                        start=(kt == 0), stop=(kt == 31))
                pvw = [(sc.h, exp_marks[kt])]
                if kt == 0:
                    pvw.append(w_ve)
                    if pvs["evac"]:
                        pvw.append((dv.h, pvs["evac"]))
                em.op("tensor", f, waits=pvw, inc=mm)
                pv_marks[kt] = mm.v

            for kt in range(32):
                sb = pb[2 + kt % 2]
                swaits = [w_m, w_ke, w_qe]
                if kt >= 2:
                    swaits.append((sc.h, exp_marks[kt - 2]))
                em.op("tensor", (lambda eng, qq=qq, kt=kt, sb=sb: eng.matmul(
                    sb[:, :],
                    A(G.Ke, kt * 128, [[KTOT, 65], [1, 128]]),
                    A(G.Qe, qq * 512, [[TOK, 65], [1, 512]]),
                    start=True, stop=True)),
                    waits=swaits if kt < 2 else swaits[-1:], inc=mm)
                em.op("scalar", (lambda eng, kt=kt, sb=sb: eng.activation(
                    A(G.Pt, (kt % 2) * 512, [[1024, 128], [1, 512]]), sb[:, :], AF.Exp)),
                    waits=[mm.at()] + ([(mm.h, pv_marks[kt - 2])] if kt >= 2 else []),
                    inc=sc)
                exp_marks[kt] = sc.v
                if stepper is not None:
                    stepper()
                if kt >= 1:
                    emit_pv(kt - 1)
            emit_pv(31)
            w_pv = mm.at()
            def f(eng):
                eng.activation(G.rlr[0:1, :], pb[4][64:65, :], AF.Ln, scale=8.0)
                return eng.activation(G.rlr[0:1, :], G.rlr[0:1, :], AF.Exp, scale=-1.0)
            em.op("scalar", f, waits=[w_pv], inc=sc)
            w_rl = sc.at()
            em.op("tensor", (lambda eng: eng.matmul(
                pb[5][0:64, :], G.ones_r[:, :], G.rlr[0:1, :], start=True, stop=True)),
                waits=[w_rl] + ([(dv.h, pvs["rlb"])] if pvs["rlb"] else []), inc=mm)
            em.op("vector", (lambda eng: eng.tensor_copy(G.rlb[:, :], pb[5][0:64, :])),
                  waits=[mm.at()], inc=dv)
            pvs["rlb"] = dv.v
            sg = G.stage[pvs["idx"] % 2]
            tw = [(dv.h, pvs["rlb"])]
            prev_st = pvs["st"].get(pvs["idx"] - 2)
            if prev_st:
                tw.append((G.s1st.h, prev_st))
            em.op("vector", (lambda eng, sg=sg: eng.tensor_tensor(
                sg[:, :], pb[4][0:64, :], G.rlb[:, :], ALU.mult)), waits=tw, inc=dv)
            pvs["evac"] = dv.v
            em.op("gpsimd", (lambda eng, jp=jp, hh=hh, qq=qq, sg=sg: eng.dma_start(
                A(s1_dram, hh * 64 * NJ * TOK + jp * TOK + qq * 512,
                  [[NJ * TOK, 64], [1, 512]]), sg[:, :])),
                waits=[(dv.h, pvs["evac"])], inc=G.s1st, k=16)
            pvs["st"][pvs["idx"]] = G.s1st.v
            pvs["idx"] += 1

    w_qe, w_ke, w_ve = emit_extract(0)
    for _ in gen_passA(0):
        pass
    w_m = finalize_passA(0, 2 * jp)
    emit_passB(0, w_m, w_qe, w_ke, w_ve)
    w_qe, w_ke, w_ve = emit_extract(1)
    for _ in gen_passA(1):
        pass
    w_m = finalize_passA(1, 2 * jp + 1)
    emit_passB(1, w_m, w_qe, w_ke, w_ve)

    pair_done[jp] = mm.v
    pair_done[("dv", jp)] = dv.v
    em.flush()


def _l3_rwkv_tile(nc, G, t, x1_d, wr_d, wkr_d, wvr_d, s1_dram, out_d):
    mm, dv, sc = G.mm, G.dv, G.sc
    pb = G.pb
    wsem = [G.w0, G.w1]
    t0 = t * TT

    def xnv(j, lo=0, n=TT):
        return A(G.bXN, j * TOK + t0 + lo, [[NJ * TOK, 128], [1, n]])

    def xv(buf, j, lo=0, n=TT):
        return A(buf, j * TT + lo, [[PITCH, 128], [1, n]])

    em = Em(nc)

    def emit_mix(mu_sb, dst, gate):
        for j in range(NJ):
            if t == 0:
                def f(eng, j=j, mu_sb=mu_sb, dst=dst):
                    eng.tensor_tensor(xv(G.bMX, j, 1, TT - 1), xnv(j, 1, TT - 1),
                                      xnv(j, 0, TT - 1), ALU.subtract)
                    eng.scalar_tensor_tensor(xv(dst, j, 1, TT - 1),
                                             xv(G.bMX, j, 1, TT - 1), mu_sb[:, j:j + 1],
                                             xnv(j, 0, TT - 1), ALU.mult, ALU.add)
                    eng.tensor_tensor(xv(G.bMX, j, 0, 1), xnv(j, 0, 1),
                                      G.bcol[:, j:j + 1], ALU.subtract)
                    return eng.scalar_tensor_tensor(xv(dst, j, 0, 1), xv(G.bMX, j, 0, 1),
                                                    mu_sb[:, j:j + 1], G.bcol[:, j:j + 1],
                                                    ALU.mult, ALU.add)
            else:
                def f(eng, j=j, mu_sb=mu_sb, dst=dst):
                    eng.tensor_tensor(xv(G.bMX, j), xnv(j), xnv(j, -1), ALU.subtract)
                    return eng.scalar_tensor_tensor(xv(dst, j), xv(G.bMX, j),
                                                    mu_sb[:, j:j + 1], xnv(j, -1),
                                                    ALU.mult, ALU.add)
            em.op("vector", f, waits=list(gate) if j == 0 else [], inc=dv)
        return dv.at()

    w_xr = emit_mix(G.mur, G.bKA, [(mm.h, mm.v)])

    def proj(w_d, rhsbuf, evac, gate, bankbase=4):
        grp = {}
        evm = {}
        for oc in range(NJ):
            bi2 = oc % 2
            dwaits = [(mm.h, grp[oc - 2])] if oc >= 2 else list(gate)
            em.op("sync", (lambda eng, oc=oc, bi2=bi2, w_d=w_d: eng.dma_start(
                G.wbuf[bi2][:], A(w_d, 128 * oc, [[DM, 128], [128 * DM, NJ], [1, 128]]))),
                waits=dwaits, inc=wsem[bi2], k=16)
            w_wl = wsem[bi2].at()
            bank = pb[bankbase + bi2]
            mwaits = [w_wl] + list(gate) + ([(sc.h, evm[oc - 2])] if oc >= 2 else [])

            def f(eng, bi2=bi2, bank=bank, rhsbuf=rhsbuf):
                for kj in range(NJ):
                    ii = eng.matmul(bank[:, :],
                                    A(G.wbuf[bi2], kj * 128, [[NJ * 128, 128], [1, 128]]),
                                    xv(rhsbuf, kj), start=(kj == 0), stop=(kj == NJ - 1))
                return ii
            em.op("tensor", f, waits=mwaits, inc=mm)
            grp[oc] = mm.v
            evac(oc, bank)
            evm[oc] = sc.v
        return mm.at()

    def ev_r(oc, bank):
        em.op("scalar", (lambda eng, oc=oc, bank=bank: eng.activation(
            xv(G.bR, oc), bank[:, :], AF.Sigmoid)), waits=[(mm.h, mm.v)], inc=sc)
    w_rproj = proj(wr_d, G.bKA, ev_r, [w_xr])

    w_xk = emit_mix(G.muk, G.bMX, [(mm.h, w_rproj[1])])

    def ev_k(oc, bank):
        em.op("vector", (lambda eng, oc=oc, bank=bank: eng.tensor_scalar(
            xv(G.bKA, oc), bank[:, :], 0.0, None, ALU.max)), waits=[(mm.h, mm.v)], inc=dv)
        em.op("scalar", (lambda eng, oc=oc: eng.activation(
            xv(G.bKA, oc), xv(G.bKA, oc), AF.Square)), waits=[dv.at()], inc=sc)
    w_kproj = proj(wkr_d, G.bMX, ev_k, [w_xk])

    def ev_v(oc, bank):
        em.op("vector", (lambda eng, oc=oc, bank=bank: eng.tensor_tensor(
            xv(G.bG, oc), bank[:, :], xv(G.bR, oc), ALU.mult)), waits=[(mm.h, mm.v)], inc=dv)
        em.op("scalar", (lambda eng, oc=oc: eng.activation(
            xv(G.bG, oc), xv(G.bG, oc), AF.Silu)), waits=[dv.at()], inc=sc)
    w_vproj = proj(wvr_d, G.bKA, ev_v, [w_kproj, (sc.h, sc.v)])

    bS1 = G.bS1[t % 2]
    s1w = [G.s1st.at()]
    if t >= 2:
        s1w.append((dv.h, G.rw_cmb[t - 2]))
    for js in range(0, NJ, 2):
        em.op("sync", (lambda eng, t=t, js=js, bS1=bS1: eng.dma_start(
            A(bS1, js * TT, [[PITCH, 128], [TT, 2], [1, TT]]),
            A(s1_dram, t0 + js * TOK, [[NJ * TOK, 128], [TOK, 2], [1, TT]]))),
            waits=s1w if js == 0 else [], inc=G.s1ld, k=16)
    w_s1 = G.s1ld.at()

    for js in range(0, NJ, 2):
        em.op("sync", (lambda eng, t=t, js=js: eng.dma_start(
            A(G.bX1, js * TT, [[PITCH, 128], [TT, 2], [1, TT]]),
            A(x1_d, t0 + js * TOK, [[NJ * TOK, 128], [TOK, 2], [1, TT]]))),
            waits=[G.ost.at()] if js == 0 else [], inc=G.x1ld, k=16)
    w_x1 = G.x1ld.at()
    for j in range(NJ):
        def f(eng, j=j, bS1=bS1):
            eng.tensor_tensor(xv(bS1, j), xv(bS1, j), xv(G.bG, j), ALU.mult)
            return eng.tensor_tensor(xv(G.bX1, j), xv(G.bX1, j), xv(bS1, j), ALU.add)
        em.op("vector", f, waits=[w_x1, w_s1, (sc.h, sc.v)] if j == 0 else [], inc=dv)
    G.rw_cmb[t] = dv.v
    for js in range(0, NJ, 2):
        em.op("gpsimd", (lambda eng, t=t, js=js: eng.dma_start(
            A(out_d, t0 + js * TOK, [[NJ * TOK, 128], [TOK, 2], [1, TT]]),
            A(G.bX1, js * TT, [[PITCH, 128], [TT, 2], [1, TT]]))),
            waits=[dv.at()] if js == 0 else [], inc=G.ost, k=16)

    em.flush()


def l3_inputs(l2outA, l2outB, ins, half, x1, xn1, bcol):
    kf = np.empty((128, NJ, KTOT), np.float32)
    vf = np.empty((128, 32, DM), np.float32)
    for sl, r in enumerate((l2outA, l2outB)):
        for t in range(NT):
            kf[:, :, sl * TOK + t * TT:sl * TOK + (t + 1) * TT] = r[f"k1_{t}"]
            vf[:, sl * 16 + 4 * t:sl * 16 + 4 * t + 4, :] = r[f"v1_{t}"]
    return {
        "x1": x1, "xn1": xn1, "kf": kf, "vf": vf,
        "wq1": np.ascontiguousarray(ins["L1_attn_Wq"].transpose(1, 0, 2).reshape(DM, DM)),
        "wr": np.ascontiguousarray(ins["L1_ffn_W_r"].T),
        "wkr": np.ascontiguousarray(ins["L1_ffn_W_k"].T),
        "wvr": np.ascontiguousarray(ins["L1_ffn_W_v"].T),
        "mur": _vec128(ins["L1_ffn_mu_r"]), "muk": _vec128(ins["L1_ffn_mu_k"]),
        "bcol": bcol, "id128": np.eye(128, dtype=np.float32),
    }



def _fm(a):
    T = a.shape[0]
    return np.ascontiguousarray(a.T.reshape(NJ, 128, T).transpose(1, 0, 2))


def _from_fm(a):
    return np.ascontiguousarray(a.transpose(1, 0, 2).reshape(DM, -1).T)


def _vec128(v):
    return np.ascontiguousarray(v.reshape(-1, 128).T)


def _build_consts():
    eye64t = np.tile(np.eye(64, dtype=np.float32), (2, 2)).astype(np.float32)
    tril = np.tril(np.ones((64, 64), np.float32))
    cmask = np.tile(np.where(tril > 0, 0.0, 8.0 * NEG).astype(np.float32), (2, 8))
    strict = (np.arange(64)[:, None] < np.arange(64)[None, :]).astype(np.float32)
    smask = np.tile(strict, (2, 8))
    return eye64t, np.ascontiguousarray(cmask), np.ascontiguousarray(smask)


def l1_inputs(x_half, ins):
    eye64t, cmask, smask = _build_consts()
    return {
        "x": _fm(x_half),
        "wq": ins["L0_attn_Wq"], "wk": ins["L0_attn_Wk"], "wv": ins["L0_attn_Wv"],
        "bq": _vec128(ins["L0_attn_bq"]), "bk": _vec128(ins["L0_attn_bk"]),
        "bv": ins["L0_attn_bv"].reshape(1, DM),
        "wg1": ins["L0_attn_Wg1"], "bg1": ins["L0_attn_bg1"].reshape(GH, 1),
        "wg2": ins["L0_attn_Wg2"], "bg2": _vec128(ins["L0_attn_bg2"]),
        "w1t": np.ascontiguousarray(ins["L0_ffn_w1"].T),
        "w2t": np.ascontiguousarray(ins["L0_ffn_w2"].T),
        "id64": eye64t, "cmask": cmask, "smask": smask,
    }


RG_PAIRS = [[0, 1], [2, 3], [4, 5], [6, 7]]


def _uniq_names(nc):
    sb0, ps0 = nc.sbuf_tensor, nc.psum_tensor
    ctr = [0]

    def sb(name, *a, **k):
        ctr[0] += 1
        return sb0(f"{name}_u{ctr[0]}", *a, **k)

    def ps(name, *a, **k):
        ctr[0] += 1
        return ps0(f"{name}_u{ctr[0]}", *a, **k)
    nc.sbuf_tensor, nc.psum_tensor = sb, ps


def _join(em, jn, terms):
    base = jn.v
    for e in Em.ENG:
        em.op(e, lambda eng: eng.nop(), waits=(list(terms) if e == "sync" else []),
              inc=jn, k=1)
    for e in Em.ENG:
        em.op(e, lambda eng: eng.nop(), waits=[(jn.h, base + 5)])


def build_fused():
    nc = bass.Bass("TRN2", target_bir_lowering=False)
    _uniq_names(nc)
    D = _l1_dram(nc, kind_out="Internal", t_dt=BF16)
    D.flag = nc.dram_tensor("flag", [128, 1], F32, kind="ExternalInput")
    D.sendg = nc.dram_tensor("sendg", [2, 128, 512], F32)
    D.kgs = [nc.dram_tensor(f"kg_{t}", [2, 128, NJ, TT], F32) for t in range(NT)]
    D.vgs = [nc.dram_tensor(f"vg_{t}", [2, 128, 4, DM], F32) for t in range(NT)]
    D.bcg = nc.dram_tensor("bcg", [2, 128, NJ], F32)
    D.xn1col = nc.dram_tensor("xn1col", [128, NJ], F32)
    _l2_dram(nc, kind_out="Internal", D=D)
    _l3_dram(nc, D=D)
    D.sp_ap = A(D.sendg, 0, [[512, 128], [1, 512]])

    with contextlib.ExitStack() as st:
        sems = {nm: Sem(st, nc, nm) for nm in L3_SEMS}
        jn = Sem(st, nc, "jn")
        SER_SEMS.clear()
        SER_SEMS.extend([sems["mm"], sems["dv"], sems["sc"]])
        cc1, cc2 = sems["cc1"], sems["cc2"]

        with contextlib.ExitStack() as st1:
            G1 = _l1_emit(nc, st1, sems, D)
            em = Em(nc)
            em.op("gpsimd", lambda eng: eng.collective_compute(
                "AllGather", ALU.bypass, replica_groups=RG_PAIRS,
                ins=[D.send.ap().opt()], outs=[D.sendg.ap().opt()]),
                waits=[G1.fin.at()], inc=cc1, k=1)
            _join(em, jn, [G1.fin.at(), G1.xns.at(), G1.ss.at(), G1.qs.at(),
                           G1.l0.at(), G1.l1.at(), G1.t0.at(), G1.t1.at()])
            em.flush()

        D.sp_gate = (cc1.h, 1)
        with contextlib.ExitStack() as st2:
            G2 = _l2_emit(nc, st2, sems, D)
            em = Em(nc)
            l2_term = [G2.sst.at(), G2.l0.at(), G2.l1.at()]
            _join(em, jn, l2_term)
            em.flush()

        D.kv_gate = (cc2.h, 2 * NT)
        D.bc_gate = (cc1.h, 2)
        with contextlib.ExitStack() as st3:
            _l3_emit(nc, st3, sems, D)
    return nc


def fused_inputs(x_half, half, ins):
    d = l1_inputs(x_half, ins)
    d.update({
        "l1g": _vec128(ins["L0_ffn_ln1_g"]), "l1b": _vec128(ins["L0_ffn_ln1_b"]),
        "b1": _vec128(ins["L0_ffn_b1"]), "l2g": _vec128(ins["L0_ffn_ln2_g"]),
        "l2b": _vec128(ins["L0_ffn_ln2_b"]), "b2": _vec128(ins["L0_ffn_b2"]),
        "wk1": np.ascontiguousarray(ins["L1_attn_Wk"].transpose(1, 0, 2).reshape(DM, DM)),
        "wv1": np.ascontiguousarray(ins["L1_attn_Wv"].transpose(1, 0, 2).reshape(DM, DM)),
        "wq1": np.ascontiguousarray(ins["L1_attn_Wq"].transpose(1, 0, 2).reshape(DM, DM)),
        "wr": np.ascontiguousarray(ins["L1_ffn_W_r"].T),
        "wkr": np.ascontiguousarray(ins["L1_ffn_W_k"].T),
        "wvr": np.ascontiguousarray(ins["L1_ffn_W_v"].T),
        "mur": _vec128(ins["L1_ffn_mu_r"]), "muk": _vec128(ins["L1_ffn_mu_k"]),
        "id128": np.eye(128, dtype=np.float32),
        "flag": np.full((128, 1), float(half), np.float32),
    })
    return d


_NC_CACHE = {}


def _get_ncs():
    if "ncs" not in _NC_CACHE:
        _NC_CACHE["ncs"] = (build_launch1(), build_launch2(), build_launch3())
    return _NC_CACHE["ncs"]


def _get_fused():
    if "fused" not in _NC_CACHE:
        _NC_CACHE["fused"] = build_fused()
    return _NC_CACHE["fused"]


def kernel(**inputs):
    import os
    if bool(int(os.environ.get("KERNEL_UNFUSED", "0"))):
        return _kernel_unfused(**inputs)
    ins = inputs
    nc = _get_fused()
    x_full = np.asarray(ins["L0_attn_x"], dtype=np.float32)
    cores = list(range(NCORES))
    im = []
    for core in cores:
        b, half = core // 2, core % 2
        x_half = np.ascontiguousarray(x_full[b, half * TOK:(half + 1) * TOK, :])
        im.append(fused_inputs(x_half, half, ins))
    r = run_bass_kernel_spmd(nc, im, core_ids=cores).results
    out = np.empty((4, 2 * TOK, DM), dtype=np.float32)
    for core in cores:
        b, half = core // 2, core % 2
        out[b, half * TOK:(half + 1) * TOK, :] = _from_fm(r[core]["out"])
    return out


def _kernel_unfused(**inputs):
    ins = inputs
    nc1, nc2, nc3 = _get_ncs()
    x_full = np.asarray(ins["L0_attn_x"], dtype=np.float32)
    cores = list(range(NCORES))

    im1 = []
    for core in cores:
        b, half = core // 2, core % 2
        x_half = np.ascontiguousarray(x_full[b, half * TOK:(half + 1) * TOK, :])
        im1.append(l1_inputs(x_half, ins))
    r1 = run_bass_kernel_spmd(nc1, im1, core_ids=cores).results

    im2 = []
    for core in cores:
        b, half = core // 2, core % 2
        r1c = dict(r1[core])
        r1c["__x"] = im1[core]["x"]
        sp = (np.ascontiguousarray(r1[core - 1]["send"]) if half == 1
              else np.zeros((128, 512), np.float32))
        im2.append(l2_inputs(r1c, ins, sp))
    r2 = run_bass_kernel_spmd(nc2, im2, core_ids=cores).results

    im3 = []
    for core in cores:
        b, half = core // 2, core % 2
        a, bb = 2 * b, 2 * b + 1
        bcol = (np.ascontiguousarray(r2[a]["xn1"][:, :, -1]) if half == 1
                else np.zeros((128, NJ), np.float32))
        im3.append(l3_inputs(r2[a], r2[bb], ins, half,
                             np.ascontiguousarray(r2[core]["x1"]),
                             np.ascontiguousarray(r2[core]["xn1"]), bcol))
    r3 = run_bass_kernel_spmd(nc3, im3, core_ids=cores).results

    out = np.empty((4, 2 * TOK, DM), dtype=np.float32)
    for core in cores:
        b, half = core // 2, core % 2
        out[b, half * TOK:(half + 1) * TOK, :] = _from_fm(r3[core]["out"])
    return out


# version 28
# speedup vs baseline: 1.0499x; 1.0499x over previous
import contextlib
import numpy as np
import concourse.bass as bass
import concourse.mybir as mybir
from concourse.bass_utils import run_bass_kernel_spmd

F32 = mybir.dt.float32
F32R = mybir.dt.float32r
BF16 = mybir.dt.bfloat16
AF = mybir.ActivationFunctionType
ALU = mybir.AluOpType
AX = mybir.AxisListType

NCORES = 8
TOK = 2048
NT = 4
TT = 512
NJ = 8
DM = 1024
NCH = 32
CPT = 8
CH = 64
H = 16
DH = 64
HID = 4096
GH = 16
LNEPS = 1e-5
NEG = -1e9
PITCH = 4096


def A(t, off, dims):
    return bass.AP(t, off, dims)


class Sem:
    def __init__(self, stack, nc, name):
        self.h = stack.enter_context(nc.semaphore(name))
        self.v = 0

    def at(self):
        return (self.h, self.v)


SER_SEMS = []


class Em:

    ENG = ("sync", "gpsimd", "tensor", "vector", "scalar")

    def __init__(self, nc):
        import os
        self.nc = nc
        self.q = {e: [] for e in self.ENG}
        self.ser = bool(int(os.environ.get("EM_SERIALIZE", "0")))

    def op(self, engine, fn, waits=(), inc=None, k=1):
        waits = [w for w in waits if w is not None]
        if self.ser:
            waits = waits + [s.at() for s in SER_SEMS]
        self.q[engine].append((fn, waits, inc, k))
        if inc is not None:
            inc.v += k
        return inc.v if inc is not None else None

    def flush(self):
        with self.nc.Block() as blk:
            def runner(eng, ops):
                for fn, waits, inc, k in ops:
                    for (sh, v) in waits:
                        if v > 0:
                            eng.wait_ge(sh, v)
                    instr = fn(eng)
                    if inc is not None:
                        instr.then_inc(inc.h, k)
            for name in self.ENG:
                ops = self.q[name]
                if ops:
                    getattr(blk, name)(lambda eng, ops=ops: runner(eng, ops))
        self.q = {e: [] for e in self.ENG}



L1_SEMS = ("mm", "dv", "sc", "c", "x", "w0", "w1", "v0", "v1", "v2", "v3",
           "xns", "ss", "qs", "l0", "l1", "t0", "t1", "fin")
L2_SEMS = L1_SEMS + ("sst",)
L3_SEMS = L2_SEMS + ("kb0", "kb1", "vb0", "vb1", "qe", "ke", "me",
                     "s1st", "s1ld", "x1ld", "ost", "cc1", "cc2")


class NS:
    def __init__(self, **kw):
        self.__dict__.update(kw)


def _l1_dram(nc, kind_in="ExternalInput", kind_out="ExternalOutput", t_dt=F32):
    D = NS()
    D.x = nc.dram_tensor("x", [128, NJ, TOK], F32, kind=kind_in)
    D.wq = nc.dram_tensor("wq", [DM, DM], F32, kind=kind_in)
    D.wk = nc.dram_tensor("wk", [DM, DM], F32, kind=kind_in)
    D.wv = nc.dram_tensor("wv", [DM, DM], F32, kind=kind_in)
    D.bq = nc.dram_tensor("bq", [128, NJ], F32, kind=kind_in)
    D.bk = nc.dram_tensor("bk", [128, NJ], F32, kind=kind_in)
    D.bv = nc.dram_tensor("bv", [1, DM], F32, kind=kind_in)
    D.wg1 = nc.dram_tensor("wg1", [DM, GH], F32, kind=kind_in)
    D.bg1 = nc.dram_tensor("bg1", [GH, 1], F32, kind=kind_in)
    D.wg2 = nc.dram_tensor("wg2", [GH, DM], F32, kind=kind_in)
    D.bg2 = nc.dram_tensor("bg2", [128, NJ], F32, kind=kind_in)
    D.w1t = nc.dram_tensor("w1t", [DM, HID], F32, kind=kind_in)
    D.w2t = nc.dram_tensor("w2t", [HID, DM], F32, kind=kind_in)
    D.id64 = nc.dram_tensor("id64", [128, 128], F32, kind=kind_in)
    D.cmask = nc.dram_tensor("cmask", [128, 512], F32, kind=kind_in)
    D.smask = nc.dram_tensor("smask", [128, 512], F32, kind=kind_in)
    D.xn = nc.dram_tensor("xn", [128, NJ, TOK], F32, kind=kind_out)
    D.s = nc.dram_tensor("s", [128, NJ, TOK], F32, kind=kind_out)
    D.qh = nc.dram_tensor("qh", [128, NJ, TOK], F32, kind=kind_out)
    D.lent = nc.dram_tensor("lent", [NCH, 128, 512], F32, kind=kind_out)
    D.dcum = nc.dram_tensor("dcum", [NCH, 128, 8], F32, kind=kind_out)
    D.send = nc.dram_tensor("send", [128, 512], F32, kind=kind_out)
    D.t1 = nc.dram_tensor("t1", [DM, HID], t_dt, kind=kind_out)
    D.t2 = nc.dram_tensor("t2", [HID, DM], t_dt, kind=kind_out)
    D.m1 = nc.dram_tensor("m1", [128, 2], F32, kind=kind_out)
    D.m2 = nc.dram_tensor("m2", [128, 2], F32, kind=kind_out)
    return D


def _l1_emit(nc, st, sems, D):
    E = st.enter_context
    g = {}
    for nm in ("bA", "bB", "bC", "bD", "bE", "bF"):
        g[nm] = E(nc.sbuf_tensor(nm, [128, PITCH], F32))
    for nm in ("pE1", "pE2", "pQT", "pKT", "pKH", "pQH"):
        g[nm] = E(nc.sbuf_tensor(nm, [128, 1024], F32))
    g["wbuf"] = [E(nc.sbuf_tensor(f"wbuf{i}", [128, NJ * 128], F32)) for i in range(2)]
    g["scr"] = [E(nc.sbuf_tensor(f"scr{i}", [128, 512], F32)) for i in range(4)]
    g["bBb"] = E(nc.sbuf_tensor("bBb", [128, PITCH], BF16))
    g["wbufb"] = [E(nc.sbuf_tensor(f"wbufb{i}", [128, NJ * 128], BF16)) for i in range(2)]
    g["wg1_sb"] = E(nc.sbuf_tensor("wg1s", [128, NJ * GH], F32))
    g["wg2_sb"] = E(nc.sbuf_tensor("wg2s", [GH, DM], F32))
    g["bg1_sb"] = E(nc.sbuf_tensor("bg1s", [GH, 1], F32))
    g["bq_sb"] = E(nc.sbuf_tensor("bqs", [128, NJ], F32))
    g["bk_sb"] = E(nc.sbuf_tensor("bks", [128, NJ], F32))
    g["bg2_sb"] = E(nc.sbuf_tensor("bg2s", [128, NJ], F32))
    g["bvbc"] = E(nc.sbuf_tensor("bvbc", [128, DM], F32))
    g["bv_row"] = E(nc.sbuf_tensor("bvrow", [1, DM], F32))
    g["id64"] = E(nc.sbuf_tensor("id64s", [128, 128], F32))
    g["cmask"] = E(nc.sbuf_tensor("cmasks", [128, 512], F32))
    g["smask"] = E(nc.sbuf_tensor("smasks", [128, 512], F32))
    g["ones_c"] = E(nc.sbuf_tensor("onesc", [128, 1], F32))
    g["ones_r"] = E(nc.sbuf_tensor("onesr", [1, 128], F32))
    g["g1_sb"] = E(nc.sbuf_tensor("g1s", [GH, TT], F32))
    g["P_sb"] = E(nc.sbuf_tensor("Psb", [128, 1024], F32))
    g["PT_sb"] = E(nc.sbuf_tensor("PTsb", [128, 1024], F32))
    g["KT_sb"] = E(nc.sbuf_tensor("KTsb", [128, 1024], F32))
    g["At_sb"] = E(nc.sbuf_tensor("Atsb", [128, 1024], F32))
    g["S_all"] = E(nc.sbuf_tensor("Sall", [128, 512], F32))
    g["lstage"] = [E(nc.sbuf_tensor(f"lst{i}", [128, 512], F32)) for i in range(2)]
    g["msb"] = E(nc.sbuf_tensor("msb", [128, 64], F32))
    g["tcb"] = E(nc.sbuf_tensor("tcb", [128, 64], F32))
    g["tcm"] = E(nc.sbuf_tensor("tcms", [128, 64], F32))
    g["etb"] = E(nc.sbuf_tensor("etb", [128, 64], F32))
    g["tcum"] = E(nc.sbuf_tensor("tcum", [128, 8], F32))
    g["dstage"] = E(nc.sbuf_tensor("dstage", [128, NCH * 8], F32))
    g["redc"] = E(nc.sbuf_tensor("redc", [128, 64], F32))
    g["mcols"] = E(nc.sbuf_tensor("mcols", [128, 2], F32))
    g["mrow"] = E(nc.sbuf_tensor("mrow", [1, 2], F32))
    g["sgb"] = E(nc.sbuf_tensor("sgb", [128, 512], F32))
    g["sgb2"] = E(nc.sbuf_tensor("sgb2", [128, 512], F32))
    g["cns"] = E(nc.sbuf_tensor("cns", [128, 2], F32))
    g["tqb"] = [E(nc.sbuf_tensor(f"tqb{i}", [128, 512], F32)) for i in range(2)]
    g["tscr"] = [E(nc.sbuf_tensor(f"tscr{i}", [128, 512], F32)) for i in range(2)]
    g["tred"] = E(nc.sbuf_tensor("tred", [128, 128], F32))
    g["pb"] = [E(nc.psum_tensor(f"pb{i}", [128, 512], F32)) for i in range(8)]
    g.update(sems)
    G = type("G", (), g)

    import os
    lim = int(os.environ.get("L1_PHASES", "99"))
    ph = 0
    with nc.named_scope("l1c"):
        _l1_consts(nc, G, D.id64, D.cmask, D.smask, D.bq, D.bk, D.bg2, D.bg1,
                   D.wg2, D.bv, D.wg1)
    ph += 1
    gens = [_l1_tern_p1(nc, G, D.w1t, DM, HID, 0),
            _l1_tern_p1(nc, G, D.w2t, HID, DM, 64)]

    def drive(k):
        em = Em(nc)
        G.tern_em = em
        while k > 0 and gens:
            try:
                next(gens[0])
                k -= 1
            except StopIteration:
                gens.pop(0)
        em.flush()

    for t in range(NT):
        if ph < lim:
            with nc.named_scope(f"l1p{t}"):
                _l1_tile_proj(nc, G, t, D.x, D.xn, D.wq, D.wk, D.wv)
            ph += 1
        drive(16)
        if ph < lim:
            with nc.named_scope(f"l1k{t}"):
                _l1_tile_chunks(nc, G, t, D.s, D.qh, D.lent)
            ph += 1
        drive(16)
    if ph < lim:
        with nc.named_scope("l1f"):
            _l1_finalize(nc, G, D.send, D.dcum)
        ph += 1
    drive(1 << 30)
    if ph < lim:
        with nc.named_scope("l1t1"):
            _l1_ternary(nc, G, D.w1t, D.t1, D.m1, DM, HID, 0)
        ph += 1
    if ph < lim:
        with nc.named_scope("l1t2"):
            _l1_ternary(nc, G, D.w2t, D.t2, D.m2, HID, DM, 64)
    return G


def build_launch1():
    nc = bass.Bass("TRN2", target_bir_lowering=False)
    D = _l1_dram(nc)
    with contextlib.ExitStack() as st:
        sems = {nm: Sem(st, nc, nm) for nm in L1_SEMS}
        SER_SEMS.clear()
        SER_SEMS.extend([sems["mm"], sems["dv"], sems["sc"]])
        with contextlib.ExitStack() as st_b:
            G = _l1_emit(nc, st_b, sems, D)
        em = Em(nc)
        em.op("gpsimd", lambda eng: eng.nop(),
              waits=[G.fin.at(), G.xns.at(), G.ss.at(), G.qs.at(),
                     G.l0.at(), G.l1.at(), G.t0.at(), G.t1.at()])
        em.flush()
    return nc


def _l1_consts(nc, G, id64_d, cmask_d, smask_d, bq_d, bk_d, bg2_d, bg1_d,
               wg2_d, bv_d, wg1_d):
    em = Em(nc)
    ld, dv, mm = G.c, G.dv, G.mm

    for dst, src in (
        (G.id64[:], id64_d.ap()), (G.cmask[:], cmask_d.ap()), (G.smask[:], smask_d.ap()),
        (G.bq_sb[:], bq_d.ap()), (G.bk_sb[:], bk_d.ap()), (G.bg2_sb[:], bg2_d.ap()),
        (G.bg1_sb[:], bg1_d.ap()), (G.wg2_sb[:], wg2_d.ap()), (G.bv_row[:], bv_d.ap()),
        (G.wg1_sb[:], A(wg1_d, 0, [[GH, 128], [128 * GH, NJ], [1, GH]])),
    ):
        em.op("sync", (lambda eng, a=dst, b=src: eng.dma_start(a, b)), inc=ld, k=16)

    def f(eng):
        eng.memset(G.ones_c[:], 1.0)
        eng.memset(G.ones_r[:], 1.0)
        eng.memset(G.tcum[:], 0.0)
        eng.memset(G.cns[:, 0:1], float(LNEPS))
        eng.memset(G.cns[:, 1:2], 0.1)
        return eng.memset(G.S_all[:], 0.0)
    em.op("vector", f, inc=dv)

    def f(eng):
        eng.matmul(G.pb[0][:, :], G.ones_r[:], G.bv_row[0:1, 0:512], start=True, stop=True)
        return eng.matmul(G.pb[1][:, :], G.ones_r[:], G.bv_row[0:1, 512:1024],
                          start=True, stop=True)
    em.op("tensor", f, waits=[ld.at(), dv.at()], inc=mm)

    def f(eng):
        eng.tensor_copy(G.bvbc[:, 0:512], G.pb[0][:, :])
        return eng.tensor_copy(G.bvbc[:, 512:1024], G.pb[1][:, :])
    em.op("vector", f, waits=[mm.at()], inc=dv)

    em.flush()


def _l1_tile_proj(nc, G, t, x_d, xn_d, wq_d, wk_d, wv_d):
    em = Em(nc)
    mm, dv, sc = G.mm, G.dv, G.sc
    wsem = [G.w0, G.w1]
    vsem = [G.v0, G.v1]
    bA, bB, bC, bD, bE, bF = G.bA, G.bB, G.bC, G.bD, G.bE, G.bF
    pb = G.pb
    t0 = t * TT

    def xv(buf, j, lo=0, n=TT):
        return A(buf, j * TT + lo, [[PITCH, 128], [1, n]])

    for js in range(0, NJ, 2):
        em.op("sync", (lambda eng, js=js: eng.dma_start(
            A(bA, js * TT, [[PITCH, 128], [TT, 2], [1, TT]]),
            A(x_d, t0 + js * TOK, [[NJ * TOK, 128], [TOK, 2], [1, TT]]))),
            waits=[G.c.at()] if js == 0 else [], inc=G.x, k=16)
    w_x = G.x.at()

    mm_sq = {}
    for j in range(NJ):
        waits = [w_x] + ([(mm.h, mm_sq[j - 2])] if j >= 2 else [])
        em.op("scalar", (lambda eng, j=j: eng.activation(
            G.scr[j % 2][:], xv(bA, j), AF.Square)), waits=waits, inc=sc)
        w_sqj = sc.at()
        em.op("tensor", (lambda eng, j=j: eng.matmul(
            pb[1][0:1, :], G.ones_c[:], G.scr[j % 2][:],
            start=(j == 0), stop=(j == NJ - 1))), waits=[w_sqj], inc=mm)
        mm_sq[j] = mm.v
    w_sq_done = mm.at()

    def f(eng):
        for j in range(NJ):
            ii = eng.matmul(pb[0][0:1, :], G.ones_c[:], xv(bA, j),
                            start=(j == 0), stop=(j == NJ - 1))
        return ii
    em.op("tensor", f, waits=[w_x], inc=mm)
    w_stats = mm.at()

    def r1(off, n=TT):
        return A(G.pE1, off, [[1024, 1], [1, n]])

    def r2(off, n=TT):
        return A(G.pE2, off, [[1024, 1], [1, n]])

    def f(eng):
        eng.tensor_scalar(r1(0), pb[0][0:1, :], 1.0 / DM, None, ALU.mult)
        eng.tensor_scalar(r1(512), pb[1][0:1, :], 1.0 / DM, None, ALU.mult)
        eng.tensor_tensor(r2(0), r1(0), r1(0), ALU.mult)
        return eng.tensor_tensor(r2(512), r1(512), r2(0), ALU.subtract)
    em.op("vector", f, waits=[w_stats], inc=dv)

    def f(eng):
        eng.activation(r2(512), r2(512), AF.Ln, bias=G.cns[0:1, 0:1])
        return eng.activation(r2(512), r2(512), AF.Exp, scale=-0.5)
    em.op("scalar", f, waits=[dv.at()], inc=sc)

    em.op("vector", lambda eng: eng.scalar_tensor_tensor(
        r2(0), r1(0), -1.0, r2(512), ALU.mult, ALU.mult), waits=[sc.at()], inc=dv)

    def f(eng):
        eng.matmul(pb[2][:, :], G.ones_r[:], r2(512), start=True, stop=True)
        return eng.matmul(pb[3][:, :], G.ones_r[:], r2(0), start=True, stop=True)
    em.op("tensor", f, waits=[dv.at()], inc=mm)
    w_bc = mm.at()

    for j in range(NJ):
        def f(eng, j=j):
            eng.tensor_tensor(xv(bB, j), xv(bA, j), pb[2][:, :], ALU.mult)
            eng.tensor_tensor(xv(bB, j), xv(bB, j), pb[3][:, :], ALU.add)
            return eng.tensor_copy(A(G.bBb, j * TT, [[PITCH, 128], [1, TT]]), xv(bB, j))
        em.op("vector", f, waits=([w_bc, G.ss.at()] if j == 0 else []), inc=dv)
    w_xn = dv.at()

    for js in range(0, NJ, 2):
        em.op("gpsimd", (lambda eng, js=js: eng.dma_start(
            A(xn_d, t0 + js * TOK, [[NJ * TOK, 128], [TOK, 2], [1, TT]]),
            A(bB, js * TT, [[PITCH, 128], [TT, 2], [1, TT]]))),
            waits=[w_xn] if js == 0 else [], inc=G.xns, k=16)

    wplan = [(wq_d, G.bq_sb, bC, oc) for oc in range(8)] + \
            [(wk_d, G.bk_sb, bD, oc) for oc in range(8)]
    grp_done = {}
    ev_done = {}
    wc_done = {}
    prevc = getattr(G, "qk_wcp", None)
    for n, (w_d_, b_sb, dst, oc) in enumerate(wplan):
        bi = n % 2
        dwaits = ([(dv.h, wc_done[n - 2])] if n >= 2 else
                  ([(dv.h, prevc)] if prevc is not None else []))
        em.op("sync", (lambda eng, w_d_=w_d_, oc=oc, bi=bi: eng.dma_start(
            G.wbuf[bi][:], A(w_d_, 128 * oc, [[DM, 128], [128 * DM, NJ], [1, 128]]))),
            waits=dwaits, inc=wsem[bi], k=16)
        w_wld = wsem[bi].at()
        cwaits = [w_wld] + ([(mm.h, grp_done[n - 2])] if n >= 2 else [])
        em.op("vector", (lambda eng, bi=bi: eng.tensor_copy(
            G.wbufb[bi][:, :], G.wbuf[bi][:, :])), waits=cwaits, inc=dv)
        wc_done[n] = dv.v
        w_wc = dv.at()
        bank = pb[4 + bi]
        mwaits = [w_wc, w_xn] + ([(dv.h, ev_done[n - 2])] if n >= 2 else [])

        def f(eng, bi=bi, bank=bank):
            for kj in range(NJ):
                ii = eng.matmul(bank[:, :], A(G.wbufb[bi], kj * 128, [[NJ * 128, 128], [1, 128]]),
                                A(G.bBb, kj * TT, [[PITCH, 128], [1, TT]]),
                                start=(kj == 0), stop=(kj == NJ - 1))
            return ii
        em.op("tensor", f, waits=mwaits, inc=mm)
        grp_done[n] = mm.v
        em.op("vector", (lambda eng, bank=bank, b_sb=b_sb, dst=dst, oc=oc:
                         eng.tensor_scalar(xv(dst, oc), bank[:, :], b_sb[:, oc:oc + 1],
                                           None, ALU.add)),
              waits=[(mm.h, grp_done[n])], inc=dv)
        ev_done[n] = dv.v
    G.qk_wcp = wc_done[len(wplan) - 1]

    def f(eng):
        for j in range(NJ):
            ii = eng.matmul(pb[6][0:GH, :], A(G.wg1_sb, j * GH, [[NJ * GH, 128], [1, GH]]),
                            xv(bB, j), start=(j == 0), stop=(j == NJ - 1))
        return ii
    em.op("tensor", f, waits=[w_xn], inc=mm)
    em.op("scalar", lambda eng: eng.activation(
        G.g1_sb[:, :], pb[6][0:GH, :], AF.Silu, bias=G.bg1_sb[:, :]),
        waits=[mm.at()], inc=sc)
    w_g1 = sc.at()

    gl_marks = {}
    for j in range(NJ):
        waits = [w_g1] + ([(sc.h, gl_marks[j - 2])] if j >= 2 else [])
        em.op("tensor", (lambda eng, j=j: eng.matmul(
            pb[6 + (j % 2)][:, :], G.wg2_sb[0:GH, 128 * j:128 * (j + 1)], G.g1_sb[:, :],
            start=True, stop=True)), waits=waits, inc=mm)
        em.op("scalar", (lambda eng, j=j: eng.activation(
            xv(bE, j), pb[6 + (j % 2)][:, :], AF.Sigmoid, bias=G.bg2_sb[:, j:j + 1])),
            waits=[mm.at()], inc=sc)
        gl_marks[j] = sc.v
    for j in range(NJ):
        em.op("scalar", (lambda eng, j=j: eng.activation(
            xv(bE, j), xv(bE, j), AF.Ln, scale=0.9, bias=G.cns[:, 1:2])), inc=sc)
    w_L = sc.at()

    vmm = {}
    vev = {}
    vc_done = {}
    prevv = getattr(G, "v_wcp", None)
    for half in range(2):
        for kj in range(NJ):
            n = half * NJ + kj
            bi = n % 2
            dwaits = [w_sq_done] + ([(dv.h, vc_done[n - 2])] if n >= 2 else
                                    ([(dv.h, prevv)] if prevv is not None else []))
            em.op("sync", (lambda eng, kj=kj, half=half, bi=bi: eng.dma_start(
                G.scr[bi][:], A(wv_d, kj * 128 * DM + half * 512, [[DM, 128], [1, 512]]))),
                waits=dwaits, inc=vsem[bi], k=16)
            w_vld = vsem[bi].at()
            cwaits = [w_vld] + ([(mm.h, vmm[n - 2])] if n >= 2 else [])
            em.op("vector", (lambda eng, bi=bi: eng.tensor_copy(
                A(G.wbufb[bi], 0, [[NJ * 128, 128], [1, 512]]), G.scr[bi][:, :])),
                waits=cwaits, inc=dv)
            vc_done[n] = dv.v
            w_vc = dv.at()
            mwaits = [w_vc, w_xn]
            if kj == 0 and half == 1:
                mwaits.append((dv.h, vev[0]))

            def f(eng, kj=kj, bi=bi):
                for ts in range(4):
                    ii = eng.matmul(pb[ts][:, :],
                                    A(G.bBb, kj * TT + ts * 128, [[PITCH, 128], [1, 128]]),
                                    A(G.wbufb[bi], 0, [[NJ * 128, 128], [1, 512]]),
                                    start=(kj == 0), stop=(kj == NJ - 1))
                return ii
            em.op("tensor", f, waits=mwaits, inc=mm)
            vmm[n] = mm.v
        for ts in range(4):
            em.op("vector", (lambda eng, ts=ts, half=half: eng.tensor_tensor(
                A(bA, ts * 1024 + half * 512, [[PITCH, 128], [1, 512]]),
                pb[ts][:, :], G.bvbc[:, half * 512:(half + 1) * 512], ALU.add)),
                waits=[(mm.h, vmm[half * NJ + NJ - 1])], inc=dv)
        vev[half] = dv.v
    G.v_wcp = vc_done[2 * NJ - 1]

    for j in range(NJ):
        for c in range(CPT):
            em.op("vector", (lambda eng, j=j, c=c: eng.tensor_tensor_scan(
                A(bF, j * TT + c * CH, [[PITCH, 128], [1, CH]]),
                A(bE, j * TT + c * CH, [[PITCH, 128], [1, CH]]),
                A(bE, j * TT + c * CH, [[PITCH, 128], [1, CH]]),
                0.0, ALU.add, ALU.bypass)),
                waits=[w_L] if (j == 0 and c == 0) else [], inc=dv)

    def f(eng):
        eng.tensor_copy(G.msb[:, :], A(bF, 31, [[PITCH, 128], [TT, NJ], [CH, CPT]]))
        return eng.tensor_copy(G.tcb[:, :], A(bF, 63, [[PITCH, 128], [TT, NJ], [CH, CPT]]))
    em.op("vector", f, inc=dv)
    em.op("scalar", lambda eng: eng.activation(G.etb[:, :], G.tcb[:, :], AF.Exp),
          waits=[dv.at()], inc=sc)
    em.op("vector", lambda eng: eng.tensor_tensor(
        G.tcb[:, :], G.tcb[:, :], G.msb[:, :], ALU.subtract), waits=[sc.at()], inc=dv)
    em.op("scalar", lambda eng: eng.activation(G.tcm[:, :], G.tcb[:, :], AF.Exp),
          waits=[dv.at()], inc=sc)

    def f(eng):
        for j in range(NJ):
            ii = eng.tensor_tensor(xv(bE, j), xv(bF, j), xv(bE, j), ALU.subtract)
        return ii
    em.op("vector", f, inc=dv)

    em.op("vector", lambda eng: eng.memset(bB[:, :], 0.0),
          waits=[mm.at(), G.xns.at()], inc=dv)

    em.flush()


def _l1_tile_chunks(nc, G, t, s_d, qh_d, lent_d):
    import os
    sub = int(os.environ.get("L1_CHUNK_SUB", "99"))
    em = Em(nc)
    mm, dv, sc = G.mm, G.dv, G.sc
    lsem = [G.l0, G.l1]
    bA, bB, bC, bD, bE, bF = G.bA, G.bB, G.bC, G.bD, G.bE, G.bF
    pb = G.pb
    t0 = t * TT

    last_soft = {0: 0, 1: 0}
    last_at = 0
    last_sadd = 0
    last_ds = 0
    last_tcopy = 0
    prev_pair_mm = 0
    prev_ptr_mm = 0

    npair = int(os.environ.get("L1_PAIRS", "4"))
    for p in range(npair):
        c0, c1 = 2 * p, 2 * p + 1
        def f(eng, p=p):
            for j in range(NJ):
                eng.tensor_tensor(
                    A(G.pE1, j * 128, [[1024, 128], [1, 128]]),
                    A(bE, j * TT + p * 128, [[PITCH, 128], [1, 128]]),
                    A(G.msb, j * CPT + 2 * p, [[64, 128], [1, 2], [0, 64]]),
                    ALU.subtract)
                ii = eng.tensor_tensor(
                    A(G.pE2, j * 128, [[1024, 128], [1, 128]]),
                    A(bF, j * TT + p * 128, [[PITCH, 128], [1, 128]]),
                    A(G.msb, j * CPT + 2 * p, [[64, 128], [1, 2], [0, 64]]),
                    ALU.subtract)
            return ii
        em.op("vector", f, waits=[sc.at(), (mm.h, prev_pair_mm)], inc=dv)

        def f(eng):
            eng.activation(G.pE1[:, :], G.pE1[:, :], AF.Exp)
            return eng.activation(G.pE2[:, :], G.pE2[:, :], AF.Exp, scale=-1.0)
        em.op("scalar", f, waits=[dv.at()], inc=sc)
        w_e12 = sc.at()

        def f(eng, p=p):
            for j in range(NJ):
                eng.tensor_tensor(A(G.pQT, j * 128, [[1024, 128], [1, 128]]),
                                  A(bC, j * TT + p * 128, [[PITCH, 128], [1, 128]]),
                                  A(G.pE1, j * 128, [[1024, 128], [1, 128]]), ALU.mult)
                eng.tensor_tensor(A(G.pKT, j * 128, [[1024, 128], [1, 128]]),
                                  A(bD, j * TT + p * 128, [[PITCH, 128], [1, 128]]),
                                  A(G.pE2, j * 128, [[1024, 128], [1, 128]]), ALU.mult)
                ii = eng.tensor_tensor(
                    A(G.pKH, j * 128, [[1024, 128], [1, 128]]),
                    A(G.pKT, j * 128, [[1024, 128], [1, 128]]),
                    A(G.tcm, j * CPT + 2 * p, [[64, 128], [1, 2], [0, 64]]), ALU.mult)
            return ii
        em.op("vector", f, waits=[w_e12], inc=dv)
        w_qkt = dv.at()

        em.op("scalar", (lambda eng, p=p: eng.activation(
            G.pE1[:, :], A(bE, p * 128, [[PITCH, 128], [TT, NJ], [1, 128]]), AF.Exp)),
            waits=[w_qkt], inc=sc)

        def f(eng, p=p):
            for j in range(NJ):
                ii = eng.tensor_tensor(A(G.pQH, j * 128, [[1024, 128], [1, 128]]),
                                       A(bC, j * TT + p * 128, [[PITCH, 128], [1, 128]]),
                                       A(G.pE1, j * 128, [[1024, 128], [1, 128]]), ALU.mult)
            return ii
        em.op("vector", f, waits=[sc.at(), G.qs.at()], inc=dv)

        em.op("gpsimd", (lambda eng, p=p: eng.dma_start(
            A(qh_d, t0 + p * 128, [[NJ * TOK, 128], [TOK, NJ], [1, 128]]),
            A(G.pQH, 0, [[1024, 128], [128, NJ], [1, 128]]))),
            waits=[dv.at()], inc=G.qs, k=16)

        if sub < 2:
            continue
        for ci, c in enumerate((c0, c1)):
            bank = pb[ci]

            for par in (0, 1):
                def f(eng, c=c, bank=bank, par=par):
                    for h in range(par, H, 2):
                        ii = eng.matmul(
                            A(bank, (h % 2) * 64 * 512 + (h // 2) * 64, [[512, 64], [1, 64]]),
                            A(bC, (h % 2) * 64 * PITCH + (h // 2) * TT + c * 64, [[PITCH, 64], [1, 64]]),
                            A(bD, (h % 2) * 64 * PITCH + (h // 2) * TT + c * 64, [[PITCH, 64], [1, 64]]),
                            start=True, stop=True,
                            tile_position=(64 * (h % 2), 64 * (h % 2)))
                    return ii
                em.op("tensor", f, waits=[(dv.h, last_soft[ci])] if par == 0 else [], inc=mm)
            w_sc_mm = mm.at()

            em.op("vector", (lambda eng, bank=bank: eng.tensor_reduce(
                G.redc[:, 0:8], A(bank, 0, [[512, 128], [64, 8], [1, 64]]),
                AX.X, ALU.max)), waits=[w_sc_mm], inc=dv)
            em.op("vector", (lambda eng, bank=bank: eng.tensor_tensor(
                G.sgb[:, :], bank[:, :], G.cmask[:, :], ALU.add)), inc=dv)
            em.op("vector", (lambda eng: eng.tensor_tensor(
                G.sgb[:, :], G.sgb[:, :], A(G.redc, 0, [[64, 128], [1, 8], [0, 64]]),
                ALU.subtract)), inc=dv)
            em.op("scalar", lambda eng: eng.activation(G.sgb[:, :], G.sgb[:, :], AF.Exp,
                                                       scale=0.125),
                  waits=[dv.at()], inc=sc)

            em.op("vector", lambda eng: eng.tensor_reduce(
                G.redc[:, 8:16], A(G.sgb, 0, [[512, 128], [64, 8], [1, 64]]),
                AX.X, ALU.add), waits=[sc.at()], inc=dv)
            def f(eng):
                eng.activation(G.redc[:, 16:24], G.redc[:, 8:16], AF.Ln)
                return eng.activation(G.redc[:, 16:24], G.redc[:, 16:24], AF.Exp, scale=-1.0)
            em.op("scalar", f, waits=[dv.at()], inc=sc)
            em.op("vector", (lambda eng, ci=ci: eng.tensor_tensor(
                A(G.P_sb, ci * 64, [[1024, 128], [128, 8], [1, 64]]), G.sgb[:, :],
                A(G.redc, 16, [[64, 128], [1, 8], [0, 64]]), ALU.mult)),
                waits=[sc.at(), (mm.h, prev_ptr_mm)], inc=dv)
            last_soft[ci] = dv.v
        w_soft = dv.at()

        if sub < 3:
            continue
        for par in (0, 1):
            def f(eng, par=par):
                for h in range(par, H, 2):
                    ii = eng.transpose(
                        A(pb[2 + h % 2], (h // 2) * 64, [[512, 128], [1, 64]]),
                        A(G.P_sb, (h % 2) * 64 * 1024 + (h // 2) * 128, [[1024, 64], [1, 128]]),
                        G.id64[(h % 2) * 64:(h % 2) * 64 + 64, 0:64])
                return ii
            em.op("tensor", f, waits=[w_soft, (dv.h, last_tcopy)] if par == 0 else [], inc=mm)
        prev_ptr_mm = mm.v

        def f(eng):
            eng.tensor_copy(A(G.PT_sb, 0, [[1024, 128], [128, 8], [1, 64]]),
                            A(pb[2], 0, [[512, 128], [64, 8], [1, 64]]))
            return eng.tensor_copy(A(G.PT_sb, 64, [[1024, 128], [128, 8], [1, 64]]),
                                   A(pb[3], 0, [[512, 128], [64, 8], [1, 64]]))
        em.op("vector", f, waits=[mm.at()], inc=dv)
        w_ptc = dv.at()

        for par in (0, 1):
            def f(eng, par=par):
                for h in range(par, H, 2):
                    ii = eng.transpose(
                        A(pb[2 + h % 2], (h // 2) * 64, [[512, 128], [1, 64]]),
                        A(G.pKH, (h % 2) * 64 * 1024 + (h // 2) * 128, [[1024, 64], [1, 128]]),
                        G.id64[(h % 2) * 64:(h % 2) * 64 + 64, 0:64])
                return ii
            em.op("tensor", f, waits=[w_ptc] if par == 0 else [], inc=mm)

        def f(eng):
            eng.tensor_copy(A(G.KT_sb, 0, [[1024, 128], [128, 8], [1, 64]]),
                            A(pb[2], 0, [[512, 128], [64, 8], [1, 64]]))
            return eng.tensor_copy(A(G.KT_sb, 64, [[1024, 128], [128, 8], [1, 64]]),
                                   A(pb[3], 0, [[512, 128], [64, 8], [1, 64]]))
        em.op("vector", f, waits=[mm.at()], inc=dv)
        last_tcopy = dv.v

        if sub < 4:
            continue
        for ci, c in enumerate((c0, c1)):
            cb = 64 * (c % 2)
            n_glob = t * CPT + c

            for par in (0, 1):
                def f(eng, c=c, cb=cb, par=par):
                    for h in range(par, H, 2):
                        ii = eng.matmul(
                            A(pb[4 + h % 2], cb * 512 + (h // 2) * 64, [[512, 64], [1, 64]]),
                            A(G.pKT, (h % 2) * 64 * 1024 + (h // 2) * 128 + (c % 2) * 64,
                              [[1024, 64], [1, 64]]),
                            A(G.pQT, (h % 2) * 64 * 1024 + (h // 2) * 128 + (c % 2) * 64,
                              [[1024, 64], [1, 64]]),
                            start=True, stop=True,
                            tile_position=(64 * (h % 2), cb))
                    return ii
                em.op("tensor", f,
                      waits=[(dv.h, last_at), (dv.h, last_tcopy)] if par == 0 else [], inc=mm)
            w_atmm = mm.at()

            def f(eng, cb=cb):
                eng.tensor_tensor(A(G.At_sb, cb * 1024, [[1024, 64], [128, 8], [1, 64]]),
                                  A(pb[4], cb * 512, [[512, 64], [64, 8], [1, 64]]),
                                  A(G.smask, cb * 512, [[512, 64], [64, 8], [1, 64]]), ALU.mult)
                return eng.tensor_tensor(A(G.At_sb, cb * 1024 + 64, [[1024, 64], [128, 8], [1, 64]]),
                                         A(pb[5], cb * 512, [[512, 64], [64, 8], [1, 64]]),
                                         A(G.smask, cb * 512, [[512, 64], [64, 8], [1, 64]]), ALU.mult)
            em.op("vector", f, waits=[w_atmm], inc=dv)
            last_at = dv.v
            w_at = dv.at()

            if sub < 5:
                continue

            def f(eng, c=c, cb=cb):
                for h in range(H):
                    vsl = A(bA, cb * PITCH + (c // 2) * 1024 + h * 64, [[PITCH, 64], [1, 64]])
                    eng.matmul(
                        A(pb[6], (h % 2) * 64 * 512 + (h // 2) * 64, [[512, 64], [1, 64]]),
                        vsl,
                        A(G.PT_sb, cb * 1024 + h * 64, [[1024, 64], [1, 64]]),
                        start=True, stop=False, tile_position=(cb, 64 * (h % 2)))
                    ii = eng.matmul(
                        A(pb[6], (h % 2) * 64 * 512 + (h // 2) * 64, [[512, 64], [1, 64]]),
                        vsl,
                        A(G.At_sb, cb * 1024 + h * 64, [[1024, 64], [1, 64]]),
                        start=False, stop=True, tile_position=(cb, 64 * (h % 2)))
                return ii
            em.op("tensor", f, waits=[w_at, (dv.h, last_sadd)], inc=mm)
            w_pv = mm.at()

            if sub < 6:
                continue

            def f(eng, c=c, cb=cb):
                for h in range(H):
                    vsl = A(bA, cb * PITCH + (c // 2) * 1024 + h * 64, [[PITCH, 64], [1, 64]])
                    ii = eng.matmul(
                        A(pb[7], (h % 2) * 64 * 512 + (h // 2) * 64, [[512, 64], [1, 64]]),
                        A(G.KT_sb, cb * 1024 + h * 64, [[1024, 64], [1, 64]]),
                        vsl,
                        start=True, stop=True, tile_position=(cb, 64 * (h % 2)))
                return ii
            em.op("tensor", f, waits=[(dv.h, last_ds)], inc=mm)
            w_ds = mm.at()

            def f(eng, c=c):
                return eng.tensor_tensor(
                    A(bB, c * 64, [[PITCH, 128], [TT, NJ], [1, 64]]),
                    A(bB, c * 64, [[PITCH, 128], [TT, NJ], [1, 64]]),
                    A(pb[6], 0, [[512, 128], [64, NJ], [1, 64]]), ALU.add)
            em.op("vector", f, waits=[w_pv], inc=dv)
            last_sadd = dv.v

            def f(eng, ci=ci):
                return eng.tensor_copy(G.lstage[ci][:, :], G.S_all[:, :])
            em.op("vector", f, waits=[lsem[ci].at()], inc=dv)
            w_snap = dv.at()
            em.op("gpsimd", (lambda eng, ci=ci, n_glob=n_glob: eng.dma_start(
                A(lent_d, n_glob * 128 * 512, [[512, 128], [1, 512]]),
                G.lstage[ci][:, :])), waits=[w_snap], inc=lsem[ci], k=16)

            em.op("scalar", (lambda eng, n_glob=n_glob: eng.activation(
                A(G.dstage, n_glob * 8, [[NCH * 8, 128], [1, 8]]), G.tcum[:, :], AF.Exp)),
                waits=[dv.at()], inc=sc)
            em.op("vector", (lambda eng, c=c: eng.tensor_tensor(
                G.tcum[:, :], G.tcum[:, :],
                A(bF, c * 64 + 63, [[PITCH, 128], [TT, NJ]]), ALU.add)),
                waits=[sc.at()], inc=dv)

            def f(eng, c=c):
                eng.tensor_tensor(G.S_all[:, :], G.S_all[:, :],
                                  A(G.etb, c, [[64, 128], [8, 8], [0, 64]]), ALU.mult)
                return eng.tensor_tensor(G.S_all[:, :], G.S_all[:, :],
                                         A(pb[7], 0, [[512, 128], [64, 8], [1, 64]]), ALU.add)
            em.op("vector", f, waits=[w_snap, w_ds], inc=dv)
            last_ds = dv.v
        prev_pair_mm = mm.v

    for js in range(0, NJ, 2):
        em.op("gpsimd", (lambda eng, js=js: eng.dma_start(
            A(s_d, t0 + js * TOK, [[NJ * TOK, 128], [TOK, 2], [1, TT]]),
            A(bB, js * TT, [[PITCH, 128], [TT, 2], [1, TT]]))),
            waits=[dv.at()] if js == 0 else [], inc=G.ss, k=16)

    em.flush()


def _l1_finalize(nc, G, send_d, dcum_d):
    em = Em(nc)

    em.op("gpsimd", lambda eng: eng.dma_start(send_d.ap(), G.S_all[:, :]),
          waits=[G.dv.at(), G.sc.at()], inc=G.fin, k=16)
    em.op("gpsimd", lambda eng: eng.dma_start(
        A(dcum_d, 0, [[8, 128], [1024, NCH], [1, 8]]), G.dstage[:, :]), inc=G.fin, k=16)
    em.flush()


def _l1_tern_p1(nc, G, w_d, nrows, ncols, roff):
    dv = G.dv
    vsem = [G.v2, G.v3]
    ncc = ncols // 512
    ntile = (nrows // 128) * ncc
    red_marks = {}
    w_prev = (dv.h, dv.v)
    for n in range(ntile):
        pc, cc = n // ncc, n % ncc
        em = G.tern_em
        dwaits = [(dv.h, red_marks[n - 2])] if n >= 2 else [w_prev]
        em.op("sync", (lambda eng, pc=pc, cc=cc, bi=n % 2: eng.dma_start(
            G.tscr[bi][:], A(w_d, pc * 128 * ncols + cc * 512, [[ncols, 128], [1, 512]]))),
            waits=dwaits, inc=vsem[n % 2], k=16)
        em.op("vector", (lambda eng, n=n, bi=n % 2, roff=roff: eng.tensor_reduce(
            G.tred[:, roff + n:roff + n + 1], G.tscr[bi][:], AX.X, ALU.add,
            apply_absolute_value=True)),
            waits=[vsem[n % 2].at()], inc=dv)
        red_marks[n] = dv.v
        yield


def _l1_ternary(nc, G, w_d, t_out, m_out, nrows, ncols, roff):
    mm, dv, sc = G.mm, G.dv, G.sc
    vsem = [G.v0, G.v1, G.v2, G.v3]
    tsem = [G.t0, G.t1]
    npc = nrows // 128
    ncc = ncols // 512
    ntile = npc * ncc
    assert ntile <= 64

    em = Em(nc)
    em.op("vector", (lambda eng: eng.tensor_reduce(
        G.mcols[:, 0:1], G.tred[:, roff:roff + ntile], AX.X, ALU.add)),
        waits=[G.fin.at()], inc=dv)
    em.op("tensor", lambda eng: eng.matmul(
        G.pb[0][0:1, 0:1], G.mcols[:, 0:1], G.ones_c[:, :], start=True, stop=True),
        waits=[dv.at()], inc=mm)

    em.op("vector", lambda eng: eng.tensor_scalar(
        G.mrow[0:1, 0:1], G.pb[0][0:1, 0:1], 1.0 / (nrows * ncols), None, ALU.mult),
        waits=[mm.at()], inc=dv)

    def f(eng):
        eng.tensor_scalar(G.mrow[0:1, 0:1], G.mrow[0:1, 0:1], 1e-5, None, ALU.max)
        return eng.tensor_scalar(G.mrow[0:1, 1:2], G.mrow[0:1, 0:1], 0.5, None, ALU.mult)
    em.op("vector", f, waits=[dv.at()], inc=dv)
    em.op("tensor", lambda eng: eng.matmul(
        G.pb[1][:, 0:2], G.ones_r[:], G.mrow[0:1, 0:2], start=True, stop=True),
        waits=[dv.at()], inc=mm)
    em.op("vector", lambda eng: eng.tensor_copy(G.mcols[:, :], G.pb[1][:, 0:2]),
          waits=[mm.at()], inc=dv)
    w_m = dv.at()
    em.op("gpsimd", lambda eng: eng.dma_start(m_out.ap(), G.mcols[:, :]),
          waits=[w_m], inc=G.fin, k=16)
    em.flush()

    em = Em(nc)
    sgbs = [G.sgb, G.sgb2]
    thr_marks = {}
    w_prev = (dv.h, dv.v)
    for n in range(ntile):
        pc, cc = n // ncc, n % ncc
        bi = n % 4
        si = n % 2
        dwaits = [(dv.h, thr_marks[n - 4])] if n >= 4 else [w_prev]
        em.op("sync", (lambda eng, pc=pc, cc=cc, bi=bi: eng.dma_start(
            G.scr[bi][:], A(w_d, pc * 128 * ncols + cc * 512, [[ncols, 128], [1, 512]]))),
            waits=dwaits, inc=vsem[bi], k=16)
        w_wl = vsem[bi].at()
        swaits = [w_wl] + ([(dv.h, thr_marks[n - 2])] if n >= 2 else [])
        em.op("scalar", (lambda eng, bi=bi, si=si: eng.sign(
            sgbs[si][:, :], G.scr[bi][:])), waits=swaits, inc=sc)
        w_sg = sc.at()

        def f(eng, bi=bi, n=n, si=si):
            eng.tensor_tensor(G.scr[bi][:], G.scr[bi][:], sgbs[si][:, :], ALU.mult)
            eng.tensor_scalar(G.scr[bi][:], G.scr[bi][:], G.mcols[:, 1:2], None, ALU.is_ge)
            return eng.tensor_tensor(G.tqb[n % 2][:, :], sgbs[si][:, :], G.scr[bi][:],
                                     ALU.mult)
        em.op("vector", f, waits=[w_sg, tsem[n % 2].at()], inc=dv)
        thr_marks[n] = dv.v
        em.op("gpsimd", (lambda eng, n=n, pc=pc, cc=cc: eng.dma_start(
            A(t_out, pc * 128 * ncols + cc * 512, [[ncols, 128], [1, 512]]),
            G.tqb[n % 2][:, :])), waits=[(dv.h, thr_marks[n])], inc=tsem[n % 2], k=16)
    em.flush()




def _l2_dram(nc, kind_in="ExternalInput", kind_out="ExternalOutput", D=None):
    if D is None:
        D = NS()
        D.x = nc.dram_tensor("x", [128, NJ, TOK], F32, kind=kind_in)
        D.xn = nc.dram_tensor("xn", [128, NJ, TOK], F32, kind=kind_in)
        D.s = nc.dram_tensor("s", [128, NJ, TOK], F32, kind=kind_in)
        D.qh = nc.dram_tensor("qh", [128, NJ, TOK], F32, kind=kind_in)
        D.lent = nc.dram_tensor("lent", [NCH, 128, 512], F32, kind=kind_in)
        D.dcum = nc.dram_tensor("dcum", [NCH, 128, 8], F32, kind=kind_in)
        D.t1 = nc.dram_tensor("t1", [DM, HID], F32, kind=kind_in)
        D.t2 = nc.dram_tensor("t2", [HID, DM], F32, kind=kind_in)
        D.m1 = nc.dram_tensor("m1", [128, 2], F32, kind=kind_in)
        D.m2 = nc.dram_tensor("m2", [128, 2], F32, kind=kind_in)
        D.sp = nc.dram_tensor("spart", [128, 512], F32, kind=kind_in)
    D.l1g = nc.dram_tensor("l1g", [128, NJ], F32, kind="ExternalInput")
    D.l1b = nc.dram_tensor("l1b", [128, NJ], F32, kind="ExternalInput")
    D.b1 = nc.dram_tensor("b1", [128, 32], F32, kind="ExternalInput")
    D.l2g = nc.dram_tensor("l2g", [128, 32], F32, kind="ExternalInput")
    D.l2b = nc.dram_tensor("l2b", [128, 32], F32, kind="ExternalInput")
    D.b2 = nc.dram_tensor("b2", [128, NJ], F32, kind="ExternalInput")
    D.wk1 = nc.dram_tensor("wk1", [DM, DM], F32, kind="ExternalInput")
    D.wv1 = nc.dram_tensor("wv1", [DM, DM], F32, kind="ExternalInput")
    D.x1 = nc.dram_tensor("x1", [128, NJ, TOK], F32, kind=kind_out)
    D.xn1 = nc.dram_tensor("xn1", [128, NJ, TOK], F32, kind=kind_out)
    D.k1s = [nc.dram_tensor(f"k1_{t}", [128, NJ, TT], F32, kind=kind_out)
             for t in range(NT)]
    D.v1s = [nc.dram_tensor(f"v1_{t}", [128, 4, DM], F32, kind=kind_out)
             for t in range(NT)]
    return D


def _l2_emit(nc, st, sems, D):
    E = st.enter_context
    g = {}
    for nm in ("bS", "bQH", "bXN", "bX", "bG"):
        g[nm] = E(nc.sbuf_tensor(nm, [128, PITCH], F32))
    g["bH"] = E(nc.sbuf_tensor("bH", [128, 32 * 512], F32))
    g["lh_hi"] = E(nc.sbuf_tensor("lh_hi", [128, PITCH], BF16))
    g["lh_lo"] = E(nc.sbuf_tensor("lh_lo", [128, PITCH], BF16))
    g["hh"] = [E(nc.sbuf_tensor(f"hh{i}", [128, 512], BF16)) for i in range(2)]
    g["hl"] = [E(nc.sbuf_tensor(f"hl{i}", [128, 512], BF16)) for i in range(2)]
    g["t1b"] = [E(nc.sbuf_tensor(f"t1b{i}", [128, NJ * 128], BF16)) for i in range(2)]
    g["t2b"] = [E(nc.sbuf_tensor(f"t2b{i}", [128, NJ * 128], BF16)) for i in range(2)]
    g["wbuf"] = [E(nc.sbuf_tensor(f"wbuf{i}", [128, NJ * 128], F32)) for i in range(2)]
    g["scr"] = [E(nc.sbuf_tensor(f"scr{i}", [128, 512], F32)) for i in range(4)]
    g["lbuf"] = [E(nc.sbuf_tensor(f"lbuf{i}", [128, 512], F32)) for i in range(2)]
    g["sentb"] = E(nc.sbuf_tensor("sentb", [128, 512], F32))
    g["spart"] = E(nc.sbuf_tensor("spartb", [128, 512], F32))
    g["dcum"] = E(nc.sbuf_tensor("dcumb", [128, NCH * 8], F32))
    g["l1g"] = E(nc.sbuf_tensor("l1gb", [128, NJ], F32))
    g["l1b"] = E(nc.sbuf_tensor("l1bb", [128, NJ], F32))
    g["b1"] = E(nc.sbuf_tensor("b1b", [128, 32], F32))
    g["l2g"] = E(nc.sbuf_tensor("l2gb", [128, 32], F32))
    g["l2b"] = E(nc.sbuf_tensor("l2bb", [128, 32], F32))
    g["b2"] = E(nc.sbuf_tensor("b2b", [128, NJ], F32))
    g["m1"] = E(nc.sbuf_tensor("m1b", [128, 2], F32))
    g["m2"] = E(nc.sbuf_tensor("m2b", [128, 2], F32))
    g["ones_c"] = E(nc.sbuf_tensor("onesc", [128, 1], F32))
    g["ones_r"] = E(nc.sbuf_tensor("onesr", [1, 128], F32))
    g["cns"] = E(nc.sbuf_tensor("cns", [128, 2], F32))
    g["flagb"] = E(nc.sbuf_tensor("flagb", [128, 1], F32))
    g["pb"] = [E(nc.psum_tensor(f"pb{i}", [128, 512], F32)) for i in range(8)]
    g.update(sems)
    G = type("G", (), g)

    em = Em(nc)
    loads = [
        (G.dcum[:], A(D.dcum, 0, [[8, 128], [1024, NCH], [1, 8]])),
        (G.l1g[:], D.l1g.ap()), (G.l1b[:], D.l1b.ap()), (G.b1[:], D.b1.ap()),
        (G.l2g[:], D.l2g.ap()), (G.l2b[:], D.l2b.ap()), (G.b2[:], D.b2.ap()),
        (G.m1[:], D.m1.ap()), (G.m2[:], D.m2.ap()),
    ]
    spw = []
    if getattr(D, "sp_gate", None) is not None:
        spw = [D.sp_gate]
    em.op("sync", (lambda eng: eng.dma_start(G.spart[:], D.sp_ap)), waits=spw,
          inc=G.c, k=16)
    for dst, src_ in loads:
        em.op("sync", (lambda eng, a=dst, b=src_: eng.dma_start(a, b)), inc=G.c, k=16)
    if getattr(D, "flag", None) is not None:
        em.op("sync", (lambda eng: eng.dma_start(G.flagb[:], D.flag.ap())), inc=G.c, k=16)

    def f(eng):
        eng.memset(G.ones_c[:], 1.0)
        eng.memset(G.ones_r[:], 1.0)
        return eng.memset(G.cns[:, 0:1], float(LNEPS))
    em.op("vector", f, inc=G.dv)
    if getattr(D, "flag", None) is not None:
        em.op("vector", lambda eng: eng.tensor_scalar(
            G.spart[:, :], G.spart[:, :], G.flagb[:, 0:1], None, ALU.mult),
            waits=[G.c.at()], inc=G.dv)
    em.flush()

    cc2 = getattr(G, "cc2", None)
    import os
    cc_defer = bool(int(os.environ.get("KCC_DEFER", "0")))
    pend = []
    for t in range(NT):
        with nc.named_scope(f"l2t{t}"):
            _l2_tile(nc, G, t, D.x, D.xn, D.s, D.qh, D.lent, D.t1, D.t2,
                     D.wk1, D.wv1, D.x1, D.xn1, D.k1s[t], D.v1s[t], D)
        if getattr(D, "kgs", None) is not None:
            w_st = (G.sst.h, G.sst.v)
            pairs = [(D.k1s[t], D.kgs[t], cc2), (D.v1s[t], D.vgs[t], cc2)]
            if t == NT - 1:
                pairs.append((D.xn1col, D.bcg, G.cc1))
            if cc_defer:
                pend.append((w_st, pairs))
                continue
            em = Em(nc)
            for a, b, cs in pairs:
                em.op("gpsimd", (lambda eng, a=a, b=b: eng.collective_compute(
                    "AllGather", ALU.bypass, replica_groups=RG_PAIRS,
                    ins=[a.ap().opt()], outs=[b.ap().opt()])),
                    waits=[w_st], inc=cs, k=1)
            em.flush()
    if pend:
        em = Em(nc)
        w_fin = (G.sst.h, G.sst.v)
        for _, pairs in pend:
            for a, b, cs in pairs:
                em.op("gpsimd", (lambda eng, a=a, b=b: eng.collective_compute(
                    "AllGather", ALU.bypass, replica_groups=RG_PAIRS,
                    ins=[a.ap().opt()], outs=[b.ap().opt()])),
                    waits=[w_fin], inc=cs, k=1)
        em.flush()
    return G


def build_launch2():
    nc = bass.Bass("TRN2", target_bir_lowering=False)
    D = _l2_dram(nc)
    D.sp_ap = D.sp.ap()
    D.flag = None
    with contextlib.ExitStack() as st:
        sems = {nm: Sem(st, nc, nm) for nm in L2_SEMS}
        SER_SEMS.clear()
        SER_SEMS.extend([sems["mm"], sems["dv"], sems["sc"]])
        with contextlib.ExitStack() as st_b:
            G = _l2_emit(nc, st_b, sems, D)
        em = Em(nc)
        em.op("gpsimd", lambda eng: eng.nop(),
              waits=[G.sst.at(), G.l0.at(), G.l1.at()])
        em.flush()
    return nc


def _l2_ln_stats(nc, em, G, src_ap, j, nch, gate, st):
    mm, sc = G.mm, G.sc
    pb = G.pb
    waits = list(gate)
    if j >= 2:
        waits.append((mm.h, st[j - 2]))
    em.op("scalar", (lambda eng: eng.activation(
        G.scr[j % 2][:], src_ap, AF.Square)), waits=waits, inc=sc)

    def f(eng, j=j, nch=nch, src_ap=src_ap):
        eng.matmul(pb[0][0:1, :], G.ones_c[:], src_ap,
                   start=(j == 0), stop=(j == nch - 1))
        return eng.matmul(pb[1][0:1, :], G.ones_c[:], G.scr[j % 2][:],
                          start=(j == 0), stop=(j == nch - 1))
    em.op("tensor", f, waits=[sc.at()], inc=mm)
    st[j] = mm.v


def _l2_ln(nc, em, G, srcv, nch, gate_waits):
    st = {}
    for j in range(nch):
        _l2_ln_stats(nc, em, G, srcv(j), j, nch,
                     (list(gate_waits) if j == 0 else []), st)
    return _l2_ln_fin(nc, em, G, nch)


def _l2_ln_fin(nc, em, G, nch):
    mm, dv, sc = G.mm, G.dv, G.sc
    pb = G.pb
    inv = 1.0 / (nch * 128)

    def row(off, n=512):
        buf = (G.scr[2], G.scr[3], G.sentb)[off // 512]
        return A(buf, 0, [[512, 1], [1, n]])

    def f(eng, inv=inv):
        eng.tensor_scalar(row(0), pb[0][0:1, :], inv, None, ALU.mult)
        eng.tensor_scalar(row(512), pb[1][0:1, :], inv, None, ALU.mult)
        eng.tensor_tensor(row(1024), row(0), row(0), ALU.mult)
        return eng.tensor_tensor(row(512), row(512), row(1024), ALU.subtract)
    em.op("vector", f, waits=[mm.at()], inc=dv)

    def f(eng):
        eng.activation(row(512), row(512), AF.Ln, bias=G.cns[0:1, 0:1])
        return eng.activation(row(512), row(512), AF.Exp, scale=-0.5)
    em.op("scalar", f, waits=[dv.at()], inc=sc)

    em.op("vector", lambda eng: eng.scalar_tensor_tensor(
        row(1024), row(0), -1.0, row(512), ALU.mult, ALU.mult),
        waits=[sc.at()], inc=dv)

    def f(eng):
        eng.matmul(pb[2][:, :], G.ones_r[:], row(512), start=True, stop=True)
        return eng.matmul(pb[3][:, :], G.ones_r[:], row(1024), start=True, stop=True)
    em.op("tensor", f, waits=[dv.at()], inc=mm)
    return mm.at()


def _l2_tile(nc, G, t, x_d, xn_d, s_d, qh_d, lent_d, t1_d, t2_d,
             wk1_d, wv1_d, x1_d, xn1_d, k1_d, v1_d, D=None):
    mm, dv, sc = G.mm, G.dv, G.sc
    pb = G.pb
    t0 = t * TT
    lsem = [G.l0, G.l1]
    tsem = [G.t0, G.t1]
    wsem = [G.w0, G.w1]
    vsem = [G.v0, G.v1]

    def xv(buf, j, lo=0, n=TT):
        return A(buf, j * TT + lo, [[PITCH, 128], [1, n]])

    def hv(oc, lo=0, n=512):
        return A(G.bH, oc * 512 + lo, [[32 * 512, 128], [1, n]])

    em = Em(nc)
    for js in range(0, NJ, 2):
        em.op("sync", (lambda eng, js=js: eng.dma_start(
            A(G.bS, js * TT, [[PITCH, 128], [TT, 2], [1, TT]]),
            A(s_d, t0 + js * TOK, [[NJ * TOK, 128], [TOK, 2], [1, TT]]))),
            waits=[G.c.at(), G.sst.at()] if js == 0 else [], inc=G.c, k=16)
    for js in range(0, NJ, 2):
        em.op("sync", (lambda eng, js=js: eng.dma_start(
            A(G.bQH, js * TT, [[PITCH, 128], [TT, 2], [1, TT]]),
            A(qh_d, t0 + js * TOK, [[NJ * TOK, 128], [TOK, 2], [1, TT]]))),
            waits=[G.sst.at()] if js == 0 else [], inc=G.c, k=16)
    w_sq = G.c.at()

    em.flush()

    em = Em(nc)
    for js in range(0, NJ, 2):
        em.op("sync", (lambda eng, js=js: eng.dma_start(
            A(G.bXN, js * TT, [[PITCH, 128], [TT, 2], [1, TT]]),
            A(xn_d, t0 + js * TOK, [[NJ * TOK, 128], [TOK, 2], [1, TT]]))),
            inc=G.c, k=16)
    w_xn = G.c.at()

    w_bc = _l2_ln(nc, em, G, lambda j: xv(G.bXN, j), NJ, [w_xn])

    for j in range(NJ):
        def f(eng, j=j):
            eng.tensor_tensor(xv(G.bX, j), xv(G.bXN, j), pb[2][:, :], ALU.mult)
            eng.tensor_tensor(xv(G.bX, j), xv(G.bX, j), pb[3][:, :], ALU.add)
            return eng.tensor_scalar(xv(G.bX, j), xv(G.bX, j),
                                     G.l1g[:, j:j + 1], G.l1b[:, j:j + 1],
                                     ALU.mult, ALU.add)
        em.op("vector", f, waits=[w_bc] if j == 0 else [], inc=dv)

        def f(eng, j=j):
            return eng.tensor_copy(A(G.lh_hi, j * TT, [[PITCH, 128], [1, TT]]),
                                   xv(G.bX, j))
        em.op("vector", f, inc=dv)
    w_lh = dv.at()

    up_ev = {}
    ln2_st = {}
    for oc in range(32):
        bi = oc % 2
        dwaits = [(mm.h, up_ev.get(oc - 2, (0, 0))[0])] if oc >= 2 else []
        em.op("gpsimd", (lambda eng, oc=oc, bi=bi: eng.dma_start(
            G.t1b[bi][:], A(t1_d, 128 * oc, [[HID, 128], [128 * HID, NJ], [1, 128]]))),
            waits=dwaits, inc=tsem[bi], k=16)
        w_t = tsem[bi].at()
        bank = pb[4 + bi]
        mwaits = [w_t] + ([w_lh] if oc < 2 else [])
        if oc >= 2:
            mwaits.append((sc.h, up_ev[oc - 2][1]))

        def f(eng, bi=bi, bank=bank):
            for kj in range(NJ):
                ii = eng.matmul(bank[:, :], A(G.t1b[bi], kj * 128, [[NJ * 128, 128], [1, 128]]),
                                A(G.lh_hi, kj * TT, [[PITCH, 128], [1, TT]]),
                                start=(kj == 0), stop=(kj == NJ - 1))
            return ii
        em.op("tensor", f, waits=mwaits, inc=mm)
        w_up = mm.v
        if oc >= 1:
            _l2_ln_stats(nc, em, G, hv(oc - 1), oc - 1, 32, [], ln2_st)
        em.op("scalar", (lambda eng, oc=oc, bank=bank: eng.activation(
            hv(oc), bank[:, :], AF.Gelu, bias=G.b1[:, oc:oc + 1], scale=G.m1[:, 0:1])),
            waits=[(mm.h, w_up)], inc=sc)
        up_ev[oc] = (w_up, sc.v)
    _l2_ln_stats(nc, em, G, hv(31), 31, 32, [], ln2_st)
    em.flush()

    em = Em(nc)
    last_sadd = 0
    sent_marks = {}
    for c in range(CPT):
        n_glob = t * CPT + c
        ci = c % 2
        em.op("sync", (lambda eng, ci=ci, n_glob=n_glob: eng.dma_start(
            G.lbuf[ci][:], A(lent_d, n_glob * 128 * 512, [[512, 128], [1, 512]]))),
            waits=[(G.dv.h, sent_marks[c - 2])] if c >= 2 else [], inc=lsem[ci], k=16)
        w_l = lsem[ci].at()

        def f(eng, n_glob=n_glob, ci=ci):
            eng.tensor_tensor(G.sentb[:, :], G.spart[:, :],
                              A(G.dcum, n_glob * 8, [[NCH * 8, 128], [1, 8], [0, 64]]),
                              ALU.mult)
            return eng.tensor_tensor(G.sentb[:, :], G.sentb[:, :], G.lbuf[ci][:, :],
                                     ALU.add)
        em.op("vector", f, waits=[w_l, (mm.h, mm.v)], inc=dv)
        w_sent = dv.at()
        sent_marks[c] = dv.v
        for par in (0, 1):
            def f(eng, c=c, par=par):
                for h in range(par, H, 2):
                    ii = eng.matmul(
                        A(pb[6], (h % 2) * 64 * 512 + (h // 2) * 64, [[512, 64], [1, 64]]),
                        A(G.sentb, (h % 2) * 64 * 512 + (h // 2) * 64, [[512, 64], [1, 64]]),
                        A(G.bQH, (h % 2) * 64 * PITCH + (h // 2) * TT + c * 64,
                          [[PITCH, 64], [1, 64]]),
                        start=True, stop=True,
                        tile_position=(64 * (h % 2), 64 * (h % 2)))
                return ii
            em.op("tensor", f,
                  waits=([w_sent, w_sq, (dv.h, last_sadd)] if par == 0 else []), inc=mm)
        em.op("vector", (lambda eng, c=c: eng.tensor_tensor(
            A(G.bS, c * 64, [[PITCH, 128], [TT, NJ], [1, 64]]),
            A(G.bS, c * 64, [[PITCH, 128], [TT, NJ], [1, 64]]),
            A(pb[6], 0, [[512, 128], [64, NJ], [1, 64]]), ALU.add)),
            waits=[mm.at()], inc=dv)
        last_sadd = dv.v
    G.w_pa = (dv.h, dv.v)
    em.flush()




    em = Em(nc)
    w_bc2 = _l2_ln_fin(nc, em, G, 32)

    for oc in range(32):
        def f(eng, oc=oc):
            eng.tensor_tensor(hv(oc), hv(oc), pb[2][:, :], ALU.mult)
            eng.tensor_tensor(hv(oc), hv(oc), pb[3][:, :], ALU.add)
            return eng.tensor_scalar(hv(oc), hv(oc), G.l2g[:, oc:oc + 1],
                                     G.l2b[:, oc:oc + 1], ALU.mult, ALU.add)
        em.op("vector", f, waits=[w_bc2] if oc == 0 else [], inc=dv)
    w_n2 = dv.at()

    dn_marks = {}
    for oc in range(32):
        bi = oc % 2
        swaits = [w_n2] if oc < 2 else [(mm.h, dn_marks[oc - 2])]

        def f(eng, oc=oc, bi=bi):
            return eng.tensor_copy(G.hh[bi][:, :], hv(oc))
        em.op("vector", f, waits=swaits, inc=dv)
        w_split = dv.at()
        em.op("gpsimd", (lambda eng, oc=oc, bi=bi: eng.dma_start(
            G.t2b[bi][:], A(t2_d, 128 * oc * DM, [[DM, 128], [1, NJ * 128]]))),
            waits=[(mm.h, dn_marks.get(oc - 2, 0))], inc=tsem[bi], k=16)
        w_t2 = tsem[bi].at()

        def f(eng, oc=oc, bi=bi):
            for oj in range(NJ):
                ii = eng.matmul(pb[oj][:, :], A(G.t2b[bi], oj * 128, [[NJ * 128, 128], [1, 128]]),
                                G.hh[bi][:, :], start=(oc == 0), stop=(oc == 31))
            return ii
        em.op("tensor", f,
              waits=[w_split, w_t2] + ([G.w_pa] if oc == 0 else []), inc=mm)
        dn_marks[oc] = mm.v
    w_dn = mm.at()

    for oj in range(NJ):
        em.op("scalar", (lambda eng, oj=oj: eng.activation(
            xv(G.bG, oj), pb[oj][:, :], AF.Silu, bias=G.b2[:, oj:oj + 1],
            scale=G.m2[:, 0:1])), waits=[w_dn] if oj == 0 else [], inc=sc)
    em.flush()

    em = Em(nc)
    for js in range(0, NJ, 2):
        em.op("sync", (lambda eng, js=js: eng.dma_start(
            A(G.bX, js * TT, [[PITCH, 128], [TT, 2], [1, TT]]),
            A(x_d, t0 + js * TOK, [[NJ * TOK, 128], [TOK, 2], [1, TT]]))),
            waits=[G.sst.at()] if js == 0 else [], inc=G.c, k=16)
    w_x = G.c.at()

    ln3_st = {}
    cmb = {}
    for j in range(NJ):
        def f(eng, j=j):
            eng.tensor_tensor(xv(G.bS, j), xv(G.bS, j), xv(G.bG, j), ALU.mult)
            return eng.tensor_tensor(xv(G.bX, j), xv(G.bX, j), xv(G.bS, j), ALU.add)
        em.op("vector", f, waits=[w_x, sc.at(), G.w_pa] if j == 0 else [], inc=dv)
        cmb[j] = dv.v
        if j >= 1:
            _l2_ln_stats(nc, em, G, xv(G.bX, j - 1), j - 1, NJ,
                         [(dv.h, cmb[j - 1])], ln3_st)
    _l2_ln_stats(nc, em, G, xv(G.bX, NJ - 1), NJ - 1, NJ,
                 [(dv.h, cmb[NJ - 1])], ln3_st)
    w_x1 = dv.at()
    for js in range(0, NJ, 2):
        em.op("gpsimd", (lambda eng, js=js: eng.dma_start(
            A(x1_d, t0 + js * TOK, [[NJ * TOK, 128], [TOK, 2], [1, TT]]),
            A(G.bX, js * TT, [[PITCH, 128], [TT, 2], [1, TT]]))),
            waits=[w_x1] if js == 0 else [], inc=G.sst, k=16)

    w_bc3 = _l2_ln_fin(nc, em, G, NJ)
    for j in range(NJ):
        def f(eng, j=j):
            eng.tensor_tensor(xv(G.bG, j), xv(G.bX, j), pb[2][:, :], ALU.mult)
            return eng.tensor_tensor(xv(G.bG, j), xv(G.bG, j), pb[3][:, :], ALU.add)
        em.op("vector", f, waits=[w_bc3] if j == 0 else [], inc=dv)
    w_xn1 = dv.at()
    for js in range(0, NJ, 2):
        em.op("gpsimd", (lambda eng, js=js: eng.dma_start(
            A(xn1_d, t0 + js * TOK, [[NJ * TOK, 128], [TOK, 2], [1, TT]]),
            A(G.bG, js * TT, [[PITCH, 128], [TT, 2], [1, TT]]))),
            waits=[w_xn1] if js == 0 else [], inc=G.sst, k=16)
    if t == NT - 1 and D is not None and getattr(D, "xn1col", None) is not None:
        em.op("vector", lambda eng: eng.tensor_copy(
            G.sentb[:, 0:NJ], A(G.bG, TT - 1, [[PITCH, 128], [TT, NJ]])),
            waits=[w_xn1], inc=dv)
        em.op("vector", lambda eng: eng.memset(G.sentb[:, NJ:16], 0.0), inc=dv)
        em.op("vector", lambda eng: eng.memset(G.sentb[:, 16:32], 0.0), inc=dv)
        em.op("gpsimd", lambda eng: eng.dma_start(
            D.xn1col.ap(), G.sentb[:, 0:NJ]), waits=[dv.at()], inc=G.sst, k=16)

    grp = {}
    ev = {}
    for oc in range(NJ):
        bi = oc % 2
        dwaits = [(mm.h, grp[oc - 2])] if oc >= 2 else []
        em.op("sync", (lambda eng, oc=oc, bi=bi: eng.dma_start(
            G.wbuf[bi][:], A(wk1_d, 128 * oc, [[DM, 128], [128 * DM, NJ], [1, 128]]))),
            waits=dwaits, inc=wsem[bi], k=16)
        w_w = wsem[bi].at()
        bank = pb[4 + bi]
        mwaits = [w_w] + ([w_xn1] if oc < 2 else []) + (
            [(dv.h, ev[oc - 2])] if oc >= 2 else [])

        def f(eng, bi=bi, bank=bank):
            for kj in range(NJ):
                ii = eng.matmul(bank[:, :], A(G.wbuf[bi], kj * 128, [[NJ * 128, 128], [1, 128]]),
                                xv(G.bG, kj), start=(kj == 0), stop=(kj == NJ - 1))
            return ii
        em.op("tensor", f, waits=mwaits, inc=mm)
        grp[oc] = mm.v
        em.op("vector", (lambda eng, oc=oc, bank=bank: eng.tensor_copy(
            xv(G.bS, oc), bank[:, :])), waits=[mm.at()], inc=dv)
        ev[oc] = dv.v
    for js in range(0, NJ, 2):
        em.op("gpsimd", (lambda eng, js=js: eng.dma_start(
            A(k1_d, js * TT, [[NJ * TT, 128], [TT, 2], [1, TT]]),
            A(G.bS, js * TT, [[PITCH, 128], [TT, 2], [1, TT]]))),
            waits=[dv.at()] if js == 0 else [], inc=G.sst, k=16)

    vmm = {}
    vev = {}
    for half in range(2):
        for kj in range(NJ):
            n = half * NJ + kj
            bi = n % 2
            dwaits = [(mm.h, vmm[n - 2])] if n >= 2 else [(dv.h, dv.v)]
            em.op("sync", (lambda eng, kj=kj, half=half, bi=bi: eng.dma_start(
                G.scr[bi][:], A(wv1_d, kj * 128 * DM + half * 512, [[DM, 128], [1, 512]]))),
                waits=dwaits, inc=vsem[bi], k=16)
            w_v = vsem[bi].at()
            mwaits = [w_v] + ([w_xn1] if n < 2 else [])
            if kj == 0 and half == 0:
                mwaits.append((dv.h, dv.v))
            if kj == 0 and half == 1:
                mwaits.append((dv.h, vev[0]))

            def f(eng, kj=kj, bi=bi):
                for ts in range(4):
                    ii = eng.matmul(pb[ts][:, :],
                                    A(G.bG, kj * TT + ts * 128, [[PITCH, 128], [1, 128]]),
                                    G.scr[bi][:], start=(kj == 0), stop=(kj == NJ - 1))
                return ii
            em.op("tensor", f, waits=mwaits, inc=mm)
            vmm[n] = mm.v
        for ts in range(4):
            em.op("vector", (lambda eng, ts=ts, half=half: eng.tensor_copy(
                A(G.bQH, ts * 1024 + half * 512, [[PITCH, 128], [1, 512]]),
                pb[ts][:, :])), waits=[(mm.h, vmm[half * NJ + NJ - 1])], inc=dv)
        vev[half] = dv.v
    for js in range(4):
        em.op("gpsimd", (lambda eng, js=js: eng.dma_start(
            A(v1_d, js * DM, [[4 * DM, 128], [1, DM]]),
            A(G.bQH, js * DM, [[PITCH, 128], [1, DM]]))),
            waits=[dv.at()] if js == 0 else [], inc=G.sst, k=16)

    em.flush()


def l2_inputs(l1out, ins, spart):
    return {
        "x": l1out["__x"], "xn": l1out["xn"], "s": l1out["s"], "qh": l1out["qh"],
        "lent": l1out["lent"], "dcum": l1out["dcum"], "spart": spart,
        "t1": l1out["t1"], "t2": l1out["t2"], "m1": l1out["m1"], "m2": l1out["m2"],
        "l1g": _vec128(ins["L0_ffn_ln1_g"]), "l1b": _vec128(ins["L0_ffn_ln1_b"]),
        "b1": _vec128(ins["L0_ffn_b1"]), "l2g": _vec128(ins["L0_ffn_ln2_g"]),
        "l2b": _vec128(ins["L0_ffn_ln2_b"]), "b2": _vec128(ins["L0_ffn_b2"]),
        "wk1": np.ascontiguousarray(ins["L1_attn_Wk"].transpose(1, 0, 2).reshape(DM, DM)),
        "wv1": np.ascontiguousarray(ins["L1_attn_Wv"].transpose(1, 0, 2).reshape(DM, DM)),
    }




KTOT = 2 * TOK


def _l3_dram(nc, kind_in="ExternalInput", kind_out="ExternalOutput", D=None):
    import os
    if D is None:
        D = NS()
        D.x1 = nc.dram_tensor("x1", [128, NJ, TOK], F32, kind=kind_in)
        D.xn1 = nc.dram_tensor("xn1", [128, NJ, TOK], F32, kind=kind_in)
        D.kf = nc.dram_tensor("kf", [128, NJ, KTOT], F32, kind=kind_in)
        D.vf = nc.dram_tensor("vf", [128, 32, DM], F32, kind=kind_in)
        D.bcol = nc.dram_tensor("bcol", [128, NJ], F32, kind=kind_in)
    D.wq1 = nc.dram_tensor("wq1", [DM, DM], F32, kind="ExternalInput")
    D.wr = nc.dram_tensor("wr", [DM, DM], F32, kind="ExternalInput")
    D.wkr = nc.dram_tensor("wkr", [DM, DM], F32, kind="ExternalInput")
    D.wvr = nc.dram_tensor("wvr", [DM, DM], F32, kind="ExternalInput")
    D.mur = nc.dram_tensor("mur", [128, NJ], F32, kind="ExternalInput")
    D.muk = nc.dram_tensor("muk", [128, NJ], F32, kind="ExternalInput")
    D.id128 = nc.dram_tensor("id128", [128, 128], F32, kind="ExternalInput")
    D.out = nc.dram_tensor("out", [128, NJ, TOK], F32, kind=kind_out)
    D.s1 = nc.dram_tensor("s1scratch", [128, NJ, TOK], F32)
    dbg = bool(int(os.environ.get("L3_DBG", "0")))
    D.dbg = nc.dram_tensor("dbg", [128, TOK], F32, kind="ExternalOutput") if dbg else None
    D.dbg2 = (nc.dram_tensor("dbg2", [128, NJ, TOK], F32, kind="ExternalOutput")
              if dbg else None)
    return D


def _l3_emit(nc, st, sems, D):
    E = st.enter_context
    s1_dram = D.s1
    g = {}
    g["bXN"] = E(nc.sbuf_tensor("bXN", [128, NJ * TOK], F32))
    g["id128"] = E(nc.sbuf_tensor("id128s", [128, 128], F32))
    g["wbuf"] = [E(nc.sbuf_tensor(f"wbuf{i}", [128, NJ * 128], F32)) for i in range(2)]
    g["mur"] = E(nc.sbuf_tensor("murb", [128, NJ], F32))
    g["muk"] = E(nc.sbuf_tensor("mukb", [128, NJ], F32))
    g["bcol"] = E(nc.sbuf_tensor("bcolb", [128, NJ], F32))
    g["flagb"] = E(nc.sbuf_tensor("flagb", [128, 1], F32))
    g["pb"] = [E(nc.psum_tensor(f"pb{i}", [128, 512], F32)) for i in range(8)]
    g.update(sems)

    with contextlib.ExitStack() as st_a:
        Ea = st_a.enter_context
        g["Kb"] = [Ea(nc.sbuf_tensor(f"Kb{i}", [128, KTOT], F32)) for i in range(2)]
        g["Kbr"] = Ea(nc.sbuf_tensor("Kbr", [128, KTOT], F32R))
        g["Qbr"] = Ea(nc.sbuf_tensor("Qbr", [128, TOK], F32R))
        g["Ve"] = Ea(nc.sbuf_tensor("Ve", [128, 32 * 65], F32))
        g["Qb"] = Ea(nc.sbuf_tensor("Qb", [128, TOK], F32))
        g["Qe"] = Ea(nc.sbuf_tensor("Qe", [65, TOK], F32))
        g["Ke"] = Ea(nc.sbuf_tensor("Ke", [65, KTOT], F32))
        g["Pt"] = Ea(nc.sbuf_tensor("Pt", [128, 1024], F32))
        g["msb"] = Ea(nc.sbuf_tensor("msb", [128, 16], F32))
        g["m16"] = Ea(nc.sbuf_tensor("m16", [16, 128], F32))
        g["mc"] = Ea(nc.sbuf_tensor("mc", [128, 128], F32))
        g["lb"] = Ea(nc.sbuf_tensor("lb", [128, 16], F32))
        g["rl"] = Ea(nc.sbuf_tensor("rl", [128, 16], F32))
        g["rlr"] = Ea(nc.sbuf_tensor("rlr", [1, 512], F32))
        g["rlb"] = Ea(nc.sbuf_tensor("rlb", [64, 512], F32))
        g["stage"] = [Ea(nc.sbuf_tensor(f"stg{i}", [64, 512], F32)) for i in range(2)]
        g["ones_r"] = Ea(nc.sbuf_tensor("onesr3", [1, 64], F32))
        g["pvs"] = {"evac": 0, "rlb": 0, "st": {}, "idx": 0}
        g["dbg_d"] = D.dbg
        G = type("G", (), g)

        em = Em(nc)
        for dst, src_ in (
            (G.id128[:], D.id128.ap()), (G.mur[:], D.mur.ap()),
            (G.muk[:], D.muk.ap()),
        ):
            em.op("sync", (lambda eng, a=dst, b=src_: eng.dma_start(a, b)), inc=G.c, k=16)
        if getattr(D, "bcol", None) is not None:
            em.op("sync", lambda eng: eng.dma_start(G.bcol[:], D.bcol.ap()),
                  inc=G.x, k=16)
        else:
            em.op("sync", lambda eng: eng.dma_start(
                G.bcol[:], A(D.bcg, 0, [[NJ, 128], [1, NJ]])),
                waits=[D.bc_gate], inc=G.x, k=16)
            em.op("sync", lambda eng: eng.dma_start(G.flagb[:], D.flag.ap()),
                  inc=G.x, k=16)
            em.op("vector", lambda eng: eng.tensor_scalar(
                G.bcol[:, :], G.bcol[:, :], G.flagb[:, 0:1], None, ALU.mult),
                waits=[G.x.at()], inc=G.dv)
        for js in range(NJ):
            em.op("sync", (lambda eng, js=js: eng.dma_start(
                A(G.bXN, js * TOK, [[NJ * TOK, 128], [1, TOK]]),
                A(D.xn1, js * TOK, [[NJ * TOK, 128], [1, TOK]]))), inc=G.c, k=16)
        em.op("vector", lambda eng: eng.memset(G.Ve[:], 1.0), inc=G.dv)
        em.op("vector", lambda eng: eng.memset(G.Ke[64:65, :], 1.0), inc=G.dv)
        em.op("vector", lambda eng: eng.memset(G.ones_r[:, :], 1.0), inc=G.dv)
        em.flush()

        pair_done = {}
        for jp in range(8):
            with nc.named_scope(f"hp{jp}"):
                _l3_headpair(nc, G, jp, D, s1_dram, pair_done)

    if D.dbg2 is not None:
        em = Em(nc)
        em.op("gpsimd", lambda eng: eng.dma_start(D.dbg2.ap(), s1_dram.ap()),
              waits=[g["s1st"].at()], inc=g["ost"], k=16)
        em.flush()

    with contextlib.ExitStack() as st_b:
        Eb = st_b.enter_context
        for nm in ("bMX", "bR", "bKA", "bG", "bX1"):
            g[nm] = Eb(nc.sbuf_tensor(nm, [128, PITCH], F32))
        g["bS1"] = [Eb(nc.sbuf_tensor(f"bS1{i}", [128, PITCH], F32)) for i in range(2)]
        g["rw_cmb"] = {}
        G = type("G", (), g)

        for t in range(NT):
            with nc.named_scope(f"rw{t}"):
                _l3_rwkv_tile(nc, G, t, D.x1, D.wr, D.wkr, D.wvr, s1_dram, D.out)

        em = Em(nc)
        em.op("gpsimd", lambda eng: eng.nop(), waits=[G.ost.at()])
        em.flush()
    return type("G", (), g)


def build_launch3():
    nc = bass.Bass("TRN2", target_bir_lowering=False)
    D = _l3_dram(nc)
    with contextlib.ExitStack() as st:
        sems = {nm: Sem(st, nc, nm) for nm in L3_SEMS}
        SER_SEMS.clear()
        SER_SEMS.extend([sems["mm"], sems["dv"], sems["sc"]])
        with contextlib.ExitStack() as st_b:
            _l3_emit(nc, st_b, sems, D)
    return nc


def _l3_headpair(nc, G, jp, D, s1_dram, pair_done):
    mm, dv, sc = G.mm, G.dv, G.sc
    pb = G.pb
    kbsem = [G.kb0, G.kb1]
    vbsem = [G.vb0, G.vb1]
    wsem = [G.w0, G.w1]
    bi = jp % 2

    em = Em(nc)
    dguard = [(mm.h, pair_done[jp - 2])] if jp >= 2 else [(G.c.h, G.c.v)]
    gate = [D.kv_gate] if (getattr(D, "kf", None) is None and jp < 2) else []
    if getattr(D, "kf", None) is not None:
        for js in range(4):
            em.op("sync", (lambda eng, jp=jp, bi=bi, js=js: eng.dma_start(
                A(G.Kb[bi], js * (KTOT // 4), [[KTOT, 128], [1, KTOT // 4]]),
                A(D.kf, jp * KTOT + js * (KTOT // 4),
                  [[NJ * KTOT, 128], [1, KTOT // 4]]))),
                waits=dguard if js == 0 else [], inc=kbsem[bi], k=16)
        w_kb = kbsem[bi].at()
    else:
        for sl in range(2):
            for t in range(NT):
                em.op("sync", (lambda eng, jp=jp, bi=bi, sl=sl, t=t: eng.dma_start(
                    A(G.Kb[bi], sl * TOK + t * TT, [[KTOT, 128], [1, TT]]),
                    A(D.kgs[t], sl * 128 * NJ * TT + jp * TT,
                      [[NJ * TT, 128], [1, TT]]))),
                    waits=(dguard + gate) if (sl == 0 and t == 0) else [],
                    inc=kbsem[bi], k=16)
        w_kb = kbsem[bi].at()
    em.op("sync", (lambda eng, jp=jp, bi=bi: eng.dma_start(
        G.wbuf[bi][:], A(D.wq1, 128 * jp, [[DM, 128], [128 * DM, NJ], [1, 128]]))),
        waits=list(dguard), inc=wsem[bi], k=16)
    w_w = wsem[bi].at()
    em.op("vector", (lambda eng, bi=bi: eng.tensor_copy(G.Kbr[:, :], G.Kb[bi][:, :])),
          waits=[w_kb] + ([(mm.h, pair_done[jp - 1])] if jp >= 1 else []), inc=dv)
    w_kr = dv.at()

    prevg = [(mm.h, pair_done[jp - 1])] if jp >= 1 else []
    qev = {}
    for qc in range(4):
        def f(eng, qc=qc, bi=bi):
            for kj in range(NJ):
                ii = eng.matmul(pb[6 + qc % 2][:, :],
                                A(G.wbuf[bi], kj * 128, [[NJ * 128, 128], [1, 128]]),
                                A(G.bXN, kj * TOK + qc * 512, [[NJ * TOK, 128], [1, 512]]),
                                start=(kj == 0), stop=(kj == NJ - 1))
            return ii
        waits = [w_w] + list(prevg) + ([(dv.h, qev[qc - 2])] if qc >= 2 else [])
        em.op("tensor", f, waits=waits, inc=mm)
        em.op("vector", (lambda eng, qc=qc: eng.tensor_copy(
            A(G.Qb, qc * 512, [[TOK, 128], [1, 512]]), pb[6 + qc % 2][:, :])),
            waits=[mm.at()] + list(prevg), inc=dv)
        qev[qc] = dv.v
    w_q = dv.at()
    em.op("vector", lambda eng: eng.tensor_copy(G.Qbr[:, :], G.Qb[:, :]), inc=dv)
    w_qr = dv.at()

    def emit_extract(hh):
        em.op("gpsimd", (lambda eng, hh=hh: eng.dma_start(
            G.Qe[0:64, :], A(G.Qb, hh * 64 * TOK, [[TOK, 64], [1, TOK]]))),
            waits=[w_q, (mm.h, mm.v)], inc=G.qe, k=16)
        em.op("gpsimd", (lambda eng, hh=hh, bi=bi: eng.dma_start(
            G.Ke[0:64, :], A(G.Kb[bi], hh * 64 * KTOT, [[KTOT, 64], [1, KTOT]]))),
            waits=[w_kb, (mm.h, mm.v)], inc=G.ke, k=16)
        if getattr(D, "vf", None) is not None:
            em.op("gpsimd", (lambda eng, jp=jp, hh=hh: eng.dma_start(
                A(G.Ve, 0, [[32 * 65, 128], [65, 32], [1, 64]]),
                A(D.vf, jp * 128 + hh * 64, [[32 * DM, 128], [DM, 32], [1, 64]]))),
                waits=[(mm.h, mm.v)], inc=G.vb0, k=16)
        else:
            for sl in range(2):
                for t in range(NT):
                    em.op("gpsimd", (lambda eng, jp=jp, hh=hh, sl=sl, t=t: eng.dma_start(
                        A(G.Ve, (sl * 16 + 4 * t) * 65, [[32 * 65, 128], [65, 4], [1, 64]]),
                        A(D.vgs[t], sl * 128 * 4 * DM + jp * 128 + hh * 64,
                          [[4 * DM, 128], [DM, 4], [1, 64]]))),
                        waits=([(mm.h, mm.v)] + gate) if (sl == 0 and t == 0) else [],
                        inc=G.vb0, k=16)
        return G.qe.at(), G.ke.at(), G.vb0.at()

    def gen_passA(hh):
        red_marks = {}
        for qt in range(16):
            for kt in range(8):
                n = qt * 8 + kt
                bk = pb[n % 4]
                if n == 0:
                    waits = [w_qr, w_kr, (dv.h, dv.v)]
                elif n >= 4:
                    waits = [(dv.h, red_marks[n - 4])]
                else:
                    waits = []
                em.op("tensor", (lambda eng, qt=qt, kt=kt, bk=bk, hh=hh: eng.matmul(
                    bk[:, :],
                    A(G.Qbr, hh * 64 * TOK + qt * 128, [[TOK, 64], [1, 128]]),
                    A(G.Kbr, hh * 64 * KTOT + kt * 512, [[KTOT, 64], [1, 512]]),
                    start=True, stop=True)),
                    waits=waits, inc=mm)
                em.op("vector", (lambda eng, n=n, bk=bk: eng.tensor_reduce(
                    G.mc[:, n:n + 1], bk[:, :], AX.X, ALU.max)),
                    waits=[mm.at()], inc=dv)
                red_marks[n] = dv.v
                yield

    def finalize_passA(hh, h):
        for qt in range(16):
            em.op("vector", (lambda eng, qt=qt: eng.tensor_reduce(
                G.msb[:, qt:qt + 1], G.mc[:, qt * 8:qt * 8 + 8], AX.X, ALU.max)),
                inc=dv)
        em.op("vector", lambda eng: eng.memset(G.lb[:, :], 0.0), inc=dv)
        em.op("vector", lambda eng: eng.memset(G.rl[:, :], 0.0), inc=dv)
        em.op("vector", lambda eng: eng.tensor_scalar(
            G.msb[:, :], G.msb[:, :], -1.0, None, ALU.mult), inc=dv)
        w_ma = dv.at()

        em.op("tensor", lambda eng: eng.transpose(
            pb[2][0:16, 0:128], G.msb[:, :], G.id128[:, :]),
            waits=[w_ma, (sc.h, sc.v)], inc=mm)
        em.op("vector", lambda eng: eng.tensor_copy(G.m16[:, :], pb[2][0:16, 0:128]),
              waits=[mm.at()], inc=dv)
        em.op("vector", lambda eng: eng.memset(G.mc[:, :], 0.0), inc=dv)
        em.op("vector", lambda eng: eng.memset(G.lb[:, :], 0.0), inc=dv)
        em.op("gpsimd", lambda eng: eng.dma_start(
            A(G.Qe, 64 * TOK, [[TOK, 1], [128, 16], [1, 128]]),
            A(G.m16, 0, [[128, 16], [1, 128]])), waits=[dv.at()], inc=G.me, k=16)
        w_m = G.me.at()
        if getattr(G, "dbg_d", None) is not None:
            em.op("gpsimd", (lambda eng, h=h: eng.dma_start(
                A(G.dbg_d, h * TOK, [[TOK, 1], [1, TOK]]),
                A(G.Qe, 64 * TOK, [[TOK, 1], [1, TOK]]))),
                waits=[w_m], inc=G.me, k=16)
        return w_m

    def emit_passB(hh, w_m, w_qe, w_ke, w_ve, stepper=None):
        pvs = G.pvs
        for qq in range(4):
            exp_marks = {}
            pv_marks = {}

            def emit_pv(kt, qq=qq):
                def f(eng, kt=kt):
                    return eng.matmul(
                        pb[4][0:65, :],
                        A(G.Ve, kt * 65, [[32 * 65, 128], [1, 65]]),
                        A(G.Pt, (kt % 2) * 512, [[1024, 128], [1, 512]]),
                        start=(kt == 0), stop=(kt == 31))
                pvw = [(sc.h, exp_marks[kt])]
                if kt == 0:
                    pvw.append(w_ve)
                    if pvs["evac"]:
                        pvw.append((dv.h, pvs["evac"]))
                em.op("tensor", f, waits=pvw, inc=mm)
                pv_marks[kt] = mm.v

            for kt in range(32):
                sb = pb[2 + kt % 2]
                swaits = [w_m, w_ke, w_qe]
                if kt >= 2:
                    swaits.append((sc.h, exp_marks[kt - 2]))
                em.op("tensor", (lambda eng, qq=qq, kt=kt, sb=sb: eng.matmul(
                    sb[:, :],
                    A(G.Ke, kt * 128, [[KTOT, 65], [1, 128]]),
                    A(G.Qe, qq * 512, [[TOK, 65], [1, 512]]),
                    start=True, stop=True)),
                    waits=swaits if kt < 2 else swaits[-1:], inc=mm)
                em.op("scalar", (lambda eng, kt=kt, sb=sb: eng.activation(
                    A(G.Pt, (kt % 2) * 512, [[1024, 128], [1, 512]]), sb[:, :], AF.Exp)),
                    waits=[mm.at()] + ([(mm.h, pv_marks[kt - 2])] if kt >= 2 else []),
                    inc=sc)
                exp_marks[kt] = sc.v
                if stepper is not None:
                    stepper()
                if kt >= 1:
                    emit_pv(kt - 1)
            emit_pv(31)
            w_pv = mm.at()
            def f(eng):
                eng.activation(G.rlr[0:1, :], pb[4][64:65, :], AF.Ln, scale=8.0)
                return eng.activation(G.rlr[0:1, :], G.rlr[0:1, :], AF.Exp, scale=-1.0)
            em.op("scalar", f, waits=[w_pv], inc=sc)
            w_rl = sc.at()
            em.op("tensor", (lambda eng: eng.matmul(
                pb[5][0:64, :], G.ones_r[:, :], G.rlr[0:1, :], start=True, stop=True)),
                waits=[w_rl] + ([(dv.h, pvs["rlb"])] if pvs["rlb"] else []), inc=mm)
            em.op("vector", (lambda eng: eng.tensor_copy(G.rlb[:, :], pb[5][0:64, :])),
                  waits=[mm.at()], inc=dv)
            pvs["rlb"] = dv.v
            sg = G.stage[pvs["idx"] % 2]
            tw = [(dv.h, pvs["rlb"])]
            prev_st = pvs["st"].get(pvs["idx"] - 2)
            if prev_st:
                tw.append((G.s1st.h, prev_st))
            em.op("vector", (lambda eng, sg=sg: eng.tensor_tensor(
                sg[:, :], pb[4][0:64, :], G.rlb[:, :], ALU.mult)), waits=tw, inc=dv)
            pvs["evac"] = dv.v
            em.op("gpsimd", (lambda eng, jp=jp, hh=hh, qq=qq, sg=sg: eng.dma_start(
                A(s1_dram, hh * 64 * NJ * TOK + jp * TOK + qq * 512,
                  [[NJ * TOK, 64], [1, 512]]), sg[:, :])),
                waits=[(dv.h, pvs["evac"])], inc=G.s1st, k=16)
            pvs["st"][pvs["idx"]] = G.s1st.v
            pvs["idx"] += 1

    w_qe, w_ke, w_ve = emit_extract(0)
    for _ in gen_passA(0):
        pass
    w_m = finalize_passA(0, 2 * jp)
    emit_passB(0, w_m, w_qe, w_ke, w_ve)
    w_qe, w_ke, w_ve = emit_extract(1)
    for _ in gen_passA(1):
        pass
    w_m = finalize_passA(1, 2 * jp + 1)
    emit_passB(1, w_m, w_qe, w_ke, w_ve)

    pair_done[jp] = mm.v
    pair_done[("dv", jp)] = dv.v
    em.flush()


def _l3_rwkv_tile(nc, G, t, x1_d, wr_d, wkr_d, wvr_d, s1_dram, out_d):
    mm, dv, sc = G.mm, G.dv, G.sc
    pb = G.pb
    wsem = [G.w0, G.w1]
    t0 = t * TT

    def xnv(j, lo=0, n=TT):
        return A(G.bXN, j * TOK + t0 + lo, [[NJ * TOK, 128], [1, n]])

    def xv(buf, j, lo=0, n=TT):
        return A(buf, j * TT + lo, [[PITCH, 128], [1, n]])

    em = Em(nc)

    def emit_mix(mu_sb, dst, gate):
        for j in range(NJ):
            if t == 0:
                def f(eng, j=j, mu_sb=mu_sb, dst=dst):
                    eng.tensor_tensor(xv(G.bMX, j, 1, TT - 1), xnv(j, 1, TT - 1),
                                      xnv(j, 0, TT - 1), ALU.subtract)
                    eng.scalar_tensor_tensor(xv(dst, j, 1, TT - 1),
                                             xv(G.bMX, j, 1, TT - 1), mu_sb[:, j:j + 1],
                                             xnv(j, 0, TT - 1), ALU.mult, ALU.add)
                    eng.tensor_tensor(xv(G.bMX, j, 0, 1), xnv(j, 0, 1),
                                      G.bcol[:, j:j + 1], ALU.subtract)
                    return eng.scalar_tensor_tensor(xv(dst, j, 0, 1), xv(G.bMX, j, 0, 1),
                                                    mu_sb[:, j:j + 1], G.bcol[:, j:j + 1],
                                                    ALU.mult, ALU.add)
            else:
                def f(eng, j=j, mu_sb=mu_sb, dst=dst):
                    eng.tensor_tensor(xv(G.bMX, j), xnv(j), xnv(j, -1), ALU.subtract)
                    return eng.scalar_tensor_tensor(xv(dst, j), xv(G.bMX, j),
                                                    mu_sb[:, j:j + 1], xnv(j, -1),
                                                    ALU.mult, ALU.add)
            em.op("vector", f, waits=list(gate) if j == 0 else [], inc=dv)
        return dv.at()

    w_xr = emit_mix(G.mur, G.bKA, [(mm.h, mm.v)])

    def proj(w_d, rhsbuf, evac, gate, bankbase=4):
        grp = {}
        evm = {}
        for oc in range(NJ):
            bi2 = oc % 2
            dwaits = [(mm.h, grp[oc - 2])] if oc >= 2 else list(gate)
            em.op("sync", (lambda eng, oc=oc, bi2=bi2, w_d=w_d: eng.dma_start(
                G.wbuf[bi2][:], A(w_d, 128 * oc, [[DM, 128], [128 * DM, NJ], [1, 128]]))),
                waits=dwaits, inc=wsem[bi2], k=16)
            w_wl = wsem[bi2].at()
            bank = pb[bankbase + bi2]
            mwaits = [w_wl] + list(gate) + ([(sc.h, evm[oc - 2])] if oc >= 2 else [])

            def f(eng, bi2=bi2, bank=bank, rhsbuf=rhsbuf):
                for kj in range(NJ):
                    ii = eng.matmul(bank[:, :],
                                    A(G.wbuf[bi2], kj * 128, [[NJ * 128, 128], [1, 128]]),
                                    xv(rhsbuf, kj), start=(kj == 0), stop=(kj == NJ - 1))
                return ii
            em.op("tensor", f, waits=mwaits, inc=mm)
            grp[oc] = mm.v
            evac(oc, bank)
            evm[oc] = sc.v
        return mm.at()

    def ev_r(oc, bank):
        em.op("scalar", (lambda eng, oc=oc, bank=bank: eng.activation(
            xv(G.bR, oc), bank[:, :], AF.Sigmoid)), waits=[(mm.h, mm.v)], inc=sc)
    w_rproj = proj(wr_d, G.bKA, ev_r, [w_xr])

    w_xk = emit_mix(G.muk, G.bMX, [(mm.h, w_rproj[1])])

    def ev_k(oc, bank):
        em.op("vector", (lambda eng, oc=oc, bank=bank: eng.tensor_scalar(
            xv(G.bKA, oc), bank[:, :], 0.0, None, ALU.max)), waits=[(mm.h, mm.v)], inc=dv)
        em.op("scalar", (lambda eng, oc=oc: eng.activation(
            xv(G.bKA, oc), xv(G.bKA, oc), AF.Square)), waits=[dv.at()], inc=sc)
    w_kproj = proj(wkr_d, G.bMX, ev_k, [w_xk])

    def ev_v(oc, bank):
        em.op("vector", (lambda eng, oc=oc, bank=bank: eng.tensor_tensor(
            xv(G.bG, oc), bank[:, :], xv(G.bR, oc), ALU.mult)), waits=[(mm.h, mm.v)], inc=dv)
        em.op("scalar", (lambda eng, oc=oc: eng.activation(
            xv(G.bG, oc), xv(G.bG, oc), AF.Silu)), waits=[dv.at()], inc=sc)
    w_vproj = proj(wvr_d, G.bKA, ev_v, [w_kproj, (sc.h, sc.v)])

    bS1 = G.bS1[t % 2]
    s1w = [G.s1st.at()]
    if t >= 2:
        s1w.append((dv.h, G.rw_cmb[t - 2]))
    for js in range(0, NJ, 2):
        em.op("sync", (lambda eng, t=t, js=js, bS1=bS1: eng.dma_start(
            A(bS1, js * TT, [[PITCH, 128], [TT, 2], [1, TT]]),
            A(s1_dram, t0 + js * TOK, [[NJ * TOK, 128], [TOK, 2], [1, TT]]))),
            waits=s1w if js == 0 else [], inc=G.s1ld, k=16)
    w_s1 = G.s1ld.at()

    for js in range(0, NJ, 2):
        em.op("sync", (lambda eng, t=t, js=js: eng.dma_start(
            A(G.bX1, js * TT, [[PITCH, 128], [TT, 2], [1, TT]]),
            A(x1_d, t0 + js * TOK, [[NJ * TOK, 128], [TOK, 2], [1, TT]]))),
            waits=[G.ost.at()] if js == 0 else [], inc=G.x1ld, k=16)
    w_x1 = G.x1ld.at()
    for j in range(NJ):
        def f(eng, j=j, bS1=bS1):
            eng.tensor_tensor(xv(bS1, j), xv(bS1, j), xv(G.bG, j), ALU.mult)
            return eng.tensor_tensor(xv(G.bX1, j), xv(G.bX1, j), xv(bS1, j), ALU.add)
        em.op("vector", f, waits=[w_x1, w_s1, (sc.h, sc.v)] if j == 0 else [], inc=dv)
    G.rw_cmb[t] = dv.v
    for js in range(0, NJ, 2):
        em.op("gpsimd", (lambda eng, t=t, js=js: eng.dma_start(
            A(out_d, t0 + js * TOK, [[NJ * TOK, 128], [TOK, 2], [1, TT]]),
            A(G.bX1, js * TT, [[PITCH, 128], [TT, 2], [1, TT]]))),
            waits=[dv.at()] if js == 0 else [], inc=G.ost, k=16)

    em.flush()


def l3_inputs(l2outA, l2outB, ins, half, x1, xn1, bcol):
    kf = np.empty((128, NJ, KTOT), np.float32)
    vf = np.empty((128, 32, DM), np.float32)
    for sl, r in enumerate((l2outA, l2outB)):
        for t in range(NT):
            kf[:, :, sl * TOK + t * TT:sl * TOK + (t + 1) * TT] = r[f"k1_{t}"]
            vf[:, sl * 16 + 4 * t:sl * 16 + 4 * t + 4, :] = r[f"v1_{t}"]
    return {
        "x1": x1, "xn1": xn1, "kf": kf, "vf": vf,
        "wq1": np.ascontiguousarray(ins["L1_attn_Wq"].transpose(1, 0, 2).reshape(DM, DM)),
        "wr": np.ascontiguousarray(ins["L1_ffn_W_r"].T),
        "wkr": np.ascontiguousarray(ins["L1_ffn_W_k"].T),
        "wvr": np.ascontiguousarray(ins["L1_ffn_W_v"].T),
        "mur": _vec128(ins["L1_ffn_mu_r"]), "muk": _vec128(ins["L1_ffn_mu_k"]),
        "bcol": bcol, "id128": np.eye(128, dtype=np.float32),
    }



def _fm(a):
    T = a.shape[0]
    return np.ascontiguousarray(a.T.reshape(NJ, 128, T).transpose(1, 0, 2))


def _from_fm(a):
    return np.ascontiguousarray(a.transpose(1, 0, 2).reshape(DM, -1).T)


def _vec128(v):
    return np.ascontiguousarray(v.reshape(-1, 128).T)


def _build_consts():
    eye64t = np.tile(np.eye(64, dtype=np.float32), (2, 2)).astype(np.float32)
    tril = np.tril(np.ones((64, 64), np.float32))
    cmask = np.tile(np.where(tril > 0, 0.0, 8.0 * NEG).astype(np.float32), (2, 8))
    strict = (np.arange(64)[:, None] < np.arange(64)[None, :]).astype(np.float32)
    smask = np.tile(strict, (2, 8))
    return eye64t, np.ascontiguousarray(cmask), np.ascontiguousarray(smask)


def l1_inputs(x_half, ins):
    eye64t, cmask, smask = _build_consts()
    return {
        "x": _fm(x_half),
        "wq": ins["L0_attn_Wq"], "wk": ins["L0_attn_Wk"], "wv": ins["L0_attn_Wv"],
        "bq": _vec128(ins["L0_attn_bq"]), "bk": _vec128(ins["L0_attn_bk"]),
        "bv": ins["L0_attn_bv"].reshape(1, DM),
        "wg1": ins["L0_attn_Wg1"], "bg1": ins["L0_attn_bg1"].reshape(GH, 1),
        "wg2": ins["L0_attn_Wg2"], "bg2": _vec128(ins["L0_attn_bg2"]),
        "w1t": np.ascontiguousarray(ins["L0_ffn_w1"].T),
        "w2t": np.ascontiguousarray(ins["L0_ffn_w2"].T),
        "id64": eye64t, "cmask": cmask, "smask": smask,
    }


RG_PAIRS = [[0, 1], [2, 3], [4, 5], [6, 7]]


def _uniq_names(nc):
    sb0, ps0 = nc.sbuf_tensor, nc.psum_tensor
    ctr = [0]

    def sb(name, *a, **k):
        ctr[0] += 1
        return sb0(f"{name}_u{ctr[0]}", *a, **k)

    def ps(name, *a, **k):
        ctr[0] += 1
        return ps0(f"{name}_u{ctr[0]}", *a, **k)
    nc.sbuf_tensor, nc.psum_tensor = sb, ps


def _join(em, jn, terms):
    base = jn.v
    for e in Em.ENG:
        em.op(e, lambda eng: eng.nop(), waits=(list(terms) if e == "sync" else []),
              inc=jn, k=1)
    for e in Em.ENG:
        em.op(e, lambda eng: eng.nop(), waits=[(jn.h, base + 5)])


def build_fused():
    nc = bass.Bass("TRN2", target_bir_lowering=False)
    _uniq_names(nc)
    D = _l1_dram(nc, kind_out="Internal", t_dt=BF16)
    D.flag = nc.dram_tensor("flag", [128, 1], F32, kind="ExternalInput")
    D.sendg = nc.dram_tensor("sendg", [2, 128, 512], F32)
    D.kgs = [nc.dram_tensor(f"kg_{t}", [2, 128, NJ, TT], F32) for t in range(NT)]
    D.vgs = [nc.dram_tensor(f"vg_{t}", [2, 128, 4, DM], F32) for t in range(NT)]
    D.bcg = nc.dram_tensor("bcg", [2, 128, NJ], F32)
    D.xn1col = nc.dram_tensor("xn1col", [128, NJ], F32)
    _l2_dram(nc, kind_out="Internal", D=D)
    _l3_dram(nc, D=D)
    D.sp_ap = A(D.sendg, 0, [[512, 128], [1, 512]])

    with contextlib.ExitStack() as st:
        sems = {nm: Sem(st, nc, nm) for nm in L3_SEMS}
        jn = Sem(st, nc, "jn")
        SER_SEMS.clear()
        SER_SEMS.extend([sems["mm"], sems["dv"], sems["sc"]])
        cc1, cc2 = sems["cc1"], sems["cc2"]

        with contextlib.ExitStack() as st1:
            G1 = _l1_emit(nc, st1, sems, D)
            em = Em(nc)
            em.op("gpsimd", lambda eng: eng.collective_compute(
                "AllGather", ALU.bypass, replica_groups=RG_PAIRS,
                ins=[D.send.ap().opt()], outs=[D.sendg.ap().opt()]),
                waits=[G1.fin.at()], inc=cc1, k=1)
            _join(em, jn, [G1.fin.at(), G1.xns.at(), G1.ss.at(), G1.qs.at(),
                           G1.l0.at(), G1.l1.at(), G1.t0.at(), G1.t1.at()])
            em.flush()

        D.sp_gate = (cc1.h, 1)
        with contextlib.ExitStack() as st2:
            G2 = _l2_emit(nc, st2, sems, D)
            em = Em(nc)
            l2_term = [G2.sst.at(), G2.l0.at(), G2.l1.at()]
            _join(em, jn, l2_term)
            em.flush()

        D.kv_gate = (cc2.h, 2 * NT)
        D.bc_gate = (cc1.h, 2)
        with contextlib.ExitStack() as st3:
            _l3_emit(nc, st3, sems, D)
    return nc


def fused_inputs(x_half, half, ins):
    d = l1_inputs(x_half, ins)
    d.update({
        "l1g": _vec128(ins["L0_ffn_ln1_g"]), "l1b": _vec128(ins["L0_ffn_ln1_b"]),
        "b1": _vec128(ins["L0_ffn_b1"]), "l2g": _vec128(ins["L0_ffn_ln2_g"]),
        "l2b": _vec128(ins["L0_ffn_ln2_b"]), "b2": _vec128(ins["L0_ffn_b2"]),
        "wk1": np.ascontiguousarray(ins["L1_attn_Wk"].transpose(1, 0, 2).reshape(DM, DM)),
        "wv1": np.ascontiguousarray(ins["L1_attn_Wv"].transpose(1, 0, 2).reshape(DM, DM)),
        "wq1": np.ascontiguousarray(ins["L1_attn_Wq"].transpose(1, 0, 2).reshape(DM, DM)),
        "wr": np.ascontiguousarray(ins["L1_ffn_W_r"].T),
        "wkr": np.ascontiguousarray(ins["L1_ffn_W_k"].T),
        "wvr": np.ascontiguousarray(ins["L1_ffn_W_v"].T),
        "mur": _vec128(ins["L1_ffn_mu_r"]), "muk": _vec128(ins["L1_ffn_mu_k"]),
        "id128": np.eye(128, dtype=np.float32),
        "flag": np.full((128, 1), float(half), np.float32),
    })
    return d


_NC_CACHE = {}


def _get_ncs():
    if "ncs" not in _NC_CACHE:
        _NC_CACHE["ncs"] = (build_launch1(), build_launch2(), build_launch3())
    return _NC_CACHE["ncs"]


def _get_fused():
    if "fused" not in _NC_CACHE:
        _NC_CACHE["fused"] = build_fused()
    return _NC_CACHE["fused"]


def kernel(**inputs):
    import os
    if bool(int(os.environ.get("KERNEL_UNFUSED", "0"))):
        return _kernel_unfused(**inputs)
    ins = inputs
    nc = _get_fused()
    x_full = np.asarray(ins["L0_attn_x"], dtype=np.float32)
    cores = list(range(NCORES))
    im = []
    for core in cores:
        b, half = core // 2, core % 2
        x_half = np.ascontiguousarray(x_full[b, half * TOK:(half + 1) * TOK, :])
        im.append(fused_inputs(x_half, half, ins))
    r = run_bass_kernel_spmd(nc, im, core_ids=cores).results
    out = np.empty((4, 2 * TOK, DM), dtype=np.float32)
    for core in cores:
        b, half = core // 2, core % 2
        out[b, half * TOK:(half + 1) * TOK, :] = _from_fm(r[core]["out"])
    return out


def _kernel_unfused(**inputs):
    ins = inputs
    nc1, nc2, nc3 = _get_ncs()
    x_full = np.asarray(ins["L0_attn_x"], dtype=np.float32)
    cores = list(range(NCORES))

    im1 = []
    for core in cores:
        b, half = core // 2, core % 2
        x_half = np.ascontiguousarray(x_full[b, half * TOK:(half + 1) * TOK, :])
        im1.append(l1_inputs(x_half, ins))
    r1 = run_bass_kernel_spmd(nc1, im1, core_ids=cores).results

    im2 = []
    for core in cores:
        b, half = core // 2, core % 2
        r1c = dict(r1[core])
        r1c["__x"] = im1[core]["x"]
        sp = (np.ascontiguousarray(r1[core - 1]["send"]) if half == 1
              else np.zeros((128, 512), np.float32))
        im2.append(l2_inputs(r1c, ins, sp))
    r2 = run_bass_kernel_spmd(nc2, im2, core_ids=cores).results

    im3 = []
    for core in cores:
        b, half = core // 2, core % 2
        a, bb = 2 * b, 2 * b + 1
        bcol = (np.ascontiguousarray(r2[a]["xn1"][:, :, -1]) if half == 1
                else np.zeros((128, NJ), np.float32))
        im3.append(l3_inputs(r2[a], r2[bb], ins, half,
                             np.ascontiguousarray(r2[core]["x1"]),
                             np.ascontiguousarray(r2[core]["xn1"]), bcol))
    r3 = run_bass_kernel_spmd(nc3, im3, core_ids=cores).results

    out = np.empty((4, 2 * TOK, DM), dtype=np.float32)
    for core in cores:
        b, half = core // 2, core % 2
        out[b, half * TOK:(half + 1) * TOK, :] = _from_fm(r3[core]["out"])
    return out


# version 30 (final)
# speedup vs baseline: 1.1308x; 1.0771x over previous
import contextlib
import numpy as np
import concourse.bass as bass
import concourse.mybir as mybir
from concourse.bass_utils import run_bass_kernel_spmd

F32 = mybir.dt.float32
F32R = mybir.dt.float32r
BF16 = mybir.dt.bfloat16
AF = mybir.ActivationFunctionType
ALU = mybir.AluOpType
AX = mybir.AxisListType

NCORES = 8
TOK = 2048
NT = 4
TT = 512
NJ = 8
DM = 1024
NCH = 32
CPT = 8
CH = 64
H = 16
DH = 64
HID = 4096
GH = 16
LNEPS = 1e-5
NEG = -1e9
PITCH = 4096


def A(t, off, dims):
    return bass.AP(t, off, dims)


class Sem:
    def __init__(self, stack, nc, name):
        self.h = stack.enter_context(nc.semaphore(name))
        self.v = 0

    def at(self):
        return (self.h, self.v)


SER_SEMS = []


class Em:

    ENG = ("sync", "gpsimd", "tensor", "vector", "scalar")

    def __init__(self, nc):
        import os
        self.nc = nc
        self.q = {e: [] for e in self.ENG}
        self.ser = bool(int(os.environ.get("EM_SERIALIZE", "0")))

    def op(self, engine, fn, waits=(), inc=None, k=1):
        waits = [w for w in waits if w is not None]
        if self.ser:
            waits = waits + [s.at() for s in SER_SEMS]
        self.q[engine].append((fn, waits, inc, k))
        if inc is not None:
            inc.v += k
        return inc.v if inc is not None else None

    def flush(self):
        with self.nc.Block() as blk:
            def runner(eng, ops):
                for fn, waits, inc, k in ops:
                    for (sh, v) in waits:
                        if v > 0:
                            eng.wait_ge(sh, v)
                    instr = fn(eng)
                    if inc is not None:
                        instr.then_inc(inc.h, k)
            for name in self.ENG:
                ops = self.q[name]
                if ops:
                    getattr(blk, name)(lambda eng, ops=ops: runner(eng, ops))
        self.q = {e: [] for e in self.ENG}



L1_SEMS = ("mm", "dv", "sc", "c", "x", "w0", "w1", "v0", "v1", "v2", "v3",
           "xns", "ss", "qs", "l0", "l1", "t0", "t1", "fin")
L2_SEMS = L1_SEMS + ("sst",)
L3_SEMS = L2_SEMS + ("kb0", "kb1", "vb0", "vb1", "qe", "ke", "me",
                     "s1st", "s1ld", "x1ld", "ost", "cc1", "cc2")


class NS:
    def __init__(self, **kw):
        self.__dict__.update(kw)


def _l1_dram(nc, kind_in="ExternalInput", kind_out="ExternalOutput", t_dt=F32):
    D = NS()
    D.x = nc.dram_tensor("x", [128, NJ, TOK], F32, kind=kind_in)
    D.wq = nc.dram_tensor("wq", [DM, DM], F32, kind=kind_in)
    D.wk = nc.dram_tensor("wk", [DM, DM], F32, kind=kind_in)
    D.wv = nc.dram_tensor("wv", [DM, DM], F32, kind=kind_in)
    D.bq = nc.dram_tensor("bq", [128, NJ], F32, kind=kind_in)
    D.bk = nc.dram_tensor("bk", [128, NJ], F32, kind=kind_in)
    D.bv = nc.dram_tensor("bv", [1, DM], F32, kind=kind_in)
    D.wg1 = nc.dram_tensor("wg1", [DM, GH], F32, kind=kind_in)
    D.bg1 = nc.dram_tensor("bg1", [GH, 1], F32, kind=kind_in)
    D.wg2 = nc.dram_tensor("wg2", [GH, DM], F32, kind=kind_in)
    D.bg2 = nc.dram_tensor("bg2", [128, NJ], F32, kind=kind_in)
    D.w1t = nc.dram_tensor("w1t", [DM, HID], F32, kind=kind_in)
    D.w2t = nc.dram_tensor("w2t", [HID, DM], F32, kind=kind_in)
    D.id64 = nc.dram_tensor("id64", [128, 128], F32, kind=kind_in)
    D.cmask = nc.dram_tensor("cmask", [128, 512], F32, kind=kind_in)
    D.smask = nc.dram_tensor("smask", [128, 512], F32, kind=kind_in)
    D.xn = nc.dram_tensor("xn", [128, NJ, TOK], F32, kind=kind_out)
    D.s = nc.dram_tensor("s", [128, NJ, TOK], F32, kind=kind_out)
    D.qh = nc.dram_tensor("qh", [128, NJ, TOK], F32, kind=kind_out)
    D.lent = nc.dram_tensor("lent", [NCH, 128, 512], F32, kind=kind_out)
    D.dcum = nc.dram_tensor("dcum", [NCH, 128, 8], F32, kind=kind_out)
    D.send = nc.dram_tensor("send", [128, 512], F32, kind=kind_out)
    D.t1 = nc.dram_tensor("t1", [DM, HID], t_dt, kind=kind_out)
    D.t2 = nc.dram_tensor("t2", [HID, DM], t_dt, kind=kind_out)
    D.m1 = nc.dram_tensor("m1", [128, 2], F32, kind=kind_out)
    D.m2 = nc.dram_tensor("m2", [128, 2], F32, kind=kind_out)
    return D


def _l1_emit(nc, st, sems, D):
    E = st.enter_context
    g = {}
    for nm in ("bA", "bB", "bC", "bD", "bE", "bF"):
        g[nm] = E(nc.sbuf_tensor(nm, [128, PITCH], F32))
    for nm in ("pE1", "pE2", "pQT", "pKT", "pKH", "pQH"):
        g[nm] = E(nc.sbuf_tensor(nm, [128, 1024], F32))
    g["wbuf"] = [E(nc.sbuf_tensor(f"wbuf{i}", [128, NJ * 128], F32)) for i in range(2)]
    g["scr"] = [E(nc.sbuf_tensor(f"scr{i}", [128, 512], F32)) for i in range(4)]
    g["bBb"] = E(nc.sbuf_tensor("bBb", [128, PITCH], BF16))
    g["wbufb"] = [E(nc.sbuf_tensor(f"wbufb{i}", [128, NJ * 128], BF16)) for i in range(2)]
    g["wg1_sb"] = E(nc.sbuf_tensor("wg1s", [128, NJ * GH], F32))
    g["wg2_sb"] = E(nc.sbuf_tensor("wg2s", [GH, DM], F32))
    g["bg1_sb"] = E(nc.sbuf_tensor("bg1s", [GH, 1], F32))
    g["bq_sb"] = E(nc.sbuf_tensor("bqs", [128, NJ], F32))
    g["bk_sb"] = E(nc.sbuf_tensor("bks", [128, NJ], F32))
    g["bg2_sb"] = E(nc.sbuf_tensor("bg2s", [128, NJ], F32))
    g["bvbc"] = E(nc.sbuf_tensor("bvbc", [128, DM], F32))
    g["bv_row"] = E(nc.sbuf_tensor("bvrow", [1, DM], F32))
    g["id64"] = E(nc.sbuf_tensor("id64s", [128, 128], F32))
    g["cmask"] = E(nc.sbuf_tensor("cmasks", [128, 512], F32))
    g["smask"] = E(nc.sbuf_tensor("smasks", [128, 512], F32))
    g["ones_c"] = E(nc.sbuf_tensor("onesc", [128, 1], F32))
    g["ones_r"] = E(nc.sbuf_tensor("onesr", [1, 128], F32))
    g["g1_sb"] = E(nc.sbuf_tensor("g1s", [GH, TT], F32))
    g["P_sb"] = E(nc.sbuf_tensor("Psb", [128, 1024], F32))
    g["PT_sb"] = E(nc.sbuf_tensor("PTsb", [128, 1024], F32))
    g["KT_sb"] = E(nc.sbuf_tensor("KTsb", [128, 1024], F32))
    g["At_sb"] = E(nc.sbuf_tensor("Atsb", [128, 1024], F32))
    g["S_all"] = E(nc.sbuf_tensor("Sall", [128, 512], F32))
    g["lstage"] = [E(nc.sbuf_tensor(f"lst{i}", [128, 512], F32)) for i in range(2)]
    g["msb"] = E(nc.sbuf_tensor("msb", [128, 64], F32))
    g["tcb"] = E(nc.sbuf_tensor("tcb", [128, 64], F32))
    g["tcm"] = E(nc.sbuf_tensor("tcms", [128, 64], F32))
    g["etb"] = E(nc.sbuf_tensor("etb", [128, 64], F32))
    g["tcum"] = E(nc.sbuf_tensor("tcum", [128, 8], F32))
    g["dstage"] = E(nc.sbuf_tensor("dstage", [128, NCH * 8], F32))
    g["redc"] = E(nc.sbuf_tensor("redc", [128, 64], F32))
    g["mcols"] = E(nc.sbuf_tensor("mcols", [128, 2], F32))
    g["mrow"] = E(nc.sbuf_tensor("mrow", [1, 2], F32))
    g["sgb"] = E(nc.sbuf_tensor("sgb", [128, 512], F32))
    g["sgb2"] = E(nc.sbuf_tensor("sgb2", [128, 512], F32))
    g["cns"] = E(nc.sbuf_tensor("cns", [128, 2], F32))
    g["tqb"] = [E(nc.sbuf_tensor(f"tqb{i}", [128, 512], F32)) for i in range(2)]
    g["tscr"] = [E(nc.sbuf_tensor(f"tscr{i}", [128, 512], F32)) for i in range(2)]
    g["tred"] = E(nc.sbuf_tensor("tred", [128, 128], F32))
    g["pb"] = [E(nc.psum_tensor(f"pb{i}", [128, 512], F32)) for i in range(8)]
    g.update(sems)
    G = type("G", (), g)

    import os
    lim = int(os.environ.get("L1_PHASES", "99"))
    ph = 0
    with nc.named_scope("l1c"):
        _l1_consts(nc, G, D.id64, D.cmask, D.smask, D.bq, D.bk, D.bg2, D.bg1,
                   D.wg2, D.bv, D.wg1)
    ph += 1
    gens = [_l1_tern_p1(nc, G, D.w1t, DM, HID, 0),
            _l1_tern_p1(nc, G, D.w2t, HID, DM, 64)]

    def drive(k):
        em = Em(nc)
        G.tern_em = em
        while k > 0 and gens:
            try:
                next(gens[0])
                k -= 1
            except StopIteration:
                gens.pop(0)
        em.flush()

    for t in range(NT):
        if ph < lim:
            with nc.named_scope(f"l1p{t}"):
                _l1_tile_proj(nc, G, t, D.x, D.xn, D.wq, D.wk, D.wv)
            ph += 1
        drive(16)
        if ph < lim:
            with nc.named_scope(f"l1k{t}"):
                _l1_tile_chunks(nc, G, t, D.s, D.qh, D.lent)
            ph += 1
        drive(16)
    if ph < lim:
        with nc.named_scope("l1f"):
            _l1_finalize(nc, G, D.send, D.dcum)
        ph += 1
    drive(1 << 30)
    if ph < lim:
        with nc.named_scope("l1t1"):
            _l1_ternary(nc, G, D.w1t, D.t1, D.m1, DM, HID, 0)
        ph += 1
    if ph < lim:
        with nc.named_scope("l1t2"):
            _l1_ternary(nc, G, D.w2t, D.t2, D.m2, HID, DM, 64)
    return G


def build_launch1():
    nc = bass.Bass("TRN2", target_bir_lowering=False)
    D = _l1_dram(nc)
    with contextlib.ExitStack() as st:
        sems = {nm: Sem(st, nc, nm) for nm in L1_SEMS}
        SER_SEMS.clear()
        SER_SEMS.extend([sems["mm"], sems["dv"], sems["sc"]])
        with contextlib.ExitStack() as st_b:
            G = _l1_emit(nc, st_b, sems, D)
        em = Em(nc)
        em.op("gpsimd", lambda eng: eng.nop(),
              waits=[G.fin.at(), G.xns.at(), G.ss.at(), G.qs.at(),
                     G.l0.at(), G.l1.at(), G.t0.at(), G.t1.at()])
        em.flush()
    return nc


def _l1_consts(nc, G, id64_d, cmask_d, smask_d, bq_d, bk_d, bg2_d, bg1_d,
               wg2_d, bv_d, wg1_d):
    em = Em(nc)
    ld, dv, mm = G.c, G.dv, G.mm

    for dst, src in (
        (G.id64[:], id64_d.ap()), (G.cmask[:], cmask_d.ap()), (G.smask[:], smask_d.ap()),
        (G.bq_sb[:], bq_d.ap()), (G.bk_sb[:], bk_d.ap()), (G.bg2_sb[:], bg2_d.ap()),
        (G.bg1_sb[:], bg1_d.ap()), (G.wg2_sb[:], wg2_d.ap()), (G.bv_row[:], bv_d.ap()),
        (G.wg1_sb[:], A(wg1_d, 0, [[GH, 128], [128 * GH, NJ], [1, GH]])),
    ):
        em.op("sync", (lambda eng, a=dst, b=src: eng.dma_start(a, b)), inc=ld, k=16)

    def f(eng):
        eng.memset(G.ones_c[:], 1.0)
        eng.memset(G.ones_r[:], 1.0)
        eng.memset(G.tcum[:], 0.0)
        eng.memset(G.cns[:, 0:1], float(LNEPS))
        eng.memset(G.cns[:, 1:2], 0.1)
        return eng.memset(G.S_all[:], 0.0)
    em.op("vector", f, inc=dv)

    def f(eng):
        eng.matmul(G.pb[0][:, :], G.ones_r[:], G.bv_row[0:1, 0:512], start=True, stop=True)
        return eng.matmul(G.pb[1][:, :], G.ones_r[:], G.bv_row[0:1, 512:1024],
                          start=True, stop=True)
    em.op("tensor", f, waits=[ld.at(), dv.at()], inc=mm)

    def f(eng):
        eng.tensor_copy(G.bvbc[:, 0:512], G.pb[0][:, :])
        return eng.tensor_copy(G.bvbc[:, 512:1024], G.pb[1][:, :])
    em.op("vector", f, waits=[mm.at()], inc=dv)

    em.flush()


def _l1_tile_proj(nc, G, t, x_d, xn_d, wq_d, wk_d, wv_d):
    em = Em(nc)
    mm, dv, sc = G.mm, G.dv, G.sc
    wsem = [G.w0, G.w1]
    vsem = [G.v0, G.v1]
    bA, bB, bC, bD, bE, bF = G.bA, G.bB, G.bC, G.bD, G.bE, G.bF
    pb = G.pb
    t0 = t * TT

    def xv(buf, j, lo=0, n=TT):
        return A(buf, j * TT + lo, [[PITCH, 128], [1, n]])

    for js in range(0, NJ, 2):
        em.op("sync", (lambda eng, js=js: eng.dma_start(
            A(bA, js * TT, [[PITCH, 128], [TT, 2], [1, TT]]),
            A(x_d, t0 + js * TOK, [[NJ * TOK, 128], [TOK, 2], [1, TT]]))),
            waits=[G.c.at()] if js == 0 else [], inc=G.x, k=16)
    w_x = G.x.at()

    mm_sq = {}
    for j in range(NJ):
        waits = [w_x] + ([(mm.h, mm_sq[j - 2])] if j >= 2 else [])
        em.op("scalar", (lambda eng, j=j: eng.activation(
            G.scr[j % 2][:], xv(bA, j), AF.Square)), waits=waits, inc=sc)
        w_sqj = sc.at()
        em.op("tensor", (lambda eng, j=j: eng.matmul(
            pb[1][0:1, :], G.ones_c[:], G.scr[j % 2][:],
            start=(j == 0), stop=(j == NJ - 1))), waits=[w_sqj], inc=mm)
        mm_sq[j] = mm.v
    w_sq_done = mm.at()

    def f(eng):
        for j in range(NJ):
            ii = eng.matmul(pb[0][0:1, :], G.ones_c[:], xv(bA, j),
                            start=(j == 0), stop=(j == NJ - 1))
        return ii
    em.op("tensor", f, waits=[w_x], inc=mm)
    w_stats = mm.at()

    def r1(off, n=TT):
        return A(G.pE1, off, [[1024, 1], [1, n]])

    def r2(off, n=TT):
        return A(G.pE2, off, [[1024, 1], [1, n]])

    def f(eng):
        eng.tensor_scalar(r1(0), pb[0][0:1, :], 1.0 / DM, None, ALU.mult)
        eng.tensor_scalar(r1(512), pb[1][0:1, :], 1.0 / DM, None, ALU.mult)
        eng.tensor_tensor(r2(0), r1(0), r1(0), ALU.mult)
        return eng.tensor_tensor(r2(512), r1(512), r2(0), ALU.subtract)
    em.op("vector", f, waits=[w_stats], inc=dv)

    def f(eng):
        eng.activation(r2(512), r2(512), AF.Ln, bias=G.cns[0:1, 0:1])
        return eng.activation(r2(512), r2(512), AF.Exp, scale=-0.5)
    em.op("scalar", f, waits=[dv.at()], inc=sc)

    em.op("vector", lambda eng: eng.scalar_tensor_tensor(
        r2(0), r1(0), -1.0, r2(512), ALU.mult, ALU.mult), waits=[sc.at()], inc=dv)

    def f(eng):
        eng.matmul(pb[2][:, :], G.ones_r[:], r2(512), start=True, stop=True)
        return eng.matmul(pb[3][:, :], G.ones_r[:], r2(0), start=True, stop=True)
    em.op("tensor", f, waits=[dv.at()], inc=mm)
    w_bc = mm.at()

    for j in range(NJ):
        def f(eng, j=j):
            eng.tensor_tensor(xv(bB, j), xv(bA, j), pb[2][:, :], ALU.mult)
            eng.tensor_tensor(xv(bB, j), xv(bB, j), pb[3][:, :], ALU.add)
            return eng.tensor_copy(A(G.bBb, j * TT, [[PITCH, 128], [1, TT]]), xv(bB, j))
        em.op("vector", f, waits=([w_bc, G.ss.at()] if j == 0 else []), inc=dv)
    w_xn = dv.at()

    for js in range(0, NJ, 2):
        em.op("gpsimd", (lambda eng, js=js: eng.dma_start(
            A(xn_d, t0 + js * TOK, [[NJ * TOK, 128], [TOK, 2], [1, TT]]),
            A(bB, js * TT, [[PITCH, 128], [TT, 2], [1, TT]]))),
            waits=[w_xn] if js == 0 else [], inc=G.xns, k=16)

    wplan = [(wq_d, G.bq_sb, bC, oc) for oc in range(8)] + \
            [(wk_d, G.bk_sb, bD, oc) for oc in range(8)]
    grp_done = {}
    ev_done = {}
    wc_done = {}
    prevc = getattr(G, "qk_wcp", None)
    for n, (w_d_, b_sb, dst, oc) in enumerate(wplan):
        bi = n % 2
        dwaits = ([(dv.h, wc_done[n - 2])] if n >= 2 else
                  ([(dv.h, prevc)] if prevc is not None else []))
        em.op("sync", (lambda eng, w_d_=w_d_, oc=oc, bi=bi: eng.dma_start(
            G.wbuf[bi][:], A(w_d_, 128 * oc, [[DM, 128], [128 * DM, NJ], [1, 128]]))),
            waits=dwaits, inc=wsem[bi], k=16)
        w_wld = wsem[bi].at()
        cwaits = [w_wld] + ([(mm.h, grp_done[n - 2])] if n >= 2 else [])
        em.op("vector", (lambda eng, bi=bi: eng.tensor_copy(
            G.wbufb[bi][:, :], G.wbuf[bi][:, :])), waits=cwaits, inc=dv)
        wc_done[n] = dv.v
        w_wc = dv.at()
        bank = pb[4 + bi]
        mwaits = [w_wc, w_xn] + ([(dv.h, ev_done[n - 2])] if n >= 2 else [])

        def f(eng, bi=bi, bank=bank):
            for kj in range(NJ):
                ii = eng.matmul(bank[:, :], A(G.wbufb[bi], kj * 128, [[NJ * 128, 128], [1, 128]]),
                                A(G.bBb, kj * TT, [[PITCH, 128], [1, TT]]),
                                start=(kj == 0), stop=(kj == NJ - 1))
            return ii
        em.op("tensor", f, waits=mwaits, inc=mm)
        grp_done[n] = mm.v
        em.op("vector", (lambda eng, bank=bank, b_sb=b_sb, dst=dst, oc=oc:
                         eng.tensor_scalar(xv(dst, oc), bank[:, :], b_sb[:, oc:oc + 1],
                                           None, ALU.add)),
              waits=[(mm.h, grp_done[n])], inc=dv)
        ev_done[n] = dv.v
    G.qk_wcp = wc_done[len(wplan) - 1]

    def f(eng):
        for j in range(NJ):
            ii = eng.matmul(pb[6][0:GH, :], A(G.wg1_sb, j * GH, [[NJ * GH, 128], [1, GH]]),
                            xv(bB, j), start=(j == 0), stop=(j == NJ - 1))
        return ii
    em.op("tensor", f, waits=[w_xn], inc=mm)
    em.op("scalar", lambda eng: eng.activation(
        G.g1_sb[:, :], pb[6][0:GH, :], AF.Silu, bias=G.bg1_sb[:, :]),
        waits=[mm.at()], inc=sc)
    w_g1 = sc.at()

    gl_marks = {}
    for j in range(NJ):
        waits = [w_g1] + ([(sc.h, gl_marks[j - 2])] if j >= 2 else [])
        em.op("tensor", (lambda eng, j=j: eng.matmul(
            pb[6 + (j % 2)][:, :], G.wg2_sb[0:GH, 128 * j:128 * (j + 1)], G.g1_sb[:, :],
            start=True, stop=True)), waits=waits, inc=mm)
        em.op("scalar", (lambda eng, j=j: eng.activation(
            xv(bE, j), pb[6 + (j % 2)][:, :], AF.Sigmoid, bias=G.bg2_sb[:, j:j + 1])),
            waits=[mm.at()], inc=sc)
        gl_marks[j] = sc.v
    for j in range(NJ):
        em.op("scalar", (lambda eng, j=j: eng.activation(
            xv(bE, j), xv(bE, j), AF.Ln, scale=0.9, bias=G.cns[:, 1:2])), inc=sc)
    w_L = sc.at()

    vmm = {}
    vev = {}
    vc_done = {}
    prevv = getattr(G, "v_wcp", None)
    for half in range(2):
        for kj in range(NJ):
            n = half * NJ + kj
            bi = n % 2
            dwaits = [w_sq_done] + ([(dv.h, vc_done[n - 2])] if n >= 2 else
                                    ([(dv.h, prevv)] if prevv is not None else []))
            em.op("sync", (lambda eng, kj=kj, half=half, bi=bi: eng.dma_start(
                G.scr[bi][:], A(wv_d, kj * 128 * DM + half * 512, [[DM, 128], [1, 512]]))),
                waits=dwaits, inc=vsem[bi], k=16)
            w_vld = vsem[bi].at()
            cwaits = [w_vld] + ([(mm.h, vmm[n - 2])] if n >= 2 else [])
            em.op("vector", (lambda eng, bi=bi: eng.tensor_copy(
                A(G.wbufb[bi], 0, [[NJ * 128, 128], [1, 512]]), G.scr[bi][:, :])),
                waits=cwaits, inc=dv)
            vc_done[n] = dv.v
            w_vc = dv.at()
            mwaits = [w_vc, w_xn]
            if kj == 0 and half == 1:
                mwaits.append((dv.h, vev[0]))

            def f(eng, kj=kj, bi=bi):
                for ts in range(4):
                    ii = eng.matmul(pb[ts][:, :],
                                    A(G.bBb, kj * TT + ts * 128, [[PITCH, 128], [1, 128]]),
                                    A(G.wbufb[bi], 0, [[NJ * 128, 128], [1, 512]]),
                                    start=(kj == 0), stop=(kj == NJ - 1))
                return ii
            em.op("tensor", f, waits=mwaits, inc=mm)
            vmm[n] = mm.v
        for ts in range(4):
            em.op("vector", (lambda eng, ts=ts, half=half: eng.tensor_tensor(
                A(bA, ts * 1024 + half * 512, [[PITCH, 128], [1, 512]]),
                pb[ts][:, :], G.bvbc[:, half * 512:(half + 1) * 512], ALU.add)),
                waits=[(mm.h, vmm[half * NJ + NJ - 1])], inc=dv)
        vev[half] = dv.v
    G.v_wcp = vc_done[2 * NJ - 1]

    for j in range(NJ):
        for c in range(CPT):
            em.op("vector", (lambda eng, j=j, c=c: eng.tensor_tensor_scan(
                A(bF, j * TT + c * CH, [[PITCH, 128], [1, CH]]),
                A(bE, j * TT + c * CH, [[PITCH, 128], [1, CH]]),
                A(bE, j * TT + c * CH, [[PITCH, 128], [1, CH]]),
                0.0, ALU.add, ALU.bypass)),
                waits=[w_L] if (j == 0 and c == 0) else [], inc=dv)

    def f(eng):
        eng.tensor_copy(G.msb[:, :], A(bF, 31, [[PITCH, 128], [TT, NJ], [CH, CPT]]))
        return eng.tensor_copy(G.tcb[:, :], A(bF, 63, [[PITCH, 128], [TT, NJ], [CH, CPT]]))
    em.op("vector", f, inc=dv)
    em.op("scalar", lambda eng: eng.activation(G.etb[:, :], G.tcb[:, :], AF.Exp),
          waits=[dv.at()], inc=sc)
    em.op("vector", lambda eng: eng.tensor_tensor(
        G.tcb[:, :], G.tcb[:, :], G.msb[:, :], ALU.subtract), waits=[sc.at()], inc=dv)
    em.op("scalar", lambda eng: eng.activation(G.tcm[:, :], G.tcb[:, :], AF.Exp),
          waits=[dv.at()], inc=sc)

    def f(eng):
        for j in range(NJ):
            ii = eng.tensor_tensor(xv(bE, j), xv(bF, j), xv(bE, j), ALU.subtract)
        return ii
    em.op("vector", f, inc=dv)

    em.op("vector", lambda eng: eng.memset(bB[:, :], 0.0),
          waits=[mm.at(), G.xns.at()], inc=dv)

    em.flush()


def _l1_tile_chunks(nc, G, t, s_d, qh_d, lent_d):
    import os
    sub = int(os.environ.get("L1_CHUNK_SUB", "99"))
    em = Em(nc)
    mm, dv, sc = G.mm, G.dv, G.sc
    lsem = [G.l0, G.l1]
    bA, bB, bC, bD, bE, bF = G.bA, G.bB, G.bC, G.bD, G.bE, G.bF
    pb = G.pb
    t0 = t * TT

    last_soft = {0: 0, 1: 0}
    last_at = 0
    last_sadd = 0
    last_ds = 0
    last_tcopy = 0
    prev_pair_mm = 0
    prev_ptr_mm = 0

    npair = int(os.environ.get("L1_PAIRS", "4"))
    for p in range(npair):
        c0, c1 = 2 * p, 2 * p + 1
        def f(eng, p=p):
            for j in range(NJ):
                eng.tensor_tensor(
                    A(G.pE1, j * 128, [[1024, 128], [1, 128]]),
                    A(bE, j * TT + p * 128, [[PITCH, 128], [1, 128]]),
                    A(G.msb, j * CPT + 2 * p, [[64, 128], [1, 2], [0, 64]]),
                    ALU.subtract)
                ii = eng.tensor_tensor(
                    A(G.pE2, j * 128, [[1024, 128], [1, 128]]),
                    A(bF, j * TT + p * 128, [[PITCH, 128], [1, 128]]),
                    A(G.msb, j * CPT + 2 * p, [[64, 128], [1, 2], [0, 64]]),
                    ALU.subtract)
            return ii
        em.op("vector", f, waits=[sc.at(), (mm.h, prev_pair_mm)], inc=dv)

        def f(eng):
            eng.activation(G.pE1[:, :], G.pE1[:, :], AF.Exp)
            return eng.activation(G.pE2[:, :], G.pE2[:, :], AF.Exp, scale=-1.0)
        em.op("scalar", f, waits=[dv.at()], inc=sc)
        w_e12 = sc.at()

        def f(eng, p=p):
            for j in range(NJ):
                eng.tensor_tensor(A(G.pQT, j * 128, [[1024, 128], [1, 128]]),
                                  A(bC, j * TT + p * 128, [[PITCH, 128], [1, 128]]),
                                  A(G.pE1, j * 128, [[1024, 128], [1, 128]]), ALU.mult)
                eng.tensor_tensor(A(G.pKT, j * 128, [[1024, 128], [1, 128]]),
                                  A(bD, j * TT + p * 128, [[PITCH, 128], [1, 128]]),
                                  A(G.pE2, j * 128, [[1024, 128], [1, 128]]), ALU.mult)
                ii = eng.tensor_tensor(
                    A(G.pKH, j * 128, [[1024, 128], [1, 128]]),
                    A(G.pKT, j * 128, [[1024, 128], [1, 128]]),
                    A(G.tcm, j * CPT + 2 * p, [[64, 128], [1, 2], [0, 64]]), ALU.mult)
            return ii
        em.op("vector", f, waits=[w_e12], inc=dv)
        w_qkt = dv.at()

        em.op("scalar", (lambda eng, p=p: eng.activation(
            G.pE1[:, :], A(bE, p * 128, [[PITCH, 128], [TT, NJ], [1, 128]]), AF.Exp)),
            waits=[w_qkt], inc=sc)

        def f(eng, p=p):
            for j in range(NJ):
                ii = eng.tensor_tensor(A(G.pQH, j * 128, [[1024, 128], [1, 128]]),
                                       A(bC, j * TT + p * 128, [[PITCH, 128], [1, 128]]),
                                       A(G.pE1, j * 128, [[1024, 128], [1, 128]]), ALU.mult)
            return ii
        em.op("vector", f, waits=[sc.at(), G.qs.at()], inc=dv)

        em.op("gpsimd", (lambda eng, p=p: eng.dma_start(
            A(qh_d, t0 + p * 128, [[NJ * TOK, 128], [TOK, NJ], [1, 128]]),
            A(G.pQH, 0, [[1024, 128], [128, NJ], [1, 128]]))),
            waits=[dv.at()], inc=G.qs, k=16)

        if sub < 2:
            continue
        for ci, c in enumerate((c0, c1)):
            bank = pb[ci]

            for par in (0, 1):
                def f(eng, c=c, bank=bank, par=par):
                    for h in range(par, H, 2):
                        ii = eng.matmul(
                            A(bank, (h % 2) * 64 * 512 + (h // 2) * 64, [[512, 64], [1, 64]]),
                            A(bC, (h % 2) * 64 * PITCH + (h // 2) * TT + c * 64, [[PITCH, 64], [1, 64]]),
                            A(bD, (h % 2) * 64 * PITCH + (h // 2) * TT + c * 64, [[PITCH, 64], [1, 64]]),
                            start=True, stop=True,
                            tile_position=(64 * (h % 2), 64 * (h % 2)))
                    return ii
                em.op("tensor", f, waits=[(dv.h, last_soft[ci])] if par == 0 else [], inc=mm)
            w_sc_mm = mm.at()

            em.op("vector", (lambda eng, bank=bank: eng.tensor_reduce(
                G.redc[:, 0:8], A(bank, 0, [[512, 128], [64, 8], [1, 64]]),
                AX.X, ALU.max)), waits=[w_sc_mm], inc=dv)
            em.op("vector", (lambda eng, bank=bank: eng.tensor_tensor(
                G.sgb[:, :], bank[:, :], G.cmask[:, :], ALU.add)), inc=dv)
            em.op("vector", (lambda eng: eng.tensor_tensor(
                G.sgb[:, :], G.sgb[:, :], A(G.redc, 0, [[64, 128], [1, 8], [0, 64]]),
                ALU.subtract)), inc=dv)
            em.op("scalar", lambda eng: eng.activation(G.sgb[:, :], G.sgb[:, :], AF.Exp,
                                                       scale=0.125),
                  waits=[dv.at()], inc=sc)

            em.op("vector", lambda eng: eng.tensor_reduce(
                G.redc[:, 8:16], A(G.sgb, 0, [[512, 128], [64, 8], [1, 64]]),
                AX.X, ALU.add), waits=[sc.at()], inc=dv)
            def f(eng):
                eng.activation(G.redc[:, 16:24], G.redc[:, 8:16], AF.Ln)
                return eng.activation(G.redc[:, 16:24], G.redc[:, 16:24], AF.Exp, scale=-1.0)
            em.op("scalar", f, waits=[dv.at()], inc=sc)
            em.op("vector", (lambda eng, ci=ci: eng.tensor_tensor(
                A(G.P_sb, ci * 64, [[1024, 128], [128, 8], [1, 64]]), G.sgb[:, :],
                A(G.redc, 16, [[64, 128], [1, 8], [0, 64]]), ALU.mult)),
                waits=[sc.at(), (mm.h, prev_ptr_mm)], inc=dv)
            last_soft[ci] = dv.v
        w_soft = dv.at()

        if sub < 3:
            continue
        for par in (0, 1):
            def f(eng, par=par):
                for h in range(par, H, 2):
                    ii = eng.transpose(
                        A(pb[2 + h % 2], (h // 2) * 64, [[512, 128], [1, 64]]),
                        A(G.P_sb, (h % 2) * 64 * 1024 + (h // 2) * 128, [[1024, 64], [1, 128]]),
                        G.id64[(h % 2) * 64:(h % 2) * 64 + 64, 0:64])
                return ii
            em.op("tensor", f, waits=[w_soft, (dv.h, last_tcopy)] if par == 0 else [], inc=mm)
        prev_ptr_mm = mm.v

        def f(eng):
            eng.tensor_copy(A(G.PT_sb, 0, [[1024, 128], [128, 8], [1, 64]]),
                            A(pb[2], 0, [[512, 128], [64, 8], [1, 64]]))
            return eng.tensor_copy(A(G.PT_sb, 64, [[1024, 128], [128, 8], [1, 64]]),
                                   A(pb[3], 0, [[512, 128], [64, 8], [1, 64]]))
        em.op("vector", f, waits=[mm.at()], inc=dv)
        w_ptc = dv.at()

        for par in (0, 1):
            def f(eng, par=par):
                for h in range(par, H, 2):
                    ii = eng.transpose(
                        A(pb[2 + h % 2], (h // 2) * 64, [[512, 128], [1, 64]]),
                        A(G.pKH, (h % 2) * 64 * 1024 + (h // 2) * 128, [[1024, 64], [1, 128]]),
                        G.id64[(h % 2) * 64:(h % 2) * 64 + 64, 0:64])
                return ii
            em.op("tensor", f, waits=[w_ptc] if par == 0 else [], inc=mm)

        def f(eng):
            eng.tensor_copy(A(G.KT_sb, 0, [[1024, 128], [128, 8], [1, 64]]),
                            A(pb[2], 0, [[512, 128], [64, 8], [1, 64]]))
            return eng.tensor_copy(A(G.KT_sb, 64, [[1024, 128], [128, 8], [1, 64]]),
                                   A(pb[3], 0, [[512, 128], [64, 8], [1, 64]]))
        em.op("vector", f, waits=[mm.at()], inc=dv)
        last_tcopy = dv.v

        if sub < 4:
            continue
        for ci, c in enumerate((c0, c1)):
            cb = 64 * (c % 2)
            n_glob = t * CPT + c

            for par in (0, 1):
                def f(eng, c=c, cb=cb, par=par):
                    for h in range(par, H, 2):
                        ii = eng.matmul(
                            A(pb[4 + h % 2], cb * 512 + (h // 2) * 64, [[512, 64], [1, 64]]),
                            A(G.pKT, (h % 2) * 64 * 1024 + (h // 2) * 128 + (c % 2) * 64,
                              [[1024, 64], [1, 64]]),
                            A(G.pQT, (h % 2) * 64 * 1024 + (h // 2) * 128 + (c % 2) * 64,
                              [[1024, 64], [1, 64]]),
                            start=True, stop=True,
                            tile_position=(64 * (h % 2), cb))
                    return ii
                em.op("tensor", f,
                      waits=[(dv.h, last_at), (dv.h, last_tcopy)] if par == 0 else [], inc=mm)
            w_atmm = mm.at()

            def f(eng, cb=cb):
                eng.tensor_tensor(A(G.At_sb, cb * 1024, [[1024, 64], [128, 8], [1, 64]]),
                                  A(pb[4], cb * 512, [[512, 64], [64, 8], [1, 64]]),
                                  A(G.smask, cb * 512, [[512, 64], [64, 8], [1, 64]]), ALU.mult)
                return eng.tensor_tensor(A(G.At_sb, cb * 1024 + 64, [[1024, 64], [128, 8], [1, 64]]),
                                         A(pb[5], cb * 512, [[512, 64], [64, 8], [1, 64]]),
                                         A(G.smask, cb * 512, [[512, 64], [64, 8], [1, 64]]), ALU.mult)
            em.op("vector", f, waits=[w_atmm], inc=dv)
            last_at = dv.v
            w_at = dv.at()

            if sub < 5:
                continue

            def f(eng, c=c, cb=cb):
                for h in range(H):
                    vsl = A(bA, cb * PITCH + (c // 2) * 1024 + h * 64, [[PITCH, 64], [1, 64]])
                    eng.matmul(
                        A(pb[6], (h % 2) * 64 * 512 + (h // 2) * 64, [[512, 64], [1, 64]]),
                        vsl,
                        A(G.PT_sb, cb * 1024 + h * 64, [[1024, 64], [1, 64]]),
                        start=True, stop=False, tile_position=(cb, 64 * (h % 2)))
                    ii = eng.matmul(
                        A(pb[6], (h % 2) * 64 * 512 + (h // 2) * 64, [[512, 64], [1, 64]]),
                        vsl,
                        A(G.At_sb, cb * 1024 + h * 64, [[1024, 64], [1, 64]]),
                        start=False, stop=True, tile_position=(cb, 64 * (h % 2)))
                return ii
            em.op("tensor", f, waits=[w_at, (dv.h, last_sadd)], inc=mm)
            w_pv = mm.at()

            if sub < 6:
                continue

            def f(eng, c=c, cb=cb):
                for h in range(H):
                    vsl = A(bA, cb * PITCH + (c // 2) * 1024 + h * 64, [[PITCH, 64], [1, 64]])
                    ii = eng.matmul(
                        A(pb[7], (h % 2) * 64 * 512 + (h // 2) * 64, [[512, 64], [1, 64]]),
                        A(G.KT_sb, cb * 1024 + h * 64, [[1024, 64], [1, 64]]),
                        vsl,
                        start=True, stop=True, tile_position=(cb, 64 * (h % 2)))
                return ii
            em.op("tensor", f, waits=[(dv.h, last_ds)], inc=mm)
            w_ds = mm.at()

            def f(eng, c=c):
                return eng.tensor_tensor(
                    A(bB, c * 64, [[PITCH, 128], [TT, NJ], [1, 64]]),
                    A(bB, c * 64, [[PITCH, 128], [TT, NJ], [1, 64]]),
                    A(pb[6], 0, [[512, 128], [64, NJ], [1, 64]]), ALU.add)
            em.op("vector", f, waits=[w_pv], inc=dv)
            last_sadd = dv.v

            def f(eng, ci=ci):
                return eng.tensor_copy(G.lstage[ci][:, :], G.S_all[:, :])
            em.op("vector", f, waits=[lsem[ci].at()], inc=dv)
            w_snap = dv.at()
            em.op("gpsimd", (lambda eng, ci=ci, n_glob=n_glob: eng.dma_start(
                A(lent_d, n_glob * 128 * 512, [[512, 128], [1, 512]]),
                G.lstage[ci][:, :])), waits=[w_snap], inc=lsem[ci], k=16)

            em.op("scalar", (lambda eng, n_glob=n_glob: eng.activation(
                A(G.dstage, n_glob * 8, [[NCH * 8, 128], [1, 8]]), G.tcum[:, :], AF.Exp)),
                waits=[dv.at()], inc=sc)
            em.op("vector", (lambda eng, c=c: eng.tensor_tensor(
                G.tcum[:, :], G.tcum[:, :],
                A(bF, c * 64 + 63, [[PITCH, 128], [TT, NJ]]), ALU.add)),
                waits=[sc.at()], inc=dv)

            def f(eng, c=c):
                eng.tensor_tensor(G.S_all[:, :], G.S_all[:, :],
                                  A(G.etb, c, [[64, 128], [8, 8], [0, 64]]), ALU.mult)
                return eng.tensor_tensor(G.S_all[:, :], G.S_all[:, :],
                                         A(pb[7], 0, [[512, 128], [64, 8], [1, 64]]), ALU.add)
            em.op("vector", f, waits=[w_snap, w_ds], inc=dv)
            last_ds = dv.v
        prev_pair_mm = mm.v

    for js in range(0, NJ, 2):
        em.op("gpsimd", (lambda eng, js=js: eng.dma_start(
            A(s_d, t0 + js * TOK, [[NJ * TOK, 128], [TOK, 2], [1, TT]]),
            A(bB, js * TT, [[PITCH, 128], [TT, 2], [1, TT]]))),
            waits=[dv.at()] if js == 0 else [], inc=G.ss, k=16)

    em.flush()


def _l1_finalize(nc, G, send_d, dcum_d):
    em = Em(nc)

    em.op("gpsimd", lambda eng: eng.dma_start(send_d.ap(), G.S_all[:, :]),
          waits=[G.dv.at(), G.sc.at()], inc=G.fin, k=16)
    em.op("gpsimd", lambda eng: eng.dma_start(
        A(dcum_d, 0, [[8, 128], [1024, NCH], [1, 8]]), G.dstage[:, :]), inc=G.fin, k=16)
    em.flush()


def _l1_tern_p1(nc, G, w_d, nrows, ncols, roff):
    dv = G.dv
    vsem = [G.v2, G.v3]
    ncc = ncols // 512
    ntile = (nrows // 128) * ncc
    red_marks = {}
    w_prev = (dv.h, dv.v)
    for n in range(ntile):
        pc, cc = n // ncc, n % ncc
        em = G.tern_em
        dwaits = [(dv.h, red_marks[n - 2])] if n >= 2 else [w_prev]
        em.op("sync", (lambda eng, pc=pc, cc=cc, bi=n % 2: eng.dma_start(
            G.tscr[bi][:], A(w_d, pc * 128 * ncols + cc * 512, [[ncols, 128], [1, 512]]))),
            waits=dwaits, inc=vsem[n % 2], k=16)
        em.op("vector", (lambda eng, n=n, bi=n % 2, roff=roff: eng.tensor_reduce(
            G.tred[:, roff + n:roff + n + 1], G.tscr[bi][:], AX.X, ALU.add,
            apply_absolute_value=True)),
            waits=[vsem[n % 2].at()], inc=dv)
        red_marks[n] = dv.v
        yield


def _l1_ternary(nc, G, w_d, t_out, m_out, nrows, ncols, roff):
    mm, dv, sc = G.mm, G.dv, G.sc
    vsem = [G.v0, G.v1, G.v2, G.v3]
    tsem = [G.t0, G.t1]
    npc = nrows // 128
    ncc = ncols // 512
    ntile = npc * ncc
    assert ntile <= 64

    em = Em(nc)
    em.op("vector", (lambda eng: eng.tensor_reduce(
        G.mcols[:, 0:1], G.tred[:, roff:roff + ntile], AX.X, ALU.add)),
        waits=[G.fin.at()], inc=dv)
    em.op("tensor", lambda eng: eng.matmul(
        G.pb[0][0:1, 0:1], G.mcols[:, 0:1], G.ones_c[:, :], start=True, stop=True),
        waits=[dv.at()], inc=mm)

    em.op("vector", lambda eng: eng.tensor_scalar(
        G.mrow[0:1, 0:1], G.pb[0][0:1, 0:1], 1.0 / (nrows * ncols), None, ALU.mult),
        waits=[mm.at()], inc=dv)

    def f(eng):
        eng.tensor_scalar(G.mrow[0:1, 0:1], G.mrow[0:1, 0:1], 1e-5, None, ALU.max)
        return eng.tensor_scalar(G.mrow[0:1, 1:2], G.mrow[0:1, 0:1], 0.5, None, ALU.mult)
    em.op("vector", f, waits=[dv.at()], inc=dv)
    em.op("tensor", lambda eng: eng.matmul(
        G.pb[1][:, 0:2], G.ones_r[:], G.mrow[0:1, 0:2], start=True, stop=True),
        waits=[dv.at()], inc=mm)
    em.op("vector", lambda eng: eng.tensor_copy(G.mcols[:, :], G.pb[1][:, 0:2]),
          waits=[mm.at()], inc=dv)
    w_m = dv.at()
    em.op("gpsimd", lambda eng: eng.dma_start(m_out.ap(), G.mcols[:, :]),
          waits=[w_m], inc=G.fin, k=16)
    em.flush()

    em = Em(nc)
    sgbs = [G.sgb, G.sgb2]
    thr_marks = {}
    w_prev = (dv.h, dv.v)
    for n in range(ntile):
        pc, cc = n // ncc, n % ncc
        bi = n % 4
        si = n % 2
        dwaits = [(dv.h, thr_marks[n - 4])] if n >= 4 else [w_prev]
        em.op("sync", (lambda eng, pc=pc, cc=cc, bi=bi: eng.dma_start(
            G.scr[bi][:], A(w_d, pc * 128 * ncols + cc * 512, [[ncols, 128], [1, 512]]))),
            waits=dwaits, inc=vsem[bi], k=16)
        w_wl = vsem[bi].at()
        swaits = [w_wl] + ([(dv.h, thr_marks[n - 2])] if n >= 2 else [])
        em.op("scalar", (lambda eng, bi=bi, si=si: eng.sign(
            sgbs[si][:, :], G.scr[bi][:])), waits=swaits, inc=sc)
        w_sg = sc.at()

        def f(eng, bi=bi, n=n, si=si):
            eng.tensor_tensor(G.scr[bi][:], G.scr[bi][:], sgbs[si][:, :], ALU.mult)
            eng.tensor_scalar(G.scr[bi][:], G.scr[bi][:], G.mcols[:, 1:2], None, ALU.is_ge)
            return eng.tensor_tensor(G.tqb[n % 2][:, :], sgbs[si][:, :], G.scr[bi][:],
                                     ALU.mult)
        em.op("vector", f, waits=[w_sg, tsem[n % 2].at()], inc=dv)
        thr_marks[n] = dv.v
        em.op("gpsimd", (lambda eng, n=n, pc=pc, cc=cc: eng.dma_start(
            A(t_out, pc * 128 * ncols + cc * 512, [[ncols, 128], [1, 512]]),
            G.tqb[n % 2][:, :])), waits=[(dv.h, thr_marks[n])], inc=tsem[n % 2], k=16)
    em.flush()




def _l2_dram(nc, kind_in="ExternalInput", kind_out="ExternalOutput", D=None):
    if D is None:
        D = NS()
        D.x = nc.dram_tensor("x", [128, NJ, TOK], F32, kind=kind_in)
        D.xn = nc.dram_tensor("xn", [128, NJ, TOK], F32, kind=kind_in)
        D.s = nc.dram_tensor("s", [128, NJ, TOK], F32, kind=kind_in)
        D.qh = nc.dram_tensor("qh", [128, NJ, TOK], F32, kind=kind_in)
        D.lent = nc.dram_tensor("lent", [NCH, 128, 512], F32, kind=kind_in)
        D.dcum = nc.dram_tensor("dcum", [NCH, 128, 8], F32, kind=kind_in)
        D.t1 = nc.dram_tensor("t1", [DM, HID], F32, kind=kind_in)
        D.t2 = nc.dram_tensor("t2", [HID, DM], F32, kind=kind_in)
        D.m1 = nc.dram_tensor("m1", [128, 2], F32, kind=kind_in)
        D.m2 = nc.dram_tensor("m2", [128, 2], F32, kind=kind_in)
        D.sp = nc.dram_tensor("spart", [128, 512], F32, kind=kind_in)
    D.l1g = nc.dram_tensor("l1g", [128, NJ], F32, kind="ExternalInput")
    D.l1b = nc.dram_tensor("l1b", [128, NJ], F32, kind="ExternalInput")
    D.b1 = nc.dram_tensor("b1", [128, 32], F32, kind="ExternalInput")
    D.l2g = nc.dram_tensor("l2g", [128, 32], F32, kind="ExternalInput")
    D.l2b = nc.dram_tensor("l2b", [128, 32], F32, kind="ExternalInput")
    D.b2 = nc.dram_tensor("b2", [128, NJ], F32, kind="ExternalInput")
    D.wk1 = nc.dram_tensor("wk1", [DM, DM], F32, kind="ExternalInput")
    D.wv1 = nc.dram_tensor("wv1", [DM, DM], F32, kind="ExternalInput")
    D.x1 = nc.dram_tensor("x1", [128, NJ, TOK], F32, kind=kind_out)
    D.xn1 = nc.dram_tensor("xn1", [128, NJ, TOK], F32, kind=kind_out)
    D.k1s = [nc.dram_tensor(f"k1_{t}", [128, NJ, TT], F32, kind=kind_out)
             for t in range(NT)]
    D.v1s = [nc.dram_tensor(f"v1_{t}", [128, 4, DM], F32, kind=kind_out)
             for t in range(NT)]
    return D


def _l2_emit(nc, st, sems, D):
    E = st.enter_context
    g = {}
    for nm in ("bS", "bQH", "bXN", "bX", "bG"):
        g[nm] = E(nc.sbuf_tensor(nm, [128, PITCH], F32))
    g["bH"] = E(nc.sbuf_tensor("bH", [128, 32 * 512], F32))
    g["lh_hi"] = E(nc.sbuf_tensor("lh_hi", [128, PITCH], BF16))
    g["lh_lo"] = E(nc.sbuf_tensor("lh_lo", [128, PITCH], BF16))
    g["hh"] = [E(nc.sbuf_tensor(f"hh{i}", [128, 512], BF16)) for i in range(2)]
    g["hl"] = [E(nc.sbuf_tensor(f"hl{i}", [128, 512], BF16)) for i in range(2)]
    g["t1b"] = [E(nc.sbuf_tensor(f"t1b{i}", [128, NJ * 128], BF16)) for i in range(2)]
    g["t2b"] = [E(nc.sbuf_tensor(f"t2b{i}", [128, NJ * 128], BF16)) for i in range(2)]
    g["wbuf"] = [E(nc.sbuf_tensor(f"wbuf{i}", [128, NJ * 128], F32)) for i in range(2)]
    g["scr"] = [E(nc.sbuf_tensor(f"scr{i}", [128, 512], F32)) for i in range(4)]
    g["lbuf"] = [E(nc.sbuf_tensor(f"lbuf{i}", [128, 512], F32)) for i in range(2)]
    g["sentb"] = E(nc.sbuf_tensor("sentb", [128, 512], F32))
    g["spart"] = E(nc.sbuf_tensor("spartb", [128, 512], F32))
    g["dcum"] = E(nc.sbuf_tensor("dcumb", [128, NCH * 8], F32))
    g["l1g"] = E(nc.sbuf_tensor("l1gb", [128, NJ], F32))
    g["l1b"] = E(nc.sbuf_tensor("l1bb", [128, NJ], F32))
    g["b1"] = E(nc.sbuf_tensor("b1b", [128, 32], F32))
    g["l2g"] = E(nc.sbuf_tensor("l2gb", [128, 32], F32))
    g["l2b"] = E(nc.sbuf_tensor("l2bb", [128, 32], F32))
    g["b2"] = E(nc.sbuf_tensor("b2b", [128, NJ], F32))
    g["m1"] = E(nc.sbuf_tensor("m1b", [128, 2], F32))
    g["m2"] = E(nc.sbuf_tensor("m2b", [128, 2], F32))
    g["ones_c"] = E(nc.sbuf_tensor("onesc", [128, 1], F32))
    g["ones_r"] = E(nc.sbuf_tensor("onesr", [1, 128], F32))
    g["cns"] = E(nc.sbuf_tensor("cns", [128, 2], F32))
    g["flagb"] = E(nc.sbuf_tensor("flagb", [128, 1], F32))
    g["pb"] = [E(nc.psum_tensor(f"pb{i}", [128, 512], F32)) for i in range(8)]
    g.update(sems)
    G = type("G", (), g)

    em = Em(nc)
    loads = [
        (G.dcum[:], A(D.dcum, 0, [[8, 128], [1024, NCH], [1, 8]])),
        (G.l1g[:], D.l1g.ap()), (G.l1b[:], D.l1b.ap()), (G.b1[:], D.b1.ap()),
        (G.l2g[:], D.l2g.ap()), (G.l2b[:], D.l2b.ap()), (G.b2[:], D.b2.ap()),
        (G.m1[:], D.m1.ap()), (G.m2[:], D.m2.ap()),
    ]
    spw = []
    if getattr(D, "sp_gate", None) is not None:
        spw = [D.sp_gate]
    em.op("sync", (lambda eng: eng.dma_start(G.spart[:], D.sp_ap)), waits=spw,
          inc=G.c, k=16)
    for dst, src_ in loads:
        em.op("sync", (lambda eng, a=dst, b=src_: eng.dma_start(a, b)), inc=G.c, k=16)
    if getattr(D, "flag", None) is not None:
        em.op("sync", (lambda eng: eng.dma_start(G.flagb[:], D.flag.ap())), inc=G.c, k=16)

    def f(eng):
        eng.memset(G.ones_c[:], 1.0)
        eng.memset(G.ones_r[:], 1.0)
        return eng.memset(G.cns[:, 0:1], float(LNEPS))
    em.op("vector", f, inc=G.dv)
    if getattr(D, "flag", None) is not None:
        em.op("vector", lambda eng: eng.tensor_scalar(
            G.spart[:, :], G.spart[:, :], G.flagb[:, 0:1], None, ALU.mult),
            waits=[G.c.at()], inc=G.dv)
    em.flush()

    cc2 = getattr(G, "cc2", None)
    import os
    cc_defer = bool(int(os.environ.get("KCC_DEFER", "0")))
    pend = []
    for t in range(NT):
        with nc.named_scope(f"l2t{t}"):
            _l2_tile(nc, G, t, D.x, D.xn, D.s, D.qh, D.lent, D.t1, D.t2,
                     D.wk1, D.wv1, D.x1, D.xn1, D.k1s[t], D.v1s[t], D)
        if getattr(D, "kgs", None) is not None:
            w_st = (G.sst.h, G.sst.v)
            pairs = [(D.k1s[t], D.kgs[t], cc2), (D.v1s[t], D.vgs[t], cc2)]
            if t == NT - 1:
                pairs.append((D.xn1col, D.bcg, G.cc1))
            if cc_defer:
                pend.append((w_st, pairs))
                continue
            em = Em(nc)
            for a, b, cs in pairs:
                em.op("gpsimd", (lambda eng, a=a, b=b: eng.collective_compute(
                    "AllGather", ALU.bypass, replica_groups=RG_PAIRS,
                    ins=[a.ap().opt()], outs=[b.ap().opt()])),
                    waits=[w_st], inc=cs, k=1)
            em.flush()
    if pend:
        em = Em(nc)
        w_fin = (G.sst.h, G.sst.v)
        for _, pairs in pend:
            for a, b, cs in pairs:
                em.op("gpsimd", (lambda eng, a=a, b=b: eng.collective_compute(
                    "AllGather", ALU.bypass, replica_groups=RG_PAIRS,
                    ins=[a.ap().opt()], outs=[b.ap().opt()])),
                    waits=[w_fin], inc=cs, k=1)
        em.flush()
    return G


def build_launch2():
    nc = bass.Bass("TRN2", target_bir_lowering=False)
    D = _l2_dram(nc)
    D.sp_ap = D.sp.ap()
    D.flag = None
    with contextlib.ExitStack() as st:
        sems = {nm: Sem(st, nc, nm) for nm in L2_SEMS}
        SER_SEMS.clear()
        SER_SEMS.extend([sems["mm"], sems["dv"], sems["sc"]])
        with contextlib.ExitStack() as st_b:
            G = _l2_emit(nc, st_b, sems, D)
        em = Em(nc)
        em.op("gpsimd", lambda eng: eng.nop(),
              waits=[G.sst.at(), G.l0.at(), G.l1.at()])
        em.flush()
    return nc


def _l2_ln_stats(nc, em, G, src_ap, j, nch, gate, st):
    mm, sc = G.mm, G.sc
    pb = G.pb
    waits = list(gate)
    if j >= 2:
        waits.append((mm.h, st[j - 2]))
    em.op("scalar", (lambda eng: eng.activation(
        G.scr[j % 2][:], src_ap, AF.Square)), waits=waits, inc=sc)

    def f(eng, j=j, nch=nch, src_ap=src_ap):
        eng.matmul(pb[0][0:1, :], G.ones_c[:], src_ap,
                   start=(j == 0), stop=(j == nch - 1))
        return eng.matmul(pb[1][0:1, :], G.ones_c[:], G.scr[j % 2][:],
                          start=(j == 0), stop=(j == nch - 1))
    em.op("tensor", f, waits=[sc.at()], inc=mm)
    st[j] = mm.v


def _l2_ln(nc, em, G, srcv, nch, gate_waits):
    st = {}
    for j in range(nch):
        _l2_ln_stats(nc, em, G, srcv(j), j, nch,
                     (list(gate_waits) if j == 0 else []), st)
    return _l2_ln_fin(nc, em, G, nch)


def _l2_ln_fin(nc, em, G, nch):
    mm, dv, sc = G.mm, G.dv, G.sc
    pb = G.pb
    inv = 1.0 / (nch * 128)

    def row(off, n=512):
        buf = (G.scr[2], G.scr[3], G.sentb)[off // 512]
        return A(buf, 0, [[512, 1], [1, n]])

    def f(eng, inv=inv):
        eng.tensor_scalar(row(0), pb[0][0:1, :], inv, None, ALU.mult)
        eng.tensor_scalar(row(512), pb[1][0:1, :], inv, None, ALU.mult)
        eng.tensor_tensor(row(1024), row(0), row(0), ALU.mult)
        return eng.tensor_tensor(row(512), row(512), row(1024), ALU.subtract)
    em.op("vector", f, waits=[mm.at()], inc=dv)

    def f(eng):
        eng.activation(row(512), row(512), AF.Ln, bias=G.cns[0:1, 0:1])
        return eng.activation(row(512), row(512), AF.Exp, scale=-0.5)
    em.op("scalar", f, waits=[dv.at()], inc=sc)

    em.op("vector", lambda eng: eng.scalar_tensor_tensor(
        row(1024), row(0), -1.0, row(512), ALU.mult, ALU.mult),
        waits=[sc.at()], inc=dv)

    def f(eng):
        eng.matmul(pb[2][:, :], G.ones_r[:], row(512), start=True, stop=True)
        return eng.matmul(pb[3][:, :], G.ones_r[:], row(1024), start=True, stop=True)
    em.op("tensor", f, waits=[dv.at()], inc=mm)
    return mm.at()


def _l2_tile(nc, G, t, x_d, xn_d, s_d, qh_d, lent_d, t1_d, t2_d,
             wk1_d, wv1_d, x1_d, xn1_d, k1_d, v1_d, D=None):
    mm, dv, sc = G.mm, G.dv, G.sc
    pb = G.pb
    t0 = t * TT
    lsem = [G.l0, G.l1]
    tsem = [G.t0, G.t1]
    wsem = [G.w0, G.w1]
    vsem = [G.v0, G.v1]

    def xv(buf, j, lo=0, n=TT):
        return A(buf, j * TT + lo, [[PITCH, 128], [1, n]])

    def hv(oc, lo=0, n=512):
        return A(G.bH, oc * 512 + lo, [[32 * 512, 128], [1, n]])

    em = Em(nc)
    for js in range(0, NJ, 2):
        em.op("sync", (lambda eng, js=js: eng.dma_start(
            A(G.bS, js * TT, [[PITCH, 128], [TT, 2], [1, TT]]),
            A(s_d, t0 + js * TOK, [[NJ * TOK, 128], [TOK, 2], [1, TT]]))),
            waits=[G.c.at(), G.sst.at()] if js == 0 else [], inc=G.c, k=16)
    for js in range(0, NJ, 2):
        em.op("sync", (lambda eng, js=js: eng.dma_start(
            A(G.bQH, js * TT, [[PITCH, 128], [TT, 2], [1, TT]]),
            A(qh_d, t0 + js * TOK, [[NJ * TOK, 128], [TOK, 2], [1, TT]]))),
            waits=[G.sst.at()] if js == 0 else [], inc=G.c, k=16)
    w_sq = G.c.at()

    em.flush()

    em = Em(nc)
    for js in range(0, NJ, 2):
        em.op("sync", (lambda eng, js=js: eng.dma_start(
            A(G.bXN, js * TT, [[PITCH, 128], [TT, 2], [1, TT]]),
            A(xn_d, t0 + js * TOK, [[NJ * TOK, 128], [TOK, 2], [1, TT]]))),
            inc=G.c, k=16)
    w_xn = G.c.at()

    w_bc = _l2_ln(nc, em, G, lambda j: xv(G.bXN, j), NJ, [w_xn])

    for j in range(NJ):
        def f(eng, j=j):
            eng.tensor_tensor(xv(G.bX, j), xv(G.bXN, j), pb[2][:, :], ALU.mult)
            eng.tensor_tensor(xv(G.bX, j), xv(G.bX, j), pb[3][:, :], ALU.add)
            return eng.tensor_scalar(xv(G.bX, j), xv(G.bX, j),
                                     G.l1g[:, j:j + 1], G.l1b[:, j:j + 1],
                                     ALU.mult, ALU.add)
        em.op("vector", f, waits=[w_bc] if j == 0 else [], inc=dv)

        def f(eng, j=j):
            return eng.tensor_copy(A(G.lh_hi, j * TT, [[PITCH, 128], [1, TT]]),
                                   xv(G.bX, j))
        em.op("vector", f, inc=dv)
    w_lh = dv.at()

    up_ev = {}
    ln2_st = {}
    for oc in range(32):
        bi = oc % 2
        dwaits = [(mm.h, up_ev.get(oc - 2, (0, 0))[0])] if oc >= 2 else []
        em.op("gpsimd", (lambda eng, oc=oc, bi=bi: eng.dma_start(
            G.t1b[bi][:], A(t1_d, 128 * oc, [[HID, 128], [128 * HID, NJ], [1, 128]]))),
            waits=dwaits, inc=tsem[bi], k=16)
        w_t = tsem[bi].at()
        bank = pb[4 + bi]
        mwaits = [w_t] + ([w_lh] if oc < 2 else [])
        if oc >= 2:
            mwaits.append((sc.h, up_ev[oc - 2][1]))

        def f(eng, bi=bi, bank=bank):
            for kj in range(NJ):
                ii = eng.matmul(bank[:, :], A(G.t1b[bi], kj * 128, [[NJ * 128, 128], [1, 128]]),
                                A(G.lh_hi, kj * TT, [[PITCH, 128], [1, TT]]),
                                start=(kj == 0), stop=(kj == NJ - 1))
            return ii
        em.op("tensor", f, waits=mwaits, inc=mm)
        w_up = mm.v
        if oc >= 1:
            _l2_ln_stats(nc, em, G, hv(oc - 1), oc - 1, 32, [], ln2_st)
        em.op("scalar", (lambda eng, oc=oc, bank=bank: eng.activation(
            hv(oc), bank[:, :], AF.Gelu, bias=G.b1[:, oc:oc + 1], scale=G.m1[:, 0:1])),
            waits=[(mm.h, w_up)], inc=sc)
        up_ev[oc] = (w_up, sc.v)
    _l2_ln_stats(nc, em, G, hv(31), 31, 32, [], ln2_st)
    em.flush()

    em = Em(nc)
    last_sadd = 0
    sent_marks = {}
    for c in range(CPT):
        n_glob = t * CPT + c
        ci = c % 2
        em.op("sync", (lambda eng, ci=ci, n_glob=n_glob: eng.dma_start(
            G.lbuf[ci][:], A(lent_d, n_glob * 128 * 512, [[512, 128], [1, 512]]))),
            waits=[(G.dv.h, sent_marks[c - 2])] if c >= 2 else [], inc=lsem[ci], k=16)
        w_l = lsem[ci].at()

        def f(eng, n_glob=n_glob, ci=ci):
            eng.tensor_tensor(G.sentb[:, :], G.spart[:, :],
                              A(G.dcum, n_glob * 8, [[NCH * 8, 128], [1, 8], [0, 64]]),
                              ALU.mult)
            return eng.tensor_tensor(G.sentb[:, :], G.sentb[:, :], G.lbuf[ci][:, :],
                                     ALU.add)
        em.op("vector", f, waits=[w_l, (mm.h, mm.v)], inc=dv)
        w_sent = dv.at()
        sent_marks[c] = dv.v
        for par in (0, 1):
            def f(eng, c=c, par=par):
                for h in range(par, H, 2):
                    ii = eng.matmul(
                        A(pb[6], (h % 2) * 64 * 512 + (h // 2) * 64, [[512, 64], [1, 64]]),
                        A(G.sentb, (h % 2) * 64 * 512 + (h // 2) * 64, [[512, 64], [1, 64]]),
                        A(G.bQH, (h % 2) * 64 * PITCH + (h // 2) * TT + c * 64,
                          [[PITCH, 64], [1, 64]]),
                        start=True, stop=True,
                        tile_position=(64 * (h % 2), 64 * (h % 2)))
                return ii
            em.op("tensor", f,
                  waits=([w_sent, w_sq, (dv.h, last_sadd)] if par == 0 else []), inc=mm)
        em.op("vector", (lambda eng, c=c: eng.tensor_tensor(
            A(G.bS, c * 64, [[PITCH, 128], [TT, NJ], [1, 64]]),
            A(G.bS, c * 64, [[PITCH, 128], [TT, NJ], [1, 64]]),
            A(pb[6], 0, [[512, 128], [64, NJ], [1, 64]]), ALU.add)),
            waits=[mm.at()], inc=dv)
        last_sadd = dv.v
    G.w_pa = (dv.h, dv.v)
    em.flush()




    em = Em(nc)
    w_bc2 = _l2_ln_fin(nc, em, G, 32)

    for oc in range(32):
        def f(eng, oc=oc):
            eng.tensor_tensor(hv(oc), hv(oc), pb[2][:, :], ALU.mult)
            eng.tensor_tensor(hv(oc), hv(oc), pb[3][:, :], ALU.add)
            return eng.tensor_scalar(hv(oc), hv(oc), G.l2g[:, oc:oc + 1],
                                     G.l2b[:, oc:oc + 1], ALU.mult, ALU.add)
        em.op("vector", f, waits=[w_bc2] if oc == 0 else [], inc=dv)
    w_n2 = dv.at()

    dn_marks = {}
    for oc in range(32):
        bi = oc % 2
        swaits = [w_n2] if oc < 2 else [(mm.h, dn_marks[oc - 2])]

        def f(eng, oc=oc, bi=bi):
            return eng.tensor_copy(G.hh[bi][:, :], hv(oc))
        em.op("vector", f, waits=swaits, inc=dv)
        w_split = dv.at()
        em.op("gpsimd", (lambda eng, oc=oc, bi=bi: eng.dma_start(
            G.t2b[bi][:], A(t2_d, 128 * oc * DM, [[DM, 128], [1, NJ * 128]]))),
            waits=[(mm.h, dn_marks.get(oc - 2, 0))], inc=tsem[bi], k=16)
        w_t2 = tsem[bi].at()

        def f(eng, oc=oc, bi=bi):
            for oj in range(NJ):
                ii = eng.matmul(pb[oj][:, :], A(G.t2b[bi], oj * 128, [[NJ * 128, 128], [1, 128]]),
                                G.hh[bi][:, :], start=(oc == 0), stop=(oc == 31))
            return ii
        em.op("tensor", f,
              waits=[w_split, w_t2] + ([G.w_pa] if oc == 0 else []), inc=mm)
        dn_marks[oc] = mm.v
    w_dn = mm.at()

    for oj in range(NJ):
        em.op("scalar", (lambda eng, oj=oj: eng.activation(
            xv(G.bG, oj), pb[oj][:, :], AF.Silu, bias=G.b2[:, oj:oj + 1],
            scale=G.m2[:, 0:1])), waits=[w_dn] if oj == 0 else [], inc=sc)
    em.flush()

    em = Em(nc)
    for js in range(0, NJ, 2):
        em.op("sync", (lambda eng, js=js: eng.dma_start(
            A(G.bX, js * TT, [[PITCH, 128], [TT, 2], [1, TT]]),
            A(x_d, t0 + js * TOK, [[NJ * TOK, 128], [TOK, 2], [1, TT]]))),
            waits=[G.sst.at()] if js == 0 else [], inc=G.c, k=16)
    w_x = G.c.at()

    ln3_st = {}
    cmb = {}
    for j in range(NJ):
        def f(eng, j=j):
            eng.tensor_tensor(xv(G.bS, j), xv(G.bS, j), xv(G.bG, j), ALU.mult)
            return eng.tensor_tensor(xv(G.bX, j), xv(G.bX, j), xv(G.bS, j), ALU.add)
        em.op("vector", f, waits=[w_x, sc.at(), G.w_pa] if j == 0 else [], inc=dv)
        cmb[j] = dv.v
        if j >= 1:
            _l2_ln_stats(nc, em, G, xv(G.bX, j - 1), j - 1, NJ,
                         [(dv.h, cmb[j - 1])], ln3_st)
    _l2_ln_stats(nc, em, G, xv(G.bX, NJ - 1), NJ - 1, NJ,
                 [(dv.h, cmb[NJ - 1])], ln3_st)
    w_x1 = dv.at()
    for js in range(0, NJ, 2):
        em.op("gpsimd", (lambda eng, js=js: eng.dma_start(
            A(x1_d, t0 + js * TOK, [[NJ * TOK, 128], [TOK, 2], [1, TT]]),
            A(G.bX, js * TT, [[PITCH, 128], [TT, 2], [1, TT]]))),
            waits=[w_x1] if js == 0 else [], inc=G.sst, k=16)

    w_bc3 = _l2_ln_fin(nc, em, G, NJ)
    mmD = (mm.h, mm.v)
    for j in range(NJ):
        def f(eng, j=j):
            eng.tensor_tensor(xv(G.bG, j), xv(G.bX, j), pb[2][:, :], ALU.mult)
            return eng.tensor_tensor(xv(G.bG, j), xv(G.bG, j), pb[3][:, :], ALU.add)
        em.op("vector", f, waits=[w_bc3] if j == 0 else [], inc=dv)
    w_xn1 = dv.at()
    for j in range(NJ):
        em.op("vector", (lambda eng, j=j: eng.tensor_copy(
            A(G.lh_hi, j * TT, [[PITCH, 128], [1, TT]]), xv(G.bG, j))),
            waits=[w_xn1, mmD] if j == 0 else [], inc=dv)
    w_xn1b = dv.at()
    for js in range(0, NJ, 2):
        em.op("gpsimd", (lambda eng, js=js: eng.dma_start(
            A(xn1_d, t0 + js * TOK, [[NJ * TOK, 128], [TOK, 2], [1, TT]]),
            A(G.bG, js * TT, [[PITCH, 128], [TT, 2], [1, TT]]))),
            waits=[w_xn1] if js == 0 else [], inc=G.sst, k=16)
    if t == NT - 1 and D is not None and getattr(D, "xn1col", None) is not None:
        em.op("vector", lambda eng: eng.tensor_copy(
            G.sentb[:, 0:NJ], A(G.bG, TT - 1, [[PITCH, 128], [TT, NJ]])),
            waits=[w_xn1], inc=dv)
        em.op("vector", lambda eng: eng.memset(G.sentb[:, NJ:16], 0.0), inc=dv)
        em.op("vector", lambda eng: eng.memset(G.sentb[:, 16:32], 0.0), inc=dv)
        em.op("gpsimd", lambda eng: eng.dma_start(
            D.xn1col.ap(), G.sentb[:, 0:NJ]), waits=[dv.at()], inc=G.sst, k=16)

    grp = {}
    ev = {}
    kwc = {}
    prevc = getattr(G, "l2_wcp", None)
    for oc in range(NJ):
        bi = oc % 2
        dwaits = ([(dv.h, kwc[oc - 2])] if oc >= 2 else
                  ([(dv.h, prevc)] if prevc is not None else []))
        em.op("sync", (lambda eng, oc=oc, bi=bi: eng.dma_start(
            G.wbuf[bi][:], A(wk1_d, 128 * oc, [[DM, 128], [128 * DM, NJ], [1, 128]]))),
            waits=dwaits, inc=wsem[bi], k=16)
        w_w = wsem[bi].at()
        cwaits = [w_w] + ([(mm.h, grp[oc - 2])] if oc >= 2 else [mmD])
        em.op("vector", (lambda eng, bi=bi: eng.tensor_copy(
            G.t1b[bi][:, :], G.wbuf[bi][:, :])), waits=cwaits, inc=dv)
        kwc[oc] = dv.v
        w_wc = dv.at()
        bank = pb[4 + bi]
        mwaits = [w_wc] + ([w_xn1b] if oc < 2 else []) + (
            [(dv.h, ev[oc - 2])] if oc >= 2 else [])

        def f(eng, bi=bi, bank=bank):
            for kj in range(NJ):
                ii = eng.matmul(bank[:, :], A(G.t1b[bi], kj * 128, [[NJ * 128, 128], [1, 128]]),
                                A(G.lh_hi, kj * TT, [[PITCH, 128], [1, TT]]),
                                start=(kj == 0), stop=(kj == NJ - 1))
            return ii
        em.op("tensor", f, waits=mwaits, inc=mm)
        grp[oc] = mm.v
        em.op("vector", (lambda eng, oc=oc, bank=bank: eng.tensor_copy(
            xv(G.bS, oc), bank[:, :])), waits=[mm.at()], inc=dv)
        ev[oc] = dv.v
    G.l2_wcp = kwc[NJ - 1]
    for js in range(0, NJ, 2):
        em.op("gpsimd", (lambda eng, js=js: eng.dma_start(
            A(k1_d, js * TT, [[NJ * TT, 128], [TT, 2], [1, TT]]),
            A(G.bS, js * TT, [[PITCH, 128], [TT, 2], [1, TT]]))),
            waits=[dv.at()] if js == 0 else [], inc=G.sst, k=16)

    vmm = {}
    vev = {}
    vwc = {}
    for half in range(2):
        for kj in range(NJ):
            n = half * NJ + kj
            bi = n % 2
            dwaits = [(dv.h, vwc[n - 2])] if n >= 2 else [(dv.h, dv.v)]
            em.op("sync", (lambda eng, kj=kj, half=half, bi=bi: eng.dma_start(
                G.scr[bi][:], A(wv1_d, kj * 128 * DM + half * 512, [[DM, 128], [1, 512]]))),
                waits=dwaits, inc=vsem[bi], k=16)
            w_v = vsem[bi].at()
            cwaits = [w_v] + ([(mm.h, vmm[n - 2])] if n >= 2 else [mmD])
            em.op("vector", (lambda eng, bi=bi: eng.tensor_copy(
                G.hh[bi][:, :], G.scr[bi][:, :])), waits=cwaits, inc=dv)
            vwc[n] = dv.v
            w_vc = dv.at()
            mwaits = [w_vc] + ([w_xn1b] if n < 2 else [])
            if kj == 0 and half == 0:
                mwaits.append((dv.h, dv.v))
            if kj == 0 and half == 1:
                mwaits.append((dv.h, vev[0]))

            def f(eng, kj=kj, bi=bi):
                for ts in range(4):
                    ii = eng.matmul(pb[ts][:, :],
                                    A(G.lh_hi, kj * TT + ts * 128, [[PITCH, 128], [1, 128]]),
                                    G.hh[bi][:, :], start=(kj == 0), stop=(kj == NJ - 1))
                return ii
            em.op("tensor", f, waits=mwaits, inc=mm)
            vmm[n] = mm.v
        for ts in range(4):
            em.op("vector", (lambda eng, ts=ts, half=half: eng.tensor_copy(
                A(G.bQH, ts * 1024 + half * 512, [[PITCH, 128], [1, 512]]),
                pb[ts][:, :])), waits=[(mm.h, vmm[half * NJ + NJ - 1])], inc=dv)
        vev[half] = dv.v
    for js in range(4):
        em.op("gpsimd", (lambda eng, js=js: eng.dma_start(
            A(v1_d, js * DM, [[4 * DM, 128], [1, DM]]),
            A(G.bQH, js * DM, [[PITCH, 128], [1, DM]]))),
            waits=[dv.at()] if js == 0 else [], inc=G.sst, k=16)

    em.flush()


def l2_inputs(l1out, ins, spart):
    return {
        "x": l1out["__x"], "xn": l1out["xn"], "s": l1out["s"], "qh": l1out["qh"],
        "lent": l1out["lent"], "dcum": l1out["dcum"], "spart": spart,
        "t1": l1out["t1"], "t2": l1out["t2"], "m1": l1out["m1"], "m2": l1out["m2"],
        "l1g": _vec128(ins["L0_ffn_ln1_g"]), "l1b": _vec128(ins["L0_ffn_ln1_b"]),
        "b1": _vec128(ins["L0_ffn_b1"]), "l2g": _vec128(ins["L0_ffn_ln2_g"]),
        "l2b": _vec128(ins["L0_ffn_ln2_b"]), "b2": _vec128(ins["L0_ffn_b2"]),
        "wk1": np.ascontiguousarray(ins["L1_attn_Wk"].transpose(1, 0, 2).reshape(DM, DM)),
        "wv1": np.ascontiguousarray(ins["L1_attn_Wv"].transpose(1, 0, 2).reshape(DM, DM)),
    }




KTOT = 2 * TOK


def _l3_dram(nc, kind_in="ExternalInput", kind_out="ExternalOutput", D=None):
    import os
    if D is None:
        D = NS()
        D.x1 = nc.dram_tensor("x1", [128, NJ, TOK], F32, kind=kind_in)
        D.xn1 = nc.dram_tensor("xn1", [128, NJ, TOK], F32, kind=kind_in)
        D.kf = nc.dram_tensor("kf", [128, NJ, KTOT], F32, kind=kind_in)
        D.vf = nc.dram_tensor("vf", [128, 32, DM], F32, kind=kind_in)
        D.bcol = nc.dram_tensor("bcol", [128, NJ], F32, kind=kind_in)
    D.wq1 = nc.dram_tensor("wq1", [DM, DM], F32, kind="ExternalInput")
    D.wr = nc.dram_tensor("wr", [DM, DM], F32, kind="ExternalInput")
    D.wkr = nc.dram_tensor("wkr", [DM, DM], F32, kind="ExternalInput")
    D.wvr = nc.dram_tensor("wvr", [DM, DM], F32, kind="ExternalInput")
    D.mur = nc.dram_tensor("mur", [128, NJ], F32, kind="ExternalInput")
    D.muk = nc.dram_tensor("muk", [128, NJ], F32, kind="ExternalInput")
    D.id128 = nc.dram_tensor("id128", [128, 128], F32, kind="ExternalInput")
    D.out = nc.dram_tensor("out", [128, NJ, TOK], F32, kind=kind_out)
    D.s1 = nc.dram_tensor("s1scratch", [128, NJ, TOK], F32)
    dbg = bool(int(os.environ.get("L3_DBG", "0")))
    D.dbg = nc.dram_tensor("dbg", [128, TOK], F32, kind="ExternalOutput") if dbg else None
    D.dbg2 = (nc.dram_tensor("dbg2", [128, NJ, TOK], F32, kind="ExternalOutput")
              if dbg else None)
    return D


def _l3_emit(nc, st, sems, D):
    E = st.enter_context
    s1_dram = D.s1
    g = {}
    g["bXN"] = E(nc.sbuf_tensor("bXN", [128, NJ * TOK], F32))
    g["id128"] = E(nc.sbuf_tensor("id128s", [128, 128], F32))
    g["wbuf"] = [E(nc.sbuf_tensor(f"wbuf{i}", [128, NJ * 128], F32)) for i in range(2)]
    g["mur"] = E(nc.sbuf_tensor("murb", [128, NJ], F32))
    g["muk"] = E(nc.sbuf_tensor("mukb", [128, NJ], F32))
    g["bcol"] = E(nc.sbuf_tensor("bcolb", [128, NJ], F32))
    g["flagb"] = E(nc.sbuf_tensor("flagb", [128, 1], F32))
    g["pb"] = [E(nc.psum_tensor(f"pb{i}", [128, 512], F32)) for i in range(8)]
    g.update(sems)

    with contextlib.ExitStack() as st_a:
        Ea = st_a.enter_context
        g["Kb"] = [Ea(nc.sbuf_tensor(f"Kb{i}", [128, KTOT], F32)) for i in range(2)]
        g["Kbr"] = Ea(nc.sbuf_tensor("Kbr", [128, KTOT], F32R))
        g["Qbr"] = Ea(nc.sbuf_tensor("Qbr", [128, TOK], F32R))
        g["Ve"] = Ea(nc.sbuf_tensor("Ve", [128, 32 * 65], F32))
        g["Qb"] = Ea(nc.sbuf_tensor("Qb", [128, TOK], F32))
        g["Qe"] = Ea(nc.sbuf_tensor("Qe", [65, TOK], F32))
        g["Ke"] = Ea(nc.sbuf_tensor("Ke", [65, KTOT], F32))
        g["Pt"] = Ea(nc.sbuf_tensor("Pt", [128, 1024], F32))
        g["msb"] = Ea(nc.sbuf_tensor("msb", [128, 16], F32))
        g["m16"] = Ea(nc.sbuf_tensor("m16", [16, 128], F32))
        g["mc"] = Ea(nc.sbuf_tensor("mc", [128, 128], F32))
        g["lb"] = Ea(nc.sbuf_tensor("lb", [128, 16], F32))
        g["rl"] = Ea(nc.sbuf_tensor("rl", [128, 16], F32))
        g["rlr"] = Ea(nc.sbuf_tensor("rlr", [1, 512], F32))
        g["rlb"] = Ea(nc.sbuf_tensor("rlb", [64, 512], F32))
        g["stage"] = [Ea(nc.sbuf_tensor(f"stg{i}", [64, 512], F32)) for i in range(2)]
        g["ones_r"] = Ea(nc.sbuf_tensor("onesr3", [1, 64], F32))
        g["pvs"] = {"evac": 0, "rlb": 0, "st": {}, "idx": 0}
        g["dbg_d"] = D.dbg
        G = type("G", (), g)

        em = Em(nc)
        for dst, src_ in (
            (G.id128[:], D.id128.ap()), (G.mur[:], D.mur.ap()),
            (G.muk[:], D.muk.ap()),
        ):
            em.op("sync", (lambda eng, a=dst, b=src_: eng.dma_start(a, b)), inc=G.c, k=16)
        if getattr(D, "bcol", None) is not None:
            em.op("sync", lambda eng: eng.dma_start(G.bcol[:], D.bcol.ap()),
                  inc=G.x, k=16)
        else:
            em.op("sync", lambda eng: eng.dma_start(
                G.bcol[:], A(D.bcg, 0, [[NJ, 128], [1, NJ]])),
                waits=[D.bc_gate], inc=G.x, k=16)
            em.op("sync", lambda eng: eng.dma_start(G.flagb[:], D.flag.ap()),
                  inc=G.x, k=16)
            em.op("vector", lambda eng: eng.tensor_scalar(
                G.bcol[:, :], G.bcol[:, :], G.flagb[:, 0:1], None, ALU.mult),
                waits=[G.x.at()], inc=G.dv)
        for js in range(NJ):
            em.op("sync", (lambda eng, js=js: eng.dma_start(
                A(G.bXN, js * TOK, [[NJ * TOK, 128], [1, TOK]]),
                A(D.xn1, js * TOK, [[NJ * TOK, 128], [1, TOK]]))), inc=G.c, k=16)
        em.op("vector", lambda eng: eng.memset(G.Ve[:], 1.0), inc=G.dv)
        em.op("vector", lambda eng: eng.memset(G.Ke[64:65, :], 1.0), inc=G.dv)
        em.op("vector", lambda eng: eng.memset(G.ones_r[:, :], 1.0), inc=G.dv)
        em.flush()

        pair_done = {}
        for jp in range(8):
            with nc.named_scope(f"hp{jp}"):
                _l3_headpair(nc, G, jp, D, s1_dram, pair_done)

    if D.dbg2 is not None:
        em = Em(nc)
        em.op("gpsimd", lambda eng: eng.dma_start(D.dbg2.ap(), s1_dram.ap()),
              waits=[g["s1st"].at()], inc=g["ost"], k=16)
        em.flush()

    with contextlib.ExitStack() as st_b:
        Eb = st_b.enter_context
        for nm in ("bR", "bG", "bX1"):
            g[nm] = Eb(nc.sbuf_tensor(nm, [128, PITCH], F32))
        for nm in ("bMX", "bKA"):
            g[nm] = Eb(nc.sbuf_tensor(nm, [128, PITCH], BF16))
        g["bS1"] = [Eb(nc.sbuf_tensor(f"bS1{i}", [128, PITCH], F32)) for i in range(2)]
        g["wbufb"] = [Eb(nc.sbuf_tensor(f"wbufb3{i}", [128, NJ * 128], BF16))
                      for i in range(2)]
        g["rw_cmb"] = {}
        g["rw_wcp"] = None
        g["rw_mmp"] = None
        G = type("G", (), g)

        for t in range(NT):
            with nc.named_scope(f"rw{t}"):
                _l3_rwkv_tile(nc, G, t, D.x1, D.wr, D.wkr, D.wvr, s1_dram, D.out)

        em = Em(nc)
        em.op("gpsimd", lambda eng: eng.nop(), waits=[G.ost.at()])
        em.flush()
    return type("G", (), g)


def build_launch3():
    nc = bass.Bass("TRN2", target_bir_lowering=False)
    D = _l3_dram(nc)
    with contextlib.ExitStack() as st:
        sems = {nm: Sem(st, nc, nm) for nm in L3_SEMS}
        SER_SEMS.clear()
        SER_SEMS.extend([sems["mm"], sems["dv"], sems["sc"]])
        with contextlib.ExitStack() as st_b:
            _l3_emit(nc, st_b, sems, D)
    return nc


def _l3_headpair(nc, G, jp, D, s1_dram, pair_done):
    mm, dv, sc = G.mm, G.dv, G.sc
    pb = G.pb
    kbsem = [G.kb0, G.kb1]
    vbsem = [G.vb0, G.vb1]
    wsem = [G.w0, G.w1]
    bi = jp % 2

    em = Em(nc)
    dguard = [(mm.h, pair_done[jp - 2])] if jp >= 2 else [(G.c.h, G.c.v)]
    gate = [D.kv_gate] if (getattr(D, "kf", None) is None and jp < 2) else []
    if getattr(D, "kf", None) is not None:
        for js in range(4):
            em.op("sync", (lambda eng, jp=jp, bi=bi, js=js: eng.dma_start(
                A(G.Kb[bi], js * (KTOT // 4), [[KTOT, 128], [1, KTOT // 4]]),
                A(D.kf, jp * KTOT + js * (KTOT // 4),
                  [[NJ * KTOT, 128], [1, KTOT // 4]]))),
                waits=dguard if js == 0 else [], inc=kbsem[bi], k=16)
        w_kb = kbsem[bi].at()
    else:
        for sl in range(2):
            for t in range(NT):
                em.op("sync", (lambda eng, jp=jp, bi=bi, sl=sl, t=t: eng.dma_start(
                    A(G.Kb[bi], sl * TOK + t * TT, [[KTOT, 128], [1, TT]]),
                    A(D.kgs[t], sl * 128 * NJ * TT + jp * TT,
                      [[NJ * TT, 128], [1, TT]]))),
                    waits=(dguard + gate) if (sl == 0 and t == 0) else [],
                    inc=kbsem[bi], k=16)
        w_kb = kbsem[bi].at()
    em.op("sync", (lambda eng, jp=jp, bi=bi: eng.dma_start(
        G.wbuf[bi][:], A(D.wq1, 128 * jp, [[DM, 128], [128 * DM, NJ], [1, 128]]))),
        waits=list(dguard), inc=wsem[bi], k=16)
    w_w = wsem[bi].at()
    em.op("vector", (lambda eng, bi=bi: eng.tensor_copy(G.Kbr[:, :], G.Kb[bi][:, :])),
          waits=[w_kb] + ([(mm.h, pair_done[jp - 1])] if jp >= 1 else []), inc=dv)
    w_kr = dv.at()

    prevg = [(mm.h, pair_done[jp - 1])] if jp >= 1 else []
    qev = {}
    for qc in range(4):
        def f(eng, qc=qc, bi=bi):
            for kj in range(NJ):
                ii = eng.matmul(pb[6 + qc % 2][:, :],
                                A(G.wbuf[bi], kj * 128, [[NJ * 128, 128], [1, 128]]),
                                A(G.bXN, kj * TOK + qc * 512, [[NJ * TOK, 128], [1, 512]]),
                                start=(kj == 0), stop=(kj == NJ - 1))
            return ii
        waits = [w_w] + list(prevg) + ([(dv.h, qev[qc - 2])] if qc >= 2 else [])
        em.op("tensor", f, waits=waits, inc=mm)
        em.op("vector", (lambda eng, qc=qc: eng.tensor_copy(
            A(G.Qb, qc * 512, [[TOK, 128], [1, 512]]), pb[6 + qc % 2][:, :])),
            waits=[mm.at()] + list(prevg), inc=dv)
        qev[qc] = dv.v
    w_q = dv.at()
    em.op("vector", lambda eng: eng.tensor_copy(G.Qbr[:, :], G.Qb[:, :]), inc=dv)
    w_qr = dv.at()

    def emit_extract(hh):
        em.op("gpsimd", (lambda eng, hh=hh: eng.dma_start(
            G.Qe[0:64, :], A(G.Qb, hh * 64 * TOK, [[TOK, 64], [1, TOK]]))),
            waits=[w_q, (mm.h, mm.v)], inc=G.qe, k=16)
        em.op("gpsimd", (lambda eng, hh=hh, bi=bi: eng.dma_start(
            G.Ke[0:64, :], A(G.Kb[bi], hh * 64 * KTOT, [[KTOT, 64], [1, KTOT]]))),
            waits=[w_kb, (mm.h, mm.v)], inc=G.ke, k=16)
        if getattr(D, "vf", None) is not None:
            em.op("gpsimd", (lambda eng, jp=jp, hh=hh: eng.dma_start(
                A(G.Ve, 0, [[32 * 65, 128], [65, 32], [1, 64]]),
                A(D.vf, jp * 128 + hh * 64, [[32 * DM, 128], [DM, 32], [1, 64]]))),
                waits=[(mm.h, mm.v)], inc=G.vb0, k=16)
        else:
            for sl in range(2):
                for t in range(NT):
                    em.op("gpsimd", (lambda eng, jp=jp, hh=hh, sl=sl, t=t: eng.dma_start(
                        A(G.Ve, (sl * 16 + 4 * t) * 65, [[32 * 65, 128], [65, 4], [1, 64]]),
                        A(D.vgs[t], sl * 128 * 4 * DM + jp * 128 + hh * 64,
                          [[4 * DM, 128], [DM, 4], [1, 64]]))),
                        waits=([(mm.h, mm.v)] + gate) if (sl == 0 and t == 0) else [],
                        inc=G.vb0, k=16)
        return G.qe.at(), G.ke.at(), G.vb0.at()

    def gen_passA(hh):
        red_marks = {}
        for qt in range(16):
            for kt in range(8):
                n = qt * 8 + kt
                bk = pb[n % 4]
                if n == 0:
                    waits = [w_qr, w_kr, (dv.h, dv.v)]
                elif n >= 4:
                    waits = [(dv.h, red_marks[n - 4])]
                else:
                    waits = []
                em.op("tensor", (lambda eng, qt=qt, kt=kt, bk=bk, hh=hh: eng.matmul(
                    bk[:, :],
                    A(G.Qbr, hh * 64 * TOK + qt * 128, [[TOK, 64], [1, 128]]),
                    A(G.Kbr, hh * 64 * KTOT + kt * 512, [[KTOT, 64], [1, 512]]),
                    start=True, stop=True)),
                    waits=waits, inc=mm)
                em.op("vector", (lambda eng, n=n, bk=bk: eng.tensor_reduce(
                    G.mc[:, n:n + 1], bk[:, :], AX.X, ALU.max)),
                    waits=[mm.at()], inc=dv)
                red_marks[n] = dv.v
                yield

    def finalize_passA(hh, h):
        for qt in range(16):
            em.op("vector", (lambda eng, qt=qt: eng.tensor_reduce(
                G.msb[:, qt:qt + 1], G.mc[:, qt * 8:qt * 8 + 8], AX.X, ALU.max)),
                inc=dv)
        em.op("vector", lambda eng: eng.memset(G.lb[:, :], 0.0), inc=dv)
        em.op("vector", lambda eng: eng.memset(G.rl[:, :], 0.0), inc=dv)
        em.op("vector", lambda eng: eng.tensor_scalar(
            G.msb[:, :], G.msb[:, :], -1.0, None, ALU.mult), inc=dv)
        w_ma = dv.at()

        em.op("tensor", lambda eng: eng.transpose(
            pb[2][0:16, 0:128], G.msb[:, :], G.id128[:, :]),
            waits=[w_ma, (sc.h, sc.v)], inc=mm)
        em.op("vector", lambda eng: eng.tensor_copy(G.m16[:, :], pb[2][0:16, 0:128]),
              waits=[mm.at()], inc=dv)
        em.op("vector", lambda eng: eng.memset(G.mc[:, :], 0.0), inc=dv)
        em.op("vector", lambda eng: eng.memset(G.lb[:, :], 0.0), inc=dv)
        em.op("gpsimd", lambda eng: eng.dma_start(
            A(G.Qe, 64 * TOK, [[TOK, 1], [128, 16], [1, 128]]),
            A(G.m16, 0, [[128, 16], [1, 128]])), waits=[dv.at()], inc=G.me, k=16)
        w_m = G.me.at()
        if getattr(G, "dbg_d", None) is not None:
            em.op("gpsimd", (lambda eng, h=h: eng.dma_start(
                A(G.dbg_d, h * TOK, [[TOK, 1], [1, TOK]]),
                A(G.Qe, 64 * TOK, [[TOK, 1], [1, TOK]]))),
                waits=[w_m], inc=G.me, k=16)
        return w_m

    def emit_passB(hh, w_m, w_qe, w_ke, w_ve, stepper=None):
        pvs = G.pvs
        for qq in range(4):
            exp_marks = {}
            pv_marks = {}

            def emit_pv(kt, qq=qq):
                def f(eng, kt=kt):
                    return eng.matmul(
                        pb[4][0:65, :],
                        A(G.Ve, kt * 65, [[32 * 65, 128], [1, 65]]),
                        A(G.Pt, (kt % 2) * 512, [[1024, 128], [1, 512]]),
                        start=(kt == 0), stop=(kt == 31))
                pvw = [(sc.h, exp_marks[kt])]
                if kt == 0:
                    pvw.append(w_ve)
                    if pvs["evac"]:
                        pvw.append((dv.h, pvs["evac"]))
                em.op("tensor", f, waits=pvw, inc=mm)
                pv_marks[kt] = mm.v

            for kt in range(32):
                sb = pb[2 + kt % 2]
                swaits = [w_m, w_ke, w_qe]
                if kt >= 2:
                    swaits.append((sc.h, exp_marks[kt - 2]))
                em.op("tensor", (lambda eng, qq=qq, kt=kt, sb=sb: eng.matmul(
                    sb[:, :],
                    A(G.Ke, kt * 128, [[KTOT, 65], [1, 128]]),
                    A(G.Qe, qq * 512, [[TOK, 65], [1, 512]]),
                    start=True, stop=True)),
                    waits=swaits if kt < 2 else swaits[-1:], inc=mm)
                em.op("scalar", (lambda eng, kt=kt, sb=sb: eng.activation(
                    A(G.Pt, (kt % 2) * 512, [[1024, 128], [1, 512]]), sb[:, :], AF.Exp)),
                    waits=[mm.at()] + ([(mm.h, pv_marks[kt - 2])] if kt >= 2 else []),
                    inc=sc)
                exp_marks[kt] = sc.v
                if stepper is not None:
                    stepper()
                if kt >= 1:
                    emit_pv(kt - 1)
            emit_pv(31)
            w_pv = mm.at()
            def f(eng):
                eng.activation(G.rlr[0:1, :], pb[4][64:65, :], AF.Ln, scale=8.0)
                return eng.activation(G.rlr[0:1, :], G.rlr[0:1, :], AF.Exp, scale=-1.0)
            em.op("scalar", f, waits=[w_pv], inc=sc)
            w_rl = sc.at()
            em.op("tensor", (lambda eng: eng.matmul(
                pb[5][0:64, :], G.ones_r[:, :], G.rlr[0:1, :], start=True, stop=True)),
                waits=[w_rl] + ([(dv.h, pvs["rlb"])] if pvs["rlb"] else []), inc=mm)
            em.op("vector", (lambda eng: eng.tensor_copy(G.rlb[:, :], pb[5][0:64, :])),
                  waits=[mm.at()], inc=dv)
            pvs["rlb"] = dv.v
            sg = G.stage[pvs["idx"] % 2]
            tw = [(dv.h, pvs["rlb"])]
            prev_st = pvs["st"].get(pvs["idx"] - 2)
            if prev_st:
                tw.append((G.s1st.h, prev_st))
            em.op("vector", (lambda eng, sg=sg: eng.tensor_tensor(
                sg[:, :], pb[4][0:64, :], G.rlb[:, :], ALU.mult)), waits=tw, inc=dv)
            pvs["evac"] = dv.v
            em.op("gpsimd", (lambda eng, jp=jp, hh=hh, qq=qq, sg=sg: eng.dma_start(
                A(s1_dram, hh * 64 * NJ * TOK + jp * TOK + qq * 512,
                  [[NJ * TOK, 64], [1, 512]]), sg[:, :])),
                waits=[(dv.h, pvs["evac"])], inc=G.s1st, k=16)
            pvs["st"][pvs["idx"]] = G.s1st.v
            pvs["idx"] += 1

    w_qe, w_ke, w_ve = emit_extract(0)
    for _ in gen_passA(0):
        pass
    w_m = finalize_passA(0, 2 * jp)
    emit_passB(0, w_m, w_qe, w_ke, w_ve)
    w_qe, w_ke, w_ve = emit_extract(1)
    for _ in gen_passA(1):
        pass
    w_m = finalize_passA(1, 2 * jp + 1)
    emit_passB(1, w_m, w_qe, w_ke, w_ve)

    pair_done[jp] = mm.v
    pair_done[("dv", jp)] = dv.v
    em.flush()


def _l3_rwkv_tile(nc, G, t, x1_d, wr_d, wkr_d, wvr_d, s1_dram, out_d):
    mm, dv, sc = G.mm, G.dv, G.sc
    pb = G.pb
    wsem = [G.w0, G.w1]
    t0 = t * TT

    def xnv(j, lo=0, n=TT):
        return A(G.bXN, j * TOK + t0 + lo, [[NJ * TOK, 128], [1, n]])

    def xv(buf, j, lo=0, n=TT):
        return A(buf, j * TT + lo, [[PITCH, 128], [1, n]])

    em = Em(nc)

    def emit_mix(mu_sb, dst, gate):
        for j in range(NJ):
            if t == 0:
                def f(eng, j=j, mu_sb=mu_sb, dst=dst):
                    eng.tensor_tensor(xv(G.bMX, j, 1, TT - 1), xnv(j, 1, TT - 1),
                                      xnv(j, 0, TT - 1), ALU.subtract)
                    eng.scalar_tensor_tensor(xv(dst, j, 1, TT - 1),
                                             xv(G.bMX, j, 1, TT - 1), mu_sb[:, j:j + 1],
                                             xnv(j, 0, TT - 1), ALU.mult, ALU.add)
                    eng.tensor_tensor(xv(G.bMX, j, 0, 1), xnv(j, 0, 1),
                                      G.bcol[:, j:j + 1], ALU.subtract)
                    return eng.scalar_tensor_tensor(xv(dst, j, 0, 1), xv(G.bMX, j, 0, 1),
                                                    mu_sb[:, j:j + 1], G.bcol[:, j:j + 1],
                                                    ALU.mult, ALU.add)
            else:
                def f(eng, j=j, mu_sb=mu_sb, dst=dst):
                    eng.tensor_tensor(xv(G.bMX, j), xnv(j), xnv(j, -1), ALU.subtract)
                    return eng.scalar_tensor_tensor(xv(dst, j), xv(G.bMX, j),
                                                    mu_sb[:, j:j + 1], xnv(j, -1),
                                                    ALU.mult, ALU.add)
            em.op("vector", f, waits=list(gate) if j == 0 else [], inc=dv)
        return dv.at()

    w_xr = emit_mix(G.mur, G.bKA, [(mm.h, mm.v)])

    def proj(w_d, rhsbuf, evac, gate, bankbase=4):
        grp = {}
        evm = {}
        wc = {}
        for oc in range(NJ):
            bi2 = oc % 2
            dwaits = ([(dv.h, wc[oc - 2])] if oc >= 2 else
                      ([(dv.h, G.rw_wcp)] if G.rw_wcp is not None else []) + list(gate))
            em.op("sync", (lambda eng, oc=oc, bi2=bi2, w_d=w_d: eng.dma_start(
                G.wbuf[bi2][:], A(w_d, 128 * oc, [[DM, 128], [128 * DM, NJ], [1, 128]]))),
                waits=dwaits, inc=wsem[bi2], k=16)
            w_wl = wsem[bi2].at()
            cwaits = [w_wl] + ([(mm.h, grp[oc - 2])] if oc >= 2 else
                               ([(mm.h, G.rw_mmp)] if G.rw_mmp is not None else []))
            em.op("vector", (lambda eng, bi2=bi2: eng.tensor_copy(
                G.wbufb[bi2][:, :], G.wbuf[bi2][:, :])), waits=cwaits, inc=dv)
            wc[oc] = dv.v
            w_wc = dv.at()
            bank = pb[bankbase + bi2]
            mwaits = [w_wc] + list(gate) + ([(sc.h, evm[oc - 2])] if oc >= 2 else [])

            def f(eng, bi2=bi2, bank=bank, rhsbuf=rhsbuf):
                for kj in range(NJ):
                    ii = eng.matmul(bank[:, :],
                                    A(G.wbufb[bi2], kj * 128, [[NJ * 128, 128], [1, 128]]),
                                    xv(rhsbuf, kj), start=(kj == 0), stop=(kj == NJ - 1))
                return ii
            em.op("tensor", f, waits=mwaits, inc=mm)
            grp[oc] = mm.v
            evac(oc, bank)
            evm[oc] = sc.v
        G.rw_wcp = wc[NJ - 1]
        G.rw_mmp = grp[NJ - 1]
        return mm.at()

    def ev_r(oc, bank):
        em.op("scalar", (lambda eng, oc=oc, bank=bank: eng.activation(
            xv(G.bR, oc), bank[:, :], AF.Sigmoid)), waits=[(mm.h, mm.v)], inc=sc)
    w_rproj = proj(wr_d, G.bKA, ev_r, [w_xr])

    w_xk = emit_mix(G.muk, G.bMX, [(mm.h, w_rproj[1])])

    def ev_k(oc, bank):
        em.op("vector", (lambda eng, oc=oc, bank=bank: eng.tensor_scalar(
            xv(G.bKA, oc), bank[:, :], 0.0, None, ALU.max)), waits=[(mm.h, mm.v)], inc=dv)
        em.op("scalar", (lambda eng, oc=oc: eng.activation(
            xv(G.bKA, oc), xv(G.bKA, oc), AF.Square)), waits=[dv.at()], inc=sc)
    w_kproj = proj(wkr_d, G.bMX, ev_k, [w_xk])

    def ev_v(oc, bank):
        em.op("vector", (lambda eng, oc=oc, bank=bank: eng.tensor_tensor(
            xv(G.bG, oc), bank[:, :], xv(G.bR, oc), ALU.mult)), waits=[(mm.h, mm.v)], inc=dv)
        em.op("scalar", (lambda eng, oc=oc: eng.activation(
            xv(G.bG, oc), xv(G.bG, oc), AF.Silu)), waits=[dv.at()], inc=sc)
    w_vproj = proj(wvr_d, G.bKA, ev_v, [w_kproj, (sc.h, sc.v)])

    bS1 = G.bS1[t % 2]
    s1w = [G.s1st.at()]
    if t >= 2:
        s1w.append((dv.h, G.rw_cmb[t - 2]))
    for js in range(0, NJ, 2):
        em.op("sync", (lambda eng, t=t, js=js, bS1=bS1: eng.dma_start(
            A(bS1, js * TT, [[PITCH, 128], [TT, 2], [1, TT]]),
            A(s1_dram, t0 + js * TOK, [[NJ * TOK, 128], [TOK, 2], [1, TT]]))),
            waits=s1w if js == 0 else [], inc=G.s1ld, k=16)
    w_s1 = G.s1ld.at()

    for js in range(0, NJ, 2):
        em.op("sync", (lambda eng, t=t, js=js: eng.dma_start(
            A(G.bX1, js * TT, [[PITCH, 128], [TT, 2], [1, TT]]),
            A(x1_d, t0 + js * TOK, [[NJ * TOK, 128], [TOK, 2], [1, TT]]))),
            waits=[G.ost.at()] if js == 0 else [], inc=G.x1ld, k=16)
    w_x1 = G.x1ld.at()
    for j in range(NJ):
        def f(eng, j=j, bS1=bS1):
            eng.tensor_tensor(xv(bS1, j), xv(bS1, j), xv(G.bG, j), ALU.mult)
            return eng.tensor_tensor(xv(G.bX1, j), xv(G.bX1, j), xv(bS1, j), ALU.add)
        em.op("vector", f, waits=[w_x1, w_s1, (sc.h, sc.v)] if j == 0 else [], inc=dv)
    G.rw_cmb[t] = dv.v
    for js in range(0, NJ, 2):
        em.op("gpsimd", (lambda eng, t=t, js=js: eng.dma_start(
            A(out_d, t0 + js * TOK, [[NJ * TOK, 128], [TOK, 2], [1, TT]]),
            A(G.bX1, js * TT, [[PITCH, 128], [TT, 2], [1, TT]]))),
            waits=[dv.at()] if js == 0 else [], inc=G.ost, k=16)

    em.flush()


def l3_inputs(l2outA, l2outB, ins, half, x1, xn1, bcol):
    kf = np.empty((128, NJ, KTOT), np.float32)
    vf = np.empty((128, 32, DM), np.float32)
    for sl, r in enumerate((l2outA, l2outB)):
        for t in range(NT):
            kf[:, :, sl * TOK + t * TT:sl * TOK + (t + 1) * TT] = r[f"k1_{t}"]
            vf[:, sl * 16 + 4 * t:sl * 16 + 4 * t + 4, :] = r[f"v1_{t}"]
    return {
        "x1": x1, "xn1": xn1, "kf": kf, "vf": vf,
        "wq1": np.ascontiguousarray(ins["L1_attn_Wq"].transpose(1, 0, 2).reshape(DM, DM)),
        "wr": np.ascontiguousarray(ins["L1_ffn_W_r"].T),
        "wkr": np.ascontiguousarray(ins["L1_ffn_W_k"].T),
        "wvr": np.ascontiguousarray(ins["L1_ffn_W_v"].T),
        "mur": _vec128(ins["L1_ffn_mu_r"]), "muk": _vec128(ins["L1_ffn_mu_k"]),
        "bcol": bcol, "id128": np.eye(128, dtype=np.float32),
    }



def _fm(a):
    T = a.shape[0]
    return np.ascontiguousarray(a.T.reshape(NJ, 128, T).transpose(1, 0, 2))


def _from_fm(a):
    return np.ascontiguousarray(a.transpose(1, 0, 2).reshape(DM, -1).T)


def _vec128(v):
    return np.ascontiguousarray(v.reshape(-1, 128).T)


def _build_consts():
    eye64t = np.tile(np.eye(64, dtype=np.float32), (2, 2)).astype(np.float32)
    tril = np.tril(np.ones((64, 64), np.float32))
    cmask = np.tile(np.where(tril > 0, 0.0, 8.0 * NEG).astype(np.float32), (2, 8))
    strict = (np.arange(64)[:, None] < np.arange(64)[None, :]).astype(np.float32)
    smask = np.tile(strict, (2, 8))
    return eye64t, np.ascontiguousarray(cmask), np.ascontiguousarray(smask)


def l1_inputs(x_half, ins):
    eye64t, cmask, smask = _build_consts()
    return {
        "x": _fm(x_half),
        "wq": ins["L0_attn_Wq"], "wk": ins["L0_attn_Wk"], "wv": ins["L0_attn_Wv"],
        "bq": _vec128(ins["L0_attn_bq"]), "bk": _vec128(ins["L0_attn_bk"]),
        "bv": ins["L0_attn_bv"].reshape(1, DM),
        "wg1": ins["L0_attn_Wg1"], "bg1": ins["L0_attn_bg1"].reshape(GH, 1),
        "wg2": ins["L0_attn_Wg2"], "bg2": _vec128(ins["L0_attn_bg2"]),
        "w1t": np.ascontiguousarray(ins["L0_ffn_w1"].T),
        "w2t": np.ascontiguousarray(ins["L0_ffn_w2"].T),
        "id64": eye64t, "cmask": cmask, "smask": smask,
    }


RG_PAIRS = [[0, 1], [2, 3], [4, 5], [6, 7]]


def _uniq_names(nc):
    sb0, ps0 = nc.sbuf_tensor, nc.psum_tensor
    ctr = [0]

    def sb(name, *a, **k):
        ctr[0] += 1
        return sb0(f"{name}_u{ctr[0]}", *a, **k)

    def ps(name, *a, **k):
        ctr[0] += 1
        return ps0(f"{name}_u{ctr[0]}", *a, **k)
    nc.sbuf_tensor, nc.psum_tensor = sb, ps


def _join(em, jn, terms):
    base = jn.v
    for e in Em.ENG:
        em.op(e, lambda eng: eng.nop(), waits=(list(terms) if e == "sync" else []),
              inc=jn, k=1)
    for e in Em.ENG:
        em.op(e, lambda eng: eng.nop(), waits=[(jn.h, base + 5)])


def build_fused():
    nc = bass.Bass("TRN2", target_bir_lowering=False)
    _uniq_names(nc)
    D = _l1_dram(nc, kind_out="Internal", t_dt=BF16)
    D.flag = nc.dram_tensor("flag", [128, 1], F32, kind="ExternalInput")
    D.sendg = nc.dram_tensor("sendg", [2, 128, 512], F32)
    D.kgs = [nc.dram_tensor(f"kg_{t}", [2, 128, NJ, TT], F32) for t in range(NT)]
    D.vgs = [nc.dram_tensor(f"vg_{t}", [2, 128, 4, DM], F32) for t in range(NT)]
    D.bcg = nc.dram_tensor("bcg", [2, 128, NJ], F32)
    D.xn1col = nc.dram_tensor("xn1col", [128, NJ], F32)
    _l2_dram(nc, kind_out="Internal", D=D)
    _l3_dram(nc, D=D)
    D.sp_ap = A(D.sendg, 0, [[512, 128], [1, 512]])

    with contextlib.ExitStack() as st:
        sems = {nm: Sem(st, nc, nm) for nm in L3_SEMS}
        jn = Sem(st, nc, "jn")
        SER_SEMS.clear()
        SER_SEMS.extend([sems["mm"], sems["dv"], sems["sc"]])
        cc1, cc2 = sems["cc1"], sems["cc2"]

        with contextlib.ExitStack() as st1:
            G1 = _l1_emit(nc, st1, sems, D)
            em = Em(nc)
            em.op("gpsimd", lambda eng: eng.collective_compute(
                "AllGather", ALU.bypass, replica_groups=RG_PAIRS,
                ins=[D.send.ap().opt()], outs=[D.sendg.ap().opt()]),
                waits=[G1.fin.at()], inc=cc1, k=1)
            _join(em, jn, [G1.fin.at(), G1.xns.at(), G1.ss.at(), G1.qs.at(),
                           G1.l0.at(), G1.l1.at(), G1.t0.at(), G1.t1.at()])
            em.flush()

        D.sp_gate = (cc1.h, 1)
        with contextlib.ExitStack() as st2:
            G2 = _l2_emit(nc, st2, sems, D)
            em = Em(nc)
            l2_term = [G2.sst.at(), G2.l0.at(), G2.l1.at()]
            _join(em, jn, l2_term)
            em.flush()

        D.kv_gate = (cc2.h, 2 * NT)
        D.bc_gate = (cc1.h, 2)
        with contextlib.ExitStack() as st3:
            _l3_emit(nc, st3, sems, D)
    return nc


def fused_inputs(x_half, half, ins):
    d = l1_inputs(x_half, ins)
    d.update({
        "l1g": _vec128(ins["L0_ffn_ln1_g"]), "l1b": _vec128(ins["L0_ffn_ln1_b"]),
        "b1": _vec128(ins["L0_ffn_b1"]), "l2g": _vec128(ins["L0_ffn_ln2_g"]),
        "l2b": _vec128(ins["L0_ffn_ln2_b"]), "b2": _vec128(ins["L0_ffn_b2"]),
        "wk1": np.ascontiguousarray(ins["L1_attn_Wk"].transpose(1, 0, 2).reshape(DM, DM)),
        "wv1": np.ascontiguousarray(ins["L1_attn_Wv"].transpose(1, 0, 2).reshape(DM, DM)),
        "wq1": np.ascontiguousarray(ins["L1_attn_Wq"].transpose(1, 0, 2).reshape(DM, DM)),
        "wr": np.ascontiguousarray(ins["L1_ffn_W_r"].T),
        "wkr": np.ascontiguousarray(ins["L1_ffn_W_k"].T),
        "wvr": np.ascontiguousarray(ins["L1_ffn_W_v"].T),
        "mur": _vec128(ins["L1_ffn_mu_r"]), "muk": _vec128(ins["L1_ffn_mu_k"]),
        "id128": np.eye(128, dtype=np.float32),
        "flag": np.full((128, 1), float(half), np.float32),
    })
    return d


_NC_CACHE = {}


def _get_ncs():
    if "ncs" not in _NC_CACHE:
        _NC_CACHE["ncs"] = (build_launch1(), build_launch2(), build_launch3())
    return _NC_CACHE["ncs"]


def _get_fused():
    if "fused" not in _NC_CACHE:
        _NC_CACHE["fused"] = build_fused()
    return _NC_CACHE["fused"]


def kernel(**inputs):
    import os
    if bool(int(os.environ.get("KERNEL_UNFUSED", "0"))):
        return _kernel_unfused(**inputs)
    ins = inputs
    nc = _get_fused()
    x_full = np.asarray(ins["L0_attn_x"], dtype=np.float32)
    cores = list(range(NCORES))
    im = []
    for core in cores:
        b, half = core // 2, core % 2
        x_half = np.ascontiguousarray(x_full[b, half * TOK:(half + 1) * TOK, :])
        im.append(fused_inputs(x_half, half, ins))
    r = run_bass_kernel_spmd(nc, im, core_ids=cores).results
    out = np.empty((4, 2 * TOK, DM), dtype=np.float32)
    for core in cores:
        b, half = core // 2, core % 2
        out[b, half * TOK:(half + 1) * TOK, :] = _from_fm(r[core]["out"])
    return out


def _kernel_unfused(**inputs):
    ins = inputs
    nc1, nc2, nc3 = _get_ncs()
    x_full = np.asarray(ins["L0_attn_x"], dtype=np.float32)
    cores = list(range(NCORES))

    im1 = []
    for core in cores:
        b, half = core // 2, core % 2
        x_half = np.ascontiguousarray(x_full[b, half * TOK:(half + 1) * TOK, :])
        im1.append(l1_inputs(x_half, ins))
    r1 = run_bass_kernel_spmd(nc1, im1, core_ids=cores).results

    im2 = []
    for core in cores:
        b, half = core // 2, core % 2
        r1c = dict(r1[core])
        r1c["__x"] = im1[core]["x"]
        sp = (np.ascontiguousarray(r1[core - 1]["send"]) if half == 1
              else np.zeros((128, 512), np.float32))
        im2.append(l2_inputs(r1c, ins, sp))
    r2 = run_bass_kernel_spmd(nc2, im2, core_ids=cores).results

    im3 = []
    for core in cores:
        b, half = core // 2, core % 2
        a, bb = 2 * b, 2 * b + 1
        bcol = (np.ascontiguousarray(r2[a]["xn1"][:, :, -1]) if half == 1
                else np.zeros((128, NJ), np.float32))
        im3.append(l3_inputs(r2[a], r2[bb], ins, half,
                             np.ascontiguousarray(r2[core]["x1"]),
                             np.ascontiguousarray(r2[core]["xn1"]), bcol))
    r3 = run_bass_kernel_spmd(nc3, im3, core_ids=cores).results

    out = np.empty((4, 2 * TOK, DM), dtype=np.float32)
    for core in cores:
        b, half = core // 2, core % 2
        out[b, half * TOK:(half + 1) * TOK, :] = _from_fm(r3[core]["out"])
    return out


# version 33
# speedup vs baseline: 1.1336x; 1.0025x over previous
import contextlib
import numpy as np
import concourse.bass as bass
import concourse.mybir as mybir
from concourse.bass_utils import run_bass_kernel_spmd

F32 = mybir.dt.float32
F32R = mybir.dt.float32r
BF16 = mybir.dt.bfloat16
AF = mybir.ActivationFunctionType
ALU = mybir.AluOpType
AX = mybir.AxisListType

NCORES = 8
TOK = 2048
NT = 4
TT = 512
NJ = 8
DM = 1024
NCH = 32
CPT = 8
CH = 64
H = 16
DH = 64
HID = 4096
GH = 16
LNEPS = 1e-5
NEG = -1e9
PITCH = 4096


def A(t, off, dims):
    return bass.AP(t, off, dims)


class Sem:
    def __init__(self, stack, nc, name):
        self.h = stack.enter_context(nc.semaphore(name))
        self.v = 0

    def at(self):
        return (self.h, self.v)


SER_SEMS = []


class Em:

    ENG = ("sync", "gpsimd", "tensor", "vector", "scalar")

    def __init__(self, nc):
        import os
        self.nc = nc
        self.q = {e: [] for e in self.ENG}
        self.ser = bool(int(os.environ.get("EM_SERIALIZE", "0")))

    def op(self, engine, fn, waits=(), inc=None, k=1):
        waits = [w for w in waits if w is not None]
        if self.ser:
            waits = waits + [s.at() for s in SER_SEMS]
        self.q[engine].append((fn, waits, inc, k))
        if inc is not None:
            inc.v += k
        return inc.v if inc is not None else None

    def flush(self):
        with self.nc.Block() as blk:
            def runner(eng, ops):
                for fn, waits, inc, k in ops:
                    for (sh, v) in waits:
                        if v > 0:
                            eng.wait_ge(sh, v)
                    instr = fn(eng)
                    if inc is not None:
                        instr.then_inc(inc.h, k)
            for name in self.ENG:
                ops = self.q[name]
                if ops:
                    getattr(blk, name)(lambda eng, ops=ops: runner(eng, ops))
        self.q = {e: [] for e in self.ENG}



L1_SEMS = ("mm", "dv", "sc", "c", "x", "w0", "w1", "v0", "v1", "v2", "v3",
           "xns", "ss", "qs", "l0", "l1", "t0", "t1", "fin")
L2_SEMS = L1_SEMS + ("sst",)
L3_SEMS = L2_SEMS + ("kb0", "kb1", "vb0", "vb1", "qe", "ke", "me",
                     "s1st", "s1ld", "x1ld", "ost", "cc1", "cc2")


class NS:
    def __init__(self, **kw):
        self.__dict__.update(kw)


def _l1_dram(nc, kind_in="ExternalInput", kind_out="ExternalOutput", t_dt=F32):
    D = NS()
    D.x = nc.dram_tensor("x", [128, NJ, TOK], F32, kind=kind_in)
    D.wq = nc.dram_tensor("wq", [DM, DM], F32, kind=kind_in)
    D.wk = nc.dram_tensor("wk", [DM, DM], F32, kind=kind_in)
    D.wv = nc.dram_tensor("wv", [DM, DM], F32, kind=kind_in)
    D.bq = nc.dram_tensor("bq", [128, NJ], F32, kind=kind_in)
    D.bk = nc.dram_tensor("bk", [128, NJ], F32, kind=kind_in)
    D.bv = nc.dram_tensor("bv", [1, DM], F32, kind=kind_in)
    D.wg1 = nc.dram_tensor("wg1", [DM, GH], F32, kind=kind_in)
    D.bg1 = nc.dram_tensor("bg1", [GH, 1], F32, kind=kind_in)
    D.wg2 = nc.dram_tensor("wg2", [GH, DM], F32, kind=kind_in)
    D.bg2 = nc.dram_tensor("bg2", [128, NJ], F32, kind=kind_in)
    D.w1t = nc.dram_tensor("w1t", [DM, HID], F32, kind=kind_in)
    D.w2t = nc.dram_tensor("w2t", [HID, DM], F32, kind=kind_in)
    D.id64 = nc.dram_tensor("id64", [128, 128], F32, kind=kind_in)
    D.cmask = nc.dram_tensor("cmask", [128, 512], F32, kind=kind_in)
    D.smask = nc.dram_tensor("smask", [128, 512], F32, kind=kind_in)
    D.xn = nc.dram_tensor("xn", [128, NJ, TOK], F32, kind=kind_out)
    D.s = nc.dram_tensor("s", [128, NJ, TOK], F32, kind=kind_out)
    D.qh = nc.dram_tensor("qh", [128, NJ, TOK], F32, kind=kind_out)
    D.lent = nc.dram_tensor("lent", [NCH, 128, 512], F32, kind=kind_out)
    D.dcum = nc.dram_tensor("dcum", [NCH, 128, 8], F32, kind=kind_out)
    D.send = nc.dram_tensor("send", [128, 512], F32, kind=kind_out)
    D.t1 = nc.dram_tensor("t1", [DM, HID], t_dt, kind=kind_out)
    D.t2 = nc.dram_tensor("t2", [HID, DM], t_dt, kind=kind_out)
    D.m1 = nc.dram_tensor("m1", [128, 2], F32, kind=kind_out)
    D.m2 = nc.dram_tensor("m2", [128, 2], F32, kind=kind_out)
    return D


def _l1_emit(nc, st, sems, D):
    E = st.enter_context
    g = {}
    for nm in ("bA", "bB", "bC", "bD", "bE", "bF"):
        g[nm] = E(nc.sbuf_tensor(nm, [128, PITCH], F32))
    for nm in ("pE1", "pE2", "pQT", "pKT", "pKH", "pQH"):
        g[nm] = E(nc.sbuf_tensor(nm, [128, 1024], F32))
    g["wbuf"] = [E(nc.sbuf_tensor(f"wbuf{i}", [128, NJ * 128], F32)) for i in range(2)]
    g["scr"] = [E(nc.sbuf_tensor(f"scr{i}", [128, 512], F32)) for i in range(4)]
    g["bBb"] = E(nc.sbuf_tensor("bBb", [128, PITCH], BF16))
    g["wbufb"] = [E(nc.sbuf_tensor(f"wbufb{i}", [128, NJ * 128], BF16)) for i in range(2)]
    g["wg1_sb"] = E(nc.sbuf_tensor("wg1s", [128, NJ * GH], F32))
    g["wg2_sb"] = E(nc.sbuf_tensor("wg2s", [GH, DM], F32))
    g["bg1_sb"] = E(nc.sbuf_tensor("bg1s", [GH, 1], F32))
    g["bq_sb"] = E(nc.sbuf_tensor("bqs", [128, NJ], F32))
    g["bk_sb"] = E(nc.sbuf_tensor("bks", [128, NJ], F32))
    g["bg2_sb"] = E(nc.sbuf_tensor("bg2s", [128, NJ], F32))
    g["bvbc"] = E(nc.sbuf_tensor("bvbc", [128, DM], F32))
    g["bv_row"] = E(nc.sbuf_tensor("bvrow", [1, DM], F32))
    g["id64"] = E(nc.sbuf_tensor("id64s", [128, 128], F32))
    g["cmask"] = E(nc.sbuf_tensor("cmasks", [128, 512], F32))
    g["smask"] = E(nc.sbuf_tensor("smasks", [128, 512], F32))
    g["ones_c"] = E(nc.sbuf_tensor("onesc", [128, 1], F32))
    g["ones_r"] = E(nc.sbuf_tensor("onesr", [1, 128], F32))
    g["g1_sb"] = E(nc.sbuf_tensor("g1s", [GH, TT], F32))
    g["P_sb"] = E(nc.sbuf_tensor("Psb", [128, 1024], F32))
    g["PT_sb"] = E(nc.sbuf_tensor("PTsb", [128, 1024], F32))
    g["KT_sb"] = E(nc.sbuf_tensor("KTsb", [128, 1024], F32))
    g["At_sb"] = E(nc.sbuf_tensor("Atsb", [128, 1024], F32))
    g["S_all"] = E(nc.sbuf_tensor("Sall", [128, 512], F32))
    g["lstage"] = [E(nc.sbuf_tensor(f"lst{i}", [128, 512], F32)) for i in range(2)]
    g["msb"] = E(nc.sbuf_tensor("msb", [128, 64], F32))
    g["tcb"] = E(nc.sbuf_tensor("tcb", [128, 64], F32))
    g["tcm"] = E(nc.sbuf_tensor("tcms", [128, 64], F32))
    g["etb"] = E(nc.sbuf_tensor("etb", [128, 64], F32))
    g["tcum"] = E(nc.sbuf_tensor("tcum", [128, 8], F32))
    g["dstage"] = E(nc.sbuf_tensor("dstage", [128, NCH * 8], F32))
    g["redc"] = E(nc.sbuf_tensor("redc", [128, 64], F32))
    g["mcols"] = E(nc.sbuf_tensor("mcols", [128, 2], F32))
    g["mrow"] = E(nc.sbuf_tensor("mrow", [1, 2], F32))
    g["sgb"] = E(nc.sbuf_tensor("sgb", [128, 512], F32))
    g["sgb2"] = E(nc.sbuf_tensor("sgb2", [128, 512], F32))
    g["cns"] = E(nc.sbuf_tensor("cns", [128, 2], F32))
    g["tqb"] = [E(nc.sbuf_tensor(f"tqb{i}", [128, 512], F32)) for i in range(2)]
    g["tscr"] = [E(nc.sbuf_tensor(f"tscr{i}", [128, 512], F32)) for i in range(2)]
    g["tred"] = E(nc.sbuf_tensor("tred", [128, 128], F32))
    g["pb"] = [E(nc.psum_tensor(f"pb{i}", [128, 512], F32)) for i in range(8)]
    g.update(sems)
    G = type("G", (), g)

    import os
    lim = int(os.environ.get("L1_PHASES", "99"))
    ph = 0
    with nc.named_scope("l1c"):
        _l1_consts(nc, G, D.id64, D.cmask, D.smask, D.bq, D.bk, D.bg2, D.bg1,
                   D.wg2, D.bv, D.wg1)
    ph += 1
    gens = [_l1_tern_p1(nc, G, D.w1t, DM, HID, 0),
            _l1_tern_p1(nc, G, D.w2t, HID, DM, 64)]

    def drive(k):
        em = Em(nc)
        G.tern_em = em
        while k > 0 and gens:
            try:
                next(gens[0])
                k -= 1
            except StopIteration:
                gens.pop(0)
        em.flush()

    for t in range(NT):
        if ph < lim:
            with nc.named_scope(f"l1p{t}"):
                _l1_tile_proj(nc, G, t, D.x, D.xn, D.wq, D.wk, D.wv)
            ph += 1
        drive(16)
        if ph < lim:
            with nc.named_scope(f"l1k{t}"):
                _l1_tile_chunks(nc, G, t, D.s, D.qh, D.lent)
            ph += 1
        drive(16)
    if ph < lim:
        with nc.named_scope("l1f"):
            _l1_finalize(nc, G, D.send, D.dcum)
        ph += 1
    drive(1 << 30)
    if ph < lim:
        with nc.named_scope("l1t1"):
            _l1_ternary(nc, G, D.w1t, D.t1, D.m1, DM, HID, 0)
        ph += 1
    if ph < lim:
        with nc.named_scope("l1t2"):
            _l1_ternary(nc, G, D.w2t, D.t2, D.m2, HID, DM, 64)
    return G


def build_launch1():
    nc = bass.Bass("TRN2", target_bir_lowering=False)
    D = _l1_dram(nc)
    with contextlib.ExitStack() as st:
        sems = {nm: Sem(st, nc, nm) for nm in L1_SEMS}
        SER_SEMS.clear()
        SER_SEMS.extend([sems["mm"], sems["dv"], sems["sc"]])
        with contextlib.ExitStack() as st_b:
            G = _l1_emit(nc, st_b, sems, D)
        em = Em(nc)
        em.op("gpsimd", lambda eng: eng.nop(),
              waits=[G.fin.at(), G.xns.at(), G.ss.at(), G.qs.at(),
                     G.l0.at(), G.l1.at(), G.t0.at(), G.t1.at()])
        em.flush()
    return nc


def _l1_consts(nc, G, id64_d, cmask_d, smask_d, bq_d, bk_d, bg2_d, bg1_d,
               wg2_d, bv_d, wg1_d):
    em = Em(nc)
    ld, dv, mm = G.c, G.dv, G.mm

    for dst, src in (
        (G.id64[:], id64_d.ap()), (G.cmask[:], cmask_d.ap()), (G.smask[:], smask_d.ap()),
        (G.bq_sb[:], bq_d.ap()), (G.bk_sb[:], bk_d.ap()), (G.bg2_sb[:], bg2_d.ap()),
        (G.bg1_sb[:], bg1_d.ap()), (G.wg2_sb[:], wg2_d.ap()), (G.bv_row[:], bv_d.ap()),
        (G.wg1_sb[:], A(wg1_d, 0, [[GH, 128], [128 * GH, NJ], [1, GH]])),
    ):
        em.op("sync", (lambda eng, a=dst, b=src: eng.dma_start(a, b)), inc=ld, k=16)

    def f(eng):
        eng.memset(G.ones_c[:], 1.0)
        eng.memset(G.ones_r[:], 1.0)
        eng.memset(G.tcum[:], 0.0)
        eng.memset(G.cns[:, 0:1], float(LNEPS))
        eng.memset(G.cns[:, 1:2], 0.1)
        return eng.memset(G.S_all[:], 0.0)
    em.op("vector", f, inc=dv)

    def f(eng):
        eng.matmul(G.pb[0][:, :], G.ones_r[:], G.bv_row[0:1, 0:512], start=True, stop=True)
        return eng.matmul(G.pb[1][:, :], G.ones_r[:], G.bv_row[0:1, 512:1024],
                          start=True, stop=True)
    em.op("tensor", f, waits=[ld.at(), dv.at()], inc=mm)

    def f(eng):
        eng.tensor_copy(G.bvbc[:, 0:512], G.pb[0][:, :])
        return eng.tensor_copy(G.bvbc[:, 512:1024], G.pb[1][:, :])
    em.op("vector", f, waits=[mm.at()], inc=dv)

    em.flush()


def _l1_tile_proj(nc, G, t, x_d, xn_d, wq_d, wk_d, wv_d):
    em = Em(nc)
    mm, dv, sc = G.mm, G.dv, G.sc
    wsem = [G.w0, G.w1]
    vsem = [G.v0, G.v1]
    bA, bB, bC, bD, bE, bF = G.bA, G.bB, G.bC, G.bD, G.bE, G.bF
    pb = G.pb
    t0 = t * TT

    def xv(buf, j, lo=0, n=TT):
        return A(buf, j * TT + lo, [[PITCH, 128], [1, n]])

    for js in range(0, NJ, 2):
        em.op("sync", (lambda eng, js=js: eng.dma_start(
            A(bA, js * TT, [[PITCH, 128], [TT, 2], [1, TT]]),
            A(x_d, t0 + js * TOK, [[NJ * TOK, 128], [TOK, 2], [1, TT]]))),
            waits=[G.c.at()] if js == 0 else [], inc=G.x, k=16)
    w_x = G.x.at()

    mm_sq = {}
    for j in range(NJ):
        waits = [w_x] + ([(mm.h, mm_sq[j - 2])] if j >= 2 else [])
        em.op("scalar", (lambda eng, j=j: eng.activation(
            G.scr[j % 2][:], xv(bA, j), AF.Square)), waits=waits, inc=sc)
        w_sqj = sc.at()
        em.op("tensor", (lambda eng, j=j: eng.matmul(
            pb[1][0:1, :], G.ones_c[:], G.scr[j % 2][:],
            start=(j == 0), stop=(j == NJ - 1))), waits=[w_sqj], inc=mm)
        mm_sq[j] = mm.v
    w_sq_done = mm.at()

    def f(eng):
        for j in range(NJ):
            ii = eng.matmul(pb[0][0:1, :], G.ones_c[:], xv(bA, j),
                            start=(j == 0), stop=(j == NJ - 1))
        return ii
    em.op("tensor", f, waits=[w_x], inc=mm)
    w_stats = mm.at()

    def r1(off, n=TT):
        return A(G.pE1, off, [[1024, 1], [1, n]])

    def r2(off, n=TT):
        return A(G.pE2, off, [[1024, 1], [1, n]])

    def f(eng):
        eng.tensor_scalar(r1(0), pb[0][0:1, :], 1.0 / DM, None, ALU.mult)
        eng.tensor_scalar(r1(512), pb[1][0:1, :], 1.0 / DM, None, ALU.mult)
        eng.tensor_tensor(r2(0), r1(0), r1(0), ALU.mult)
        return eng.tensor_tensor(r2(512), r1(512), r2(0), ALU.subtract)
    em.op("vector", f, waits=[w_stats], inc=dv)

    def f(eng):
        eng.activation(r2(512), r2(512), AF.Ln, bias=G.cns[0:1, 0:1])
        return eng.activation(r2(512), r2(512), AF.Exp, scale=-0.5)
    em.op("scalar", f, waits=[dv.at()], inc=sc)

    em.op("vector", lambda eng: eng.scalar_tensor_tensor(
        r2(0), r1(0), -1.0, r2(512), ALU.mult, ALU.mult), waits=[sc.at()], inc=dv)

    def f(eng):
        eng.matmul(pb[2][:, :], G.ones_r[:], r2(512), start=True, stop=True)
        return eng.matmul(pb[3][:, :], G.ones_r[:], r2(0), start=True, stop=True)
    em.op("tensor", f, waits=[dv.at()], inc=mm)
    w_bc = mm.at()

    for j in range(NJ):
        def f(eng, j=j):
            eng.tensor_tensor(xv(bB, j), xv(bA, j), pb[2][:, :], ALU.mult)
            eng.tensor_tensor(xv(bB, j), xv(bB, j), pb[3][:, :], ALU.add)
            return eng.tensor_copy(A(G.bBb, j * TT, [[PITCH, 128], [1, TT]]), xv(bB, j))
        em.op("vector", f, waits=([w_bc, G.ss.at()] if j == 0 else []), inc=dv)
    w_xn = dv.at()

    for js in range(0, NJ, 2):
        em.op("gpsimd", (lambda eng, js=js: eng.dma_start(
            A(xn_d, t0 + js * TOK, [[NJ * TOK, 128], [TOK, 2], [1, TT]]),
            A(bB, js * TT, [[PITCH, 128], [TT, 2], [1, TT]]))),
            waits=[w_xn] if js == 0 else [], inc=G.xns, k=16)

    wplan = [(wq_d, G.bq_sb, bC, oc) for oc in range(8)] + \
            [(wk_d, G.bk_sb, bD, oc) for oc in range(8)]
    grp_done = {}
    ev_done = {}
    wc_done = {}
    prevc = getattr(G, "qk_wcp", None)
    for n, (w_d_, b_sb, dst, oc) in enumerate(wplan):
        bi = n % 2
        dwaits = ([(dv.h, wc_done[n - 2])] if n >= 2 else
                  ([(dv.h, prevc)] if prevc is not None else []))
        em.op("sync", (lambda eng, w_d_=w_d_, oc=oc, bi=bi: eng.dma_start(
            G.wbuf[bi][:], A(w_d_, 128 * oc, [[DM, 128], [128 * DM, NJ], [1, 128]]))),
            waits=dwaits, inc=wsem[bi], k=16)
        w_wld = wsem[bi].at()
        cwaits = [w_wld] + ([(mm.h, grp_done[n - 2])] if n >= 2 else [])
        em.op("vector", (lambda eng, bi=bi: eng.tensor_copy(
            G.wbufb[bi][:, :], G.wbuf[bi][:, :])), waits=cwaits, inc=dv)
        wc_done[n] = dv.v
        w_wc = dv.at()
        bank = pb[4 + bi]
        mwaits = [w_wc, w_xn] + ([(dv.h, ev_done[n - 2])] if n >= 2 else [])

        def f(eng, bi=bi, bank=bank):
            for kj in range(NJ):
                ii = eng.matmul(bank[:, :], A(G.wbufb[bi], kj * 128, [[NJ * 128, 128], [1, 128]]),
                                A(G.bBb, kj * TT, [[PITCH, 128], [1, TT]]),
                                start=(kj == 0), stop=(kj == NJ - 1))
            return ii
        em.op("tensor", f, waits=mwaits, inc=mm)
        grp_done[n] = mm.v
        em.op("vector", (lambda eng, bank=bank, b_sb=b_sb, dst=dst, oc=oc:
                         eng.tensor_scalar(xv(dst, oc), bank[:, :], b_sb[:, oc:oc + 1],
                                           None, ALU.add)),
              waits=[(mm.h, grp_done[n])], inc=dv)
        ev_done[n] = dv.v
    G.qk_wcp = wc_done[len(wplan) - 1]

    def f(eng):
        for j in range(NJ):
            ii = eng.matmul(pb[6][0:GH, :], A(G.wg1_sb, j * GH, [[NJ * GH, 128], [1, GH]]),
                            xv(bB, j), start=(j == 0), stop=(j == NJ - 1))
        return ii
    em.op("tensor", f, waits=[w_xn], inc=mm)
    em.op("scalar", lambda eng: eng.activation(
        G.g1_sb[:, :], pb[6][0:GH, :], AF.Silu, bias=G.bg1_sb[:, :]),
        waits=[mm.at()], inc=sc)
    w_g1 = sc.at()

    gl_marks = {}
    for j in range(NJ):
        waits = [w_g1] + ([(sc.h, gl_marks[j - 2])] if j >= 2 else [])
        em.op("tensor", (lambda eng, j=j: eng.matmul(
            pb[6 + (j % 2)][:, :], G.wg2_sb[0:GH, 128 * j:128 * (j + 1)], G.g1_sb[:, :],
            start=True, stop=True)), waits=waits, inc=mm)
        em.op("scalar", (lambda eng, j=j: eng.activation(
            xv(bE, j), pb[6 + (j % 2)][:, :], AF.Sigmoid, bias=G.bg2_sb[:, j:j + 1])),
            waits=[mm.at()], inc=sc)
        gl_marks[j] = sc.v
    for j in range(NJ):
        em.op("scalar", (lambda eng, j=j: eng.activation(
            xv(bE, j), xv(bE, j), AF.Ln, scale=0.9, bias=G.cns[:, 1:2])), inc=sc)
    w_L = sc.at()

    vmm = {}
    vev = {}
    vc_done = {}
    prevv = getattr(G, "v_wcp", None)
    for half in range(2):
        for kj in range(NJ):
            n = half * NJ + kj
            bi = n % 2
            dwaits = [w_sq_done] + ([(dv.h, vc_done[n - 2])] if n >= 2 else
                                    ([(dv.h, prevv)] if prevv is not None else []))
            em.op("sync", (lambda eng, kj=kj, half=half, bi=bi: eng.dma_start(
                G.scr[bi][:], A(wv_d, kj * 128 * DM + half * 512, [[DM, 128], [1, 512]]))),
                waits=dwaits, inc=vsem[bi], k=16)
            w_vld = vsem[bi].at()
            cwaits = [w_vld] + ([(mm.h, vmm[n - 2])] if n >= 2 else [])
            em.op("vector", (lambda eng, bi=bi: eng.tensor_copy(
                A(G.wbufb[bi], 0, [[NJ * 128, 128], [1, 512]]), G.scr[bi][:, :])),
                waits=cwaits, inc=dv)
            vc_done[n] = dv.v
            w_vc = dv.at()
            mwaits = [w_vc, w_xn]
            if kj == 0 and half == 1:
                mwaits.append((dv.h, vev[0]))

            def f(eng, kj=kj, bi=bi):
                for ts in range(4):
                    ii = eng.matmul(pb[ts][:, :],
                                    A(G.bBb, kj * TT + ts * 128, [[PITCH, 128], [1, 128]]),
                                    A(G.wbufb[bi], 0, [[NJ * 128, 128], [1, 512]]),
                                    start=(kj == 0), stop=(kj == NJ - 1))
                return ii
            em.op("tensor", f, waits=mwaits, inc=mm)
            vmm[n] = mm.v
        for ts in range(4):
            em.op("vector", (lambda eng, ts=ts, half=half: eng.tensor_tensor(
                A(bA, ts * 1024 + half * 512, [[PITCH, 128], [1, 512]]),
                pb[ts][:, :], G.bvbc[:, half * 512:(half + 1) * 512], ALU.add)),
                waits=[(mm.h, vmm[half * NJ + NJ - 1])], inc=dv)
        vev[half] = dv.v
    G.v_wcp = vc_done[2 * NJ - 1]

    for j in range(NJ):
        for c in range(CPT):
            em.op("vector", (lambda eng, j=j, c=c: eng.tensor_tensor_scan(
                A(bF, j * TT + c * CH, [[PITCH, 128], [1, CH]]),
                A(bE, j * TT + c * CH, [[PITCH, 128], [1, CH]]),
                A(bE, j * TT + c * CH, [[PITCH, 128], [1, CH]]),
                0.0, ALU.add, ALU.bypass)),
                waits=[w_L] if (j == 0 and c == 0) else [], inc=dv)

    def f(eng):
        eng.tensor_copy(G.msb[:, :], A(bF, 31, [[PITCH, 128], [TT, NJ], [CH, CPT]]))
        return eng.tensor_copy(G.tcb[:, :], A(bF, 63, [[PITCH, 128], [TT, NJ], [CH, CPT]]))
    em.op("vector", f, inc=dv)
    em.op("scalar", lambda eng: eng.activation(G.etb[:, :], G.tcb[:, :], AF.Exp),
          waits=[dv.at()], inc=sc)
    em.op("vector", lambda eng: eng.tensor_tensor(
        G.tcb[:, :], G.tcb[:, :], G.msb[:, :], ALU.subtract), waits=[sc.at()], inc=dv)
    em.op("scalar", lambda eng: eng.activation(G.tcm[:, :], G.tcb[:, :], AF.Exp),
          waits=[dv.at()], inc=sc)

    def f(eng):
        for j in range(NJ):
            ii = eng.tensor_tensor(xv(bE, j), xv(bF, j), xv(bE, j), ALU.subtract)
        return ii
    em.op("vector", f, inc=dv)

    em.op("vector", lambda eng: eng.memset(bB[:, :], 0.0),
          waits=[mm.at(), G.xns.at()], inc=dv)

    em.flush()


def _l1_tile_chunks(nc, G, t, s_d, qh_d, lent_d):
    import os
    sub = int(os.environ.get("L1_CHUNK_SUB", "99"))
    em = Em(nc)
    mm, dv, sc = G.mm, G.dv, G.sc
    lsem = [G.l0, G.l1]
    bA, bB, bC, bD, bE, bF = G.bA, G.bB, G.bC, G.bD, G.bE, G.bF
    pb = G.pb
    t0 = t * TT

    last_soft = {0: 0, 1: 0}
    last_at = 0
    last_sadd = 0
    last_ds = 0
    last_tcopy = 0
    prev_pair_mm = 0
    prev_ptr_mm = 0

    npair = int(os.environ.get("L1_PAIRS", "4"))
    for p in range(npair):
        c0, c1 = 2 * p, 2 * p + 1
        def f(eng, p=p):
            for j in range(NJ):
                eng.tensor_tensor(
                    A(G.pE1, j * 128, [[1024, 128], [1, 128]]),
                    A(bE, j * TT + p * 128, [[PITCH, 128], [1, 128]]),
                    A(G.msb, j * CPT + 2 * p, [[64, 128], [1, 2], [0, 64]]),
                    ALU.subtract)
                ii = eng.tensor_tensor(
                    A(G.pE2, j * 128, [[1024, 128], [1, 128]]),
                    A(bF, j * TT + p * 128, [[PITCH, 128], [1, 128]]),
                    A(G.msb, j * CPT + 2 * p, [[64, 128], [1, 2], [0, 64]]),
                    ALU.subtract)
            return ii
        em.op("vector", f, waits=[sc.at(), (mm.h, prev_pair_mm)], inc=dv)

        def f(eng):
            eng.activation(G.pE1[:, :], G.pE1[:, :], AF.Exp)
            return eng.activation(G.pE2[:, :], G.pE2[:, :], AF.Exp, scale=-1.0)
        em.op("scalar", f, waits=[dv.at()], inc=sc)
        w_e12 = sc.at()

        def f(eng, p=p):
            for j in range(NJ):
                eng.tensor_tensor(A(G.pQT, j * 128, [[1024, 128], [1, 128]]),
                                  A(bC, j * TT + p * 128, [[PITCH, 128], [1, 128]]),
                                  A(G.pE1, j * 128, [[1024, 128], [1, 128]]), ALU.mult)
                eng.tensor_tensor(A(G.pKT, j * 128, [[1024, 128], [1, 128]]),
                                  A(bD, j * TT + p * 128, [[PITCH, 128], [1, 128]]),
                                  A(G.pE2, j * 128, [[1024, 128], [1, 128]]), ALU.mult)
                ii = eng.tensor_tensor(
                    A(G.pKH, j * 128, [[1024, 128], [1, 128]]),
                    A(G.pKT, j * 128, [[1024, 128], [1, 128]]),
                    A(G.tcm, j * CPT + 2 * p, [[64, 128], [1, 2], [0, 64]]), ALU.mult)
            return ii
        em.op("vector", f, waits=[w_e12], inc=dv)
        w_qkt = dv.at()

        em.op("scalar", (lambda eng, p=p: eng.activation(
            G.pE1[:, :], A(bE, p * 128, [[PITCH, 128], [TT, NJ], [1, 128]]), AF.Exp)),
            waits=[w_qkt], inc=sc)

        def f(eng, p=p):
            for j in range(NJ):
                ii = eng.tensor_tensor(A(G.pQH, j * 128, [[1024, 128], [1, 128]]),
                                       A(bC, j * TT + p * 128, [[PITCH, 128], [1, 128]]),
                                       A(G.pE1, j * 128, [[1024, 128], [1, 128]]), ALU.mult)
            return ii
        em.op("vector", f, waits=[sc.at(), G.qs.at()], inc=dv)

        em.op("gpsimd", (lambda eng, p=p: eng.dma_start(
            A(qh_d, t0 + p * 128, [[NJ * TOK, 128], [TOK, NJ], [1, 128]]),
            A(G.pQH, 0, [[1024, 128], [128, NJ], [1, 128]]))),
            waits=[dv.at()], inc=G.qs, k=16)

        if sub < 2:
            continue
        for ci, c in enumerate((c0, c1)):
            bank = pb[ci]

            for par in (0, 1):
                def f(eng, c=c, bank=bank, par=par):
                    for h in range(par, H, 2):
                        ii = eng.matmul(
                            A(bank, (h % 2) * 64 * 512 + (h // 2) * 64, [[512, 64], [1, 64]]),
                            A(bC, (h % 2) * 64 * PITCH + (h // 2) * TT + c * 64, [[PITCH, 64], [1, 64]]),
                            A(bD, (h % 2) * 64 * PITCH + (h // 2) * TT + c * 64, [[PITCH, 64], [1, 64]]),
                            start=True, stop=True,
                            tile_position=(64 * (h % 2), 64 * (h % 2)))
                    return ii
                em.op("tensor", f, waits=[(dv.h, last_soft[ci])] if par == 0 else [], inc=mm)
            w_sc_mm = mm.at()

            em.op("vector", (lambda eng, bank=bank: eng.tensor_reduce(
                G.redc[:, 0:8], A(bank, 0, [[512, 128], [64, 8], [1, 64]]),
                AX.X, ALU.max)), waits=[w_sc_mm], inc=dv)
            em.op("vector", (lambda eng, bank=bank: eng.tensor_tensor(
                G.sgb[:, :], bank[:, :], G.cmask[:, :], ALU.add)), inc=dv)
            em.op("vector", (lambda eng: eng.tensor_tensor(
                G.sgb[:, :], G.sgb[:, :], A(G.redc, 0, [[64, 128], [1, 8], [0, 64]]),
                ALU.subtract)), inc=dv)
            em.op("scalar", lambda eng: eng.activation(G.sgb[:, :], G.sgb[:, :], AF.Exp,
                                                       scale=0.125),
                  waits=[dv.at()], inc=sc)

            em.op("vector", lambda eng: eng.tensor_reduce(
                G.redc[:, 8:16], A(G.sgb, 0, [[512, 128], [64, 8], [1, 64]]),
                AX.X, ALU.add), waits=[sc.at()], inc=dv)
            def f(eng):
                eng.activation(G.redc[:, 16:24], G.redc[:, 8:16], AF.Ln)
                return eng.activation(G.redc[:, 16:24], G.redc[:, 16:24], AF.Exp, scale=-1.0)
            em.op("scalar", f, waits=[dv.at()], inc=sc)
            em.op("vector", (lambda eng, ci=ci: eng.tensor_tensor(
                A(G.P_sb, ci * 64, [[1024, 128], [128, 8], [1, 64]]), G.sgb[:, :],
                A(G.redc, 16, [[64, 128], [1, 8], [0, 64]]), ALU.mult)),
                waits=[sc.at(), (mm.h, prev_ptr_mm)], inc=dv)
            last_soft[ci] = dv.v
        w_soft = dv.at()

        if sub < 3:
            continue
        for par in (0, 1):
            def f(eng, par=par):
                for h in range(par, H, 2):
                    ii = eng.transpose(
                        A(pb[2 + h % 2], (h // 2) * 64, [[512, 128], [1, 64]]),
                        A(G.P_sb, (h % 2) * 64 * 1024 + (h // 2) * 128, [[1024, 64], [1, 128]]),
                        G.id64[(h % 2) * 64:(h % 2) * 64 + 64, 0:64])
                return ii
            em.op("tensor", f, waits=[w_soft, (dv.h, last_tcopy)] if par == 0 else [], inc=mm)
        prev_ptr_mm = mm.v

        def f(eng):
            eng.tensor_copy(A(G.PT_sb, 0, [[1024, 128], [128, 8], [1, 64]]),
                            A(pb[2], 0, [[512, 128], [64, 8], [1, 64]]))
            return eng.tensor_copy(A(G.PT_sb, 64, [[1024, 128], [128, 8], [1, 64]]),
                                   A(pb[3], 0, [[512, 128], [64, 8], [1, 64]]))
        em.op("vector", f, waits=[mm.at()], inc=dv)
        w_ptc = dv.at()

        for par in (0, 1):
            def f(eng, par=par):
                for h in range(par, H, 2):
                    ii = eng.transpose(
                        A(pb[2 + h % 2], (h // 2) * 64, [[512, 128], [1, 64]]),
                        A(G.pKH, (h % 2) * 64 * 1024 + (h // 2) * 128, [[1024, 64], [1, 128]]),
                        G.id64[(h % 2) * 64:(h % 2) * 64 + 64, 0:64])
                return ii
            em.op("tensor", f, waits=[w_ptc] if par == 0 else [], inc=mm)

        def f(eng):
            eng.tensor_copy(A(G.KT_sb, 0, [[1024, 128], [128, 8], [1, 64]]),
                            A(pb[2], 0, [[512, 128], [64, 8], [1, 64]]))
            return eng.tensor_copy(A(G.KT_sb, 64, [[1024, 128], [128, 8], [1, 64]]),
                                   A(pb[3], 0, [[512, 128], [64, 8], [1, 64]]))
        em.op("vector", f, waits=[mm.at()], inc=dv)
        last_tcopy = dv.v

        if sub < 4:
            continue
        for ci, c in enumerate((c0, c1)):
            cb = 64 * (c % 2)
            n_glob = t * CPT + c

            for par in (0, 1):
                def f(eng, c=c, cb=cb, par=par):
                    for h in range(par, H, 2):
                        ii = eng.matmul(
                            A(pb[4 + h % 2], cb * 512 + (h // 2) * 64, [[512, 64], [1, 64]]),
                            A(G.pKT, (h % 2) * 64 * 1024 + (h // 2) * 128 + (c % 2) * 64,
                              [[1024, 64], [1, 64]]),
                            A(G.pQT, (h % 2) * 64 * 1024 + (h // 2) * 128 + (c % 2) * 64,
                              [[1024, 64], [1, 64]]),
                            start=True, stop=True,
                            tile_position=(64 * (h % 2), cb))
                    return ii
                em.op("tensor", f,
                      waits=[(dv.h, last_at), (dv.h, last_tcopy)] if par == 0 else [], inc=mm)
            w_atmm = mm.at()

            def f(eng, cb=cb):
                eng.tensor_tensor(A(G.At_sb, cb * 1024, [[1024, 64], [128, 8], [1, 64]]),
                                  A(pb[4], cb * 512, [[512, 64], [64, 8], [1, 64]]),
                                  A(G.smask, cb * 512, [[512, 64], [64, 8], [1, 64]]), ALU.mult)
                return eng.tensor_tensor(A(G.At_sb, cb * 1024 + 64, [[1024, 64], [128, 8], [1, 64]]),
                                         A(pb[5], cb * 512, [[512, 64], [64, 8], [1, 64]]),
                                         A(G.smask, cb * 512, [[512, 64], [64, 8], [1, 64]]), ALU.mult)
            em.op("vector", f, waits=[w_atmm], inc=dv)
            last_at = dv.v
            w_at = dv.at()

            if sub < 5:
                continue

            def f(eng, c=c, cb=cb):
                for h in range(H):
                    vsl = A(bA, cb * PITCH + (c // 2) * 1024 + h * 64, [[PITCH, 64], [1, 64]])
                    eng.matmul(
                        A(pb[6], (h % 2) * 64 * 512 + (h // 2) * 64, [[512, 64], [1, 64]]),
                        vsl,
                        A(G.PT_sb, cb * 1024 + h * 64, [[1024, 64], [1, 64]]),
                        start=True, stop=False, tile_position=(cb, 64 * (h % 2)))
                    ii = eng.matmul(
                        A(pb[6], (h % 2) * 64 * 512 + (h // 2) * 64, [[512, 64], [1, 64]]),
                        vsl,
                        A(G.At_sb, cb * 1024 + h * 64, [[1024, 64], [1, 64]]),
                        start=False, stop=True, tile_position=(cb, 64 * (h % 2)))
                return ii
            em.op("tensor", f, waits=[w_at, (dv.h, last_sadd)], inc=mm)
            w_pv = mm.at()

            if sub < 6:
                continue

            def f(eng, c=c, cb=cb):
                for h in range(H):
                    vsl = A(bA, cb * PITCH + (c // 2) * 1024 + h * 64, [[PITCH, 64], [1, 64]])
                    ii = eng.matmul(
                        A(pb[7], (h % 2) * 64 * 512 + (h // 2) * 64, [[512, 64], [1, 64]]),
                        A(G.KT_sb, cb * 1024 + h * 64, [[1024, 64], [1, 64]]),
                        vsl,
                        start=True, stop=True, tile_position=(cb, 64 * (h % 2)))
                return ii
            em.op("tensor", f, waits=[(dv.h, last_ds)], inc=mm)
            w_ds = mm.at()

            def f(eng, c=c):
                return eng.tensor_tensor(
                    A(bB, c * 64, [[PITCH, 128], [TT, NJ], [1, 64]]),
                    A(bB, c * 64, [[PITCH, 128], [TT, NJ], [1, 64]]),
                    A(pb[6], 0, [[512, 128], [64, NJ], [1, 64]]), ALU.add)
            em.op("vector", f, waits=[w_pv], inc=dv)
            last_sadd = dv.v

            def f(eng, ci=ci):
                return eng.tensor_copy(G.lstage[ci][:, :], G.S_all[:, :])
            em.op("vector", f, waits=[lsem[ci].at()], inc=dv)
            w_snap = dv.at()
            em.op("gpsimd", (lambda eng, ci=ci, n_glob=n_glob: eng.dma_start(
                A(lent_d, n_glob * 128 * 512, [[512, 128], [1, 512]]),
                G.lstage[ci][:, :])), waits=[w_snap], inc=lsem[ci], k=16)

            em.op("scalar", (lambda eng, n_glob=n_glob: eng.activation(
                A(G.dstage, n_glob * 8, [[NCH * 8, 128], [1, 8]]), G.tcum[:, :], AF.Exp)),
                waits=[dv.at()], inc=sc)
            em.op("vector", (lambda eng, c=c: eng.tensor_tensor(
                G.tcum[:, :], G.tcum[:, :],
                A(bF, c * 64 + 63, [[PITCH, 128], [TT, NJ]]), ALU.add)),
                waits=[sc.at()], inc=dv)

            def f(eng, c=c):
                eng.tensor_tensor(G.S_all[:, :], G.S_all[:, :],
                                  A(G.etb, c, [[64, 128], [8, 8], [0, 64]]), ALU.mult)
                return eng.tensor_tensor(G.S_all[:, :], G.S_all[:, :],
                                         A(pb[7], 0, [[512, 128], [64, 8], [1, 64]]), ALU.add)
            em.op("vector", f, waits=[w_snap, w_ds], inc=dv)
            last_ds = dv.v
        prev_pair_mm = mm.v

    for js in range(0, NJ, 2):
        em.op("gpsimd", (lambda eng, js=js: eng.dma_start(
            A(s_d, t0 + js * TOK, [[NJ * TOK, 128], [TOK, 2], [1, TT]]),
            A(bB, js * TT, [[PITCH, 128], [TT, 2], [1, TT]]))),
            waits=[dv.at()] if js == 0 else [], inc=G.ss, k=16)

    em.flush()


def _l1_finalize(nc, G, send_d, dcum_d):
    em = Em(nc)

    em.op("gpsimd", lambda eng: eng.dma_start(send_d.ap(), G.S_all[:, :]),
          waits=[G.dv.at(), G.sc.at()], inc=G.fin, k=16)
    em.op("gpsimd", lambda eng: eng.dma_start(
        A(dcum_d, 0, [[8, 128], [1024, NCH], [1, 8]]), G.dstage[:, :]), inc=G.fin, k=16)
    em.flush()


def _l1_tern_p1(nc, G, w_d, nrows, ncols, roff):
    dv = G.dv
    vsem = [G.v2, G.v3]
    ncc = ncols // 512
    ntile = (nrows // 128) * ncc
    red_marks = {}
    w_prev = (dv.h, dv.v)
    for n in range(ntile):
        pc, cc = n // ncc, n % ncc
        em = G.tern_em
        dwaits = [(dv.h, red_marks[n - 2])] if n >= 2 else [w_prev]
        em.op("sync", (lambda eng, pc=pc, cc=cc, bi=n % 2: eng.dma_start(
            G.tscr[bi][:], A(w_d, pc * 128 * ncols + cc * 512, [[ncols, 128], [1, 512]]))),
            waits=dwaits, inc=vsem[n % 2], k=16)
        em.op("vector", (lambda eng, n=n, bi=n % 2, roff=roff: eng.tensor_reduce(
            G.tred[:, roff + n:roff + n + 1], G.tscr[bi][:], AX.X, ALU.add,
            apply_absolute_value=True)),
            waits=[vsem[n % 2].at()], inc=dv)
        red_marks[n] = dv.v
        yield


def _l1_ternary(nc, G, w_d, t_out, m_out, nrows, ncols, roff):
    mm, dv, sc = G.mm, G.dv, G.sc
    vsem = [G.v0, G.v1, G.v2, G.v3]
    tsem = [G.t0, G.t1]
    npc = nrows // 128
    ncc = ncols // 512
    ntile = npc * ncc
    assert ntile <= 64

    em = Em(nc)
    em.op("vector", (lambda eng: eng.tensor_reduce(
        G.mcols[:, 0:1], G.tred[:, roff:roff + ntile], AX.X, ALU.add)),
        waits=[G.fin.at()], inc=dv)
    em.op("tensor", lambda eng: eng.matmul(
        G.pb[0][0:1, 0:1], G.mcols[:, 0:1], G.ones_c[:, :], start=True, stop=True),
        waits=[dv.at()], inc=mm)

    em.op("vector", lambda eng: eng.tensor_scalar(
        G.mrow[0:1, 0:1], G.pb[0][0:1, 0:1], 1.0 / (nrows * ncols), None, ALU.mult),
        waits=[mm.at()], inc=dv)

    def f(eng):
        eng.tensor_scalar(G.mrow[0:1, 0:1], G.mrow[0:1, 0:1], 1e-5, None, ALU.max)
        return eng.tensor_scalar(G.mrow[0:1, 1:2], G.mrow[0:1, 0:1], 0.5, None, ALU.mult)
    em.op("vector", f, waits=[dv.at()], inc=dv)
    em.op("tensor", lambda eng: eng.matmul(
        G.pb[1][:, 0:2], G.ones_r[:], G.mrow[0:1, 0:2], start=True, stop=True),
        waits=[dv.at()], inc=mm)
    em.op("vector", lambda eng: eng.tensor_copy(G.mcols[:, :], G.pb[1][:, 0:2]),
          waits=[mm.at()], inc=dv)
    w_m = dv.at()
    em.op("gpsimd", lambda eng: eng.dma_start(m_out.ap(), G.mcols[:, :]),
          waits=[w_m], inc=G.fin, k=16)
    em.flush()

    em = Em(nc)
    sgbs = [G.sgb, G.sgb2]
    thr_marks = {}
    w_prev = (dv.h, dv.v)
    for n in range(ntile):
        pc, cc = n // ncc, n % ncc
        bi = n % 4
        si = n % 2
        dwaits = [(dv.h, thr_marks[n - 4])] if n >= 4 else [w_prev]
        em.op("sync", (lambda eng, pc=pc, cc=cc, bi=bi: eng.dma_start(
            G.scr[bi][:], A(w_d, pc * 128 * ncols + cc * 512, [[ncols, 128], [1, 512]]))),
            waits=dwaits, inc=vsem[bi], k=16)
        w_wl = vsem[bi].at()
        swaits = [w_wl] + ([(dv.h, thr_marks[n - 2])] if n >= 2 else [])
        em.op("scalar", (lambda eng, bi=bi, si=si: eng.sign(
            sgbs[si][:, :], G.scr[bi][:])), waits=swaits, inc=sc)
        w_sg = sc.at()

        def f(eng, bi=bi, n=n, si=si):
            eng.tensor_tensor(G.scr[bi][:], G.scr[bi][:], sgbs[si][:, :], ALU.mult)
            eng.tensor_scalar(G.scr[bi][:], G.scr[bi][:], G.mcols[:, 1:2], None, ALU.is_ge)
            return eng.tensor_tensor(G.tqb[n % 2][:, :], sgbs[si][:, :], G.scr[bi][:],
                                     ALU.mult)
        em.op("vector", f, waits=[w_sg, tsem[n % 2].at()], inc=dv)
        thr_marks[n] = dv.v
        em.op("gpsimd", (lambda eng, n=n, pc=pc, cc=cc: eng.dma_start(
            A(t_out, pc * 128 * ncols + cc * 512, [[ncols, 128], [1, 512]]),
            G.tqb[n % 2][:, :])), waits=[(dv.h, thr_marks[n])], inc=tsem[n % 2], k=16)
    em.flush()




def _l2_dram(nc, kind_in="ExternalInput", kind_out="ExternalOutput", D=None):
    if D is None:
        D = NS()
        D.x = nc.dram_tensor("x", [128, NJ, TOK], F32, kind=kind_in)
        D.xn = nc.dram_tensor("xn", [128, NJ, TOK], F32, kind=kind_in)
        D.s = nc.dram_tensor("s", [128, NJ, TOK], F32, kind=kind_in)
        D.qh = nc.dram_tensor("qh", [128, NJ, TOK], F32, kind=kind_in)
        D.lent = nc.dram_tensor("lent", [NCH, 128, 512], F32, kind=kind_in)
        D.dcum = nc.dram_tensor("dcum", [NCH, 128, 8], F32, kind=kind_in)
        D.t1 = nc.dram_tensor("t1", [DM, HID], F32, kind=kind_in)
        D.t2 = nc.dram_tensor("t2", [HID, DM], F32, kind=kind_in)
        D.m1 = nc.dram_tensor("m1", [128, 2], F32, kind=kind_in)
        D.m2 = nc.dram_tensor("m2", [128, 2], F32, kind=kind_in)
        D.sp = nc.dram_tensor("spart", [128, 512], F32, kind=kind_in)
    D.l1g = nc.dram_tensor("l1g", [128, NJ], F32, kind="ExternalInput")
    D.l1b = nc.dram_tensor("l1b", [128, NJ], F32, kind="ExternalInput")
    D.b1 = nc.dram_tensor("b1", [128, 32], F32, kind="ExternalInput")
    D.l2g = nc.dram_tensor("l2g", [128, 32], F32, kind="ExternalInput")
    D.l2b = nc.dram_tensor("l2b", [128, 32], F32, kind="ExternalInput")
    D.b2 = nc.dram_tensor("b2", [128, NJ], F32, kind="ExternalInput")
    D.wk1 = nc.dram_tensor("wk1", [DM, DM], F32, kind="ExternalInput")
    D.wv1 = nc.dram_tensor("wv1", [DM, DM], F32, kind="ExternalInput")
    D.x1 = nc.dram_tensor("x1", [128, NJ, TOK], F32, kind=kind_out)
    D.xn1 = nc.dram_tensor("xn1", [128, NJ, TOK], F32, kind=kind_out)
    D.k1s = [nc.dram_tensor(f"k1_{t}", [128, NJ, TT], F32, kind=kind_out)
             for t in range(NT)]
    D.v1s = [nc.dram_tensor(f"v1_{t}", [128, 4, DM], F32, kind=kind_out)
             for t in range(NT)]
    return D


def _l2_emit(nc, st, sems, D):
    E = st.enter_context
    g = {}
    for nm in ("bS", "bQH", "bXN", "bX", "bG"):
        g[nm] = E(nc.sbuf_tensor(nm, [128, PITCH], F32))
    g["bH"] = E(nc.sbuf_tensor("bH", [128, 32 * 512], F32))
    g["lh_hi"] = E(nc.sbuf_tensor("lh_hi", [128, PITCH], BF16))
    g["lh_lo"] = E(nc.sbuf_tensor("lh_lo", [128, PITCH], BF16))
    g["hh"] = [E(nc.sbuf_tensor(f"hh{i}", [128, 512], BF16)) for i in range(2)]
    g["hl"] = [E(nc.sbuf_tensor(f"hl{i}", [128, 512], BF16)) for i in range(2)]
    g["t1b"] = [E(nc.sbuf_tensor(f"t1b{i}", [128, NJ * 128], BF16)) for i in range(2)]
    g["t2b"] = [E(nc.sbuf_tensor(f"t2b{i}", [128, NJ * 128], BF16)) for i in range(2)]
    g["wbuf"] = [E(nc.sbuf_tensor(f"wbuf{i}", [128, NJ * 128], F32)) for i in range(2)]
    g["scr"] = [E(nc.sbuf_tensor(f"scr{i}", [128, 512], F32)) for i in range(4)]
    g["lbuf"] = [E(nc.sbuf_tensor(f"lbuf{i}", [128, 512], F32)) for i in range(2)]
    g["sentb"] = E(nc.sbuf_tensor("sentb", [128, 512], F32))
    g["spart"] = E(nc.sbuf_tensor("spartb", [128, 512], F32))
    g["dcum"] = E(nc.sbuf_tensor("dcumb", [128, NCH * 8], F32))
    g["l1g"] = E(nc.sbuf_tensor("l1gb", [128, NJ], F32))
    g["l1b"] = E(nc.sbuf_tensor("l1bb", [128, NJ], F32))
    g["b1"] = E(nc.sbuf_tensor("b1b", [128, 32], F32))
    g["l2g"] = E(nc.sbuf_tensor("l2gb", [128, 32], F32))
    g["l2b"] = E(nc.sbuf_tensor("l2bb", [128, 32], F32))
    g["b2"] = E(nc.sbuf_tensor("b2b", [128, NJ], F32))
    g["m1"] = E(nc.sbuf_tensor("m1b", [128, 2], F32))
    g["m2"] = E(nc.sbuf_tensor("m2b", [128, 2], F32))
    g["ones_c"] = E(nc.sbuf_tensor("onesc", [128, 1], F32))
    g["ones_r"] = E(nc.sbuf_tensor("onesr", [1, 128], F32))
    g["cns"] = E(nc.sbuf_tensor("cns", [128, 2], F32))
    g["flagb"] = E(nc.sbuf_tensor("flagb", [128, 1], F32))
    g["pb"] = [E(nc.psum_tensor(f"pb{i}", [128, 512], F32)) for i in range(8)]
    g.update(sems)
    G = type("G", (), g)

    em = Em(nc)
    loads = [
        (G.dcum[:], A(D.dcum, 0, [[8, 128], [1024, NCH], [1, 8]])),
        (G.l1g[:], D.l1g.ap()), (G.l1b[:], D.l1b.ap()), (G.b1[:], D.b1.ap()),
        (G.l2g[:], D.l2g.ap()), (G.l2b[:], D.l2b.ap()), (G.b2[:], D.b2.ap()),
        (G.m1[:], D.m1.ap()), (G.m2[:], D.m2.ap()),
    ]
    spw = []
    if getattr(D, "sp_gate", None) is not None:
        spw = [D.sp_gate]
    em.op("sync", (lambda eng: eng.dma_start(G.spart[:], D.sp_ap)), waits=spw,
          inc=G.c, k=16)
    for dst, src_ in loads:
        em.op("sync", (lambda eng, a=dst, b=src_: eng.dma_start(a, b)), inc=G.c, k=16)
    if getattr(D, "flag", None) is not None:
        em.op("sync", (lambda eng: eng.dma_start(G.flagb[:], D.flag.ap())), inc=G.c, k=16)

    def f(eng):
        eng.memset(G.ones_c[:], 1.0)
        eng.memset(G.ones_r[:], 1.0)
        return eng.memset(G.cns[:, 0:1], float(LNEPS))
    em.op("vector", f, inc=G.dv)
    if getattr(D, "flag", None) is not None:
        em.op("vector", lambda eng: eng.tensor_scalar(
            G.spart[:, :], G.spart[:, :], G.flagb[:, 0:1], None, ALU.mult),
            waits=[G.c.at()], inc=G.dv)
    em.flush()

    cc2 = getattr(G, "cc2", None)
    import os
    cc_defer = bool(int(os.environ.get("KCC_DEFER", "0")))
    pend = []
    for t in range(NT):
        with nc.named_scope(f"l2t{t}"):
            _l2_tile(nc, G, t, D.x, D.xn, D.s, D.qh, D.lent, D.t1, D.t2,
                     D.wk1, D.wv1, D.x1, D.xn1, D.k1s[t], D.v1s[t], D)
        if getattr(D, "kgs", None) is not None:
            w_st = (G.sst.h, G.sst.v)
            pairs = [(D.k1s[t], D.kgs[t], cc2), (D.v1s[t], D.vgs[t], cc2)]
            if t == NT - 1:
                pairs.append((D.xn1col, D.bcg, G.cc1))
            if cc_defer:
                pend.append((w_st, pairs))
                continue
            em = Em(nc)
            for a, b, cs in pairs:
                em.op("gpsimd", (lambda eng, a=a, b=b: eng.collective_compute(
                    "AllGather", ALU.bypass, replica_groups=RG_PAIRS,
                    ins=[a.ap().opt()], outs=[b.ap().opt()])),
                    waits=[w_st], inc=cs, k=1)
            em.flush()
    if pend:
        em = Em(nc)
        w_fin = (G.sst.h, G.sst.v)
        for _, pairs in pend:
            for a, b, cs in pairs:
                em.op("gpsimd", (lambda eng, a=a, b=b: eng.collective_compute(
                    "AllGather", ALU.bypass, replica_groups=RG_PAIRS,
                    ins=[a.ap().opt()], outs=[b.ap().opt()])),
                    waits=[w_fin], inc=cs, k=1)
        em.flush()
    return G


def build_launch2():
    nc = bass.Bass("TRN2", target_bir_lowering=False)
    D = _l2_dram(nc)
    D.sp_ap = D.sp.ap()
    D.flag = None
    with contextlib.ExitStack() as st:
        sems = {nm: Sem(st, nc, nm) for nm in L2_SEMS}
        SER_SEMS.clear()
        SER_SEMS.extend([sems["mm"], sems["dv"], sems["sc"]])
        with contextlib.ExitStack() as st_b:
            G = _l2_emit(nc, st_b, sems, D)
        em = Em(nc)
        em.op("gpsimd", lambda eng: eng.nop(),
              waits=[G.sst.at(), G.l0.at(), G.l1.at()])
        em.flush()
    return nc


def _l2_ln_stats(nc, em, G, src_ap, j, nch, gate, st):
    mm, sc = G.mm, G.sc
    pb = G.pb
    waits = list(gate)
    if j >= 2:
        waits.append((mm.h, st[j - 2]))
    em.op("scalar", (lambda eng: eng.activation(
        G.scr[j % 2][:], src_ap, AF.Square)), waits=waits, inc=sc)

    def f(eng, j=j, nch=nch, src_ap=src_ap):
        eng.matmul(pb[0][0:1, :], G.ones_c[:], src_ap,
                   start=(j == 0), stop=(j == nch - 1))
        return eng.matmul(pb[1][0:1, :], G.ones_c[:], G.scr[j % 2][:],
                          start=(j == 0), stop=(j == nch - 1))
    em.op("tensor", f, waits=[sc.at()], inc=mm)
    st[j] = mm.v


def _l2_ln(nc, em, G, srcv, nch, gate_waits):
    st = {}
    for j in range(nch):
        _l2_ln_stats(nc, em, G, srcv(j), j, nch,
                     (list(gate_waits) if j == 0 else []), st)
    return _l2_ln_fin(nc, em, G, nch)


def _l2_ln_fin(nc, em, G, nch):
    mm, dv, sc = G.mm, G.dv, G.sc
    pb = G.pb
    inv = 1.0 / (nch * 128)

    def row(off, n=512):
        buf = (G.scr[2], G.scr[3], G.sentb)[off // 512]
        return A(buf, 0, [[512, 1], [1, n]])

    def f(eng, inv=inv):
        eng.tensor_scalar(row(0), pb[0][0:1, :], inv, None, ALU.mult)
        eng.tensor_scalar(row(512), pb[1][0:1, :], inv, None, ALU.mult)
        eng.tensor_tensor(row(1024), row(0), row(0), ALU.mult)
        return eng.tensor_tensor(row(512), row(512), row(1024), ALU.subtract)
    em.op("vector", f, waits=[mm.at()], inc=dv)

    def f(eng):
        eng.activation(row(512), row(512), AF.Ln, bias=G.cns[0:1, 0:1])
        return eng.activation(row(512), row(512), AF.Exp, scale=-0.5)
    em.op("scalar", f, waits=[dv.at()], inc=sc)

    em.op("vector", lambda eng: eng.scalar_tensor_tensor(
        row(1024), row(0), -1.0, row(512), ALU.mult, ALU.mult),
        waits=[sc.at()], inc=dv)

    def f(eng):
        eng.matmul(pb[2][:, :], G.ones_r[:], row(512), start=True, stop=True)
        return eng.matmul(pb[3][:, :], G.ones_r[:], row(1024), start=True, stop=True)
    em.op("tensor", f, waits=[dv.at()], inc=mm)
    return mm.at()


def _l2_tile(nc, G, t, x_d, xn_d, s_d, qh_d, lent_d, t1_d, t2_d,
             wk1_d, wv1_d, x1_d, xn1_d, k1_d, v1_d, D=None):
    mm, dv, sc = G.mm, G.dv, G.sc
    pb = G.pb
    t0 = t * TT
    lsem = [G.l0, G.l1]
    tsem = [G.t0, G.t1]
    wsem = [G.w0, G.w1]
    vsem = [G.v0, G.v1]

    def xv(buf, j, lo=0, n=TT):
        return A(buf, j * TT + lo, [[PITCH, 128], [1, n]])

    def hv(oc, lo=0, n=512):
        return A(G.bH, oc * 512 + lo, [[32 * 512, 128], [1, n]])

    em = Em(nc)
    for js in range(0, NJ, 2):
        em.op("sync", (lambda eng, js=js: eng.dma_start(
            A(G.bS, js * TT, [[PITCH, 128], [TT, 2], [1, TT]]),
            A(s_d, t0 + js * TOK, [[NJ * TOK, 128], [TOK, 2], [1, TT]]))),
            waits=[G.c.at(), G.sst.at()] if js == 0 else [], inc=G.c, k=16)
    for js in range(0, NJ, 2):
        em.op("sync", (lambda eng, js=js: eng.dma_start(
            A(G.bQH, js * TT, [[PITCH, 128], [TT, 2], [1, TT]]),
            A(qh_d, t0 + js * TOK, [[NJ * TOK, 128], [TOK, 2], [1, TT]]))),
            waits=[G.sst.at()] if js == 0 else [], inc=G.c, k=16)
    w_sq = G.c.at()

    em.flush()

    em = Em(nc)
    for js in range(0, NJ, 2):
        em.op("sync", (lambda eng, js=js: eng.dma_start(
            A(G.bXN, js * TT, [[PITCH, 128], [TT, 2], [1, TT]]),
            A(xn_d, t0 + js * TOK, [[NJ * TOK, 128], [TOK, 2], [1, TT]]))),
            inc=G.c, k=16)
    w_xn = G.c.at()

    w_bc = _l2_ln(nc, em, G, lambda j: xv(G.bXN, j), NJ, [w_xn])

    for j in range(NJ):
        def f(eng, j=j):
            eng.tensor_tensor(xv(G.bX, j), xv(G.bXN, j), pb[2][:, :], ALU.mult)
            eng.tensor_tensor(xv(G.bX, j), xv(G.bX, j), pb[3][:, :], ALU.add)
            return eng.tensor_scalar(xv(G.bX, j), xv(G.bX, j),
                                     G.l1g[:, j:j + 1], G.l1b[:, j:j + 1],
                                     ALU.mult, ALU.add)
        em.op("vector", f, waits=[w_bc] if j == 0 else [], inc=dv)

        def f(eng, j=j):
            return eng.tensor_copy(A(G.lh_hi, j * TT, [[PITCH, 128], [1, TT]]),
                                   xv(G.bX, j))
        em.op("vector", f, inc=dv)
    w_lh = dv.at()

    up_ev = {}
    ln2_st = {}
    for oc in range(32):
        bi = oc % 2
        dwaits = [(mm.h, up_ev.get(oc - 2, (0, 0))[0])] if oc >= 2 else []
        em.op("gpsimd", (lambda eng, oc=oc, bi=bi: eng.dma_start(
            G.t1b[bi][:], A(t1_d, 128 * oc, [[HID, 128], [128 * HID, NJ], [1, 128]]))),
            waits=dwaits, inc=tsem[bi], k=16)
        w_t = tsem[bi].at()
        bank = pb[4 + bi]
        mwaits = [w_t] + ([w_lh] if oc < 2 else [])
        if oc >= 2:
            mwaits.append((sc.h, up_ev[oc - 2][1]))

        def f(eng, bi=bi, bank=bank):
            for kj in range(NJ):
                ii = eng.matmul(bank[:, :], A(G.t1b[bi], kj * 128, [[NJ * 128, 128], [1, 128]]),
                                A(G.lh_hi, kj * TT, [[PITCH, 128], [1, TT]]),
                                start=(kj == 0), stop=(kj == NJ - 1))
            return ii
        em.op("tensor", f, waits=mwaits, inc=mm)
        w_up = mm.v
        if oc >= 1:
            _l2_ln_stats(nc, em, G, hv(oc - 1), oc - 1, 32, [], ln2_st)
        em.op("scalar", (lambda eng, oc=oc, bank=bank: eng.activation(
            hv(oc), bank[:, :], AF.Gelu, bias=G.b1[:, oc:oc + 1], scale=G.m1[:, 0:1])),
            waits=[(mm.h, w_up)], inc=sc)
        up_ev[oc] = (w_up, sc.v)
    _l2_ln_stats(nc, em, G, hv(31), 31, 32, [], ln2_st)
    em.flush()

    em = Em(nc)
    last_sadd = 0
    sent_marks = {}
    for c in range(CPT):
        n_glob = t * CPT + c
        ci = c % 2
        em.op("sync", (lambda eng, ci=ci, n_glob=n_glob: eng.dma_start(
            G.lbuf[ci][:], A(lent_d, n_glob * 128 * 512, [[512, 128], [1, 512]]))),
            waits=[(G.dv.h, sent_marks[c - 2])] if c >= 2 else [], inc=lsem[ci], k=16)
        w_l = lsem[ci].at()

        def f(eng, n_glob=n_glob, ci=ci):
            eng.tensor_tensor(G.sentb[:, :], G.spart[:, :],
                              A(G.dcum, n_glob * 8, [[NCH * 8, 128], [1, 8], [0, 64]]),
                              ALU.mult)
            return eng.tensor_tensor(G.sentb[:, :], G.sentb[:, :], G.lbuf[ci][:, :],
                                     ALU.add)
        em.op("vector", f, waits=[w_l, (mm.h, mm.v)], inc=dv)
        w_sent = dv.at()
        sent_marks[c] = dv.v
        for par in (0, 1):
            def f(eng, c=c, par=par):
                for h in range(par, H, 2):
                    ii = eng.matmul(
                        A(pb[6], (h % 2) * 64 * 512 + (h // 2) * 64, [[512, 64], [1, 64]]),
                        A(G.sentb, (h % 2) * 64 * 512 + (h // 2) * 64, [[512, 64], [1, 64]]),
                        A(G.bQH, (h % 2) * 64 * PITCH + (h // 2) * TT + c * 64,
                          [[PITCH, 64], [1, 64]]),
                        start=True, stop=True,
                        tile_position=(64 * (h % 2), 64 * (h % 2)))
                return ii
            em.op("tensor", f,
                  waits=([w_sent, w_sq, (dv.h, last_sadd)] if par == 0 else []), inc=mm)
        em.op("vector", (lambda eng, c=c: eng.tensor_tensor(
            A(G.bS, c * 64, [[PITCH, 128], [TT, NJ], [1, 64]]),
            A(G.bS, c * 64, [[PITCH, 128], [TT, NJ], [1, 64]]),
            A(pb[6], 0, [[512, 128], [64, NJ], [1, 64]]), ALU.add)),
            waits=[mm.at()], inc=dv)
        last_sadd = dv.v
    G.w_pa = (dv.h, dv.v)
    em.flush()




    em = Em(nc)
    w_bc2 = _l2_ln_fin(nc, em, G, 32)

    for oc in range(32):
        def f(eng, oc=oc):
            eng.tensor_tensor(hv(oc), hv(oc), pb[2][:, :], ALU.mult)
            eng.tensor_tensor(hv(oc), hv(oc), pb[3][:, :], ALU.add)
            return eng.tensor_scalar(hv(oc), hv(oc), G.l2g[:, oc:oc + 1],
                                     G.l2b[:, oc:oc + 1], ALU.mult, ALU.add)
        em.op("vector", f, waits=[w_bc2] if oc == 0 else [], inc=dv)
    w_n2 = dv.at()

    dn_marks = {}
    for oc in range(32):
        bi = oc % 2
        swaits = [w_n2] if oc < 2 else [(mm.h, dn_marks[oc - 2])]

        def f(eng, oc=oc, bi=bi):
            return eng.tensor_copy(G.hh[bi][:, :], hv(oc))
        em.op("vector", f, waits=swaits, inc=dv)
        w_split = dv.at()
        em.op("gpsimd", (lambda eng, oc=oc, bi=bi: eng.dma_start(
            G.t2b[bi][:], A(t2_d, 128 * oc * DM, [[DM, 128], [1, NJ * 128]]))),
            waits=[(mm.h, dn_marks.get(oc - 2, 0))], inc=tsem[bi], k=16)
        w_t2 = tsem[bi].at()

        def f(eng, oc=oc, bi=bi):
            for oj in range(NJ):
                ii = eng.matmul(pb[oj][:, :], A(G.t2b[bi], oj * 128, [[NJ * 128, 128], [1, 128]]),
                                G.hh[bi][:, :], start=(oc == 0), stop=(oc == 31))
            return ii
        em.op("tensor", f,
              waits=[w_split, w_t2] + ([G.w_pa] if oc == 0 else []), inc=mm)
        dn_marks[oc] = mm.v
    w_dn = mm.at()

    for oj in range(NJ):
        em.op("scalar", (lambda eng, oj=oj: eng.activation(
            xv(G.bG, oj), pb[oj][:, :], AF.Silu, bias=G.b2[:, oj:oj + 1],
            scale=G.m2[:, 0:1])), waits=[w_dn] if oj == 0 else [], inc=sc)
    em.flush()

    em = Em(nc)
    for js in range(0, NJ, 2):
        em.op("sync", (lambda eng, js=js: eng.dma_start(
            A(G.bX, js * TT, [[PITCH, 128], [TT, 2], [1, TT]]),
            A(x_d, t0 + js * TOK, [[NJ * TOK, 128], [TOK, 2], [1, TT]]))),
            waits=[G.sst.at()] if js == 0 else [], inc=G.c, k=16)
    w_x = G.c.at()

    ln3_st = {}
    cmb = {}
    for j in range(NJ):
        def f(eng, j=j):
            eng.tensor_tensor(xv(G.bS, j), xv(G.bS, j), xv(G.bG, j), ALU.mult)
            return eng.tensor_tensor(xv(G.bX, j), xv(G.bX, j), xv(G.bS, j), ALU.add)
        em.op("vector", f, waits=[w_x, sc.at(), G.w_pa] if j == 0 else [], inc=dv)
        cmb[j] = dv.v
        if j >= 1:
            _l2_ln_stats(nc, em, G, xv(G.bX, j - 1), j - 1, NJ,
                         [(dv.h, cmb[j - 1])], ln3_st)
    _l2_ln_stats(nc, em, G, xv(G.bX, NJ - 1), NJ - 1, NJ,
                 [(dv.h, cmb[NJ - 1])], ln3_st)
    w_x1 = dv.at()
    for js in range(0, NJ, 2):
        em.op("gpsimd", (lambda eng, js=js: eng.dma_start(
            A(x1_d, t0 + js * TOK, [[NJ * TOK, 128], [TOK, 2], [1, TT]]),
            A(G.bX, js * TT, [[PITCH, 128], [TT, 2], [1, TT]]))),
            waits=[w_x1] if js == 0 else [], inc=G.sst, k=16)

    w_bc3 = _l2_ln_fin(nc, em, G, NJ)
    mmD = (mm.h, mm.v)
    for j in range(NJ):
        def f(eng, j=j):
            eng.tensor_tensor(xv(G.bG, j), xv(G.bX, j), pb[2][:, :], ALU.mult)
            return eng.tensor_tensor(xv(G.bG, j), xv(G.bG, j), pb[3][:, :], ALU.add)
        em.op("vector", f, waits=[w_bc3] if j == 0 else [], inc=dv)
    w_xn1 = dv.at()
    for j in range(NJ):
        em.op("vector", (lambda eng, j=j: eng.tensor_copy(
            A(G.lh_hi, j * TT, [[PITCH, 128], [1, TT]]), xv(G.bG, j))),
            waits=[w_xn1, mmD] if j == 0 else [], inc=dv)
    w_xn1b = dv.at()
    for js in range(0, NJ, 2):
        em.op("gpsimd", (lambda eng, js=js: eng.dma_start(
            A(xn1_d, t0 + js * TOK, [[NJ * TOK, 128], [TOK, 2], [1, TT]]),
            A(G.bG, js * TT, [[PITCH, 128], [TT, 2], [1, TT]]))),
            waits=[w_xn1] if js == 0 else [], inc=G.sst, k=16)
    if t == NT - 1 and D is not None and getattr(D, "xn1col", None) is not None:
        em.op("vector", lambda eng: eng.tensor_copy(
            G.sentb[:, 0:NJ], A(G.bG, TT - 1, [[PITCH, 128], [TT, NJ]])),
            waits=[w_xn1], inc=dv)
        em.op("vector", lambda eng: eng.memset(G.sentb[:, NJ:16], 0.0), inc=dv)
        em.op("vector", lambda eng: eng.memset(G.sentb[:, 16:32], 0.0), inc=dv)
        em.op("gpsimd", lambda eng: eng.dma_start(
            D.xn1col.ap(), G.sentb[:, 0:NJ]), waits=[dv.at()], inc=G.sst, k=16)

    grp = {}
    ev = {}
    kwc = {}
    prevc = getattr(G, "l2_wcp", None)
    for oc in range(NJ):
        bi = oc % 2
        dwaits = ([(dv.h, kwc[oc - 2])] if oc >= 2 else
                  ([(dv.h, prevc)] if prevc is not None else []))
        em.op("sync", (lambda eng, oc=oc, bi=bi: eng.dma_start(
            G.wbuf[bi][:], A(wk1_d, 128 * oc, [[DM, 128], [128 * DM, NJ], [1, 128]]))),
            waits=dwaits, inc=wsem[bi], k=16)
        w_w = wsem[bi].at()
        cwaits = [w_w] + ([(mm.h, grp[oc - 2])] if oc >= 2 else [mmD])
        em.op("vector", (lambda eng, bi=bi: eng.tensor_copy(
            G.t1b[bi][:, :], G.wbuf[bi][:, :])), waits=cwaits, inc=dv)
        kwc[oc] = dv.v
        w_wc = dv.at()
        bank = pb[4 + bi]
        mwaits = [w_wc] + ([w_xn1b] if oc < 2 else []) + (
            [(dv.h, ev[oc - 2])] if oc >= 2 else [])

        def f(eng, bi=bi, bank=bank):
            for kj in range(NJ):
                ii = eng.matmul(bank[:, :], A(G.t1b[bi], kj * 128, [[NJ * 128, 128], [1, 128]]),
                                A(G.lh_hi, kj * TT, [[PITCH, 128], [1, TT]]),
                                start=(kj == 0), stop=(kj == NJ - 1))
            return ii
        em.op("tensor", f, waits=mwaits, inc=mm)
        grp[oc] = mm.v
        em.op("vector", (lambda eng, oc=oc, bank=bank: eng.tensor_copy(
            xv(G.bS, oc), bank[:, :])), waits=[mm.at()], inc=dv)
        ev[oc] = dv.v
    G.l2_wcp = kwc[NJ - 1]
    for js in range(0, NJ, 2):
        em.op("gpsimd", (lambda eng, js=js: eng.dma_start(
            A(k1_d, js * TT, [[NJ * TT, 128], [TT, 2], [1, TT]]),
            A(G.bS, js * TT, [[PITCH, 128], [TT, 2], [1, TT]]))),
            waits=[dv.at()] if js == 0 else [], inc=G.sst, k=16)

    vmm = {}
    vev = {}
    vwc = {}
    for half in range(2):
        for kj in range(NJ):
            n = half * NJ + kj
            bi = n % 2
            dwaits = [(dv.h, vwc[n - 2])] if n >= 2 else [(dv.h, dv.v)]
            em.op("sync", (lambda eng, kj=kj, half=half, bi=bi: eng.dma_start(
                G.scr[bi][:], A(wv1_d, kj * 128 * DM + half * 512, [[DM, 128], [1, 512]]))),
                waits=dwaits, inc=vsem[bi], k=16)
            w_v = vsem[bi].at()
            cwaits = [w_v] + ([(mm.h, vmm[n - 2])] if n >= 2 else [mmD])
            em.op("vector", (lambda eng, bi=bi: eng.tensor_copy(
                G.hh[bi][:, :], G.scr[bi][:, :])), waits=cwaits, inc=dv)
            vwc[n] = dv.v
            w_vc = dv.at()
            mwaits = [w_vc] + ([w_xn1b] if n < 2 else [])
            if kj == 0 and half == 0:
                mwaits.append((dv.h, dv.v))
            if kj == 0 and half == 1:
                mwaits.append((dv.h, vev[0]))

            def f(eng, kj=kj, bi=bi):
                for ts in range(4):
                    ii = eng.matmul(pb[ts][:, :],
                                    A(G.lh_hi, kj * TT + ts * 128, [[PITCH, 128], [1, 128]]),
                                    G.hh[bi][:, :], start=(kj == 0), stop=(kj == NJ - 1))
                return ii
            em.op("tensor", f, waits=mwaits, inc=mm)
            vmm[n] = mm.v
        for ts in range(4):
            em.op("vector", (lambda eng, ts=ts, half=half: eng.tensor_copy(
                A(G.bQH, ts * 1024 + half * 512, [[PITCH, 128], [1, 512]]),
                pb[ts][:, :])), waits=[(mm.h, vmm[half * NJ + NJ - 1])], inc=dv)
        vev[half] = dv.v
    for js in range(4):
        em.op("gpsimd", (lambda eng, js=js: eng.dma_start(
            A(v1_d, js * DM, [[4 * DM, 128], [1, DM]]),
            A(G.bQH, js * DM, [[PITCH, 128], [1, DM]]))),
            waits=[dv.at()] if js == 0 else [], inc=G.sst, k=16)

    em.flush()


def l2_inputs(l1out, ins, spart):
    return {
        "x": l1out["__x"], "xn": l1out["xn"], "s": l1out["s"], "qh": l1out["qh"],
        "lent": l1out["lent"], "dcum": l1out["dcum"], "spart": spart,
        "t1": l1out["t1"], "t2": l1out["t2"], "m1": l1out["m1"], "m2": l1out["m2"],
        "l1g": _vec128(ins["L0_ffn_ln1_g"]), "l1b": _vec128(ins["L0_ffn_ln1_b"]),
        "b1": _vec128(ins["L0_ffn_b1"]), "l2g": _vec128(ins["L0_ffn_ln2_g"]),
        "l2b": _vec128(ins["L0_ffn_ln2_b"]), "b2": _vec128(ins["L0_ffn_b2"]),
        "wk1": np.ascontiguousarray(ins["L1_attn_Wk"].transpose(1, 0, 2).reshape(DM, DM)),
        "wv1": np.ascontiguousarray(ins["L1_attn_Wv"].transpose(1, 0, 2).reshape(DM, DM)),
    }




KTOT = 2 * TOK


def _l3_dram(nc, kind_in="ExternalInput", kind_out="ExternalOutput", D=None):
    import os
    if D is None:
        D = NS()
        D.x1 = nc.dram_tensor("x1", [128, NJ, TOK], F32, kind=kind_in)
        D.xn1 = nc.dram_tensor("xn1", [128, NJ, TOK], F32, kind=kind_in)
        D.kf = nc.dram_tensor("kf", [128, NJ, KTOT], F32, kind=kind_in)
        D.vf = nc.dram_tensor("vf", [128, 32, DM], F32, kind=kind_in)
        D.bcol = nc.dram_tensor("bcol", [128, NJ], F32, kind=kind_in)
    D.wq1 = nc.dram_tensor("wq1", [DM, DM], F32, kind="ExternalInput")
    D.wr = nc.dram_tensor("wr", [DM, DM], F32, kind="ExternalInput")
    D.wkr = nc.dram_tensor("wkr", [DM, DM], F32, kind="ExternalInput")
    D.wvr = nc.dram_tensor("wvr", [DM, DM], F32, kind="ExternalInput")
    D.mur = nc.dram_tensor("mur", [128, NJ], F32, kind="ExternalInput")
    D.muk = nc.dram_tensor("muk", [128, NJ], F32, kind="ExternalInput")
    D.id128 = nc.dram_tensor("id128", [128, 128], F32, kind="ExternalInput")
    D.out = nc.dram_tensor("out", [128, NJ, TOK], F32, kind=kind_out)
    D.s1 = nc.dram_tensor("s1scratch", [128, NJ, TOK], F32)
    dbg = bool(int(os.environ.get("L3_DBG", "0")))
    D.dbg = nc.dram_tensor("dbg", [128, TOK], F32, kind="ExternalOutput") if dbg else None
    D.dbg2 = (nc.dram_tensor("dbg2", [128, NJ, TOK], F32, kind="ExternalOutput")
              if dbg else None)
    return D


def _l3_emit(nc, st, sems, D):
    E = st.enter_context
    s1_dram = D.s1
    g = {}
    g["bXN"] = E(nc.sbuf_tensor("bXN", [128, NJ * TOK], F32))
    g["id128"] = E(nc.sbuf_tensor("id128s", [128, 128], F32))
    g["wbuf"] = [E(nc.sbuf_tensor(f"wbuf{i}", [128, NJ * 128], F32)) for i in range(2)]
    g["mur"] = E(nc.sbuf_tensor("murb", [128, NJ], F32))
    g["muk"] = E(nc.sbuf_tensor("mukb", [128, NJ], F32))
    g["bcol"] = E(nc.sbuf_tensor("bcolb", [128, NJ], F32))
    g["flagb"] = E(nc.sbuf_tensor("flagb", [128, 1], F32))
    g["pb"] = [E(nc.psum_tensor(f"pb{i}", [128, 512], F32)) for i in range(8)]
    g.update(sems)

    with contextlib.ExitStack() as st_a:
        Ea = st_a.enter_context
        g["Kb"] = [Ea(nc.sbuf_tensor(f"Kb{i}", [128, KTOT], F32)) for i in range(2)]
        g["Kbr"] = Ea(nc.sbuf_tensor("Kbr", [128, KTOT], F32R))
        g["Qbr"] = Ea(nc.sbuf_tensor("Qbr", [128, TOK], F32R))
        g["bXNb"] = Ea(nc.sbuf_tensor("bXNb", [128, NJ * 512], BF16))
        g["wqb"] = Ea(nc.sbuf_tensor("wqb", [128, NJ * 128], BF16))
        g["Ve"] = Ea(nc.sbuf_tensor("Ve", [128, 32 * 65], F32))
        g["Qb"] = Ea(nc.sbuf_tensor("Qb", [128, TOK], F32))
        g["Qe"] = Ea(nc.sbuf_tensor("Qe", [65, TOK], F32))
        g["Ke"] = Ea(nc.sbuf_tensor("Ke", [65, KTOT], F32))
        g["Pt"] = Ea(nc.sbuf_tensor("Pt", [128, 1024], F32))
        g["msb"] = Ea(nc.sbuf_tensor("msb", [128, 16], F32))
        g["m16"] = Ea(nc.sbuf_tensor("m16", [16, 128], F32))
        g["mc"] = Ea(nc.sbuf_tensor("mc", [128, 128], F32))
        g["lb"] = Ea(nc.sbuf_tensor("lb", [128, 16], F32))
        g["rl"] = Ea(nc.sbuf_tensor("rl", [128, 16], F32))
        g["rlr"] = Ea(nc.sbuf_tensor("rlr", [1, 512], F32))
        g["rlb"] = Ea(nc.sbuf_tensor("rlb", [64, 512], F32))
        g["stage"] = [Ea(nc.sbuf_tensor(f"stg{i}", [64, 512], F32)) for i in range(2)]
        g["ones_r"] = Ea(nc.sbuf_tensor("onesr3", [1, 64], F32))
        g["pvs"] = {"evac": 0, "rlb": 0, "st": {}, "idx": 0}
        g["dbg_d"] = D.dbg
        G = type("G", (), g)

        em = Em(nc)
        for dst, src_ in (
            (G.id128[:], D.id128.ap()), (G.mur[:], D.mur.ap()),
            (G.muk[:], D.muk.ap()),
        ):
            em.op("sync", (lambda eng, a=dst, b=src_: eng.dma_start(a, b)), inc=G.c, k=16)
        if getattr(D, "bcol", None) is not None:
            em.op("sync", lambda eng: eng.dma_start(G.bcol[:], D.bcol.ap()),
                  inc=G.x, k=16)
        else:
            em.op("sync", lambda eng: eng.dma_start(
                G.bcol[:], A(D.bcg, 0, [[NJ, 128], [1, NJ]])),
                waits=[D.bc_gate], inc=G.x, k=16)
            em.op("sync", lambda eng: eng.dma_start(G.flagb[:], D.flag.ap()),
                  inc=G.x, k=16)
            em.op("vector", lambda eng: eng.tensor_scalar(
                G.bcol[:, :], G.bcol[:, :], G.flagb[:, 0:1], None, ALU.mult),
                waits=[G.x.at()], inc=G.dv)
        for js in range(NJ):
            em.op("sync", (lambda eng, js=js: eng.dma_start(
                A(G.bXN, js * TOK, [[NJ * TOK, 128], [1, TOK]]),
                A(D.xn1, js * TOK, [[NJ * TOK, 128], [1, TOK]]))), inc=G.c, k=16)
        em.op("vector", lambda eng: eng.memset(G.Ve[:], 1.0), inc=G.dv)
        em.op("vector", lambda eng: eng.memset(G.Ke[64:65, :], 1.0), inc=G.dv)
        em.op("vector", lambda eng: eng.memset(G.ones_r[:, :], 1.0), inc=G.dv)
        em.flush()

        pair_done = {}
        for jp in range(8):
            with nc.named_scope(f"hp{jp}"):
                _l3_headpair(nc, G, jp, D, s1_dram, pair_done)

    if D.dbg2 is not None:
        em = Em(nc)
        em.op("gpsimd", lambda eng: eng.dma_start(D.dbg2.ap(), s1_dram.ap()),
              waits=[g["s1st"].at()], inc=g["ost"], k=16)
        em.flush()

    with contextlib.ExitStack() as st_b:
        Eb = st_b.enter_context
        for nm in ("bR", "bG", "bX1"):
            g[nm] = Eb(nc.sbuf_tensor(nm, [128, PITCH], F32))
        for nm in ("bMX", "bKA"):
            g[nm] = Eb(nc.sbuf_tensor(nm, [128, PITCH], BF16))
        g["bS1"] = [Eb(nc.sbuf_tensor(f"bS1{i}", [128, PITCH], F32)) for i in range(2)]
        g["wbufb"] = [Eb(nc.sbuf_tensor(f"wbufb3{i}", [128, NJ * 128], BF16))
                      for i in range(2)]
        g["rw_cmb"] = {}
        g["rw_wcp"] = None
        g["rw_mmp"] = None
        G = type("G", (), g)

        for t in range(NT):
            with nc.named_scope(f"rw{t}"):
                _l3_rwkv_tile(nc, G, t, D.x1, D.wr, D.wkr, D.wvr, s1_dram, D.out)

        em = Em(nc)
        em.op("gpsimd", lambda eng: eng.nop(), waits=[G.ost.at()])
        em.flush()
    return type("G", (), g)


def build_launch3():
    nc = bass.Bass("TRN2", target_bir_lowering=False)
    D = _l3_dram(nc)
    with contextlib.ExitStack() as st:
        sems = {nm: Sem(st, nc, nm) for nm in L3_SEMS}
        SER_SEMS.clear()
        SER_SEMS.extend([sems["mm"], sems["dv"], sems["sc"]])
        with contextlib.ExitStack() as st_b:
            _l3_emit(nc, st_b, sems, D)
    return nc


def _l3_headpair(nc, G, jp, D, s1_dram, pair_done):
    mm, dv, sc = G.mm, G.dv, G.sc
    pb = G.pb
    kbsem = [G.kb0, G.kb1]
    vbsem = [G.vb0, G.vb1]
    wsem = [G.w0, G.w1]
    bi = jp % 2

    em = Em(nc)
    dguard = [(mm.h, pair_done[jp - 2])] if jp >= 2 else [(G.c.h, G.c.v)]
    gate = [D.kv_gate] if (getattr(D, "kf", None) is None and jp < 2) else []
    if getattr(D, "kf", None) is not None:
        for js in range(4):
            em.op("sync", (lambda eng, jp=jp, bi=bi, js=js: eng.dma_start(
                A(G.Kb[bi], js * (KTOT // 4), [[KTOT, 128], [1, KTOT // 4]]),
                A(D.kf, jp * KTOT + js * (KTOT // 4),
                  [[NJ * KTOT, 128], [1, KTOT // 4]]))),
                waits=dguard if js == 0 else [], inc=kbsem[bi], k=16)
        w_kb = kbsem[bi].at()
    else:
        for sl in range(2):
            for t in range(NT):
                em.op("sync", (lambda eng, jp=jp, bi=bi, sl=sl, t=t: eng.dma_start(
                    A(G.Kb[bi], sl * TOK + t * TT, [[KTOT, 128], [1, TT]]),
                    A(D.kgs[t], sl * 128 * NJ * TT + jp * TT,
                      [[NJ * TT, 128], [1, TT]]))),
                    waits=(dguard + gate) if (sl == 0 and t == 0) else [],
                    inc=kbsem[bi], k=16)
        w_kb = kbsem[bi].at()
    em.op("sync", (lambda eng, jp=jp, bi=bi: eng.dma_start(
        G.wbuf[bi][:], A(D.wq1, 128 * jp, [[DM, 128], [128 * DM, NJ], [1, 128]]))),
        waits=list(dguard), inc=wsem[bi], k=16)
    w_w = wsem[bi].at()
    em.op("vector", (lambda eng, bi=bi: eng.tensor_copy(G.Kbr[:, :], G.Kb[bi][:, :])),
          waits=[w_kb] + ([(mm.h, pair_done[jp - 1])] if jp >= 1 else []), inc=dv)
    w_kr = dv.at()

    prevg = [(mm.h, pair_done[jp - 1])] if jp >= 1 else []
    em.op("vector", (lambda eng, bi=bi: eng.tensor_copy(
        G.wqb[:, :], G.wbuf[bi][:, :])), waits=[w_w] + list(prevg), inc=dv)
    w_wqc = dv.at()
    qev = {}
    qmm = {}
    for qc in range(4):
        def fc(eng, qc=qc):
            for kj in range(NJ):
                ii = eng.tensor_copy(
                    A(G.bXNb, kj * 512, [[NJ * 512, 128], [1, 512]]),
                    A(G.bXN, kj * TOK + qc * 512, [[NJ * TOK, 128], [1, 512]]))
            return ii
        em.op("vector", fc,
              waits=([(mm.h, qmm[qc - 1])] if qc >= 1 else list(prevg)), inc=dv)
        w_xc = dv.at()

        def f(eng, qc=qc):
            for kj in range(NJ):
                ii = eng.matmul(pb[6 + qc % 2][:, :],
                                A(G.wqb, kj * 128, [[NJ * 128, 128], [1, 128]]),
                                A(G.bXNb, kj * 512, [[NJ * 512, 128], [1, 512]]),
                                start=(kj == 0), stop=(kj == NJ - 1))
            return ii
        waits = [w_wqc, w_xc] + ([(dv.h, qev[qc - 2])] if qc >= 2 else [])
        em.op("tensor", f, waits=waits, inc=mm)
        qmm[qc] = mm.v
        em.op("vector", (lambda eng, qc=qc: eng.tensor_copy(
            A(G.Qb, qc * 512, [[TOK, 128], [1, 512]]), pb[6 + qc % 2][:, :])),
            waits=[mm.at()] + list(prevg), inc=dv)
        qev[qc] = dv.v
    w_q = dv.at()
    em.op("vector", lambda eng: eng.tensor_copy(G.Qbr[:, :], G.Qb[:, :]), inc=dv)
    w_qr = dv.at()

    def emit_extract(hh):
        em.op("gpsimd", (lambda eng, hh=hh: eng.dma_start(
            G.Qe[0:64, :], A(G.Qb, hh * 64 * TOK, [[TOK, 64], [1, TOK]]))),
            waits=[w_q, (mm.h, mm.v)], inc=G.qe, k=16)
        em.op("gpsimd", (lambda eng, hh=hh, bi=bi: eng.dma_start(
            G.Ke[0:64, :], A(G.Kb[bi], hh * 64 * KTOT, [[KTOT, 64], [1, KTOT]]))),
            waits=[w_kb, (mm.h, mm.v)], inc=G.ke, k=16)
        if getattr(D, "vf", None) is not None:
            em.op("gpsimd", (lambda eng, jp=jp, hh=hh: eng.dma_start(
                A(G.Ve, 0, [[32 * 65, 128], [65, 32], [1, 64]]),
                A(D.vf, jp * 128 + hh * 64, [[32 * DM, 128], [DM, 32], [1, 64]]))),
                waits=[(mm.h, mm.v)], inc=G.vb0, k=16)
        else:
            for sl in range(2):
                for t in range(NT):
                    em.op("gpsimd", (lambda eng, jp=jp, hh=hh, sl=sl, t=t: eng.dma_start(
                        A(G.Ve, (sl * 16 + 4 * t) * 65, [[32 * 65, 128], [65, 4], [1, 64]]),
                        A(D.vgs[t], sl * 128 * 4 * DM + jp * 128 + hh * 64,
                          [[4 * DM, 128], [DM, 4], [1, 64]]))),
                        waits=([(mm.h, mm.v)] + gate) if (sl == 0 and t == 0) else [],
                        inc=G.vb0, k=16)
        return G.qe.at(), G.ke.at(), G.vb0.at()

    def gen_passA(hh):
        red_marks = {}
        for qt in range(16):
            for kt in range(8):
                n = qt * 8 + kt
                bk = pb[n % 4]
                if n == 0:
                    waits = [w_qr, w_kr, (dv.h, dv.v)]
                elif n >= 4:
                    waits = [(dv.h, red_marks[n - 4])]
                else:
                    waits = []
                em.op("tensor", (lambda eng, qt=qt, kt=kt, bk=bk, hh=hh: eng.matmul(
                    bk[:, :],
                    A(G.Qbr, hh * 64 * TOK + qt * 128, [[TOK, 64], [1, 128]]),
                    A(G.Kbr, hh * 64 * KTOT + kt * 512, [[KTOT, 64], [1, 512]]),
                    start=True, stop=True)),
                    waits=waits, inc=mm)
                em.op("vector", (lambda eng, n=n, bk=bk: eng.tensor_reduce(
                    G.mc[:, n:n + 1], bk[:, :], AX.X, ALU.max)),
                    waits=[mm.at()], inc=dv)
                red_marks[n] = dv.v
                yield

    def finalize_passA(hh, h):
        for qt in range(16):
            em.op("vector", (lambda eng, qt=qt: eng.tensor_reduce(
                G.msb[:, qt:qt + 1], G.mc[:, qt * 8:qt * 8 + 8], AX.X, ALU.max)),
                inc=dv)
        em.op("vector", lambda eng: eng.memset(G.lb[:, :], 0.0), inc=dv)
        em.op("vector", lambda eng: eng.memset(G.rl[:, :], 0.0), inc=dv)
        em.op("vector", lambda eng: eng.tensor_scalar(
            G.msb[:, :], G.msb[:, :], -1.0, None, ALU.mult), inc=dv)
        w_ma = dv.at()

        em.op("tensor", lambda eng: eng.transpose(
            pb[2][0:16, 0:128], G.msb[:, :], G.id128[:, :]),
            waits=[w_ma, (sc.h, sc.v)], inc=mm)
        em.op("vector", lambda eng: eng.tensor_copy(G.m16[:, :], pb[2][0:16, 0:128]),
              waits=[mm.at()], inc=dv)
        em.op("vector", lambda eng: eng.memset(G.mc[:, :], 0.0), inc=dv)
        em.op("vector", lambda eng: eng.memset(G.lb[:, :], 0.0), inc=dv)
        em.op("gpsimd", lambda eng: eng.dma_start(
            A(G.Qe, 64 * TOK, [[TOK, 1], [128, 16], [1, 128]]),
            A(G.m16, 0, [[128, 16], [1, 128]])), waits=[dv.at()], inc=G.me, k=16)
        w_m = G.me.at()
        if getattr(G, "dbg_d", None) is not None:
            em.op("gpsimd", (lambda eng, h=h: eng.dma_start(
                A(G.dbg_d, h * TOK, [[TOK, 1], [1, TOK]]),
                A(G.Qe, 64 * TOK, [[TOK, 1], [1, TOK]]))),
                waits=[w_m], inc=G.me, k=16)
        return w_m

    def emit_passB(hh, w_m, w_qe, w_ke, w_ve, stepper=None):
        pvs = G.pvs
        for qq in range(4):
            exp_marks = {}
            pv_marks = {}

            def emit_pv(kt, qq=qq):
                def f(eng, kt=kt):
                    return eng.matmul(
                        pb[4][0:65, :],
                        A(G.Ve, kt * 65, [[32 * 65, 128], [1, 65]]),
                        A(G.Pt, (kt % 2) * 512, [[1024, 128], [1, 512]]),
                        start=(kt == 0), stop=(kt == 31))
                pvw = [(sc.h, exp_marks[kt])]
                if kt == 0:
                    pvw.append(w_ve)
                    if pvs["evac"]:
                        pvw.append((dv.h, pvs["evac"]))
                em.op("tensor", f, waits=pvw, inc=mm)
                pv_marks[kt] = mm.v

            for kt in range(32):
                sb = pb[2 + kt % 2]
                swaits = [w_m, w_ke, w_qe]
                if kt >= 2:
                    swaits.append((sc.h, exp_marks[kt - 2]))
                em.op("tensor", (lambda eng, qq=qq, kt=kt, sb=sb: eng.matmul(
                    sb[:, :],
                    A(G.Ke, kt * 128, [[KTOT, 65], [1, 128]]),
                    A(G.Qe, qq * 512, [[TOK, 65], [1, 512]]),
                    start=True, stop=True)),
                    waits=swaits if kt < 2 else swaits[-1:], inc=mm)
                em.op("scalar", (lambda eng, kt=kt, sb=sb: eng.activation(
                    A(G.Pt, (kt % 2) * 512, [[1024, 128], [1, 512]]), sb[:, :], AF.Exp)),
                    waits=[mm.at()] + ([(mm.h, pv_marks[kt - 2])] if kt >= 2 else []),
                    inc=sc)
                exp_marks[kt] = sc.v
                if stepper is not None:
                    stepper()
                if kt >= 1:
                    emit_pv(kt - 1)
            emit_pv(31)
            w_pv = mm.at()
            def f(eng):
                eng.activation(G.rlr[0:1, :], pb[4][64:65, :], AF.Ln, scale=8.0)
                return eng.activation(G.rlr[0:1, :], G.rlr[0:1, :], AF.Exp, scale=-1.0)
            em.op("scalar", f, waits=[w_pv], inc=sc)
            w_rl = sc.at()
            em.op("tensor", (lambda eng: eng.matmul(
                pb[5][0:64, :], G.ones_r[:, :], G.rlr[0:1, :], start=True, stop=True)),
                waits=[w_rl] + ([(dv.h, pvs["rlb"])] if pvs["rlb"] else []), inc=mm)
            em.op("vector", (lambda eng: eng.tensor_copy(G.rlb[:, :], pb[5][0:64, :])),
                  waits=[mm.at()], inc=dv)
            pvs["rlb"] = dv.v
            sg = G.stage[pvs["idx"] % 2]
            tw = [(dv.h, pvs["rlb"])]
            prev_st = pvs["st"].get(pvs["idx"] - 2)
            if prev_st:
                tw.append((G.s1st.h, prev_st))
            em.op("vector", (lambda eng, sg=sg: eng.tensor_tensor(
                sg[:, :], pb[4][0:64, :], G.rlb[:, :], ALU.mult)), waits=tw, inc=dv)
            pvs["evac"] = dv.v
            em.op("gpsimd", (lambda eng, jp=jp, hh=hh, qq=qq, sg=sg: eng.dma_start(
                A(s1_dram, hh * 64 * NJ * TOK + jp * TOK + qq * 512,
                  [[NJ * TOK, 64], [1, 512]]), sg[:, :])),
                waits=[(dv.h, pvs["evac"])], inc=G.s1st, k=16)
            pvs["st"][pvs["idx"]] = G.s1st.v
            pvs["idx"] += 1

    w_qe, w_ke, w_ve = emit_extract(0)
    for _ in gen_passA(0):
        pass
    w_m = finalize_passA(0, 2 * jp)
    emit_passB(0, w_m, w_qe, w_ke, w_ve)
    w_qe, w_ke, w_ve = emit_extract(1)
    for _ in gen_passA(1):
        pass
    w_m = finalize_passA(1, 2 * jp + 1)
    emit_passB(1, w_m, w_qe, w_ke, w_ve)

    pair_done[jp] = mm.v
    pair_done[("dv", jp)] = dv.v
    em.flush()


def _l3_rwkv_tile(nc, G, t, x1_d, wr_d, wkr_d, wvr_d, s1_dram, out_d):
    mm, dv, sc = G.mm, G.dv, G.sc
    pb = G.pb
    wsem = [G.w0, G.w1]
    t0 = t * TT

    def xnv(j, lo=0, n=TT):
        return A(G.bXN, j * TOK + t0 + lo, [[NJ * TOK, 128], [1, n]])

    def xv(buf, j, lo=0, n=TT):
        return A(buf, j * TT + lo, [[PITCH, 128], [1, n]])

    em = Em(nc)

    def emit_mix(mu_sb, dst, gate):
        for j in range(NJ):
            if t == 0:
                def f(eng, j=j, mu_sb=mu_sb, dst=dst):
                    eng.tensor_tensor(xv(G.bMX, j, 1, TT - 1), xnv(j, 1, TT - 1),
                                      xnv(j, 0, TT - 1), ALU.subtract)
                    eng.scalar_tensor_tensor(xv(dst, j, 1, TT - 1),
                                             xv(G.bMX, j, 1, TT - 1), mu_sb[:, j:j + 1],
                                             xnv(j, 0, TT - 1), ALU.mult, ALU.add)
                    eng.tensor_tensor(xv(G.bMX, j, 0, 1), xnv(j, 0, 1),
                                      G.bcol[:, j:j + 1], ALU.subtract)
                    return eng.scalar_tensor_tensor(xv(dst, j, 0, 1), xv(G.bMX, j, 0, 1),
                                                    mu_sb[:, j:j + 1], G.bcol[:, j:j + 1],
                                                    ALU.mult, ALU.add)
            else:
                def f(eng, j=j, mu_sb=mu_sb, dst=dst):
                    eng.tensor_tensor(xv(G.bMX, j), xnv(j), xnv(j, -1), ALU.subtract)
                    return eng.scalar_tensor_tensor(xv(dst, j), xv(G.bMX, j),
                                                    mu_sb[:, j:j + 1], xnv(j, -1),
                                                    ALU.mult, ALU.add)
            em.op("vector", f, waits=list(gate) if j == 0 else [], inc=dv)
        return dv.at()

    w_xr = emit_mix(G.mur, G.bKA, [(mm.h, mm.v)])

    def proj(w_d, rhsbuf, evac, gate, bankbase=4):
        grp = {}
        evm = {}
        wc = {}
        for oc in range(NJ):
            bi2 = oc % 2
            dwaits = ([(dv.h, wc[oc - 2])] if oc >= 2 else
                      ([(dv.h, G.rw_wcp)] if G.rw_wcp is not None else []) + list(gate))
            em.op("sync", (lambda eng, oc=oc, bi2=bi2, w_d=w_d: eng.dma_start(
                G.wbuf[bi2][:], A(w_d, 128 * oc, [[DM, 128], [128 * DM, NJ], [1, 128]]))),
                waits=dwaits, inc=wsem[bi2], k=16)
            w_wl = wsem[bi2].at()
            cwaits = [w_wl] + ([(mm.h, grp[oc - 2])] if oc >= 2 else
                               ([(mm.h, G.rw_mmp)] if G.rw_mmp is not None else []))
            em.op("vector", (lambda eng, bi2=bi2: eng.tensor_copy(
                G.wbufb[bi2][:, :], G.wbuf[bi2][:, :])), waits=cwaits, inc=dv)
            wc[oc] = dv.v
            w_wc = dv.at()
            bank = pb[bankbase + bi2]
            mwaits = [w_wc] + list(gate) + ([(sc.h, evm[oc - 2])] if oc >= 2 else [])

            def f(eng, bi2=bi2, bank=bank, rhsbuf=rhsbuf):
                for kj in range(NJ):
                    ii = eng.matmul(bank[:, :],
                                    A(G.wbufb[bi2], kj * 128, [[NJ * 128, 128], [1, 128]]),
                                    xv(rhsbuf, kj), start=(kj == 0), stop=(kj == NJ - 1))
                return ii
            em.op("tensor", f, waits=mwaits, inc=mm)
            grp[oc] = mm.v
            evac(oc, bank)
            evm[oc] = sc.v
        G.rw_wcp = wc[NJ - 1]
        G.rw_mmp = grp[NJ - 1]
        return mm.at()

    def ev_r(oc, bank):
        em.op("scalar", (lambda eng, oc=oc, bank=bank: eng.activation(
            xv(G.bR, oc), bank[:, :], AF.Sigmoid)), waits=[(mm.h, mm.v)], inc=sc)
    w_rproj = proj(wr_d, G.bKA, ev_r, [w_xr])

    w_xk = emit_mix(G.muk, G.bMX, [(mm.h, w_rproj[1])])

    def ev_k(oc, bank):
        em.op("vector", (lambda eng, oc=oc, bank=bank: eng.tensor_scalar(
            xv(G.bKA, oc), bank[:, :], 0.0, None, ALU.max)), waits=[(mm.h, mm.v)], inc=dv)
        em.op("scalar", (lambda eng, oc=oc: eng.activation(
            xv(G.bKA, oc), xv(G.bKA, oc), AF.Square)), waits=[dv.at()], inc=sc)
    w_kproj = proj(wkr_d, G.bMX, ev_k, [w_xk])

    def ev_v(oc, bank):
        em.op("vector", (lambda eng, oc=oc, bank=bank: eng.tensor_tensor(
            xv(G.bG, oc), bank[:, :], xv(G.bR, oc), ALU.mult)), waits=[(mm.h, mm.v)], inc=dv)
        em.op("scalar", (lambda eng, oc=oc: eng.activation(
            xv(G.bG, oc), xv(G.bG, oc), AF.Silu)), waits=[dv.at()], inc=sc)
    w_vproj = proj(wvr_d, G.bKA, ev_v, [w_kproj, (sc.h, sc.v)])

    bS1 = G.bS1[t % 2]
    s1w = [G.s1st.at()]
    if t >= 2:
        s1w.append((dv.h, G.rw_cmb[t - 2]))
    for js in range(0, NJ, 2):
        em.op("sync", (lambda eng, t=t, js=js, bS1=bS1: eng.dma_start(
            A(bS1, js * TT, [[PITCH, 128], [TT, 2], [1, TT]]),
            A(s1_dram, t0 + js * TOK, [[NJ * TOK, 128], [TOK, 2], [1, TT]]))),
            waits=s1w if js == 0 else [], inc=G.s1ld, k=16)
    w_s1 = G.s1ld.at()

    for js in range(0, NJ, 2):
        em.op("sync", (lambda eng, t=t, js=js: eng.dma_start(
            A(G.bX1, js * TT, [[PITCH, 128], [TT, 2], [1, TT]]),
            A(x1_d, t0 + js * TOK, [[NJ * TOK, 128], [TOK, 2], [1, TT]]))),
            waits=[G.ost.at()] if js == 0 else [], inc=G.x1ld, k=16)
    w_x1 = G.x1ld.at()
    for j in range(NJ):
        def f(eng, j=j, bS1=bS1):
            eng.tensor_tensor(xv(bS1, j), xv(bS1, j), xv(G.bG, j), ALU.mult)
            return eng.tensor_tensor(xv(G.bX1, j), xv(G.bX1, j), xv(bS1, j), ALU.add)
        em.op("vector", f, waits=[w_x1, w_s1, (sc.h, sc.v)] if j == 0 else [], inc=dv)
    G.rw_cmb[t] = dv.v
    for js in range(0, NJ, 2):
        em.op("gpsimd", (lambda eng, t=t, js=js: eng.dma_start(
            A(out_d, t0 + js * TOK, [[NJ * TOK, 128], [TOK, 2], [1, TT]]),
            A(G.bX1, js * TT, [[PITCH, 128], [TT, 2], [1, TT]]))),
            waits=[dv.at()] if js == 0 else [], inc=G.ost, k=16)

    em.flush()


def l3_inputs(l2outA, l2outB, ins, half, x1, xn1, bcol):
    kf = np.empty((128, NJ, KTOT), np.float32)
    vf = np.empty((128, 32, DM), np.float32)
    for sl, r in enumerate((l2outA, l2outB)):
        for t in range(NT):
            kf[:, :, sl * TOK + t * TT:sl * TOK + (t + 1) * TT] = r[f"k1_{t}"]
            vf[:, sl * 16 + 4 * t:sl * 16 + 4 * t + 4, :] = r[f"v1_{t}"]
    return {
        "x1": x1, "xn1": xn1, "kf": kf, "vf": vf,
        "wq1": np.ascontiguousarray(ins["L1_attn_Wq"].transpose(1, 0, 2).reshape(DM, DM)),
        "wr": np.ascontiguousarray(ins["L1_ffn_W_r"].T),
        "wkr": np.ascontiguousarray(ins["L1_ffn_W_k"].T),
        "wvr": np.ascontiguousarray(ins["L1_ffn_W_v"].T),
        "mur": _vec128(ins["L1_ffn_mu_r"]), "muk": _vec128(ins["L1_ffn_mu_k"]),
        "bcol": bcol, "id128": np.eye(128, dtype=np.float32),
    }



def _fm(a):
    T = a.shape[0]
    return np.ascontiguousarray(a.T.reshape(NJ, 128, T).transpose(1, 0, 2))


def _from_fm(a):
    return np.ascontiguousarray(a.transpose(1, 0, 2).reshape(DM, -1).T)


def _vec128(v):
    return np.ascontiguousarray(v.reshape(-1, 128).T)


def _build_consts():
    eye64t = np.tile(np.eye(64, dtype=np.float32), (2, 2)).astype(np.float32)
    tril = np.tril(np.ones((64, 64), np.float32))
    cmask = np.tile(np.where(tril > 0, 0.0, 8.0 * NEG).astype(np.float32), (2, 8))
    strict = (np.arange(64)[:, None] < np.arange(64)[None, :]).astype(np.float32)
    smask = np.tile(strict, (2, 8))
    return eye64t, np.ascontiguousarray(cmask), np.ascontiguousarray(smask)


def l1_inputs(x_half, ins):
    eye64t, cmask, smask = _build_consts()
    return {
        "x": _fm(x_half),
        "wq": ins["L0_attn_Wq"], "wk": ins["L0_attn_Wk"], "wv": ins["L0_attn_Wv"],
        "bq": _vec128(ins["L0_attn_bq"]), "bk": _vec128(ins["L0_attn_bk"]),
        "bv": ins["L0_attn_bv"].reshape(1, DM),
        "wg1": ins["L0_attn_Wg1"], "bg1": ins["L0_attn_bg1"].reshape(GH, 1),
        "wg2": ins["L0_attn_Wg2"], "bg2": _vec128(ins["L0_attn_bg2"]),
        "w1t": np.ascontiguousarray(ins["L0_ffn_w1"].T),
        "w2t": np.ascontiguousarray(ins["L0_ffn_w2"].T),
        "id64": eye64t, "cmask": cmask, "smask": smask,
    }


RG_PAIRS = [[0, 1], [2, 3], [4, 5], [6, 7]]


def _uniq_names(nc):
    sb0, ps0 = nc.sbuf_tensor, nc.psum_tensor
    ctr = [0]

    def sb(name, *a, **k):
        ctr[0] += 1
        return sb0(f"{name}_u{ctr[0]}", *a, **k)

    def ps(name, *a, **k):
        ctr[0] += 1
        return ps0(f"{name}_u{ctr[0]}", *a, **k)
    nc.sbuf_tensor, nc.psum_tensor = sb, ps


def _join(em, jn, terms):
    base = jn.v
    for e in Em.ENG:
        em.op(e, lambda eng: eng.nop(), waits=(list(terms) if e == "sync" else []),
              inc=jn, k=1)
    for e in Em.ENG:
        em.op(e, lambda eng: eng.nop(), waits=[(jn.h, base + 5)])


def build_fused():
    nc = bass.Bass("TRN2", target_bir_lowering=False)
    _uniq_names(nc)
    D = _l1_dram(nc, kind_out="Internal", t_dt=BF16)
    D.flag = nc.dram_tensor("flag", [128, 1], F32, kind="ExternalInput")
    D.sendg = nc.dram_tensor("sendg", [2, 128, 512], F32)
    D.kgs = [nc.dram_tensor(f"kg_{t}", [2, 128, NJ, TT], F32) for t in range(NT)]
    D.vgs = [nc.dram_tensor(f"vg_{t}", [2, 128, 4, DM], F32) for t in range(NT)]
    D.bcg = nc.dram_tensor("bcg", [2, 128, NJ], F32)
    D.xn1col = nc.dram_tensor("xn1col", [128, NJ], F32)
    _l2_dram(nc, kind_out="Internal", D=D)
    _l3_dram(nc, D=D)
    D.sp_ap = A(D.sendg, 0, [[512, 128], [1, 512]])

    with contextlib.ExitStack() as st:
        sems = {nm: Sem(st, nc, nm) for nm in L3_SEMS}
        jn = Sem(st, nc, "jn")
        SER_SEMS.clear()
        SER_SEMS.extend([sems["mm"], sems["dv"], sems["sc"]])
        cc1, cc2 = sems["cc1"], sems["cc2"]

        with contextlib.ExitStack() as st1:
            G1 = _l1_emit(nc, st1, sems, D)
            em = Em(nc)
            em.op("gpsimd", lambda eng: eng.collective_compute(
                "AllGather", ALU.bypass, replica_groups=RG_PAIRS,
                ins=[D.send.ap().opt()], outs=[D.sendg.ap().opt()]),
                waits=[G1.fin.at()], inc=cc1, k=1)
            _join(em, jn, [G1.fin.at(), G1.xns.at(), G1.ss.at(), G1.qs.at(),
                           G1.l0.at(), G1.l1.at(), G1.t0.at(), G1.t1.at()])
            em.flush()

        D.sp_gate = (cc1.h, 1)
        with contextlib.ExitStack() as st2:
            G2 = _l2_emit(nc, st2, sems, D)
            em = Em(nc)
            l2_term = [G2.sst.at(), G2.l0.at(), G2.l1.at()]
            _join(em, jn, l2_term)
            em.flush()

        D.kv_gate = (cc2.h, 2 * NT)
        D.bc_gate = (cc1.h, 2)
        with contextlib.ExitStack() as st3:
            _l3_emit(nc, st3, sems, D)
    return nc


def fused_inputs(x_half, half, ins):
    d = l1_inputs(x_half, ins)
    d.update({
        "l1g": _vec128(ins["L0_ffn_ln1_g"]), "l1b": _vec128(ins["L0_ffn_ln1_b"]),
        "b1": _vec128(ins["L0_ffn_b1"]), "l2g": _vec128(ins["L0_ffn_ln2_g"]),
        "l2b": _vec128(ins["L0_ffn_ln2_b"]), "b2": _vec128(ins["L0_ffn_b2"]),
        "wk1": np.ascontiguousarray(ins["L1_attn_Wk"].transpose(1, 0, 2).reshape(DM, DM)),
        "wv1": np.ascontiguousarray(ins["L1_attn_Wv"].transpose(1, 0, 2).reshape(DM, DM)),
        "wq1": np.ascontiguousarray(ins["L1_attn_Wq"].transpose(1, 0, 2).reshape(DM, DM)),
        "wr": np.ascontiguousarray(ins["L1_ffn_W_r"].T),
        "wkr": np.ascontiguousarray(ins["L1_ffn_W_k"].T),
        "wvr": np.ascontiguousarray(ins["L1_ffn_W_v"].T),
        "mur": _vec128(ins["L1_ffn_mu_r"]), "muk": _vec128(ins["L1_ffn_mu_k"]),
        "id128": np.eye(128, dtype=np.float32),
        "flag": np.full((128, 1), float(half), np.float32),
    })
    return d


_NC_CACHE = {}


def _get_ncs():
    if "ncs" not in _NC_CACHE:
        _NC_CACHE["ncs"] = (build_launch1(), build_launch2(), build_launch3())
    return _NC_CACHE["ncs"]


def _get_fused():
    if "fused" not in _NC_CACHE:
        _NC_CACHE["fused"] = build_fused()
    return _NC_CACHE["fused"]


def kernel(**inputs):
    import os
    if bool(int(os.environ.get("KERNEL_UNFUSED", "0"))):
        return _kernel_unfused(**inputs)
    ins = inputs
    nc = _get_fused()
    x_full = np.asarray(ins["L0_attn_x"], dtype=np.float32)
    cores = list(range(NCORES))
    im = []
    for core in cores:
        b, half = core // 2, core % 2
        x_half = np.ascontiguousarray(x_full[b, half * TOK:(half + 1) * TOK, :])
        im.append(fused_inputs(x_half, half, ins))
    r = run_bass_kernel_spmd(nc, im, core_ids=cores).results
    out = np.empty((4, 2 * TOK, DM), dtype=np.float32)
    for core in cores:
        b, half = core // 2, core % 2
        out[b, half * TOK:(half + 1) * TOK, :] = _from_fm(r[core]["out"])
    return out


def _kernel_unfused(**inputs):
    ins = inputs
    nc1, nc2, nc3 = _get_ncs()
    x_full = np.asarray(ins["L0_attn_x"], dtype=np.float32)
    cores = list(range(NCORES))

    im1 = []
    for core in cores:
        b, half = core // 2, core % 2
        x_half = np.ascontiguousarray(x_full[b, half * TOK:(half + 1) * TOK, :])
        im1.append(l1_inputs(x_half, ins))
    r1 = run_bass_kernel_spmd(nc1, im1, core_ids=cores).results

    im2 = []
    for core in cores:
        b, half = core // 2, core % 2
        r1c = dict(r1[core])
        r1c["__x"] = im1[core]["x"]
        sp = (np.ascontiguousarray(r1[core - 1]["send"]) if half == 1
              else np.zeros((128, 512), np.float32))
        im2.append(l2_inputs(r1c, ins, sp))
    r2 = run_bass_kernel_spmd(nc2, im2, core_ids=cores).results

    im3 = []
    for core in cores:
        b, half = core // 2, core % 2
        a, bb = 2 * b, 2 * b + 1
        bcol = (np.ascontiguousarray(r2[a]["xn1"][:, :, -1]) if half == 1
                else np.zeros((128, NJ), np.float32))
        im3.append(l3_inputs(r2[a], r2[bb], ins, half,
                             np.ascontiguousarray(r2[core]["x1"]),
                             np.ascontiguousarray(r2[core]["xn1"]), bcol))
    r3 = run_bass_kernel_spmd(nc3, im3, core_ids=cores).results

    out = np.empty((4, 2 * TOK, DM), dtype=np.float32)
    for core in cores:
        b, half = core // 2, core % 2
        out[b, half * TOK:(half + 1) * TOK, :] = _from_fm(r3[core]["out"])
    return out
